# version 14
# baseline (speedup 1.0000x reference)
import numpy as np
import concourse.bass as bass
import concourse.mybir as mybir
from concourse.bass_utils import run_bass_kernel_spmd

F32 = mybir.dt.float32
BF16 = mybir.dt.bfloat16
I32 = mybir.dt.int32
AF = mybir.ActivationFunctionType
ALU = mybir.AluOpType
AX = mybir.AxisListType

D = 2048
NB = 8
HK, HV, DK, DV = 16, 32, 128, 128
KC = 4
NE, FF = 8, 2048
C = 128
EPS = 1e-6
LNEPS = 1e-5
BIG = 30000.0
BIGM = 1e7

ENGINES = ("tensor", "vector", "scalar", "gpsimd", "sync")
_DTB = {F32: 4, BF16: 2, I32: 4}


class _Step:
    __slots__ = ("engine", "emit", "deps", "dma", "idx", "tag", "need_inc",
                 "inc_idx", "dma_idx")

    def __init__(self, engine, emit, dma, tag):
        self.engine = engine
        self.emit = emit
        self.dma = dma
        self.tag = tag
        self.deps = []
        self.need_inc = False
        self.inc_idx = None
        self.dma_idx = None


class Prog:

    def __init__(self):
        self.steps = {e: [] for e in ENGINES}
        self.last_w = {}
        self.readers = {}
        self.pending_bar = {e: [] for e in ENGINES}

    def step(self, engine, emit, reads=(), writes=(), dma=False, tag="",
             sereads=()):
        st = _Step(engine, emit, dma, tag)
        deps = set()
        forced = set()
        for r in reads:
            w = self.last_w.get(r)
            if w is not None:
                deps.add(w)
        for r in sereads:
            w = self.last_w.get(r)
            if w is not None:
                deps.add(w)
                forced.add(w)
        for w_ in writes:
            pw = self.last_w.get(w_)
            if pw is not None:
                deps.add(pw)
            for rd in self.readers.get(w_, ()):
                deps.add(rd)
        for d in self.pending_bar[engine]:
            deps.add(d)
        self.pending_bar[engine] = []
        st.deps = [d for d in deps
                   if d in forced or not (d.engine == engine and (d.dma == dma))]
        st.idx = len(self.steps[engine])
        self.steps[engine].append(st)
        for r in list(reads) + list(sereads):
            self.readers.setdefault(r, []).append(st)
        for w_ in writes:
            self.last_w[w_] = st
            self.readers[w_] = []
        return st

    def barrier(self):
        sigs = []
        for e in ENGINES:
            comp = next((s for s in reversed(self.steps[e]) if not s.dma), None)
            dmas = next((s for s in reversed(self.steps[e]) if s.dma), None)
            if comp is not None:
                sigs.append(comp)
            if dmas is not None:
                sigs.append(dmas)
        for e in ENGINES:
            self.pending_bar[e] = [s for s in sigs
                                   if not (s.engine == e and not s.dma)]

    def finalize(self):
        for e in ENGINES:
            for st in self.steps[e]:
                for d in st.deps:
                    d.need_inc = True
        for e in ENGINES:
            ninc = 0
            ndma = 0
            for st in self.steps[e]:
                if st.dma:
                    ndma += 1
                    st.dma_idx = ndma
                elif st.need_inc:
                    ninc += 1
                    st.inc_idx = ninc

    def emit_engine(self, engine, eng, sems, dsems):
        waited = {}
        for st in self.steps[engine]:
            for d in st.deps:
                if d.dma:
                    sem, val = dsems[d.engine], 16 * d.dma_idx
                else:
                    sem, val = sems[d.engine], d.inc_idx
                    assert val is not None, (st.tag, d.tag)
                key = id(sem)
                if waited.get(key, -1) < val:
                    eng.wait_ge(sem, val)
                    waited[key] = val
            inst = st.emit(eng)
            if st.dma:
                inst.then_inc(dsems[engine], 16)
            elif st.need_inc:
                inst.then_inc(sems[engine], 1)


def _np_cm384():
    t = np.arange(128)[:, None]
    f = np.arange(128)[None, :]
    s0 = np.where(f >= t, BIGM, 0.0)
    s1 = np.where(f <= t, -BIGM, 0.0)
    s2 = np.where(f < t, -BIGM, 0.0)
    return np.concatenate([s0, s1, s2], axis=1).astype(np.float32)


def make_consts(T):
    j = np.arange(128)[:, None]
    t = np.arange(128)[None, :]
    return {
        "c_ident": np.eye(128, dtype=np.float32),
        "c_cum": (j <= t).astype(np.float32),
        "c_rev": (j > t).astype(np.float32),
        "c_cm384": _np_cm384(),
        "c_onesrow": np.ones((1, 128), np.float32),
        "c_onescol": np.ones((128, 1), np.float32),
        "c_e127": (np.arange(128) == 127).astype(np.float32)[:, None],
        "c_iota": np.arange(T + 8, dtype=np.float32)[None, :],
        "c_iota8": np.tile(np.arange(8, dtype=np.float32), (128, 1)).copy(),
    }


class Ker:
    def __init__(self, T=2048, taps=(), phases="abcde"):
        self.phases = phases
        self.T = T
        self.nchunk = T // C
        self.TH = T // 2
        nc = bass.Bass("TRN2", target_bir_lowering=False, debug=False,
                       detect_race_conditions=False)
        self.nc = nc
        self.p = Prog()
        self.consts_np = make_consts(T)
        self.dr = {}
        self.sb = {}
        self.taps = dict()
        T_ = T

        def din(name, shape, dt=F32):
            self.dr[name] = nc.dram_tensor(name, list(shape), dt,
                                           kind="ExternalInput").ap()

        din("xT", [D, T_])
        din("Wqkv", [D, 4 * D])
        din("Wz", [D, 2 * D])
        din("Wab", [D, 64])
        din("convw", [4 * D, KC])
        din("A_log", [HV])
        din("dt_bias", [HV])
        din("norm_w", [DV])
        din("Wout", [2 * D, D])
        din("Wg", [NE, D, FF])
        din("Wu", [NE, D, FF])
        din("Wd", [NE, FF, D])
        din("Wr", [D, NE])
        din("temperature", [1])
        din("cu_seqlens", [5], I32)
        for k, v in self.consts_np.items():
            din(k, v.shape, F32)

        self.out = nc.dram_tensor("out", [D, T_], F32, kind="ExternalOutput").ap()
        self.qkv_s = nc.dram_tensor("qkv_s", [4 * D, T_], BF16, kind="Internal").ap()
        self.z_s = nc.dram_tensor("z_s", [T_, 2 * D], BF16, kind="Internal").ap()
        self.y_s = nc.dram_tensor("y_s", [2 * D, T_], BF16, kind="Internal").ap()
        self.ln2_s = nc.dram_tensor("ln2_s", [D, T_], BF16, kind="Internal").ap()
        for name, shape, dt in taps:
            self.taps[name] = nc.dram_tensor("tap_" + name, list(shape), dt,
                                             kind="ExternalOutput").ap()

        self.pb = [nc.alloc_psum_tensor("pb%d" % i, [128, 512], F32)
                   for i in range(8)]
        self._pcur = 16512
        self._top = 229344

    def persist(self, name, shape, dt):
        nbytes = int(np.prod(shape[1:])) * _DTB[dt]
        off = (self._pcur + 31) // 32 * 32
        t = self.nc.alloc_sbuf_tensor_at(name, shape, dt, offset=off)
        self._pcur = off + nbytes
        assert self._pcur <= self._top, (name, self._pcur)
        self.sb[name] = t
        return t

    def freeze_persist(self):
        self._abase = (self._pcur + 255) // 256 * 256

    def at(self, name, shape, dt, off):
        nbytes = int(np.prod(shape[1:])) * _DTB[dt]
        a = self._abase + off
        assert a + nbytes <= self._top, (name, a + nbytes, self._top)
        t = self.nc.alloc_sbuf_tensor_at(name, shape, dt, offset=a)
        self.sb[name] = t
        return t

    def build(self):
        self.alloc_persist()
        self.ph0()
        self.p.barrier()
        self.ph_a()
        self.p.barrier()
        self.ph_b()
        self.p.barrier()
        if "c" not in self.phases:
            self.emit()
            return self.nc
        self.ph_c()
        self.p.barrier()
        if "d" not in self.phases:
            self.emit()
            return self.nc
        for hh in range(2):
            self.ph_d(hh)
            self.p.barrier()
            self.ph_e(hh)
            self.p.barrier()
        for name, src in [("qkv_s", self.qkv_s), ("z_s", self.z_s),
                          ("y_s", self.y_s), ("ln2_s", self.ln2_s)]:
            if name in self.taps:
                self.p.step("sync", (lambda d_, s_: lambda eng: eng.dma_start(out=d_, in_=s_))(
                    self.taps[name], src), dma=True, tag="tap_" + name)
        self.emit()
        return self.nc

    def alloc_persist(self):
        P = self.persist
        T = self.T
        P("ident", [128, 128], F32)
        P("identb", [128, 128], BF16)
        P("cumt", [128, 128], F32)
        P("revt", [128, 128], F32)
        P("cm384", [128, 384], F32)
        P("onesrow", [1, 128], F32)
        P("onescol", [128, 1], F32)
        P("onescolb", [128, 1], BF16)
        P("onesrowb", [1, 128], BF16)
        P("e127", [128, 1], F32)
        P("iota8", [128, 8], F32)
        P("sr", [1, T + 8], BF16)
        P("msk", [128, 3 * (T + 8)], BF16)
        P("normwb", [128, DV], F32)
        P("abg", [128, T], F32)
        P("cw", [128, 64 * KC], F32)
        P("negexpa", [32, 1], F32)
        P("dtbias", [32, 1], F32)
        P("tmisc", [1, 16], F32)
        P("cvals", [128, 8], F32)
        P("iota8b", [128, 8], F32)
        P("neg18", [128, 8], F32)
        self.freeze_persist()

    def ph0(self):
        nc, p, sb, dr = self.nc, self.p, self.sb, self.dr
        T = self.T
        W = T + 8
        t_iota = self.at("t_iota", [1, W], F32, 110 * 1024)
        t_cu = self.at("t_cu", [1, 8], F32, 119 * 1024)
        t_cui = self.at("t_cui", [1, 8], I32, int(119.5 * 1024))
        t_row = self.at("t_row", [1, W], F32, 120 * 1024)
        t_sr32 = self.at("t_sr32", [1, W], F32, 129 * 1024)
        t_ns = self.at("t_ns", [1, W], F32, 138 * 1024)
        t_nw = self.at("t_nw", [1, DV], F32, 147 * 1024)
        t_al = self.at("t_al", [64, 1], F32, int(147.5 * 1024))
        t_mrow = self.at("t_mrow", [1, 3 * W], F32, 148 * 1024)

        for nm, dst in [("c_ident", sb["ident"]), ("c_cum", sb["cumt"]),
                        ("c_rev", sb["revt"]), ("c_cm384", sb["cm384"]),
                        ("c_onesrow", sb["onesrow"]), ("c_onescol", sb["onescol"]),
                        ("c_e127", sb["e127"]), ("c_iota8", sb["iota8"]),
                        ("c_iota", t_iota)]:
            p.step("sync",
                   (lambda dst_, src_: lambda eng: eng.dma_start(out=dst_[:], in_=src_))(dst, dr[nm]),
                   writes=[dst.name], dma=True, tag="ld_" + nm)
        for nm, dst in [("c_ident", sb["identb"]), ("c_onescol", sb["onescolb"]),
                        ("c_onesrow", sb["onesrowb"])]:
            p.step("gpsimd",
                   (lambda dst_, src_: lambda eng: eng.dma_start(out=dst_[:], in_=src_))(dst, dr[nm]),
                   writes=[dst.name], dma=True, tag="ldb_" + nm)
        p.step("sync", lambda eng: eng.dma_start(
            out=sb["cw"][:].rearrange("p (a k) -> p a k", k=KC),
            in_=dr["convw"].rearrange("(a p) k -> p a k", p=128)),
            writes=["cw"], dma=True, tag="ld_cw")
        p.step("sync", lambda eng: eng.dma_start(out=t_al[0:32, :], in_=dr["A_log"][:, None]),
               writes=["t_al"], dma=True, tag="ld_alog")
        p.step("sync", lambda eng: eng.dma_start(out=t_al[32:64, :], in_=dr["dt_bias"][:, None]),
               writes=["t_al"], dma=True, tag="ld_dtb")
        p.step("sync", lambda eng: eng.dma_start(out=t_nw[:], in_=dr["norm_w"][None, :]),
               writes=["t_nw"], dma=True, tag="ld_nw")
        p.step("sync", lambda eng: eng.dma_start(out=sb["tmisc"][0:1, 0:1], in_=dr["temperature"][None, :]),
               writes=["tmisc"], dma=True, tag="ld_temp")
        p.step("sync", lambda eng: eng.dma_start(out=t_cui[0:1, 0:5], in_=dr["cu_seqlens"][None, :]),
               writes=["t_cui"], dma=True, tag="ld_cu")

        p.step("scalar", lambda eng: eng.activation(sb["negexpa"][:], t_al[0:32, :], AF.Exp),
               reads=["t_al"], writes=["negexpa"], tag="acoef_exp")
        p.step("vector", lambda eng: eng.tensor_scalar(sb["negexpa"][:], sb["negexpa"][:], -1.0, None, ALU.mult),
               reads=["negexpa"], writes=["negexpa"], tag="acoef_neg")
        p.step("scalar", lambda eng: eng.copy(sb["dtbias"][:], t_al[32:64, :]),
               reads=["t_al"], writes=["dtbias"], tag="acoef_dtb")
        p.step("vector", lambda eng: eng.tensor_copy(t_cu[0:1, 0:5], t_cui[0:1, 0:5]),
               reads=["t_cui"], writes=["t_cu"], tag="cu_f32")

        sr = sb["sr"]

        def em_starts(eng):
            eng.memset(t_sr32[:], 0.0)
            for i in range(4):
                eng.tensor_scalar(t_row[:], t_iota[:], t_cu[0:1, i:i + 1], None, ALU.is_equal)
                eng.tensor_tensor(t_sr32[:], t_sr32[:], t_row[:], ALU.max)
                eng.tensor_scalar(t_row[:], t_iota[:], float(T), t_cu[0:1, i:i + 1], ALU.add, ALU.is_equal)
                eng.tensor_tensor(t_sr32[:], t_sr32[:], t_row[:], ALU.max)
            eng.memset(t_sr32[0:1, T:W], 0.0)
            return eng.tensor_copy(sr[:], t_sr32[:])
        p.step("vector", em_starts, reads=["t_iota"], sereads=["t_cu"],
               writes=["t_sr32", "sr", "t_row"], tag="starts")

        def em_masks_rows(eng):
            eng.tensor_scalar(t_ns[:], t_sr32[:], -1.0, 1.0, ALU.mult, ALU.add)
            eng.memset(t_mrow[:], 1.0)
            eng.tensor_copy(t_mrow[0:1, 3:3 + T], t_ns[0:1, 1:T + 1])
            eng.tensor_tensor(t_mrow[0:1, W + 3:W + 3 + T], t_mrow[0:1, 3:3 + T], t_ns[0:1, 2:T + 2], ALU.mult)
            return eng.tensor_tensor(t_mrow[0:1, 2 * W + 3:2 * W + 3 + T], t_mrow[0:1, W + 3:W + 3 + T], t_ns[0:1, 3:T + 3], ALU.mult)
        p.step("vector", em_masks_rows, reads=["t_sr32"],
               writes=["t_ns", "t_mrow"], tag="maskrows")

        pb, msk = self.pb, sb["msk"]
        nburst = (3 * W + 511) // 512
        for ib in range(nburst):
            c0 = ib * 512
            c1 = min(c0 + 512, 3 * W)
            bk = ib % 4
            p.step("tensor",
                   (lambda c0_, c1_, bk_: lambda eng: eng.matmul(
                       pb[bk_][:, 0:c1_ - c0_], sb["onesrow"][:],
                       t_mrow[0:1, c0_:c1_], start=True, stop=True))(c0, c1, bk),
                   reads=["t_mrow", "onesrow"], writes=["pb%d" % bk],
                   tag="mskb_mm%d" % ib)
            p.step("vector",
                   (lambda c0_, c1_, bk_: lambda eng: eng.tensor_copy(
                       msk[:, c0_:c1_], pb[bk_][:, 0:c1_ - c0_]))(c0, c1, bk),
                   reads=["pb%d" % bk], writes=["msk"], tag="mskb_ev%d" % ib)

        p.step("tensor", lambda eng: eng.matmul(pb[4][:, 0:DV], sb["onesrow"][:],
                                                t_nw[0:1, :], start=True, stop=True),
               reads=["t_nw", "onesrow"], writes=["pb4"], tag="nw_mm")
        p.step("vector", lambda eng: eng.tensor_copy(sb["normwb"][:], pb[4][:, 0:DV]),
               reads=["pb4"], writes=["normwb"], tag="nw_ev")
        p.step("vector", lambda eng: eng.reciprocal(sb["tmisc"][0:1, 1:2], sb["tmisc"][0:1, 0:1]),
               reads=["tmisc"], writes=["tmisc"], tag="invtemp")

        def em_cvals(eng):
            cv = sb["cvals"]
            eng.memset(cv[:, 0:1], LNEPS)
            eng.memset(cv[:, 1:2], 1.0)
            eng.memset(cv[:, 2:3], EPS)
            return eng.memset(cv[:, 3:4], 0.0)
        p.step("vector", em_cvals, writes=["cvals"], tag="cvals")

        def em_misc8(eng):
            eng.tensor_scalar(sb["iota8b"][:], sb["iota8"][:], 100.0, None, ALU.add)
            eng.memset(sb["neg18"][:], -1.0)
            return eng.memset(sb["cvals"][:, 5:6], float(-1.0 / np.e))
        p.step("vector", em_misc8, reads=["iota8"], writes=["iota8b", "neg18", "cvals"],
               tag="misc8")
        p.step("tensor", lambda eng: eng.matmul(pb[5][:, 0:1], sb["onesrow"][:],
                                                sb["tmisc"][0:1, 1:2], start=True, stop=True),
               reads=["tmisc", "onesrow"], writes=["pb5"], tag="it_mm")
        p.step("vector", lambda eng: eng.tensor_copy(sb["cvals"][:, 4:5], pb[5][:, 0:1]),
               reads=["pb5"], writes=["cvals"], tag="it_ev")

    def ph_a(self):
        nc, p, sb, dr, pb = self.nc, self.p, self.sb, self.dr, self.pb
        T = self.T
        TC = min(512, T)
        ln1x = self.at("ln1x", [128, 16 * T], BF16, 0)
        xa = self.at("xa", [128, 16 * TC], F32, 64 * 1024)
        xbf = self.at("xbf", [128, 16 * TC], BF16, 96 * 1024)
        xsq = self.at("xsq", [128, 16 * TC], BF16, 112 * 1024)
        arow = self.at("arow", [1, 4 * TC], F32, 128 * 1024)
        abc = self.at("abc", [128, 2 * TC], F32, 137 * 1024)
        xT = dr["xT"]
        onescolb, onesrow = sb["onescolb"], sb["onesrow"]

        for tc in range(T // TC):
            t0 = tc * TC
            for half in range(2):
                s0 = half * 8
                src = xT.rearrange("(dt p) t -> p dt t", p=128)[:, s0:s0 + 8, t0:t0 + TC]
                dst = xa[:].rearrange("p (dt t) -> p dt t", t=TC)[:, s0:s0 + 8, :]
                p.step("sync",
                       (lambda s_, d_: lambda eng: eng.dma_start(out=d_, in_=s_))(src, dst),
                       writes=["xa"], dma=True, tag="a_ldx%d_%d" % (tc, half))
            p.step("vector", lambda eng: eng.tensor_copy(xbf[:], xa[:]),
                   reads=["xa"], writes=["xbf"], tag="a_cast%d" % tc)
            p.step("scalar", lambda eng: eng.activation(xsq[:], xbf[:], AF.Square),
                   reads=["xbf"], writes=["xsq"], tag="a_sq%d" % tc)

            def em_stats(src_t, bank):
                def em(eng):
                    inst = None
                    v = src_t[:].rearrange("p (dt t) -> p dt t", t=TC)
                    for d in range(16):
                        inst = eng.matmul(bank[0:1, 0:TC], onescolb[:],
                                          v[:, d, :], start=(d == 0), stop=(d == 15))
                    return inst
                return em
            p.step("tensor", em_stats(xbf, pb[0]), reads=["xbf", "onescolb"],
                   writes=["pb0"], tag="a_mm1_%d" % tc)
            p.step("tensor", em_stats(xsq, pb[1]), reads=["xsq", "onescolb"],
                   writes=["pb1"], tag="a_mm2_%d" % tc)

            def em_rows(eng):
                eng.tensor_scalar(arow[0:1, 0:TC], pb[0][0:1, 0:TC], 1.0 / D, None, ALU.mult)
                eng.tensor_scalar(arow[0:1, 3*TC:4*TC], pb[1][0:1, 0:TC], 1.0 / D, None, ALU.mult)
                eng.tensor_tensor(arow[0:1, TC:2*TC], arow[0:1, 0:TC], arow[0:1, 0:TC], ALU.mult)
                return eng.tensor_tensor(arow[0:1, 3*TC:4*TC], arow[0:1, 3*TC:4*TC],
                                         arow[0:1, TC:2*TC], ALU.subtract)
            p.step("vector", em_rows, reads=["pb0", "pb1"], writes=["arow"],
                   tag="a_rows%d" % tc)

            def em_rstd(eng):
                eng.activation(arow[0:1, 2*TC:3*TC], arow[0:1, 3*TC:4*TC], AF.Ln, bias=sb["cvals"][0:1, 0:1])
                return eng.activation(arow[0:1, TC:2*TC], arow[0:1, 2*TC:3*TC], AF.Exp, scale=-0.5)
            p.step("scalar", em_rstd, reads=["arow", "cvals"], writes=["arow"], tag="a_rstd%d" % tc)
            p.step("vector", lambda eng: eng.tensor_tensor(
                arow[0:1, 2*TC:3*TC], arow[0:1, 0:TC], arow[0:1, TC:2*TC], ALU.mult),
                reads=["arow"], writes=["arow"], tag="a_ms%d" % tc)
            p.step("tensor", lambda eng: eng.matmul(pb[2][:, 0:TC], onesrow[:],
                                                    arow[0:1, TC:2*TC], start=True, stop=True),
                   reads=["arow", "onesrow"], writes=["pb2"], tag="a_bs%d" % tc)
            p.step("tensor", lambda eng: eng.matmul(pb[3][:, 0:TC], onesrow[:],
                                                    arow[0:1, 2*TC:3*TC], start=True, stop=True),
                   reads=["arow", "onesrow"], writes=["pb3"], tag="a_bms%d" % tc)
            p.step("vector", lambda eng: eng.tensor_copy(abc[:, 0:TC], pb[2][:, 0:TC]),
                   reads=["pb2"], writes=["abc"], tag="a_evs%d" % tc)
            p.step("vector", lambda eng: eng.tensor_copy(abc[:, TC:2*TC], pb[3][:, 0:TC]),
                   reads=["pb3"], writes=["abc"], tag="a_evms%d" % tc)

            def em_apply(d, t0):
                def em(eng):
                    xs = xa[:, d * TC:(d + 1) * TC]
                    eng.tensor_tensor(xs, xs, abc[:, 0:TC], ALU.mult)
                    return eng.tensor_tensor(ln1x[:, d * T + t0: d * T + t0 + TC],
                                             xs, abc[:, TC:2*TC], ALU.subtract)
                return em
            for d in range(16):
                p.step("vector", em_apply(d, t0), reads=["abc", "xa"],
                       writes=["ln1x", "xa"], tag="a_app%d_%d" % (tc, d))

        if "ln1x" in self.taps:
            ap = self.taps["ln1x"]
            p.step("sync", lambda eng: eng.dma_start(out=ap, in_=ln1x[:]),
                   reads=["ln1x"], dma=True, tag="tap_ln1x")

    def ph_b(self):
        nc, p, sb, dr, pb = self.nc, self.p, self.sb, self.dr, self.pb
        T = self.T
        TB = min(512, T)
        ntb = T // TB
        ln1x = self.sb["ln1x"]
        wqb = [self.at("wqb%d" % i, [128, 16 * 128], BF16, 64 * 1024 + i * 4096)
               for i in range(2)]
        qn = [self.at("qn%d" % i, [128, T], BF16, 72 * 1024 + i * 2 * T)
              for i in range(2)]
        wzb = self.at("wzb", [128, 16 * 2048], BF16, 80 * 1024)
        zev = [self.at("zev%d" % i, [128, 2 * D], BF16, 144 * 1024 + i * 4 * D)
               for i in range(2)]

        for n in range(4 * D // 128):
            par = n % 2
            wb = wqb[par]
            src = dr["Wqkv"].rearrange("(dt p) n -> p dt n", p=128)
            for half in range(2):
                p.step("gpsimd",
                       (lambda n_, h_, wb_: lambda eng: eng.dma_start(
                           out=wb_[:].rearrange("p (dt c) -> p dt c", c=128)[:, h_ * 8:(h_ + 1) * 8, :],
                           in_=src[:, h_ * 8:(h_ + 1) * 8, n_ * 128:(n_ + 1) * 128]))(n, half, wb),
                       writes=["wqb%d" % par], dma=True, tag="b_wq%d_%d" % (n, half))

            def em_mm(wb_, par_):
                def em(eng):
                    inst = None
                    for d in range(16):
                        lhsT = wb_[:, d * 128:(d + 1) * 128]
                        for tb in range(ntb):
                            inst = eng.matmul(pb[par_ * 4 + tb][:, 0:TB], lhsT,
                                              ln1x[:].rearrange("p (dt t) -> p dt t", t=T)[:, d, tb * TB:(tb + 1) * TB],
                                              start=(d == 0), stop=(d == 15))
                    return inst
                return em
            p.step("tensor", em_mm(wb, par),
                   reads=["wqb%d" % par, "ln1x"],
                   writes=["pb%d" % (par * 4 + tb) for tb in range(ntb)],
                   tag="b_qmm%d" % n)

            def em_ev(par_, n_):
                def em(eng):
                    inst = None
                    for tb in range(ntb):
                        inst = eng.copy(qn[par_][:, tb * TB:(tb + 1) * TB],
                                        pb[par_ * 4 + tb][:, 0:TB])
                    return inst
                return em
            p.step("scalar", em_ev(par, n),
                   reads=["pb%d" % (par * 4 + tb) for tb in range(ntb)],
                   writes=["qn%d" % par], tag="b_qev%d" % n)
            p.step("sync",
                   (lambda n_, par_: lambda eng: eng.dma_start(
                       out=self.qkv_s[n_ * 128:(n_ + 1) * 128, :], in_=qn[par_][:]))(n, par),
                   reads=["qn%d" % par], dma=True, tag="b_qst%d" % n)

        NH = 2 * D // 2
        for nh in range(2):
            srcz = dr["Wz"].rearrange("(dt p) n -> p dt n", p=128)
            for q in range(4):
                p.step("gpsimd",
                       (lambda nh_, q_: lambda eng: eng.dma_start(
                           out=wzb[:].rearrange("p (dt c) -> p dt c", c=NH)[:, q_ * 4:(q_ + 1) * 4, :],
                           in_=srcz[:, q_ * 4:(q_ + 1) * 4, nh_ * NH:(nh_ + 1) * NH]))(nh, q),
                       writes=["wzb"], dma=True, tag="b_wz%d_%d" % (nh, q))
            for tt in range(T // 128):
                par = tt % 2

                def em_zmm(tt_, par_):
                    def em(eng):
                        inst = None
                        for d in range(16):
                            lhsT = ln1x[:, d * T + tt_ * 128: d * T + (tt_ + 1) * 128]
                            for nb in range(NH // 512):
                                inst = eng.matmul(pb[par_ * 4 + nb][:, 0:512], lhsT,
                                                  wzb[:, d * NH + nb * 512: d * NH + (nb + 1) * 512],
                                                  start=(d == 0), stop=(d == 15))
                        return inst
                    return em
                p.step("tensor", em_zmm(tt, par),
                       reads=["wzb", "ln1x"],
                       writes=["pb%d" % (par * 4 + nb) for nb in range(NH // 512)],
                       tag="b_zmm%d_%d" % (nh, tt))

                def em_zev(par_):
                    def em(eng):
                        inst = None
                        for nb in range(NH // 512):
                            inst = eng.copy(zev[par_][:, nb * 512:(nb + 1) * 512],
                                            pb[par_ * 4 + nb][:, 0:512])
                        return inst
                    return em
                p.step("scalar", em_zev(par),
                       reads=["pb%d" % (par * 4 + nb) for nb in range(NH // 512)],
                       writes=["zev%d" % par], tag="b_zev%d_%d" % (nh, tt))
                p.step("sync",
                       (lambda nh_, tt_, par_: lambda eng: eng.dma_start(
                           out=self.z_s[tt_ * 128:(tt_ + 1) * 128, nh_ * NH:(nh_ + 1) * NH],
                           in_=zev[par_][:, 0:NH]))(nh, tt, par),
                       reads=["zev%d" % par], dma=True, tag="b_zst%d_%d" % (nh, tt))

        p.barrier()
        wabb = self.at("wabb", [128, 16 * 64], BF16, 80 * 1024)
        sc1 = self.at("sc1", [32, T], F32, 82 * 1024)
        sc2 = self.at("sc2", [32, T], F32, 82 * 1024 + 4 * T)
        sc3 = self.at("sc3", [32, T], F32, 82 * 1024 + 8 * T)
        abt = self.at("abt", [64, T], F32, 82 * 1024 + 12 * T)
        p.step("gpsimd", lambda eng: eng.dma_start(
            out=wabb[:].rearrange("p (dt c) -> p dt c", c=64),
            in_=dr["Wab"].rearrange("(dt p) n -> p dt n", p=128)),
            writes=["wabb"], dma=True, tag="b_wab")

        def em_abmm(eng):
            inst = None
            for d in range(16):
                lhsT = wabb[:, d * 64:(d + 1) * 64]
                for tb in range(ntb):
                    inst = eng.matmul(pb[4 + tb][0:64, 0:TB], lhsT,
                                      ln1x[:].rearrange("p (dt t) -> p dt t", t=T)[:, d, tb * TB:(tb + 1) * TB],
                                      start=(d == 0), stop=(d == 15))
            return inst
        p.step("tensor", em_abmm, reads=["wabb", "ln1x"],
               writes=["pb%d" % (4 + tb) for tb in range(ntb)], tag="b_abmm")

        def em_abev(eng):
            inst = None
            for tb in range(ntb):
                inst = eng.copy(abt[:, tb * TB:(tb + 1) * TB], pb[4 + tb][0:64, 0:TB])
            return inst
        p.step("scalar", em_abev, reads=["pb%d" % (4 + tb) for tb in range(ntb)],
               writes=["abt"], tag="b_abev")

        a_, b_ = abt[0:32, :], abt[32:64, :]
        abg = sb["abg"]
        cv = sb["cvals"]

        def em_g1(eng):
            return eng.tensor_scalar(sc1[:], a_, sb["dtbias"][:], None, ALU.add)
        p.step("vector", em_g1, reads=["abt", "dtbias"], writes=["sc1"], tag="b_g1")

        def em_g2(eng):
            eng.activation(sc2[:], sc1[:], AF.Abs)
            eng.activation(sc3[:], sc2[:], AF.Exp, scale=-1.0)
            return eng.activation(sc2[:], sc3[:], AF.Ln, bias=cv[0:32, 1:2])
        p.step("scalar", em_g2, reads=["sc1", "cvals"], writes=["sc2", "sc3"], tag="b_g2")

        def em_g3(eng):
            eng.tensor_scalar(sc3[:], sc1[:], 0.0, None, ALU.max)
            eng.tensor_tensor(sc3[:], sc3[:], sc2[:], ALU.add)
            return eng.tensor_scalar(abg[0:32, :], sc3[:], sb["negexpa"][:], None, ALU.mult)
        p.step("vector", em_g3, reads=["sc1", "sc2"], sereads=["negexpa"],
               writes=["sc3", "abg"], tag="b_g3")

        def em_l1(eng):
            return eng.tensor_scalar(sc1[:], b_, -1.0, None, ALU.mult)
        p.step("vector", em_l1, reads=["abt"], writes=["sc1"], tag="b_l1")

        def em_l2(eng):
            eng.activation(sc2[:], sc1[:], AF.Abs)
            eng.activation(sc3[:], sc2[:], AF.Exp, scale=-1.0)
            return eng.activation(sc2[:], sc3[:], AF.Ln, bias=cv[0:32, 1:2])
        p.step("scalar", em_l2, reads=["sc1", "cvals"], writes=["sc2", "sc3"], tag="b_l2")

        def em_l3(eng):
            eng.tensor_scalar(sc3[:], sc1[:], 0.0, None, ALU.max)
            eng.tensor_tensor(sc3[:], sc3[:], sc2[:], ALU.add)
            return eng.tensor_scalar(abg[32:64, :], sc3[:], -1.0, None, ALU.mult)
        p.step("vector", em_l3, reads=["sc1", "sc2"], writes=["sc3", "abg"], tag="b_l3")

        if "abg" in self.taps:
            ap = self.taps["abg"]
            p.step("sync", lambda eng: eng.dma_start(out=ap, in_=abg[0:64, :]),
                   reads=["abg"], dma=True, tag="tap_abg")

    def ph_c(self):
        nc, p, sb = self.nc, self.p, self.sb
        T = self.T
        nchunk = self.nchunk
        at = self.at
        bufs = {}
        bufs["qkvc"] = at("qkvc", [128, 64 * 131], BF16, 0)
        bufs["mixed"] = at("mixed", [128, 64 * 128], BF16, 17 * 1024)
        bufs["zsil"] = [at("zsil%d" % i, [128, 2 * D], BF16, 34 * 1024 + i * 4 * D)
                        for i in range(2)]
        bufs["Sst"] = at("Sst", [128, 32 * 128], BF16, 51 * 1024)
        bufs["flat"] = at("flat", [1, 96 * 128], F32, 60 * 1024)
        bufs["y1"] = at("y1", [128, 32 * 128], BF16, 109 * 1024)
        bufs["yA"] = at("yA", [128, 32 * 128], BF16, 118 * 1024)
        bufs["solT"] = [at("solT%d" % i, [128, 14 * 128], BF16, 127 * 1024 + i * 3584)
                        for i in range(3)]
        bufs["Usb"] = [at("Usb%d" % i, [128, 2 * 128], BF16, 138 * 1024 + i * 512)
                       for i in range(3)]
        bufs["Rt"] = [at("Rt%d" % i, [128, 128], BF16, 140 * 1024 + i * 256)
                      for i in range(3)]
        bufs["VBs"] = [at("VBs%d" % i, [128, 128], BF16, 141 * 1024 + i * 256)
                       for i in range(3)]
        bufs["KBt"] = [at("KBt%d" % i, [128, 128], BF16, 142 * 1024 + i * 256)
                       for i in range(3)]
        bufs["KSe"] = [at("KSe%d" % i, [128, 128], BF16, 143 * 1024 + i * 256)
                       for i in range(3)]
        bufs["QSe"] = [at("QSe%d" % i, [128, 128], F32, 144 * 1024 + i * 512)
                       for i in range(3)]
        bufs["Ot"] = [at("Ot%d" % i, [128, 128], F32, 146 * 1024 + i * 512)
                      for i in range(3)]
        bufs["Esb"] = [at("Esb%d" % i, [128, 3 * 128], F32, 148 * 1024 + i * 1536)
                       for i in range(3)]
        bufs["r3m"] = [at("r3m%d" % i, [128, 3 * 128], F32, 153 * 1024 + i * 1536)
                       for i in range(3)]
        bufs["sqb"] = [at("sqb%d" % i, [128, 128], BF16, 158 * 1024 + i * 256)
                       for i in range(3)]
        bufs["Utmp"] = [at("Utmp%d" % i, [128, 128], BF16, 168 * 1024 + i * 256)
                        for i in range(3)]
        bufs["cvt"] = [at("cvt%d" % i, [128, 128], BF16, 159 * 1024 + i * 256) for i in range(3)]
        cols = []
        for i in range(2):
            cur = [160 * 1024 + i * 4352]

            def nx(nbytes):
                a = (cur[0] + 31) // 32 * 32
                cur[0] = a + nbytes
                return a
            cols.append({
                "gBe": at("gBe%d" % i, [128, 66], F32, nx(264)),
                "cumE": at("cumE%d" % i, [128, 66], F32, nx(264)),
                "revE": at("revE%d" % i, [128, 66], F32, nx(264)),
                "gp": at("gp%d" % i, [128, 32], F32, nx(128)),
                "dp": at("dp%d" % i, [128, 32], F32, nx(128)),
                "cs": at("cs%d" % i, [128, 96], F32, nx(384)),
                "nrA": at("nrA%d" % i, [128, 32], F32, nx(128)),
                "lnr": at("lnr%d" % i, [128, 32], F32, nx(128)),
                "earg": at("earg%d" % i, [128, 32], F32, nx(128)),
                "bcol": at("bcol%d" % i, [128, 32], F32, nx(128)),
                "bGam": at("bGam%d" % i, [128, 32], F32, nx(128)),
                "qGam": at("qGam%d" % i, [128, 32], F32, nx(128)),
                "Ktl": at("Ktl%d" % i, [128, 32], F32, nx(128)),
                "coefC": at("coefC%d" % i, [128, 32], F32, nx(128)),
                "msC": at("msC%d" % i, [128, 32], F32, nx(128)),
                "rstd": at("rstd%d" % i, [128, 32], F32, nx(128)),
                "gsA": at("gsA%d" % i, [66, 128], F32, nx(512)),
                "rowsA": at("rowsA%d" % i, [96, 128], F32, nx(512)),
                "crow": at("crow%d" % i, [1, 32], F32, nx(128)),
                "c1": at("c1%d" % i, [128, 2], F32, nx(32)),
            })
        bufs["cols"] = cols

        p.step("vector", lambda eng: eng.memset(bufs["Sst"][:], 0.0),
               writes=["Sst"], tag="c_zs")
        for i in range(2):
            p.step("vector", (lambda i_: lambda eng: eng.memset(cols[i_]["gsA"][:], 0.0))(i),
                   writes=["gsA%d" % i], tag="c_zgsa%d" % i)
        p.step("vector", lambda eng: eng.memset(
            bufs["qkvc"][:].rearrange("p (ti c) -> p ti c", c=131)[:, :, 0:3], 0.0),
            writes=["qkvc"], tag="c_zq")

        nh = min(HV, getattr(self, "maxheads", HV))
        prep = (self._chunk_loads(0, 0, bufs) + self._chunk_conv(0, 0, bufs)
                + self._chunk_machA(0, 0, bufs) + self._chunk_machB(0, 0, bufs))
        for st_ in prep:
            self.p.step(st_[0], st_[1], **st_[2])
        for c in range(nchunk):
            cp = c % 2
            if c + 1 < nchunk:
                nxt_early = (self._chunk_loads(c + 1, (c + 1) % 2, bufs)
                             + self._chunk_machA(c + 1, (c + 1) % 2, bufs))
                nxt_late = (self._chunk_conv(c + 1, (c + 1) % 2, bufs)
                            + self._chunk_machB(c + 1, (c + 1) % 2, bufs))
            else:
                nxt_early, nxt_late = [], []
            pending = [self._head_steps(c, cp, h, bufs) for h in range(nh)]
            cursors = {}
            active = []
            nexth = 0
            emitted = 0
            total = sum(len(g) for g in pending)
            nxt_i = 0
            nxt_start = 2 * total // 5
            while active or nexth < nh:
                while len(active) < 3 and nexth < nh:
                    active.append(nexth)
                    cursors[nexth] = 0
                    nexth += 1
                if not active:
                    break
                for hidx in list(active):
                    g = pending[hidx]
                    if cursors[hidx] < len(g):
                        eng_, emit_, kw_ = g[cursors[hidx]]
                        self.p.step(eng_, emit_, **kw_)
                        cursors[hidx] += 1
                        emitted += 1
                        if emitted > nxt_start and nxt_i < len(nxt_early):
                            st_ = nxt_early[nxt_i]
                            self.p.step(st_[0], st_[1], **st_[2])
                            nxt_i += 1
                    if cursors[hidx] >= len(g):
                        active.remove(hidx)
            while nxt_i < len(nxt_early):
                st_ = nxt_early[nxt_i]
                self.p.step(st_[0], st_[1], **st_[2])
                nxt_i += 1
            for st_ in nxt_late:
                self.p.step(st_[0], st_[1], **st_[2])
            self._chunk_tail(c, cp, bufs)

    def _chunk_loads(self, c, cp, bufs):
        steps = []

        class _P:
            @staticmethod
            def step(eng_, emit_, **kw):
                steps.append((eng_, emit_, kw))
        p = _P
        t0 = c * C
        qkvc, zsil = bufs["qkvc"], bufs["zsil"]
        lo = max(t0 - 3, 0)
        off = 3 if c == 0 else 0
        ncols = t0 + C - lo
        for q in range(4):
            src = self.qkv_s.rearrange("(ti p) t -> p ti t", p=128)[:, q * 16:(q + 1) * 16, lo:lo + ncols]
            dst = qkvc[:].rearrange("p (ti c) -> p ti c", c=131)[:, q * 16:(q + 1) * 16, off:off + ncols]
            p.step("sync", (lambda s_, d_: lambda eng: eng.dma_start(out=d_, in_=s_))(src, dst),
                   writes=["qkvc"], dma=True, tag="c_lq%d_%d" % (c, q))
        p.step("sync", (lambda t0_, cp_: lambda eng: eng.dma_start(
            out=zsil[cp_][:], in_=self.z_s[t0_:t0_ + C, :]))(t0, cp),
            writes=["zsil%d" % cp], dma=True, tag="c_lz%d" % c)
        p.step("scalar", (lambda cp_: lambda eng: eng.activation(
            zsil[cp_][:], zsil[cp_][:], AF.Silu))(cp),
            reads=["zsil%d" % cp], writes=["zsil%d" % cp], tag="c_silz%d" % c)
        return steps

    def _chunk_conv(self, c, cp, bufs):
        steps = []

        class _P:
            @staticmethod
            def step(eng_, emit_, **kw):
                steps.append((eng_, emit_, kw))
        p = _P
        sb = self.sb
        W = self.T + 8
        t0 = c * C
        qkvc, mixed, cvt = bufs["qkvc"], bufs["mixed"], bufs["cvt"]
        msk, cw = sb["msk"], sb["cw"]

        def em_conv(ti0, ti1, ci):
            def em(eng):
                inst = None
                for ti in range(ti0, ti1):
                    qv = qkvc[:, ti * 131: ti * 131 + 131]
                    mx = mixed[:, ti * 128:(ti + 1) * 128]
                    inst = eng.tensor_scalar(mx, qv[:, 3:131],
                                             cw[:, ti * KC + 3:ti * KC + 4], None, ALU.mult)
                    for s_ in (1, 2, 3):
                        mslice = msk[:, (s_ - 1) * W + 3 + t0 - s_:
                                     (s_ - 1) * W + 3 + t0 - s_ + C]
                        inst = eng.scalar_tensor_tensor(
                            cvt[ci][:], qv[:, 3 - s_:131 - s_],
                            cw[:, ti * KC + 3 - s_:ti * KC + 4 - s_],
                            mslice, ALU.mult, ALU.mult)
                        inst = eng.tensor_tensor(mx, mx, cvt[ci][:], ALU.add)
                return inst
            return em
        def em_convg(ti0, ti1, ci):
            def em(eng):
                inst = None
                for tb in range(ti0, ti1, 2):
                    pair = (tb, tb + 1)
                    qvs = [qkvc[:, ti * 131: ti * 131 + 131] for ti in pair]
                    mxs = [mixed[:, ti * 128:(ti + 1) * 128] for ti in pair]
                    cvs = [cvt[ci][:, 0:64].rearrange("p n -> p n") for _ in pair]
                    cvs = [cvt[1][:], cvt[2][:]]
                    for j in (0, 1):
                        inst = eng.tensor_tensor(
                            mxs[j], qvs[j][:, 3:131],
                            cw[:, pair[j] * KC + 3:pair[j] * KC + 4].to_broadcast([128, C]),
                            ALU.mult)
                    for s_ in (1, 2, 3):
                        mslice = msk[:, (s_ - 1) * W + 3 + t0 - s_:
                                     (s_ - 1) * W + 3 + t0 - s_ + C]
                        for j in (0, 1):
                            inst = eng.tensor_tensor(
                                cvs[j], qvs[j][:, 3 - s_:131 - s_],
                                cw[:, pair[j] * KC + 3 - s_:pair[j] * KC + 4 - s_].to_broadcast([128, C]),
                                ALU.mult)
                        for j in (0, 1):
                            inst = eng.tensor_tensor(cvs[j], cvs[j], mslice, ALU.mult)
                        for j in (0, 1):
                            inst = eng.tensor_tensor(mxs[j], mxs[j], cvs[j], ALU.add)
                return inst
            return em
        for grp in range(6):
            p.step("vector", em_conv(grp * 8, (grp + 1) * 8, 0),
                   reads=["qkvc", "cw", "msk"], writes=["mixed%d" % grp, "cvt0"],
                   tag="c_cv%d_%d" % (c, grp))
        for grp in range(6, 8):
            p.step("gpsimd", em_convg(grp * 8, (grp + 1) * 8, 1),
                   reads=["qkvc", "cw", "msk"], writes=["mixed%d" % grp, "cvt1", "cvt2"],
                   tag="c_cg%d_%d" % (c, grp))
        for grp in range(8):
            p.step("scalar", (lambda g_: lambda eng: eng.activation(
                mixed[:, g_ * 1024:(g_ + 1) * 1024], mixed[:, g_ * 1024:(g_ + 1) * 1024],
                AF.Silu))(grp),
                reads=["mixed%d" % grp], writes=["mixed%d" % grp],
                tag="c_sil%d_%d" % (c, grp))
        return steps

    def _chunk_machA(self, c, cp, bufs):
        steps = []

        class _P:
            @staticmethod
            def step(eng_, emit_, **kw):
                steps.append((eng_, emit_, kw))
        nc, sb, pb = self.nc, self.sb, self.pb
        p = _P
        t0 = c * C
        co = bufs["cols"][cp]
        mixed, flat, sqb = bufs["mixed"], bufs["flat"], bufs["sqb"]
        abg, sr, ident = sb["abg"], sb["sr"], sb["ident"]
        cumt, revt, cv = sb["cumt"], sb["revt"], sb["cvals"]
        onesrow, onescolb, e127 = sb["onesrow"], sb["onescolb"], sb["e127"]
        bnk2 = pb[2 * (c % 3)]
        r_b2 = "pb%d" % (2 * (c % 3))
        S = p.step

        def em_gsA(eng):
            eng.tensor_copy(co["gsA"][0:64, :], abg[0:64, t0:t0 + C])
            return eng.tensor_copy(co["gsA"][64:65, :], sr[0:1, t0:t0 + C])
        S("vector", em_gsA, reads=["abg", "sr"], writes=["gsA%d" % cp], tag="c_gsa%d" % c)
        S("tensor", lambda eng: eng.transpose(bnk2[:, 0:66], co["gsA"][0:66, :], ident[0:66, 0:66]),
          reads=["gsA%d" % cp, "ident"], writes=[r_b2], tag="c_trg%d" % c)
        S("scalar", lambda eng: eng.copy(co["gBe"][:], bnk2[:, 0:66]),
          reads=[r_b2], writes=["gBe%d" % cp], tag="c_gbe%d" % c)

        def em_cum(eng):
            eng.matmul(bnk2[:, 66:132], cumt[:], co["gBe"][:, 0:66], start=True, stop=True)
            return eng.matmul(bnk2[:, 132:198], revt[:], co["gBe"][:, 0:66], start=True, stop=True)
        S("tensor", em_cum, reads=["gBe%d" % cp, "cumt", "revt"], writes=[r_b2],
          tag="c_cum%d" % c)

        def em_cumev(eng):
            eng.copy(co["cumE"][:], bnk2[:, 66:132])
            return eng.copy(co["revE"][:], bnk2[:, 132:198])
        S("scalar", em_cumev, reads=[r_b2], writes=["cumE%d" % cp, "revE%d" % cp],
          tag="c_cumev%d" % c)

        def em_c1(eng):
            eng.mul(co["c1"][:, 0:1], co["cumE"][:, 64:65], BIG)
            return eng.mul(co["c1"][:, 1:2], co["revE"][:, 64:65], BIG)
        S("scalar", em_c1, reads=["cumE%d" % cp, "revE%d" % cp],
          writes=["c1%d" % cp], tag="c_c1%d" % c)

        def em_gp(eng):
            eng.tensor_scalar(co["gp"][:], co["cumE"][:, 0:32], co["c1"][:, 0:1], None, ALU.subtract)
            return eng.tensor_scalar(co["dp"][:], co["revE"][:, 0:32], co["c1"][:, 1:2], None, ALU.subtract)
        S("vector", em_gp, reads=["cumE%d" % cp, "revE%d" % cp, "c1%d" % cp],
          writes=["gp%d" % cp, "dp%d" % cp], tag="c_gp%d" % c)

        S("tensor", lambda eng: eng.matmul(bnk2[0:1, 230:262], e127[:], co["gp"][:],
                                           start=True, stop=True),
          reads=["gp%d" % cp, "e127"], writes=[r_b2], tag="c_cfx%d" % c)
        S("scalar", lambda eng: eng.copy(co["crow"][:], bnk2[0:1, 230:262]),
          reads=[r_b2], writes=["crow%d" % cp], tag="c_cfe%d" % c)
        S("tensor", lambda eng: eng.matmul(bnk2[:, 262:294], onesrow[:], co["crow"][:],
                                           start=True, stop=True),
          reads=["crow%d" % cp, "onesrow"], writes=[r_b2], tag="c_cfb%d" % c)
        S("scalar", lambda eng: eng.activation(co["coefC"][:], bnk2[:, 262:294], AF.Exp),
          reads=[r_b2], writes=["coefC%d" % cp], tag="c_cfc%d" % c)
        return steps

    def _chunk_machB(self, c, cp, bufs):
        steps = []

        class _P:
            @staticmethod
            def step(eng_, emit_, **kw):
                steps.append((eng_, emit_, kw))
        nc, sb, pb = self.nc, self.sb, self.pb
        p = _P
        t0 = c * C
        co = bufs["cols"][cp]
        mixed, flat, sqb = bufs["mixed"], bufs["flat"], bufs["sqb"]
        abg, sr, ident = sb["abg"], sb["sr"], sb["ident"]
        cumt, revt, cv = sb["cumt"], sb["revt"], sb["cvals"]
        onesrow, onescolb, e127 = sb["onesrow"], sb["onescolb"], sb["e127"]
        bnk2 = pb[2 * (c % 3)]
        r_b2 = "pb%d" % (2 * (c % 3))
        S = p.step

        def em_sq(hk, which):
            ti = hk if which == "q" else 16 + hk
            def em(eng):
                return eng.tensor_tensor(sqb[hk % 2][:], mixed[:, ti * 128:(ti + 1) * 128],
                                         mixed[:, ti * 128:(ti + 1) * 128], ALU.mult)
            return em, ti // 8

        def em_nmm(hk, col):
            def em(eng):
                return eng.matmul(bnk2[:, 198 + col:199 + col], sqb[hk % 2][:],
                                  onescolb[:], start=True, stop=True)
            return em
        for which in ("q", "k"):
            for hk in range(16):
                col = hk if which == "q" else 16 + hk
                emf, mgrp = em_sq(hk, which)
                S("vector", emf, reads=["mixed%d" % mgrp],
                  writes=["sqb%d" % (hk % 2)], tag="c_s%s%d_%d" % (which, c, hk))
                S("tensor", em_nmm(hk, col), reads=["sqb%d" % (hk % 2), "onescolb"],
                  writes=[r_b2], tag="c_n%s%d_%d" % (which, c, hk))
        S("scalar", lambda eng: eng.activation(co["lnr"][:], bnk2[:, 198:230], AF.Ln,
                                               bias=cv[:, 2:3]),
          reads=[r_b2, "cvals"], writes=["lnr%d" % cp], tag="c_lnr%d" % c)

        lnq_rep = co["lnr"][:, 0:16].unsqueeze(2).to_broadcast([128, 16, 2])
        lnk_rep = co["lnr"][:, 16:32].unsqueeze(2).to_broadcast([128, 16, 2])

        def v3(ap):
            return ap.rearrange("p (a b) -> p a b", b=2)

        def em_cols(eng):
            cs = co["cs"]
            eng.tensor_scalar(co["lnr"][:, 0:16], co["lnr"][:, 0:16], -0.5,
                              -0.5 * float(np.log(DK)), ALU.mult, ALU.add)
            eng.tensor_scalar(co["lnr"][:, 16:32], co["lnr"][:, 16:32], -0.5, None, ALU.mult)
            eng.tensor_tensor(cs[:, 32:64], co["gp"][:], co["gBe"][:, 32:64], ALU.add)
            eng.tensor_tensor(co["earg"][:], co["gBe"][:, 32:64], co["gp"][:], ALU.add)
            eng.tensor_tensor(v3(cs[:, 0:32]), v3(co["gp"][:]), lnk_rep, ALU.subtract)
            eng.tensor_tensor(v3(cs[:, 32:64]), v3(cs[:, 32:64]), lnk_rep, ALU.add)
            eng.tensor_tensor(v3(cs[:, 64:96]), v3(co["gp"][:]), lnq_rep, ALU.add)
            eng.tensor_tensor(v3(co["Ktl"][:]), v3(co["dp"][:]), lnk_rep, ALU.add)
            return eng.tensor_scalar(co["nrA"][:], cs[:, 0:32], -1.0, None, ALU.mult)
        S("vector", em_cols,
          reads=["gp%d" % cp, "dp%d" % cp, "gBe%d" % cp, "lnr%d" % cp],
          writes=["cs%d" % cp, "nrA%d" % cp, "earg%d" % cp, "Ktl%d" % cp, "lnr%d" % cp],
          tag="c_cols%d" % c)

        def em_exps(eng):
            eng.activation(co["bcol"][:], co["gBe"][:, 32:64], AF.Exp)
            eng.activation(co["bGam"][:], co["earg"][:], AF.Exp)
            eng.activation(co["qGam"][:], co["cs"][:, 64:96], AF.Exp)
            return eng.activation(co["Ktl"][:], co["Ktl"][:], AF.Exp)
        S("scalar", em_exps,
          reads=["gBe%d" % cp, "earg%d" % cp, "cs%d" % cp, "Ktl%d" % cp],
          writes=["bcol%d" % cp, "bGam%d" % cp, "qGam%d" % cp, "Ktl%d" % cp],
          tag="c_exps%d" % c)

        S("tensor", lambda eng: eng.transpose(bnk2[0:96, 294:422], co["cs"][:, 0:96], ident[:]),
          reads=["cs%d" % cp, "ident"], writes=[r_b2], tag="c_trr%d" % c)
        S("scalar", lambda eng: eng.copy(co["rowsA"][:], bnk2[0:96, 294:422]),
          reads=[r_b2], writes=["rowsA%d" % cp], tag="c_rae%d" % c)
        for r in range(3):
            dst = flat[:].rearrange("o (h r c) -> o h r c", r=3, c=128)[:, :, r, :]
            S("sync", (lambda r_, d_: lambda eng: eng.dma_start(
                out=d_, in_=co["rowsA"][r_ * 32:(r_ + 1) * 32, :]))(r, dst),
              reads=["rowsA%d" % cp], writes=["flat"], dma=True,
              tag="c_fl%d_%d" % (c, r))

        S("vector", lambda eng: eng.memset(co["msC"][:], 0.0),
          writes=["msC%d" % cp], tag="c_ms0%d" % c)
        return steps
        return steps

    def _chunk_tail(self, c, cp, bufs):
        if getattr(self, "notail", False):
            return
        p, sb, pb = self.p, self.sb, self.pb
        t0 = c * C
        co = bufs["cols"][cp]
        y1, yA = bufs["y1"], bufs["yA"]
        cv, normwb, identb = sb["cvals"], sb["normwb"], sb["identb"]
        S = p.step

        S("vector", lambda eng: eng.tensor_scalar(co["msC"][:], co["msC"][:],
                                                  1.0 / DV, None, ALU.mult),
          reads=["msC%d" % cp], writes=["msC%d" % cp], tag="c_msn%d" % c)

        S("scalar", lambda eng: eng.activation(co["rstd"][:], co["msC"][:], AF.Ln,
                                               bias=cv[:, 2:3]),
          reads=["msC%d" % cp, "cvals"], writes=["rstd%d" % cp], tag="c_rstda%d" % c)
        S("scalar", lambda eng: eng.activation(co["rstd"][:], co["rstd"][:], AF.Exp,
                                               scale=-0.5),
          sereads=["rstd%d" % cp], writes=["rstd%d" % cp], tag="c_rstdb%d" % c)

        for h in range(HV):
            g3 = h % 3
            BT = pb[6 + (g3 & 1)][:].bitcast(BF16)
            bto = (0 if g3 != 2 else 512) + 256
            rBT = "pb%d" % (6 + (g3 & 1))

            def em_yb(h_):
                def em(eng):
                    return eng.scalar_tensor_tensor(
                        y1[:, h_ * 128:(h_ + 1) * 128], y1[:, h_ * 128:(h_ + 1) * 128],
                        co["rstd"][:, h_:h_ + 1], normwb[:], ALU.mult, ALU.mult)
                return em
            S("vector", em_yb(h), reads=["y1", "rstd%d" % cp, "normwb"],
              writes=["y1"], tag="c_yb%d_%d" % (c, h))
            S("tensor", (lambda h_, BT_, o_: lambda eng: eng.transpose(
                BT_[:, o_:o_ + 128], y1[:, h_ * 128:(h_ + 1) * 128], identb[:]))(h, BT, bto),
              reads=["y1", "identb"], writes=[rBT], tag="c_ytr%d_%d" % (c, h))
            S("scalar", (lambda h_, BT_, o_: lambda eng: eng.copy(
                yA[:, h_ * 128:(h_ + 1) * 128], BT_[:, o_:o_ + 128]))(h, BT, bto),
              reads=[rBT], writes=["yA"], tag="c_yev%d_%d" % (c, h))
        for q in range(2):
            src = yA[:].rearrange("p (hh t) -> p hh t", t=128)[:, q * 16:(q + 1) * 16, :]
            dst = self.y_s.rearrange("(hh p) t -> p hh t", p=128)[:, q * 16:(q + 1) * 16, t0:t0 + C]
            S("sync", (lambda s_, d_: lambda eng: eng.dma_start(out=d_, in_=s_))(src, dst),
              reads=["yA"], dma=True, tag="c_yst%d_%d" % (c, q))

    def _head_steps(self, c, cp, h, bufs):
        nc, p, sb, pb = self.nc, self.p, self.sb, self.pb
        co = bufs["cols"][cp]
        mixed, zsil_t, Sst = bufs["mixed"], bufs["zsil"][cp], bufs["Sst"]
        flat, y1 = bufs["flat"], bufs["y1"]
        solT, Usb, Rt, VBs = bufs["solT"], bufs["Usb"], bufs["Rt"], bufs["VBs"]
        KBt, KSe, QSe, Ot = bufs["KBt"], bufs["KSe"], bufs["QSe"], bufs["Ot"]
        Esb, r3m = bufs["Esb"], bufs["r3m"]
        g3 = h % 3
        hk = h // 2
        BA, BB = pb[g3 * 2], pb[g3 * 2 + 1]
        rBA, rBB = "pb%d" % (g3 * 2), "pb%d" % (g3 * 2 + 1)
        BT = pb[6 + (g3 & 1)][:].bitcast(BF16)
        bto = 0 if g3 != 2 else 512
        rBT = "pb%d" % (6 + (g3 & 1))
        sol = solT[g3]
        rsol = "solT%d" % g3
        U = Usb[g3]
        rU = "Usb%d" % g3
        cm384, identb, onesrow = sb["cm384"], sb["identb"], sb["onesrow"]
        qT = mixed[:, hk * 128:(hk + 1) * 128]
        kT = mixed[:, (16 + hk) * 128:(16 + hk + 1) * 128]
        vT = mixed[:, (32 + h) * 128:(32 + h + 1) * 128]
        Sh = Sst[:, h * 128:(h + 1) * 128]
        tg = "%d_%d" % (c, h)
        steps = []

        def S(engine, emit, **kw):
            steps.append((engine, emit, kw))

        S("tensor", lambda eng: eng.matmul(BA[:, 0:384], onesrow[:],
                                           flat[0:1, h * 384:(h + 1) * 384],
                                           start=True, stop=True),
          reads=["flat", "onesrow"], writes=[rBA], tag="h_bc" + tg)
        S("vector", lambda eng: eng.tensor_tensor(r3m[g3][:], BA[:, 0:384], cm384[:], ALU.add),
          reads=[rBA, "cm384"], writes=["r3m%d" % g3], tag="h_r3m" + tg)

        def em_es(eng):
            eng.activation(Esb[g3][:, 0:128], r3m[g3][:, 0:128], AF.Exp,
                           bias=co["cs"][:, 32 + h:33 + h], scale=-1.0)
            eng.activation(Esb[g3][:, 128:256], r3m[g3][:, 128:256], AF.Exp,
                           bias=co["nrA"][:, h:h + 1])
            return eng.activation(Esb[g3][:, 256:384], r3m[g3][:, 256:384], AF.Exp,
                                  bias=co["nrA"][:, h:h + 1])
        S("scalar", em_es, reads=["r3m%d" % g3, "cs%d" % cp, "nrA%d" % cp],
          writes=["Esb%d" % g3], tag="h_es" + tg)

        def em_pp(eng):
            eng.matmul(BA[:, 0:128], kT, kT, start=True, stop=True)
            return eng.matmul(BA[:, 128:256], kT, qT, start=True, stop=True)
        S("tensor", em_pp, reads=["mixed%d" % (hk // 8), "mixed%d" % (2 + hk // 8)],
          writes=[rBA], tag="h_pp" + tg)

        def em_ks(eng):
            eng.matmul(BB[:, 0:128], kT, Sh, start=True, stop=True)
            return eng.matmul(BB[:, 128:256], qT, Sh, start=True, stop=True)
        S("tensor", em_ks, reads=["mixed%d" % (hk // 8), "mixed%d" % (2 + hk // 8), "Sst"],
          writes=[rBB], tag="h_ks" + tg)

        def em_xya(eng):
            eng.tensor_tensor(sol[:, 0:128], BA[:, 0:128], Esb[g3][:, 0:128], ALU.mult)
            eng.tensor_tensor(sol[:, 6 * 128:7 * 128], BA[:, 0:128], Esb[g3][:, 128:256], ALU.mult)
            return eng.tensor_tensor(sol[:, 13 * 128:14 * 128], BA[:, 128:256],
                                     Esb[g3][:, 256:384], ALU.mult)
        S("vector", em_xya, reads=[rBA, "Esb%d" % g3], writes=[rsol], tag="h_xya" + tg)

        S("tensor", lambda eng: eng.transpose(BT[:, bto:bto + 128], vT, identb[:]),
          reads=["mixed%d" % (4 + h // 8), "identb"], writes=[rBT], tag="h_vtr" + tg)
        S("tensor", lambda eng: eng.transpose(BT[:, bto + 128:bto + 256], kT, identb[:]),
          reads=["mixed%d" % (2 + hk // 8), "identb"], writes=[rBT], tag="h_ktr" + tg)

        S("vector", lambda eng: eng.tensor_scalar(VBs[g3][:], BT[:, bto:bto + 128],
                                                  co["bcol"][:, h:h + 1], None, ALU.mult),
          reads=[rBT, "bcol%d" % cp], writes=["VBs%d" % g3], tag="h_vbe" + tg)
        S("vector", lambda eng: eng.tensor_scalar(KSe[g3][:], BB[:, 0:128],
                                                  co["bGam"][:, h:h + 1], None, ALU.mult),
          reads=[rBB, "bGam%d" % cp], writes=["KSe%d" % g3], tag="h_kse" + tg)
        S("vector", lambda eng: eng.tensor_scalar(KBt[g3][:], BT[:, bto + 128:bto + 256],
                                                  co["Ktl"][:, h:h + 1], None, ALU.mult),
          reads=[rBT, "Ktl%d" % cp], writes=["KBt%d" % g3], tag="h_kbt" + tg)
        S("vector", lambda eng: eng.tensor_tensor(Rt[g3][:], VBs[g3][:], KSe[g3][:], ALU.subtract),
          reads=["VBs%d" % g3, "KSe%d" % g3], writes=["Rt%d" % g3], tag="h_r" + tg)

        Utmp = bufs["Utmp"][g3]
        rUt = "Utmp%d" % g3
        S("tensor", lambda eng: eng.matmul(BB[:, 256:384], sol[:, 6 * 128:7 * 128],
                                           Rt[g3][:], start=True, stop=True),
          reads=[rsol, "Rt%d" % g3], writes=[rBB], tag="h_a0" + tg)
        S("scalar", lambda eng: eng.copy(Utmp[:], BB[:, 256:384]),
          reads=[rBB], writes=[rUt], tag="h_ue0" + tg)
        S("vector", lambda eng: eng.tensor_tensor(U[:, 0:128], Rt[g3][:], Utmp[:], ALU.subtract),
          reads=["Rt%d" % g3, rUt], writes=[rU], tag="h_u1" + tg)
        for k in range(1, 7):
            xk_prev = sol[:, (k - 1) * 128:k * 128]
            yk_prev = sol[:, (6 + k - 1) * 128:(6 + k) * 128]
            ups = k % 2
            uprev = U[:, (1 - ups) * 128:(2 - ups) * 128]
            unew = U[:, ups * 128:(ups + 1) * 128]
            if k < 6:
                def em_sq(yp, xp):
                    def em(eng):
                        eng.matmul(BA[:, 256:384], yp, xp, start=True, stop=True)
                        return eng.matmul(BA[:, 384:512], xp, yp, start=True, stop=True)
                    return em
                S("tensor", em_sq(yk_prev, xk_prev), reads=[rsol], writes=[rBA],
                  tag="h_sq%d" % k + tg)

                def em_sqe(kk):
                    def em(eng):
                        eng.copy(sol[:, kk * 128:(kk + 1) * 128], BA[:, 256:384])
                        return eng.copy(sol[:, (6 + kk) * 128:(6 + kk + 1) * 128],
                                        BA[:, 384:512])
                    return em
                S("scalar", em_sqe(k), reads=[rBA], writes=[rsol], tag="h_sqe%d" % k + tg)
            else:
                S("tensor", (lambda yp, xp: lambda eng: eng.matmul(
                    BA[:, 384:512], xp, yp, start=True, stop=True))(yk_prev, xk_prev),
                  reads=[rsol], writes=[rBA], tag="h_sq%d" % k + tg)
                S("scalar", (lambda kk: lambda eng: eng.copy(
                    sol[:, (6 + kk) * 128:(6 + kk + 1) * 128], BA[:, 384:512]))(k),
                  reads=[rBA], writes=[rsol], tag="h_sqe%d" % k + tg)
            S("tensor", (lambda kk, up: lambda eng: eng.matmul(
                BB[:, 256:384], sol[:, (6 + kk) * 128:(6 + kk + 1) * 128], up,
                start=True, stop=True))(k, uprev),
              reads=[rsol, rU], writes=[rBB], tag="h_a%d" % k + tg)
            S("scalar", (lambda eng_t=None: lambda eng: eng.copy(Utmp[:], BB[:, 256:384]))(),
              reads=[rBB], writes=[rUt], tag="h_ue%d" % k + tg)
            S("vector", (lambda un, up: lambda eng: eng.tensor_tensor(
                un, up, Utmp[:], ALU.add))(unew, uprev),
              reads=[rU, rUt], writes=[rU], tag="h_u%d" % (k + 1) + tg)
        ufin = U[:, 0:128]

        S("vector", lambda eng: eng.tensor_scalar(QSe[g3][:], BB[:, 128:256],
                                                  co["qGam"][:, h:h + 1], None, ALU.mult),
          reads=[rBB, "qGam%d" % cp], writes=["QSe%d" % g3], tag="h_qse" + tg)
        S("tensor", lambda eng: eng.matmul(BB[:, 384:512], sol[:, 13 * 128:14 * 128],
                                           ufin, start=True, stop=True),
          reads=[rsol, rU], writes=[rBB], tag="h_oi" + tg)
        S("vector", lambda eng: eng.tensor_tensor(Ot[g3][:], QSe[g3][:], BB[:, 384:512], ALU.add),
          reads=["QSe%d" % g3, rBB], writes=["Ot%d" % g3], tag="h_o" + tg)

        S("gpsimd", (lambda h_: lambda eng: eng.tensor_tensor(
            y1[:, h_ * 128:(h_ + 1) * 128], Ot[g3][:],
            zsil_t[:, h_ * 128:(h_ + 1) * 128], ALU.mult))(h),
          reads=["Ot%d" % g3, "zsil%d" % cp], writes=["y1"], tag="h_y1" + tg)
        S("scalar", (lambda h_: lambda eng: eng.activation(
            Esb[g3][:, 0:128], y1[:, h_ * 128:(h_ + 1) * 128], AF.Square,
            accum_out=co["msC"][:, h_:h_ + 1]))(h),
          reads=["y1", "msC%d" % cp], writes=["Esb%d" % g3, "msC%d" % cp], tag="h_ms" + tg)

        S("tensor", lambda eng: eng.matmul(BB[:, 384:512], KBt[g3][:], ufin,
                                           start=True, stop=True),
          reads=["KBt%d" % g3, rU], writes=[rBB], tag="h_st" + tg)
        S("vector", (lambda h_: lambda eng: eng.scalar_tensor_tensor(
            Sst[:, h_ * 128:(h_ + 1) * 128], Sst[:, h_ * 128:(h_ + 1) * 128],
            co["coefC"][:, h_:h_ + 1], BB[:, 384:512], ALU.mult, ALU.add))(h),
          reads=["Sst", "coefC%d" % cp, rBB], writes=["Sst"], tag="h_sn" + tg)
        return steps

    def ph_d(self, hh):
        nc, p, sb, dr, pb = self.nc, self.p, self.sb, self.dr, self.pb
        T, TH = self.T, self.TH
        TQ = min(512, TH)
        nq = TH // TQ
        at = self.at
        yh = at("yh%d" % hh, [128, 32 * TH], BF16, 0)
        xnew = at("xnew%d" % hh, [128, 16 * TH], F32, 64 * 1024)
        wco = [at("wco%d_%d" % (hh, i), [128, 32 * 128], BF16, 128 * 1024 + i * 8192)
               for i in range(2)]
        xold = at("xold%d" % hh, [128, TQ], F32, 144 * 1024)
        xbf = at("xbf2_%d" % hh, [128, TQ], BF16, 147 * 1024)
        xsq = at("xsq2_%d" % hh, [128, TQ], BF16, 149 * 1024)
        dbc = at("dbc%d" % hh, [128, 2 * TQ], F32, 151 * 1024)
        drow = at("drow%d" % hh, [1, 4 * TQ], F32, 156 * 1024)
        lnsrow = at("lnsrow%d" % hh, [1, TH], F32, 168 * 1024)
        lnmrow = at("lnmrow%d" % hh, [1, TH], F32, 172 * 1024)
        t0h = hh * TH

        for q in range(4):
            src = self.y_s.rearrange("(hv p) t -> p hv t", p=128)[:, q * 8:(q + 1) * 8, t0h:t0h + TH]
            dst = yh[:].rearrange("p (hv t) -> p hv t", t=TH)[:, q * 8:(q + 1) * 8, :]
            p.step("sync", (lambda s_, d_: lambda eng: eng.dma_start(out=d_, in_=s_))(src, dst),
                   writes=["yh"], dma=True, tag="d_ly%d_%d" % (hh, q))

        for dm in range(16):
            par = dm % 2
            src = dr["Wout"].rearrange("(hv p) c -> p hv c", p=128)[:, :, dm * 128:(dm + 1) * 128]
            for q in range(2):
                p.step("gpsimd",
                       (lambda s_, d_, q_: lambda eng: eng.dma_start(
                           out=d_[:].rearrange("p (hv c) -> p hv c", c=128)[:, q_ * 16:(q_ + 1) * 16, :],
                           in_=s_[:, q_ * 16:(q_ + 1) * 16, :]))(src, wco[par], q),
                       writes=["wco%d" % par], dma=True, tag="d_lw%d_%d_%d" % (hh, dm, q))
            for tq in range(nq):
                bnk = pb[par]
                rb = "pb%d" % par

                def em_mm(par_, tq_):
                    def em(eng):
                        inst = None
                        for hv in range(32):
                            inst = eng.matmul(
                                pb[par_][:, 0:TQ], wco[par_][:, hv * 128:(hv + 1) * 128],
                                yh[:].rearrange("p (hv t) -> p hv t", t=TH)[:, hv, tq_ * TQ:(tq_ + 1) * TQ],
                                start=(hv == 0), stop=(hv == 31))
                        return inst
                    return em
                p.step("tensor", em_mm(par, tq), reads=["wco%d" % par, "yh"],
                       writes=["pb%d" % par], tag="d_mm%d_%d_%d" % (hh, dm, tq))
                src2 = dr["xT"][dm * 128:(dm + 1) * 128, t0h + tq * TQ: t0h + (tq + 1) * TQ]
                p.step("sync", (lambda s_: lambda eng: eng.dma_start(out=xold[:], in_=s_))(src2),
                       writes=["xold"], dma=True, tag="d_lx%d_%d_%d" % (hh, dm, tq))

                def em_add(par_, dm_, tq_):
                    def em(eng):
                        return eng.tensor_tensor(
                            xnew[:, dm_ * TH + tq_ * TQ: dm_ * TH + (tq_ + 1) * TQ],
                            xold[:], pb[par_][:, 0:TQ], ALU.add)
                    return em
                p.step("vector", em_add(par, dm, tq), reads=["xold", "pb%d" % par],
                       writes=["xnew"], tag="d_add%d_%d_%d" % (hh, dm, tq))

        onescolb, onesrow, cv = sb["onescolb"], sb["onesrow"], sb["cvals"]
        for tq in range(nq):
            for dm in range(16):
                def em_cast(dm_, tq_):
                    def em(eng):
                        return eng.tensor_copy(xbf[:], xnew[:, dm_ * TH + tq_ * TQ: dm_ * TH + (tq_ + 1) * TQ])
                    return em
                p.step("vector", em_cast(dm, tq), reads=["xnew"], writes=["xbf"],
                       tag="d_c%d_%d_%d" % (hh, dm, tq))
                p.step("scalar", lambda eng: eng.activation(xsq[:], xbf[:], AF.Square),
                       reads=["xbf"], writes=["xsq"], tag="d_q%d_%d_%d" % (hh, dm, tq))

                def em_smm(dm_):
                    def em(eng):
                        eng.matmul(pb[4][0:1, 0:TQ], onescolb[:], xbf[:],
                                   start=(dm_ == 0), stop=(dm_ == 15))
                        return eng.matmul(pb[5][0:1, 0:TQ], onescolb[:], xsq[:],
                                          start=(dm_ == 0), stop=(dm_ == 15))
                    return em
                p.step("tensor", em_smm(dm), reads=["xbf", "xsq", "onescolb"],
                       writes=["pb4", "pb5"], tag="d_s%d_%d_%d" % (hh, dm, tq))

            def em_rows(eng):
                eng.tensor_scalar(drow[0:1, 0:TQ], pb[4][0:1, 0:TQ], 1.0 / D, None, ALU.mult)
                eng.tensor_scalar(drow[0:1, 3 * TQ:4 * TQ], pb[5][0:1, 0:TQ], 1.0 / D, None, ALU.mult)
                eng.tensor_tensor(drow[0:1, TQ:2 * TQ], drow[0:1, 0:TQ], drow[0:1, 0:TQ], ALU.mult)
                return eng.tensor_tensor(drow[0:1, 3 * TQ:4 * TQ], drow[0:1, 3 * TQ:4 * TQ],
                                         drow[0:1, TQ:2 * TQ], ALU.subtract)
            p.step("vector", em_rows, reads=["pb4", "pb5"], writes=["drow"],
                   tag="d_r%d_%d" % (hh, tq))

            def em_rstd(eng):
                eng.activation(drow[0:1, 2 * TQ:3 * TQ], drow[0:1, 3 * TQ:4 * TQ], AF.Ln,
                               bias=cv[0:1, 0:1])
                return eng.activation(drow[0:1, TQ:2 * TQ], drow[0:1, 2 * TQ:3 * TQ],
                                      AF.Exp, scale=-0.5)
            p.step("scalar", em_rstd, reads=["drow", "cvals"], writes=["drow"],
                   tag="d_rs%d_%d" % (hh, tq))
            p.step("vector", lambda eng: eng.tensor_tensor(
                drow[0:1, 2 * TQ:3 * TQ], drow[0:1, 0:TQ], drow[0:1, TQ:2 * TQ], ALU.mult),
                reads=["drow"], writes=["drow"], tag="d_ms%d_%d" % (hh, tq))

            def em_carry(tq_):
                def em(eng):
                    eng.tensor_copy(lnsrow[0:1, tq_ * TQ:(tq_ + 1) * TQ], drow[0:1, TQ:2 * TQ])
                    return eng.tensor_copy(lnmrow[0:1, tq_ * TQ:(tq_ + 1) * TQ], drow[0:1, 2 * TQ:3 * TQ])
                return em
            p.step("vector", em_carry(tq), reads=["drow"],
                   writes=["lnsrow", "lnmrow"], tag="d_cr%d_%d" % (hh, tq))
            p.step("tensor", lambda eng: eng.matmul(pb[6][:, 0:TQ], onesrow[:],
                                                    drow[0:1, TQ:2 * TQ], start=True, stop=True),
                   reads=["drow", "onesrow"], writes=["pb6"], tag="d_bs%d_%d" % (hh, tq))
            p.step("tensor", lambda eng: eng.matmul(pb[7][:, 0:TQ], onesrow[:],
                                                    drow[0:1, 2 * TQ:3 * TQ], start=True, stop=True),
                   reads=["drow", "onesrow"], writes=["pb7"], tag="d_bm%d_%d" % (hh, tq))
            p.step("vector", lambda eng: eng.tensor_copy(dbc[:, 0:TQ], pb[6][:, 0:TQ]),
                   reads=["pb6"], writes=["dbc"], tag="d_es%d_%d" % (hh, tq))
            p.step("vector", lambda eng: eng.tensor_copy(dbc[:, TQ:2 * TQ], pb[7][:, 0:TQ]),
                   reads=["pb7"], writes=["dbc"], tag="d_em%d_%d" % (hh, tq))

            for dm in range(16):
                def em_app(dm_, tq_):
                    def em(eng):
                        xs = xnew[:, dm_ * TH + tq_ * TQ: dm_ * TH + (tq_ + 1) * TQ]
                        eng.tensor_tensor(xbf[:], xs, dbc[:, 0:TQ], ALU.mult)
                        return eng.tensor_tensor(xbf[:], xbf[:], dbc[:, TQ:2 * TQ], ALU.subtract)
                    return em
                p.step("vector", em_app(dm, tq), reads=["xnew", "dbc"],
                       writes=["xbf"], tag="d_a%d_%d_%d" % (hh, dm, tq))
                dst = self.ln2_s[dm * 128:(dm + 1) * 128, t0h + tq * TQ: t0h + (tq + 1) * TQ]
                p.step("sync", (lambda d_: lambda eng: eng.dma_start(out=d_, in_=xbf[:]))(dst),
                       reads=["xbf"], dma=True, tag="d_st%d_%d_%d" % (hh, dm, tq))

    def ph_e(self, hh):
        nc, p, sb, dr, pb = self.nc, self.p, self.sb, self.dr, self.pb
        T, TH = self.T, self.TH
        t0h = hh * TH
        at = self.at
        Hb = at("Hb%d" % hh, [128, 16 * TH], BF16, 0)
        ln2x = at("ln2x%d" % hh, [128, 16 * TH], BF16, 32 * 1024)
        xnew = self.sb["xnew%d" % hh]
        wgb = [at("wgb%d_%d" % (hh, i), [128, 16 * 128], BF16, 128 * 1024 + i * 4096)
               for i in range(2)]
        wub = [at("wub%d_%d" % (hh, i), [128, 16 * 128], BF16, 136 * 1024 + i * 4096)
               for i in range(2)]
        wdb = [at("wdb%d_%d" % (hh, i), [128, 16 * 128], BF16, 144 * 1024 + i * 4096)
               for i in range(2)]
        wrf = at("wrf%d" % hh, [128, 16 * 8], F32, 152 * 1024)
        wfin = at("wfin%d" % hh, [128, 8 * (TH // 128)], F32, 154 * 1024)
        wT = at("wT%d" % hh, [8, TH], BF16, 156 * 1024)
        wrow = at("wrow%d" % hh, [1, TH], BF16, 158 * 1024)
        wbt = at("wbt%d" % hh, [128, TH], F32, 160 * 1024)
        sG = at("sG%d" % hh, [128, TH], BF16, 164 * 1024)
        lnsrow = self.sb["lnsrow%d" % hh]
        lnmrow = self.sb["lnmrow%d" % hh]
        ntt = TH // 128

        for q in range(4):
            src = self.ln2_s.rearrange("(dt p) t -> p dt t", p=128)[:, q * 4:(q + 1) * 4, t0h:t0h + TH]
            dst = ln2x[:].rearrange("p (dt t) -> p dt t", t=TH)[:, q * 4:(q + 1) * 4, :]
            p.step("sync", (lambda s_, d_: lambda eng: eng.dma_start(out=d_, in_=s_))(src, dst),
                   writes=["ln2x"], dma=True, tag="e_ll%d_%d" % (hh, q))
        p.step("sync", lambda eng: eng.dma_start(
            out=wrf[:].rearrange("p (dt c) -> p dt c", c=8),
            in_=dr["Wr"].rearrange("(dt p) n -> p dt n", p=128)),
            writes=["wrf"], dma=True, tag="e_lwr%d" % hh)

        iota8, iota8b, neg18, cv = sb["iota8"], sb["iota8b"], sb["neg18"], sb["cvals"]
        onescol, onesrow = sb["onescol"], sb["onesrow"]
        GL = float(0.5 - np.log(4.0 / 3.0))
        NTTE = ntt * 8
        rts = self.at("rts%d" % hh, [128, 4 * NTTE], F32, 153 * 1024)
        rtv = self.at("rtv%d" % hh, [128, 4 * NTTE], F32, 153 * 1024 + 16 * NTTE)
        rtc = self.at("rtc%d" % hh, [128, 128], F32, 153 * 1024 + 32 * NTTE)

        def g3(ap):
            return ap.rearrange("p (tt e) -> p tt e", e=8)

        def cb(ap):
            return ap.unsqueeze(2).to_broadcast([128, ntt, 8])

        def eb(ap):
            return ap.unsqueeze(1).to_broadcast([128, ntt, 8])

        def em_csum(eng):
            inst = None
            for d in range(16):
                inst = eng.matmul(pb[7][0:1, 0:8], onescol[:], wrf[:, d * 8:(d + 1) * 8],
                                  start=(d == 0), stop=(d == 15))
            return inst
        p.step("tensor", em_csum, reads=["wrf", "onescol"], writes=["pb7"],
               tag="e_cs%d" % hh)
        p.step("vector", lambda eng: eng.tensor_copy(rtc[0:1, 96:104], pb[7][0:1, 0:8]),
               reads=["pb7"], writes=["rtc"], tag="e_cse%d" % hh)
        p.step("tensor", lambda eng: eng.matmul(pb[7][:, 0:8], onesrow[:], rtc[0:1, 96:104],
                                                start=True, stop=True),
               reads=["rtc", "onesrow"], writes=["pb7"], tag="e_csb%d" % hh)
        p.step("vector", lambda eng: eng.tensor_copy(rtc[:, 104:112], pb[7][:, 0:8]),
               reads=["pb7"], writes=["rtc"], tag="e_csb2%d" % hh)

        def em_rmm(tt_):
            def em(eng):
                inst = None
                for d in range(16):
                    inst = eng.matmul(pb[6][0:128, tt_ * 8:(tt_ + 1) * 8],
                                      xnew[:, d * TH + tt_ * 128: d * TH + (tt_ + 1) * 128],
                                      wrf[:, d * 8:(d + 1) * 8],
                                      start=(d == 0), stop=(d == 15))
                return inst
            return em

        def em_scol(tt_):
            def em(eng):
                eng.matmul(pb[7][:, 16 + tt_:17 + tt_], lnsrow[0:1, tt_ * 128:(tt_ + 1) * 128],
                           onesrow[0:1, 0:1], start=True, stop=True)
                return eng.matmul(pb[7][:, 24 + tt_:25 + tt_], lnmrow[0:1, tt_ * 128:(tt_ + 1) * 128],
                                  onesrow[0:1, 0:1], start=True, stop=True)
            return em
        for tt in range(ntt):
            p.step("tensor", em_rmm(tt), reads=["xnew", "wrf"], writes=["pb6"],
                   tag="e_rm%d_%d" % (hh, tt))
            p.step("tensor", em_scol(tt), reads=["lnsrow", "lnmrow", "onesrow"],
                   writes=["pb7"], tag="e_sc%d_%d" % (hh, tt))

        def em_sme(eng):
            eng.tensor_copy(rtc[:, 80:80 + ntt], pb[7][:, 16:16 + ntt])
            return eng.tensor_copy(rtc[:, 88:88 + ntt], pb[7][:, 24:24 + ntt])
        p.step("vector", em_sme, reads=["pb7"], writes=["rtc"], tag="e_sme%d" % hh)

        def em_arg(eng):
            eng.tensor_tensor(g3(rtv[:, NTTE:2 * NTTE]),
                              cb(rtc[:, 88:88 + ntt]), eb(rtc[:, 104:112]), ALU.mult)
            eng.tensor_tensor(g3(rtv[:, 0:NTTE]),
                              g3(pb[6][0:128, 0:NTTE]), cb(rtc[:, 80:80 + ntt]), ALU.mult)
            return eng.tensor_tensor(rtv[:, 0:NTTE], rtv[:, 0:NTTE],
                                     rtv[:, NTTE:2 * NTTE], ALU.subtract)
        p.step("vector", em_arg, reads=["pb6"], sereads=["rtc"], writes=["rtv"],
               tag="e_arg%d" % hh)

        def em_racts(eng):
            eng.activation(rts[:, 0:NTTE], rtv[:, 0:NTTE], AF.Sigmoid, scale=cv[:, 4:5])
            eng.activation(rts[:, NTTE:2 * NTTE], rts[:, 0:NTTE], AF.Abs, bias=cv[:, 5:6])
            eng.activation(rts[:, 2 * NTTE:3 * NTTE], rts[:, NTTE:2 * NTTE], AF.Exp, scale=-10.0)
            return eng.activation(rts[:, 3 * NTTE:4 * NTTE], rts[:, NTTE:2 * NTTE], AF.Exp,
                                  scale=float(-1.0 / 0.3))
        p.step("scalar", em_racts, reads=["rtv", "cvals"], writes=["rts"],
               tag="e_ra%d" % hh)

        inh = rts[:, 0:NTTE]
        w0 = rts[:, 2 * NTTE:3 * NTTE]
        fb = rts[:, 3 * NTTE:4 * NTTE]
        e1 = rtv[:, 0:NTTE]
        tv = rtv[:, NTTE:2 * NTTE]
        w8 = rtv[:, 2 * NTTE:3 * NTTE]
        fbw = rtv[:, 3 * NTTE:4 * NTTE]
        sA = rtc[:, 0:ntt]
        ne = rtc[:, 8:8 + ntt]
        m1 = rtc[:, 16:16 + ntt]
        i1 = rtc[:, 24:24 + ntt]
        m2 = rtc[:, 32:32 + ntt]
        i2 = rtc[:, 40:40 + ntt]
        sB = rtc[:, 48:48 + ntt]
        sC = rtc[:, 56:56 + ntt]
        sD = rtc[:, 64:64 + ntt]

        def em_v1(eng):
            eng.tensor_scalar(e1, inh, GL, None, ALU.is_ge)
            eng.tensor_scalar(tv, inh, 0.5, None, ALU.is_le)
            eng.tensor_tensor(e1, e1, tv, ALU.mult)
            eng.tensor_tensor(w8, w0, e1, ALU.mult)
            eng.tensor_reduce(sA, g3(w8), axis=AX.X, op=ALU.add)
            eng.tensor_reduce(m1, g3(fb), axis=AX.X, op=ALU.max)
            return eng.tensor_scalar(ne, sA, 1e-8, None, ALU.is_lt)
        p.step("vector", em_v1, reads=["rts"], writes=["rtv", "rtc"],
               tag="e_v1%d" % hh)

        def em_v2(eng):
            eng.tensor_tensor(g3(tv), g3(fb), cb(m1), ALU.is_equal)
            eng.scalar_tensor_tensor(g3(tv), g3(tv), -100.0, eb(iota8b[:]), ALU.mult, ALU.add)
            return eng.tensor_reduce(i1, g3(tv), axis=AX.X, op=ALU.min)
        p.step("vector", em_v2, reads=["rts"], sereads=["rtc"], writes=["rtv", "rtc"],
               tag="e_v2%d" % hh)

        def em_v3(eng):
            eng.tensor_tensor(g3(e1), eb(iota8[:]), cb(i1), ALU.is_equal)
            eng.tensor_scalar(tv, fb, 1.0, None, ALU.add)
            eng.tensor_tensor(tv, e1, tv, ALU.mult)
            eng.tensor_tensor(tv, fb, tv, ALU.subtract)
            return eng.tensor_reduce(m2, g3(tv), axis=AX.X, op=ALU.max)
        p.step("vector", em_v3, reads=["rts"], sereads=["rtc"], writes=["rtv", "rtc"],
               tag="e_v3%d" % hh)

        def em_v4(eng):
            eng.tensor_tensor(g3(fbw), g3(tv), cb(m2), ALU.is_equal)
            eng.scalar_tensor_tensor(g3(fbw), g3(fbw), -100.0, eb(iota8b[:]), ALU.mult, ALU.add)
            return eng.tensor_reduce(i2, g3(fbw), axis=AX.X, op=ALU.min)
        p.step("vector", em_v4, sereads=["rtc"], reads=["rtv"], writes=["rtv", "rtc"],
               tag="e_v4%d" % hh)

        def em_v5(eng):
            eng.tensor_tensor(g3(tv), eb(iota8[:]), cb(i2), ALU.is_equal)
            eng.tensor_tensor(e1, e1, tv, ALU.add)
            eng.tensor_tensor(fbw, fb, e1, ALU.mult)
            return eng.tensor_reduce(sB, g3(fbw), axis=AX.X, op=ALU.add)
        p.step("vector", em_v5, reads=["rts"], sereads=["rtc"], writes=["rtv", "rtc"],
               tag="e_v5%d" % hh)

        def em_v6(eng):
            eng.tensor_scalar(sC, sB, 1e-8, None, ALU.max)
            eng.tensor_scalar(tv, fbw, 1.0, None, ALU.mult)
            return eng.reciprocal(sD, sC)
        p.step("vector", em_v6, sereads=["rtc"], reads=["rtv"], writes=["rtv", "rtc"],
               tag="e_v6%d" % hh)

        def em_v7(eng):
            eng.tensor_tensor(g3(fbw), g3(fbw), cb(sD), ALU.mult)
            eng.tensor_tensor(tv, fbw, w8, ALU.subtract)
            eng.tensor_tensor(g3(tv), g3(tv), cb(ne), ALU.mult)
            eng.tensor_tensor(w8, w8, tv, ALU.add)
            return eng.tensor_reduce(sB, g3(w8), axis=AX.X, op=ALU.add)
        p.step("vector", em_v7, sereads=["rtc"], reads=["rtv"], writes=["rtv", "rtc"],
               tag="e_v7%d" % hh)

        def em_v8(eng):
            eng.tensor_scalar(sC, sB, 1e-8, None, ALU.max)
            eng.tensor_scalar(tv, w8, 1.0, None, ALU.mult)
            return eng.reciprocal(sD, sC)
        p.step("vector", em_v8, sereads=["rtc"], reads=["rtv"], writes=["rtv", "rtc"],
               tag="e_v8%d" % hh)

        def em_v9(eng):
            return eng.tensor_tensor(g3(wfin[:]), g3(w8), cb(sD), ALU.mult)
        p.step("vector", em_v9, sereads=["rtc"], reads=["rtv"], writes=["wfin"],
               tag="e_v9%d" % hh)

        for tt in range(ntt):
            p.step("tensor", (lambda tt_: lambda eng: eng.transpose(
                pb[7][0:8, 0:128], wfin[:, tt_ * 8:(tt_ + 1) * 8], sb["ident"][:]))(tt),
                   reads=["wfin", "ident"], writes=["pb7"], tag="e_wt%d_%d" % (hh, tt))
            p.step("vector", (lambda tt_: lambda eng: eng.tensor_copy(
                wT[0:8, tt_ * 128:(tt_ + 1) * 128], pb[7][0:8, 0:128]))(tt),
                   reads=["pb7"], writes=["wT"], tag="e_we%d_%d" % (hh, tt))

        NTB = TH // 512 if TH >= 512 else 1
        TBW = min(512, TH)
        for e in range(NE):
            p.step("sync", (lambda e_: lambda eng: eng.dma_start(
                out=wrow[:], in_=wT[e_:e_ + 1, :]))(e),
                reads=["wT"], writes=["wrow"], dma=True, tag="e_wr%d_%d" % (hh, e))

            def em_wb(eng):
                inst = None
                for tb in range(NTB):
                    inst = eng.matmul(pb[6][:, 0:TBW], sb["onesrow"][:],
                                      wrow[0:1, tb * TBW:(tb + 1) * TBW], start=True, stop=True)
                return inst
            for tb in range(NTB):
                bk = 6 + (tb % 2)
                p.step("tensor", (lambda tb_, bk_: lambda eng: eng.matmul(
                    pb[bk_][:, 0:TBW], sb["onesrowb"][:],
                    wrow[0:1, tb_ * TBW:(tb_ + 1) * TBW], start=True, stop=True))(tb, bk),
                    reads=["wrow", "onesrowb"], writes=["pb%d" % bk], tag="e_wb%d_%d_%d" % (hh, e, tb))
                p.step("vector", (lambda tb_, bk_: lambda eng: eng.tensor_copy(
                    wbt[:, tb_ * TBW:(tb_ + 1) * TBW], pb[bk_][:, 0:TBW]))(tb, bk),
                    reads=["pb%d" % bk], writes=["wbt"], tag="e_we2%d_%d_%d" % (hh, e, tb))

            for f in range(16):
                par = f % 2
                srcg = dr["Wg"][e].rearrange("(dt p) n -> p dt n", p=128)[:, :, f * 128:(f + 1) * 128]
                srcu = dr["Wu"][e].rearrange("(dt p) n -> p dt n", p=128)[:, :, f * 128:(f + 1) * 128]
                for q in range(2):
                    p.step("gpsimd", (lambda s_, d_, q_: lambda eng: eng.dma_start(
                        out=d_[:].rearrange("p (dt c) -> p dt c", c=128)[:, q_ * 8:(q_ + 1) * 8, :],
                        in_=s_[:, q_ * 8:(q_ + 1) * 8, :]))(srcg, wgb[par], q),
                        writes=["wgb%d" % par], dma=True, tag="e_lg%d_%d_%d_%d" % (hh, e, f, q))
                    p.step("gpsimd", (lambda s_, d_, q_: lambda eng: eng.dma_start(
                        out=d_[:].rearrange("p (dt c) -> p dt c", c=128)[:, q_ * 8:(q_ + 1) * 8, :],
                        in_=s_[:, q_ * 8:(q_ + 1) * 8, :]))(srcu, wub[par], q),
                        writes=["wub%d" % par], dma=True, tag="e_lu%d_%d_%d_%d" % (hh, e, f, q))
                gb = [par * 4, par * 4 + 1]
                ub = [2 + par * 4 - par * 3, 0]
                gb = [0 + par * 4, 1 + par * 4]
                ubk = [2 + par * 4, 3 + par * 4]

                def em_gu(par_, f_):
                    def em(eng):
                        inst = None
                        for d in range(16):
                            lg = wgb[par_][:, d * 128:(d + 1) * 128]
                            lu = wub[par_][:, d * 128:(d + 1) * 128]
                            for tb in range(NTB):
                                inst = eng.matmul(pb[par_ * 4][:, 0:TBW] if tb == 0 else pb[par_ * 4 + 1][:, 0:TBW],
                                                  lg,
                                                  ln2x[:].rearrange("p (dt t) -> p dt t", t=TH)[:, d, tb * TBW:(tb + 1) * TBW],
                                                  start=(d == 0), stop=(d == 15))
                            for tb in range(NTB):
                                inst = eng.matmul(pb[par_ * 4 + 2][:, 0:TBW] if tb == 0 else pb[par_ * 4 + 3][:, 0:TBW],
                                                  lu,
                                                  ln2x[:].rearrange("p (dt t) -> p dt t", t=TH)[:, d, tb * TBW:(tb + 1) * TBW],
                                                  start=(d == 0), stop=(d == 15))
                        return inst
                    return em
                rbanks = ["pb%d" % (par * 4 + i) for i in range(4)]
                p.step("tensor", em_gu(par, f),
                       reads=["wgb%d" % par, "wub%d" % par, "ln2x"],
                       writes=rbanks, tag="e_gu%d_%d_%d" % (hh, e, f))

                def em_sg(par_):
                    def em(eng):
                        inst = None
                        for tb in range(NTB):
                            inst = eng.activation(sG[:, tb * TBW:(tb + 1) * TBW],
                                                  pb[par_ * 4 + min(tb, 1)][:, 0:TBW], AF.Silu)
                        return inst
                    return em
                p.step("scalar", em_sg(par), reads=rbanks[:2], writes=["sG"],
                       tag="e_sg%d_%d_%d" % (hh, e, f))

                def em_h(par_, f_):
                    def em(eng):
                        inst = None
                        for tb in range(NTB):
                            hslice = Hb[:, f_ * TH + tb * TBW: f_ * TH + (tb + 1) * TBW]
                            inst = eng.tensor_tensor(hslice, pb[par_ * 4 + 2 + min(tb, 1)][:, 0:TBW],
                                                     sG[:, tb * TBW:(tb + 1) * TBW], ALU.mult)
                            inst = eng.tensor_tensor(hslice, hslice,
                                                     wbt[:, tb * TBW:(tb + 1) * TBW], ALU.mult)
                        return inst
                    return em
                p.step("vector", em_h(par, f), reads=rbanks[2:] + ["sG", "wbt"],
                       writes=["Hb"], tag="e_h%d_%d_%d" % (hh, e, f))

            for dmt in range(16):
                par = dmt % 2
                srcd = dr["Wd"][e].rearrange("(ft p) n -> p ft n", p=128)[:, :, dmt * 128:(dmt + 1) * 128]
                for q in range(2):
                    p.step("gpsimd", (lambda s_, d_, q_: lambda eng: eng.dma_start(
                        out=d_[:].rearrange("p (ft c) -> p ft c", c=128)[:, q_ * 8:(q_ + 1) * 8, :],
                        in_=s_[:, q_ * 8:(q_ + 1) * 8, :]))(srcd, wdb[par], q),
                        writes=["wdb%d" % par], dma=True, tag="e_ld%d_%d_%d_%d" % (hh, e, dmt, q))

                def em_dmm(par_, dmt_):
                    def em(eng):
                        inst = None
                        for ft in range(16):
                            lw = wdb[par_][:, ft * 128:(ft + 1) * 128]
                            for tb in range(NTB):
                                inst = eng.matmul(pb[par_ * 4 + tb][:, 0:TBW], lw,
                                                  Hb[:].rearrange("p (ft t) -> p ft t", t=TH)[:, ft, tb * TBW:(tb + 1) * TBW],
                                                  start=(ft == 0), stop=(ft == 15))
                        return inst
                    return em
                rb2 = ["pb%d" % (par * 4 + tb) for tb in range(NTB)]
                p.step("tensor", em_dmm(par, dmt), reads=["wdb%d" % par, "Hb"],
                       writes=rb2, tag="e_dm%d_%d_%d" % (hh, e, dmt))

                def em_acc(par_, dmt_):
                    def em(eng):
                        inst = None
                        for tb in range(NTB):
                            xs = xnew[:, dmt_ * TH + tb * TBW: dmt_ * TH + (tb + 1) * TBW]
                            inst = eng.tensor_tensor(xs, xs, pb[par_ * 4 + tb][:, 0:TBW], ALU.add)
                        return inst
                    return em
                p.step("vector", em_acc(par, dmt), reads=rb2 + ["xnew"],
                       writes=["xnew"], tag="e_ac%d_%d_%d" % (hh, e, dmt))

        for dm in range(16):
            dst = self.out[dm * 128:(dm + 1) * 128, t0h:t0h + TH]
            p.step("sync", (lambda d_, dm_: lambda eng: eng.dma_start(
                out=d_, in_=xnew[:, dm_ * TH:(dm_ + 1) * TH]))(dst, dm),
                reads=["xnew"], dma=True, tag="e_out%d_%d" % (hh, dm))

    def emit(self):
        nc, p = self.nc, self.p
        import contextlib
        stack = contextlib.ExitStack()
        sems, dsems = {}, {}
        for e in ENGINES:
            sems[e] = stack.enter_context(nc.semaphore("sem_" + e))
        for e in ("sync", "gpsimd", "scalar"):
            dsems[e] = stack.enter_context(nc.semaphore("dsem_" + e))
        p.finalize()
        with stack:
            with nc.Block() as block:
                @block.sync
                def _(eng):
                    p.emit_engine("sync", eng, sems, dsems)
                    nd = sum(1 for s in p.steps["sync"] if s.dma)
                    if nd:
                        eng.wait_ge(dsems["sync"], 16 * nd)

                @block.gpsimd
                def _(eng):
                    p.emit_engine("gpsimd", eng, sems, dsems)
                    nd = sum(1 for s in p.steps["gpsimd"] if s.dma)
                    if nd:
                        eng.wait_ge(dsems["gpsimd"], 16 * nd)

                @block.tensor
                def _(eng):
                    p.emit_engine("tensor", eng, sems, dsems)

                @block.vector
                def _(eng):
                    p.emit_engine("vector", eng, sems, dsems)

                @block.scalar
                def _(eng):
                    p.emit_engine("scalar", eng, sems, dsems)
                    nd = sum(1 for s in p.steps["scalar"] if s.dma)
                    if nd:
                        eng.wait_ge(dsems["scalar"], 16 * nd)


def host_inputs(T, inputs, core):
    cons = make_consts(T)
    m = {
        "xT": np.ascontiguousarray(inputs["L0_attn_x"][core, :T, :].T),
        "Wqkv": inputs["L0_attn_Wqkv"],
        "Wz": inputs["L0_attn_Wz"],
        "Wab": np.ascontiguousarray(
            np.concatenate([inputs["L0_attn_Wa"], inputs["L0_attn_Wb"]], axis=1)),
        "convw": inputs["L0_attn_conv_w"],
        "A_log": inputs["L0_attn_A_log"],
        "dt_bias": inputs["L0_attn_dt_bias"],
        "norm_w": inputs["L0_attn_norm_w"],
        "Wout": inputs["L0_attn_Wout"],
        "Wg": inputs["L0_ffn_Wg"],
        "Wu": inputs["L0_ffn_Wu"],
        "Wd": inputs["L0_ffn_Wd"],
        "Wr": inputs["L0_ffn_Wr"],
        "temperature": inputs["L0_ffn_temperature"],
        "cu_seqlens": inputs["L0_attn_cu_seqlens"].astype(np.int32),
    }
    m.update(cons)
    return m


def kernel(**inputs):
    T = 2048
    k = Ker(T)
    nc = k.build()
    in_maps = [host_inputs(T, inputs, c) for c in range(NB)]
    res = run_bass_kernel_spmd(nc, in_maps, core_ids=list(range(NB)))
    outs = [np.ascontiguousarray(r["out"].T) for r in res.results]
    return np.stack(outs, axis=0)


# version 15
# speedup vs baseline: 1.0142x; 1.0142x over previous
import numpy as np
import concourse.bass as bass
import concourse.mybir as mybir
from concourse.bass_utils import run_bass_kernel_spmd

F32 = mybir.dt.float32
BF16 = mybir.dt.bfloat16
I32 = mybir.dt.int32
AF = mybir.ActivationFunctionType
ALU = mybir.AluOpType
AX = mybir.AxisListType

D = 2048
NB = 8
HK, HV, DK, DV = 16, 32, 128, 128
KC = 4
NE, FF = 8, 2048
C = 128
EPS = 1e-6
LNEPS = 1e-5
BIG = 30000.0
BIGM = 1e7

ENGINES = ("tensor", "vector", "scalar", "gpsimd", "sync")
_DTB = {F32: 4, BF16: 2, I32: 4}


class _Step:
    __slots__ = ("engine", "emit", "deps", "dma", "idx", "tag", "need_inc",
                 "inc_idx", "dma_idx")

    def __init__(self, engine, emit, dma, tag):
        self.engine = engine
        self.emit = emit
        self.dma = dma
        self.tag = tag
        self.deps = []
        self.need_inc = False
        self.inc_idx = None
        self.dma_idx = None


class Prog:

    def __init__(self):
        self.steps = {e: [] for e in ENGINES}
        self.last_w = {}
        self.readers = {}
        self.pending_bar = {e: [] for e in ENGINES}

    def step(self, engine, emit, reads=(), writes=(), dma=False, tag="",
             sereads=()):
        st = _Step(engine, emit, dma, tag)
        deps = set()
        forced = set()
        for r in reads:
            w = self.last_w.get(r)
            if w is not None:
                deps.add(w)
        for r in sereads:
            w = self.last_w.get(r)
            if w is not None:
                deps.add(w)
                forced.add(w)
        for w_ in writes:
            pw = self.last_w.get(w_)
            if pw is not None:
                deps.add(pw)
            for rd in self.readers.get(w_, ()):
                deps.add(rd)
        for d in self.pending_bar[engine]:
            deps.add(d)
        self.pending_bar[engine] = []
        st.deps = [d for d in deps
                   if d in forced or not (d.engine == engine and (d.dma == dma))]
        st.idx = len(self.steps[engine])
        self.steps[engine].append(st)
        for r in list(reads) + list(sereads):
            self.readers.setdefault(r, []).append(st)
        for w_ in writes:
            self.last_w[w_] = st
            self.readers[w_] = []
        return st

    def barrier(self):
        sigs = []
        for e in ENGINES:
            comp = next((s for s in reversed(self.steps[e]) if not s.dma), None)
            dmas = next((s for s in reversed(self.steps[e]) if s.dma), None)
            if comp is not None:
                sigs.append(comp)
            if dmas is not None:
                sigs.append(dmas)
        for e in ENGINES:
            self.pending_bar[e] = [s for s in sigs
                                   if not (s.engine == e and not s.dma)]

    def finalize(self):
        for e in ENGINES:
            for st in self.steps[e]:
                for d in st.deps:
                    d.need_inc = True
        for e in ENGINES:
            ninc = 0
            ndma = 0
            for st in self.steps[e]:
                if st.dma:
                    ndma += 1
                    st.dma_idx = ndma
                elif st.need_inc:
                    ninc += 1
                    st.inc_idx = ninc

    def emit_engine(self, engine, eng, sems, dsems):
        waited = {}
        for st in self.steps[engine]:
            for d in st.deps:
                if d.dma:
                    sem, val = dsems[d.engine], 16 * d.dma_idx
                else:
                    sem, val = sems[d.engine], d.inc_idx
                    assert val is not None, (st.tag, d.tag)
                key = id(sem)
                if waited.get(key, -1) < val:
                    eng.wait_ge(sem, val)
                    waited[key] = val
            inst = st.emit(eng)
            if st.dma:
                inst.then_inc(dsems[engine], 16)
            elif st.need_inc:
                inst.then_inc(sems[engine], 1)


def _np_cm384():
    t = np.arange(128)[:, None]
    f = np.arange(128)[None, :]
    s0 = np.where(f >= t, BIGM, 0.0)
    s1 = np.where(f <= t, -BIGM, 0.0)
    s2 = np.where(f < t, -BIGM, 0.0)
    return np.concatenate([s0, s1, s2], axis=1).astype(np.float32)


def make_consts(T):
    j = np.arange(128)[:, None]
    t = np.arange(128)[None, :]
    return {
        "c_ident": np.eye(128, dtype=np.float32),
        "c_cum": (j <= t).astype(np.float32),
        "c_rev": (j > t).astype(np.float32),
        "c_cm384": _np_cm384(),
        "c_onesrow": np.ones((1, 128), np.float32),
        "c_onescol": np.ones((128, 1), np.float32),
        "c_e127": (np.arange(128) == 127).astype(np.float32)[:, None],
        "c_iota": np.arange(T + 8, dtype=np.float32)[None, :],
        "c_iota8": np.tile(np.arange(8, dtype=np.float32), (128, 1)).copy(),
    }


class Ker:
    def __init__(self, T=2048, taps=(), phases="abcde"):
        self.phases = phases
        self.T = T
        self.nchunk = T // C
        self.TH = T // 2
        nc = bass.Bass("TRN2", target_bir_lowering=False, debug=False,
                       detect_race_conditions=False)
        self.nc = nc
        self.p = Prog()
        self.consts_np = make_consts(T)
        self.dr = {}
        self.sb = {}
        self.taps = dict()
        T_ = T

        def din(name, shape, dt=F32):
            self.dr[name] = nc.dram_tensor(name, list(shape), dt,
                                           kind="ExternalInput").ap()

        din("xT", [D, T_])
        din("Wqkv", [D, 4 * D])
        din("Wz", [D, 2 * D])
        din("Wab", [D, 64])
        din("convw", [4 * D, KC])
        din("A_log", [HV])
        din("dt_bias", [HV])
        din("norm_w", [DV])
        din("Wout", [2 * D, D])
        din("Wg", [NE, D, FF])
        din("Wu", [NE, D, FF])
        din("Wd", [NE, FF, D])
        din("Wr", [D, NE])
        din("temperature", [1])
        din("cu_seqlens", [5], I32)
        for k, v in self.consts_np.items():
            din(k, v.shape, F32)

        self.out = nc.dram_tensor("out", [D, T_], F32, kind="ExternalOutput").ap()
        self.qkv_s = nc.dram_tensor("qkv_s", [4 * D, T_], BF16, kind="Internal").ap()
        self.z_s = nc.dram_tensor("z_s", [T_, 2 * D], BF16, kind="Internal").ap()
        self.y_s = nc.dram_tensor("y_s", [2 * D, T_], BF16, kind="Internal").ap()
        self.ln2_s = nc.dram_tensor("ln2_s", [D, T_], BF16, kind="Internal").ap()
        for name, shape, dt in taps:
            self.taps[name] = nc.dram_tensor("tap_" + name, list(shape), dt,
                                             kind="ExternalOutput").ap()

        self.pb = [nc.alloc_psum_tensor("pb%d" % i, [128, 512], F32)
                   for i in range(8)]
        self._pcur = 16512
        self._top = 229344

    def persist(self, name, shape, dt):
        nbytes = int(np.prod(shape[1:])) * _DTB[dt]
        off = (self._pcur + 31) // 32 * 32
        t = self.nc.alloc_sbuf_tensor_at(name, shape, dt, offset=off)
        self._pcur = off + nbytes
        assert self._pcur <= self._top, (name, self._pcur)
        self.sb[name] = t
        return t

    def freeze_persist(self):
        self._abase = (self._pcur + 255) // 256 * 256

    def at(self, name, shape, dt, off):
        nbytes = int(np.prod(shape[1:])) * _DTB[dt]
        a = self._abase + off
        assert a + nbytes <= self._top, (name, a + nbytes, self._top)
        t = self.nc.alloc_sbuf_tensor_at(name, shape, dt, offset=a)
        self.sb[name] = t
        return t

    def build(self):
        self.alloc_persist()
        self.ph0()
        self.p.barrier()
        self.ph_a()
        self.p.barrier()
        self.ph_b()
        self.p.barrier()
        if "c" not in self.phases:
            self.emit()
            return self.nc
        self.ph_c()
        self.p.barrier()
        if "d" not in self.phases:
            self.emit()
            return self.nc
        for hh in range(2):
            self.ph_d(hh)
            self.p.barrier()
            self.ph_e(hh)
            self.p.barrier()
        for name, src in [("qkv_s", self.qkv_s), ("z_s", self.z_s),
                          ("y_s", self.y_s), ("ln2_s", self.ln2_s)]:
            if name in self.taps:
                self.p.step("sync", (lambda d_, s_: lambda eng: eng.dma_start(out=d_, in_=s_))(
                    self.taps[name], src), dma=True, tag="tap_" + name)
        self.emit()
        return self.nc

    def alloc_persist(self):
        P = self.persist
        T = self.T
        P("ident", [128, 128], F32)
        P("identb", [128, 128], BF16)
        P("cumt", [128, 128], F32)
        P("revt", [128, 128], F32)
        P("cm384", [128, 384], F32)
        P("onesrow", [1, 128], F32)
        P("onescol", [128, 1], F32)
        P("onescolb", [128, 1], BF16)
        P("onesrowb", [1, 128], BF16)
        P("e127", [128, 1], F32)
        P("iota8", [128, 8], F32)
        P("sr", [1, T + 8], BF16)
        P("msk", [128, 3 * (T + 8)], BF16)
        P("normwb", [128, DV], F32)
        P("abg", [128, T], F32)
        P("cw", [128, 64 * KC], F32)
        P("negexpa", [32, 1], F32)
        P("dtbias", [32, 1], F32)
        P("tmisc", [1, 16], F32)
        P("cvals", [128, 8], F32)
        P("iota8b", [128, 8], F32)
        P("neg18", [128, 8], F32)
        self.freeze_persist()

    def ph0(self):
        nc, p, sb, dr = self.nc, self.p, self.sb, self.dr
        T = self.T
        W = T + 8
        t_iota = self.at("t_iota", [1, W], F32, 110 * 1024)
        t_cu = self.at("t_cu", [1, 8], F32, 119 * 1024)
        t_cui = self.at("t_cui", [1, 8], I32, int(119.5 * 1024))
        t_row = self.at("t_row", [1, W], F32, 120 * 1024)
        t_sr32 = self.at("t_sr32", [1, W], F32, 129 * 1024)
        t_ns = self.at("t_ns", [1, W], F32, 138 * 1024)
        t_nw = self.at("t_nw", [1, DV], F32, 147 * 1024)
        t_al = self.at("t_al", [64, 1], F32, int(147.5 * 1024))
        t_mrow = self.at("t_mrow", [1, 3 * W], F32, 148 * 1024)

        for nm, dst in [("c_ident", sb["ident"]), ("c_cum", sb["cumt"]),
                        ("c_rev", sb["revt"]), ("c_cm384", sb["cm384"]),
                        ("c_onesrow", sb["onesrow"]), ("c_onescol", sb["onescol"]),
                        ("c_e127", sb["e127"]), ("c_iota8", sb["iota8"]),
                        ("c_iota", t_iota)]:
            p.step("sync",
                   (lambda dst_, src_: lambda eng: eng.dma_start(out=dst_[:], in_=src_))(dst, dr[nm]),
                   writes=[dst.name], dma=True, tag="ld_" + nm)
        for nm, dst in [("c_ident", sb["identb"]), ("c_onescol", sb["onescolb"]),
                        ("c_onesrow", sb["onesrowb"])]:
            p.step("gpsimd",
                   (lambda dst_, src_: lambda eng: eng.dma_start(out=dst_[:], in_=src_))(dst, dr[nm]),
                   writes=[dst.name], dma=True, tag="ldb_" + nm)
        p.step("sync", lambda eng: eng.dma_start(
            out=sb["cw"][:].rearrange("p (a k) -> p a k", k=KC),
            in_=dr["convw"].rearrange("(a p) k -> p a k", p=128)),
            writes=["cw"], dma=True, tag="ld_cw")
        p.step("sync", lambda eng: eng.dma_start(out=t_al[0:32, :], in_=dr["A_log"][:, None]),
               writes=["t_al"], dma=True, tag="ld_alog")
        p.step("sync", lambda eng: eng.dma_start(out=t_al[32:64, :], in_=dr["dt_bias"][:, None]),
               writes=["t_al"], dma=True, tag="ld_dtb")
        p.step("sync", lambda eng: eng.dma_start(out=t_nw[:], in_=dr["norm_w"][None, :]),
               writes=["t_nw"], dma=True, tag="ld_nw")
        p.step("sync", lambda eng: eng.dma_start(out=sb["tmisc"][0:1, 0:1], in_=dr["temperature"][None, :]),
               writes=["tmisc"], dma=True, tag="ld_temp")
        p.step("sync", lambda eng: eng.dma_start(out=t_cui[0:1, 0:5], in_=dr["cu_seqlens"][None, :]),
               writes=["t_cui"], dma=True, tag="ld_cu")

        p.step("scalar", lambda eng: eng.activation(sb["negexpa"][:], t_al[0:32, :], AF.Exp),
               reads=["t_al"], writes=["negexpa"], tag="acoef_exp")
        p.step("vector", lambda eng: eng.tensor_scalar(sb["negexpa"][:], sb["negexpa"][:], -1.0, None, ALU.mult),
               reads=["negexpa"], writes=["negexpa"], tag="acoef_neg")
        p.step("scalar", lambda eng: eng.copy(sb["dtbias"][:], t_al[32:64, :]),
               reads=["t_al"], writes=["dtbias"], tag="acoef_dtb")
        p.step("vector", lambda eng: eng.tensor_copy(t_cu[0:1, 0:5], t_cui[0:1, 0:5]),
               reads=["t_cui"], writes=["t_cu"], tag="cu_f32")

        sr = sb["sr"]

        def em_starts(eng):
            eng.memset(t_sr32[:], 0.0)
            for i in range(4):
                eng.tensor_scalar(t_row[:], t_iota[:], t_cu[0:1, i:i + 1], None, ALU.is_equal)
                eng.tensor_tensor(t_sr32[:], t_sr32[:], t_row[:], ALU.max)
                eng.tensor_scalar(t_row[:], t_iota[:], float(T), t_cu[0:1, i:i + 1], ALU.add, ALU.is_equal)
                eng.tensor_tensor(t_sr32[:], t_sr32[:], t_row[:], ALU.max)
            eng.memset(t_sr32[0:1, T:W], 0.0)
            return eng.tensor_copy(sr[:], t_sr32[:])
        p.step("vector", em_starts, reads=["t_iota"], sereads=["t_cu"],
               writes=["t_sr32", "sr", "t_row"], tag="starts")

        def em_masks_rows(eng):
            eng.tensor_scalar(t_ns[:], t_sr32[:], -1.0, 1.0, ALU.mult, ALU.add)
            eng.memset(t_mrow[:], 1.0)
            eng.tensor_copy(t_mrow[0:1, 3:3 + T], t_ns[0:1, 1:T + 1])
            eng.tensor_tensor(t_mrow[0:1, W + 3:W + 3 + T], t_mrow[0:1, 3:3 + T], t_ns[0:1, 2:T + 2], ALU.mult)
            return eng.tensor_tensor(t_mrow[0:1, 2 * W + 3:2 * W + 3 + T], t_mrow[0:1, W + 3:W + 3 + T], t_ns[0:1, 3:T + 3], ALU.mult)
        p.step("vector", em_masks_rows, reads=["t_sr32"],
               writes=["t_ns", "t_mrow"], tag="maskrows")

        pb, msk = self.pb, sb["msk"]
        nburst = (3 * W + 511) // 512
        for ib in range(nburst):
            c0 = ib * 512
            c1 = min(c0 + 512, 3 * W)
            bk = ib % 4
            p.step("tensor",
                   (lambda c0_, c1_, bk_: lambda eng: eng.matmul(
                       pb[bk_][:, 0:c1_ - c0_], sb["onesrow"][:],
                       t_mrow[0:1, c0_:c1_], start=True, stop=True))(c0, c1, bk),
                   reads=["t_mrow", "onesrow"], writes=["pb%d" % bk],
                   tag="mskb_mm%d" % ib)
            p.step("vector",
                   (lambda c0_, c1_, bk_: lambda eng: eng.tensor_copy(
                       msk[:, c0_:c1_], pb[bk_][:, 0:c1_ - c0_]))(c0, c1, bk),
                   reads=["pb%d" % bk], writes=["msk"], tag="mskb_ev%d" % ib)

        p.step("tensor", lambda eng: eng.matmul(pb[4][:, 0:DV], sb["onesrow"][:],
                                                t_nw[0:1, :], start=True, stop=True),
               reads=["t_nw", "onesrow"], writes=["pb4"], tag="nw_mm")
        p.step("vector", lambda eng: eng.tensor_copy(sb["normwb"][:], pb[4][:, 0:DV]),
               reads=["pb4"], writes=["normwb"], tag="nw_ev")
        p.step("vector", lambda eng: eng.reciprocal(sb["tmisc"][0:1, 1:2], sb["tmisc"][0:1, 0:1]),
               reads=["tmisc"], writes=["tmisc"], tag="invtemp")

        def em_cvals(eng):
            cv = sb["cvals"]
            eng.memset(cv[:, 0:1], LNEPS)
            eng.memset(cv[:, 1:2], 1.0)
            eng.memset(cv[:, 2:3], EPS)
            return eng.memset(cv[:, 3:4], 0.0)
        p.step("vector", em_cvals, writes=["cvals"], tag="cvals")

        def em_misc8(eng):
            eng.tensor_scalar(sb["iota8b"][:], sb["iota8"][:], 100.0, None, ALU.add)
            eng.memset(sb["neg18"][:], -1.0)
            return eng.memset(sb["cvals"][:, 5:6], float(-1.0 / np.e))
        p.step("vector", em_misc8, reads=["iota8"], writes=["iota8b", "neg18", "cvals"],
               tag="misc8")
        p.step("tensor", lambda eng: eng.matmul(pb[5][:, 0:1], sb["onesrow"][:],
                                                sb["tmisc"][0:1, 1:2], start=True, stop=True),
               reads=["tmisc", "onesrow"], writes=["pb5"], tag="it_mm")
        p.step("vector", lambda eng: eng.tensor_copy(sb["cvals"][:, 4:5], pb[5][:, 0:1]),
               reads=["pb5"], writes=["cvals"], tag="it_ev")

    def ph_a(self):
        nc, p, sb, dr, pb = self.nc, self.p, self.sb, self.dr, self.pb
        T = self.T
        TC = min(512, T)
        ln1x = self.at("ln1x", [128, 16 * T], BF16, 0)
        xa = self.at("xa", [128, 16 * TC], F32, 64 * 1024)
        xbf = self.at("xbf", [128, 16 * TC], BF16, 96 * 1024)
        xsq = self.at("xsq", [128, 16 * TC], BF16, 112 * 1024)
        arow = self.at("arow", [1, 4 * TC], F32, 128 * 1024)
        abc = self.at("abc", [128, 2 * TC], F32, 137 * 1024)
        xT = dr["xT"]
        onescolb, onesrow = sb["onescolb"], sb["onesrow"]

        for tc in range(T // TC):
            t0 = tc * TC
            for half in range(2):
                s0 = half * 8
                src = xT.rearrange("(dt p) t -> p dt t", p=128)[:, s0:s0 + 8, t0:t0 + TC]
                dst = xa[:].rearrange("p (dt t) -> p dt t", t=TC)[:, s0:s0 + 8, :]
                p.step("sync",
                       (lambda s_, d_: lambda eng: eng.dma_start(out=d_, in_=s_))(src, dst),
                       writes=["xa"], dma=True, tag="a_ldx%d_%d" % (tc, half))
            p.step("vector", lambda eng: eng.tensor_copy(xbf[:], xa[:]),
                   reads=["xa"], writes=["xbf"], tag="a_cast%d" % tc)
            p.step("scalar", lambda eng: eng.activation(xsq[:], xbf[:], AF.Square),
                   reads=["xbf"], writes=["xsq"], tag="a_sq%d" % tc)

            def em_stats(src_t, bank):
                def em(eng):
                    inst = None
                    v = src_t[:].rearrange("p (dt t) -> p dt t", t=TC)
                    for d in range(16):
                        inst = eng.matmul(bank[0:1, 0:TC], onescolb[:],
                                          v[:, d, :], start=(d == 0), stop=(d == 15))
                    return inst
                return em
            p.step("tensor", em_stats(xbf, pb[0]), reads=["xbf", "onescolb"],
                   writes=["pb0"], tag="a_mm1_%d" % tc)
            p.step("tensor", em_stats(xsq, pb[1]), reads=["xsq", "onescolb"],
                   writes=["pb1"], tag="a_mm2_%d" % tc)

            def em_rows(eng):
                eng.tensor_scalar(arow[0:1, 0:TC], pb[0][0:1, 0:TC], 1.0 / D, None, ALU.mult)
                eng.tensor_scalar(arow[0:1, 3*TC:4*TC], pb[1][0:1, 0:TC], 1.0 / D, None, ALU.mult)
                eng.tensor_tensor(arow[0:1, TC:2*TC], arow[0:1, 0:TC], arow[0:1, 0:TC], ALU.mult)
                return eng.tensor_tensor(arow[0:1, 3*TC:4*TC], arow[0:1, 3*TC:4*TC],
                                         arow[0:1, TC:2*TC], ALU.subtract)
            p.step("vector", em_rows, reads=["pb0", "pb1"], writes=["arow"],
                   tag="a_rows%d" % tc)

            def em_rstd(eng):
                eng.activation(arow[0:1, 2*TC:3*TC], arow[0:1, 3*TC:4*TC], AF.Ln, bias=sb["cvals"][0:1, 0:1])
                return eng.activation(arow[0:1, TC:2*TC], arow[0:1, 2*TC:3*TC], AF.Exp, scale=-0.5)
            p.step("scalar", em_rstd, reads=["arow", "cvals"], writes=["arow"], tag="a_rstd%d" % tc)
            p.step("vector", lambda eng: eng.tensor_tensor(
                arow[0:1, 2*TC:3*TC], arow[0:1, 0:TC], arow[0:1, TC:2*TC], ALU.mult),
                reads=["arow"], writes=["arow"], tag="a_ms%d" % tc)
            p.step("tensor", lambda eng: eng.matmul(pb[2][:, 0:TC], onesrow[:],
                                                    arow[0:1, TC:2*TC], start=True, stop=True),
                   reads=["arow", "onesrow"], writes=["pb2"], tag="a_bs%d" % tc)
            p.step("tensor", lambda eng: eng.matmul(pb[3][:, 0:TC], onesrow[:],
                                                    arow[0:1, 2*TC:3*TC], start=True, stop=True),
                   reads=["arow", "onesrow"], writes=["pb3"], tag="a_bms%d" % tc)
            p.step("vector", lambda eng: eng.tensor_copy(abc[:, 0:TC], pb[2][:, 0:TC]),
                   reads=["pb2"], writes=["abc"], tag="a_evs%d" % tc)
            p.step("vector", lambda eng: eng.tensor_copy(abc[:, TC:2*TC], pb[3][:, 0:TC]),
                   reads=["pb3"], writes=["abc"], tag="a_evms%d" % tc)

            def em_apply(d, t0):
                def em(eng):
                    xs = xa[:, d * TC:(d + 1) * TC]
                    eng.tensor_tensor(xs, xs, abc[:, 0:TC], ALU.mult)
                    return eng.tensor_tensor(ln1x[:, d * T + t0: d * T + t0 + TC],
                                             xs, abc[:, TC:2*TC], ALU.subtract)
                return em
            for d in range(16):
                p.step("vector", em_apply(d, t0), reads=["abc", "xa"],
                       writes=["ln1x", "xa"], tag="a_app%d_%d" % (tc, d))

        if "ln1x" in self.taps:
            ap = self.taps["ln1x"]
            p.step("sync", lambda eng: eng.dma_start(out=ap, in_=ln1x[:]),
                   reads=["ln1x"], dma=True, tag="tap_ln1x")

    def ph_b(self):
        nc, p, sb, dr, pb = self.nc, self.p, self.sb, self.dr, self.pb
        T = self.T
        TB = min(512, T)
        ntb = T // TB
        ln1x = self.sb["ln1x"]
        wqb = [self.at("wqb%d" % i, [128, 16 * 128], BF16, 64 * 1024 + i * 4096)
               for i in range(2)]
        qn = [self.at("qn%d" % i, [128, T], BF16, 72 * 1024 + i * 2 * T)
              for i in range(2)]
        wzb = self.at("wzb", [128, 16 * 2048], BF16, 80 * 1024)
        zev = [self.at("zev%d" % i, [128, 2 * D], BF16, 144 * 1024 + i * 4 * D)
               for i in range(2)]

        for n in range(4 * D // 128):
            par = n % 2
            wb = wqb[par]
            src = dr["Wqkv"].rearrange("(dt p) n -> p dt n", p=128)
            for half in range(2):
                p.step("gpsimd",
                       (lambda n_, h_, wb_: lambda eng: eng.dma_start(
                           out=wb_[:].rearrange("p (dt c) -> p dt c", c=128)[:, h_ * 8:(h_ + 1) * 8, :],
                           in_=src[:, h_ * 8:(h_ + 1) * 8, n_ * 128:(n_ + 1) * 128]))(n, half, wb),
                       writes=["wqb%d" % par], dma=True, tag="b_wq%d_%d" % (n, half))

            def em_mm(wb_, par_):
                def em(eng):
                    inst = None
                    for d in range(16):
                        lhsT = wb_[:, d * 128:(d + 1) * 128]
                        for tb in range(ntb):
                            inst = eng.matmul(pb[par_ * 4 + tb][:, 0:TB], lhsT,
                                              ln1x[:].rearrange("p (dt t) -> p dt t", t=T)[:, d, tb * TB:(tb + 1) * TB],
                                              start=(d == 0), stop=(d == 15))
                    return inst
                return em
            p.step("tensor", em_mm(wb, par),
                   reads=["wqb%d" % par, "ln1x"],
                   writes=["pb%d" % (par * 4 + tb) for tb in range(ntb)],
                   tag="b_qmm%d" % n)

            def em_ev(par_, n_):
                def em(eng):
                    inst = None
                    for tb in range(ntb):
                        inst = eng.copy(qn[par_][:, tb * TB:(tb + 1) * TB],
                                        pb[par_ * 4 + tb][:, 0:TB])
                    return inst
                return em
            p.step("scalar", em_ev(par, n),
                   reads=["pb%d" % (par * 4 + tb) for tb in range(ntb)],
                   writes=["qn%d" % par], tag="b_qev%d" % n)
            p.step("sync",
                   (lambda n_, par_: lambda eng: eng.dma_start(
                       out=self.qkv_s[n_ * 128:(n_ + 1) * 128, :], in_=qn[par_][:]))(n, par),
                   reads=["qn%d" % par], dma=True, tag="b_qst%d" % n)

        NH = 2 * D // 2
        for nh in range(2):
            srcz = dr["Wz"].rearrange("(dt p) n -> p dt n", p=128)
            for q in range(4):
                p.step("gpsimd",
                       (lambda nh_, q_: lambda eng: eng.dma_start(
                           out=wzb[:].rearrange("p (dt c) -> p dt c", c=NH)[:, q_ * 4:(q_ + 1) * 4, :],
                           in_=srcz[:, q_ * 4:(q_ + 1) * 4, nh_ * NH:(nh_ + 1) * NH]))(nh, q),
                       writes=["wzb"], dma=True, tag="b_wz%d_%d" % (nh, q))
            for tt in range(T // 128):
                par = tt % 2

                def em_zmm(tt_, par_):
                    def em(eng):
                        inst = None
                        for d in range(16):
                            lhsT = ln1x[:, d * T + tt_ * 128: d * T + (tt_ + 1) * 128]
                            for nb in range(NH // 512):
                                inst = eng.matmul(pb[par_ * 4 + nb][:, 0:512], lhsT,
                                                  wzb[:, d * NH + nb * 512: d * NH + (nb + 1) * 512],
                                                  start=(d == 0), stop=(d == 15))
                        return inst
                    return em
                p.step("tensor", em_zmm(tt, par),
                       reads=["wzb", "ln1x"],
                       writes=["pb%d" % (par * 4 + nb) for nb in range(NH // 512)],
                       tag="b_zmm%d_%d" % (nh, tt))

                def em_zev(par_):
                    def em(eng):
                        inst = None
                        for nb in range(NH // 512):
                            inst = eng.copy(zev[par_][:, nb * 512:(nb + 1) * 512],
                                            pb[par_ * 4 + nb][:, 0:512])
                        return inst
                    return em
                p.step("scalar", em_zev(par),
                       reads=["pb%d" % (par * 4 + nb) for nb in range(NH // 512)],
                       writes=["zev%d" % par], tag="b_zev%d_%d" % (nh, tt))
                p.step("sync",
                       (lambda nh_, tt_, par_: lambda eng: eng.dma_start(
                           out=self.z_s[tt_ * 128:(tt_ + 1) * 128, nh_ * NH:(nh_ + 1) * NH],
                           in_=zev[par_][:, 0:NH]))(nh, tt, par),
                       reads=["zev%d" % par], dma=True, tag="b_zst%d_%d" % (nh, tt))

        p.barrier()
        wabb = self.at("wabb", [128, 16 * 64], BF16, 80 * 1024)
        sc1 = self.at("sc1", [32, T], F32, 82 * 1024)
        sc2 = self.at("sc2", [32, T], F32, 82 * 1024 + 4 * T)
        sc3 = self.at("sc3", [32, T], F32, 82 * 1024 + 8 * T)
        abt = self.at("abt", [64, T], F32, 82 * 1024 + 12 * T)
        p.step("gpsimd", lambda eng: eng.dma_start(
            out=wabb[:].rearrange("p (dt c) -> p dt c", c=64),
            in_=dr["Wab"].rearrange("(dt p) n -> p dt n", p=128)),
            writes=["wabb"], dma=True, tag="b_wab")

        def em_abmm(eng):
            inst = None
            for d in range(16):
                lhsT = wabb[:, d * 64:(d + 1) * 64]
                for tb in range(ntb):
                    inst = eng.matmul(pb[4 + tb][0:64, 0:TB], lhsT,
                                      ln1x[:].rearrange("p (dt t) -> p dt t", t=T)[:, d, tb * TB:(tb + 1) * TB],
                                      start=(d == 0), stop=(d == 15))
            return inst
        p.step("tensor", em_abmm, reads=["wabb", "ln1x"],
               writes=["pb%d" % (4 + tb) for tb in range(ntb)], tag="b_abmm")

        def em_abev(eng):
            inst = None
            for tb in range(ntb):
                inst = eng.copy(abt[:, tb * TB:(tb + 1) * TB], pb[4 + tb][0:64, 0:TB])
            return inst
        p.step("scalar", em_abev, reads=["pb%d" % (4 + tb) for tb in range(ntb)],
               writes=["abt"], tag="b_abev")

        a_, b_ = abt[0:32, :], abt[32:64, :]
        abg = sb["abg"]
        cv = sb["cvals"]

        def em_g1(eng):
            return eng.tensor_scalar(sc1[:], a_, sb["dtbias"][:], None, ALU.add)
        p.step("vector", em_g1, reads=["abt", "dtbias"], writes=["sc1"], tag="b_g1")

        def em_g2(eng):
            eng.activation(sc2[:], sc1[:], AF.Abs)
            eng.activation(sc3[:], sc2[:], AF.Exp, scale=-1.0)
            return eng.activation(sc2[:], sc3[:], AF.Ln, bias=cv[0:32, 1:2])
        p.step("scalar", em_g2, reads=["sc1", "cvals"], writes=["sc2", "sc3"], tag="b_g2")

        def em_g3(eng):
            eng.tensor_scalar(sc3[:], sc1[:], 0.0, None, ALU.max)
            eng.tensor_tensor(sc3[:], sc3[:], sc2[:], ALU.add)
            return eng.tensor_scalar(abg[0:32, :], sc3[:], sb["negexpa"][:], None, ALU.mult)
        p.step("vector", em_g3, reads=["sc1", "sc2"], sereads=["negexpa"],
               writes=["sc3", "abg"], tag="b_g3")

        def em_l1(eng):
            return eng.tensor_scalar(sc1[:], b_, -1.0, None, ALU.mult)
        p.step("vector", em_l1, reads=["abt"], writes=["sc1"], tag="b_l1")

        def em_l2(eng):
            eng.activation(sc2[:], sc1[:], AF.Abs)
            eng.activation(sc3[:], sc2[:], AF.Exp, scale=-1.0)
            return eng.activation(sc2[:], sc3[:], AF.Ln, bias=cv[0:32, 1:2])
        p.step("scalar", em_l2, reads=["sc1", "cvals"], writes=["sc2", "sc3"], tag="b_l2")

        def em_l3(eng):
            eng.tensor_scalar(sc3[:], sc1[:], 0.0, None, ALU.max)
            eng.tensor_tensor(sc3[:], sc3[:], sc2[:], ALU.add)
            return eng.tensor_scalar(abg[32:64, :], sc3[:], -1.0, None, ALU.mult)
        p.step("vector", em_l3, reads=["sc1", "sc2"], writes=["sc3", "abg"], tag="b_l3")

        if "abg" in self.taps:
            ap = self.taps["abg"]
            p.step("sync", lambda eng: eng.dma_start(out=ap, in_=abg[0:64, :]),
                   reads=["abg"], dma=True, tag="tap_abg")

    def ph_c(self):
        nc, p, sb = self.nc, self.p, self.sb
        T = self.T
        nchunk = self.nchunk
        at = self.at
        bufs = {}
        bufs["qkvc"] = at("qkvc", [128, 64 * 131], BF16, 0)
        bufs["mixed"] = at("mixed", [128, 64 * 128], BF16, 17 * 1024)
        bufs["zsil"] = [at("zsil%d" % i, [128, 2 * D], BF16, 34 * 1024 + i * 4 * D)
                        for i in range(2)]
        bufs["Sst"] = at("Sst", [128, 32 * 128], BF16, 51 * 1024)
        bufs["flat"] = at("flat", [1, 96 * 128], F32, 60 * 1024)
        bufs["y1"] = at("y1", [128, 32 * 128], BF16, 109 * 1024)
        bufs["yA"] = at("yA", [128, 32 * 128], BF16, 118 * 1024)
        bufs["solT"] = [at("solT%d" % i, [128, 14 * 128], BF16, 127 * 1024 + i * 3584)
                       for i in range(4)]
        bufs["Usb"] = [at("Usb%d" % i, [128, 2 * 128], BF16, 141 * 1024 + i * 512)
                       for i in range(4)]
        bufs["Rt"] = [at("Rt%d" % i, [128, 128], BF16, 143 * 1024 + i * 256)
                       for i in range(4)]
        bufs["VBs"] = [at("VBs%d" % i, [128, 128], BF16, 144 * 1024 + i * 256)
                       for i in range(4)]
        bufs["KBt"] = [at("KBt%d" % i, [128, 128], BF16, 145 * 1024 + i * 256)
                       for i in range(4)]
        bufs["KSe"] = [at("KSe%d" % i, [128, 128], BF16, 146 * 1024 + i * 256)
                       for i in range(4)]
        bufs["QSe"] = [at("QSe%d" % i, [128, 128], F32, 147 * 1024 + i * 512)
                       for i in range(4)]
        bufs["Ot"] = [at("Ot%d" % i, [128, 128], F32, 149 * 1024 + i * 512)
                       for i in range(4)]
        bufs["Esb"] = [at("Esb%d" % i, [128, 3 * 128], F32, 151 * 1024 + i * 1536)
                       for i in range(4)]
        bufs["r3m"] = [at("r3m%d" % i, [128, 3 * 128], F32, 157 * 1024 + i * 1536)
                       for i in range(4)]
        bufs["sqb"] = [at("sqb%d" % i, [128, 128], BF16, 163 * 1024 + i * 256)
                       for i in range(3)]
        bufs["cvt"] = [at("cvt%d" % i, [128, 128], BF16, 164 * 1024 + i * 256) for i in range(3)]
        cols = []
        for i in range(2):
            cur = [165 * 1024 + i * 4352]

            def nx(nbytes):
                a = (cur[0] + 31) // 32 * 32
                cur[0] = a + nbytes
                return a
            cols.append({
                "gBe": at("gBe%d" % i, [128, 66], F32, nx(264)),
                "cumE": at("cumE%d" % i, [128, 66], F32, nx(264)),
                "revE": at("revE%d" % i, [128, 66], F32, nx(264)),
                "gp": at("gp%d" % i, [128, 32], F32, nx(128)),
                "dp": at("dp%d" % i, [128, 32], F32, nx(128)),
                "cs": at("cs%d" % i, [128, 96], F32, nx(384)),
                "nrA": at("nrA%d" % i, [128, 32], F32, nx(128)),
                "lnr": at("lnr%d" % i, [128, 32], F32, nx(128)),
                "earg": at("earg%d" % i, [128, 32], F32, nx(128)),
                "bcol": at("bcol%d" % i, [128, 32], F32, nx(128)),
                "bGam": at("bGam%d" % i, [128, 32], F32, nx(128)),
                "qGam": at("qGam%d" % i, [128, 32], F32, nx(128)),
                "Ktl": at("Ktl%d" % i, [128, 32], F32, nx(128)),
                "coefC": at("coefC%d" % i, [128, 32], F32, nx(128)),
                "msC": at("msC%d" % i, [128, 32], F32, nx(128)),
                "rstd": at("rstd%d" % i, [128, 32], F32, nx(128)),
                "gsA": at("gsA%d" % i, [66, 128], F32, nx(512)),
                "rowsA": at("rowsA%d" % i, [96, 128], F32, nx(512)),
                "crow": at("crow%d" % i, [1, 32], F32, nx(128)),
                "c1": at("c1%d" % i, [128, 2], F32, nx(32)),
            })
        bufs["cols"] = cols

        p.step("vector", lambda eng: eng.memset(bufs["Sst"][:], 0.0),
               writes=["Sst"], tag="c_zs")
        for i in range(2):
            p.step("vector", (lambda i_: lambda eng: eng.memset(cols[i_]["gsA"][:], 0.0))(i),
                   writes=["gsA%d" % i], tag="c_zgsa%d" % i)
        p.step("vector", lambda eng: eng.memset(
            bufs["qkvc"][:].rearrange("p (ti c) -> p ti c", c=131)[:, :, 0:3], 0.0),
            writes=["qkvc"], tag="c_zq")

        nh = min(HV, getattr(self, "maxheads", HV))
        prep = (self._chunk_loads(0, 0, bufs) + self._chunk_conv(0, 0, bufs)
                + self._chunk_machA(0, 0, bufs) + self._chunk_machB(0, 0, bufs))
        for st_ in prep:
            self.p.step(st_[0], st_[1], **st_[2])
        for c in range(nchunk):
            cp = c % 2
            if c + 1 < nchunk:
                nxt_early = (self._chunk_loads(c + 1, (c + 1) % 2, bufs)
                             + self._chunk_machA(c + 1, (c + 1) % 2, bufs))
                nxt_late = (self._chunk_conv(c + 1, (c + 1) % 2, bufs)
                            + self._chunk_machB(c + 1, (c + 1) % 2, bufs))
            else:
                nxt_early, nxt_late = [], []
            pending = [self._head_steps(c, cp, h, bufs) for h in range(nh)]
            cursors = {}
            active = []
            nexth = 0
            emitted = 0
            total = sum(len(g) for g in pending)
            nxt_i = 0
            nxt_start = 2 * total // 5
            while active or nexth < nh:
                if len(active) < 4 and nexth < nh:
                    active.append(nexth)
                    cursors[nexth] = 0
                    nexth += 1
                if not active:
                    break
                for hidx in list(active):
                    g = pending[hidx]
                    if cursors[hidx] < len(g):
                        eng_, emit_, kw_ = g[cursors[hidx]]
                        self.p.step(eng_, emit_, **kw_)
                        cursors[hidx] += 1
                        emitted += 1
                        if emitted > nxt_start and nxt_i < len(nxt_early):
                            st_ = nxt_early[nxt_i]
                            self.p.step(st_[0], st_[1], **st_[2])
                            nxt_i += 1
                    if cursors[hidx] >= len(g):
                        active.remove(hidx)
            while nxt_i < len(nxt_early):
                st_ = nxt_early[nxt_i]
                self.p.step(st_[0], st_[1], **st_[2])
                nxt_i += 1
            for st_ in nxt_late:
                self.p.step(st_[0], st_[1], **st_[2])
            self._chunk_tail(c, cp, bufs)

    def _chunk_loads(self, c, cp, bufs):
        steps = []

        class _P:
            @staticmethod
            def step(eng_, emit_, **kw):
                steps.append((eng_, emit_, kw))
        p = _P
        t0 = c * C
        qkvc, zsil = bufs["qkvc"], bufs["zsil"]
        lo = max(t0 - 3, 0)
        off = 3 if c == 0 else 0
        ncols = t0 + C - lo
        for q in range(4):
            src = self.qkv_s.rearrange("(ti p) t -> p ti t", p=128)[:, q * 16:(q + 1) * 16, lo:lo + ncols]
            dst = qkvc[:].rearrange("p (ti c) -> p ti c", c=131)[:, q * 16:(q + 1) * 16, off:off + ncols]
            p.step("sync", (lambda s_, d_: lambda eng: eng.dma_start(out=d_, in_=s_))(src, dst),
                   writes=["qkvc"], dma=True, tag="c_lq%d_%d" % (c, q))
        p.step("sync", (lambda t0_, cp_: lambda eng: eng.dma_start(
            out=zsil[cp_][:], in_=self.z_s[t0_:t0_ + C, :]))(t0, cp),
            writes=["zsil%d" % cp], dma=True, tag="c_lz%d" % c)
        p.step("scalar", (lambda cp_: lambda eng: eng.activation(
            zsil[cp_][:], zsil[cp_][:], AF.Silu))(cp),
            reads=["zsil%d" % cp], writes=["zsil%d" % cp], tag="c_silz%d" % c)
        return steps

    def _chunk_conv(self, c, cp, bufs):
        steps = []

        class _P:
            @staticmethod
            def step(eng_, emit_, **kw):
                steps.append((eng_, emit_, kw))
        p = _P
        sb = self.sb
        W = self.T + 8
        t0 = c * C
        qkvc, mixed, cvt = bufs["qkvc"], bufs["mixed"], bufs["cvt"]
        msk, cw = sb["msk"], sb["cw"]

        def em_conv(ti0, ti1, ci):
            def em(eng):
                inst = None
                for ti in range(ti0, ti1):
                    qv = qkvc[:, ti * 131: ti * 131 + 131]
                    mx = mixed[:, ti * 128:(ti + 1) * 128]
                    inst = eng.tensor_scalar(mx, qv[:, 3:131],
                                             cw[:, ti * KC + 3:ti * KC + 4], None, ALU.mult)
                    for s_ in (1, 2, 3):
                        mslice = msk[:, (s_ - 1) * W + 3 + t0 - s_:
                                     (s_ - 1) * W + 3 + t0 - s_ + C]
                        inst = eng.scalar_tensor_tensor(
                            cvt[ci][:], qv[:, 3 - s_:131 - s_],
                            cw[:, ti * KC + 3 - s_:ti * KC + 4 - s_],
                            mslice, ALU.mult, ALU.mult)
                        inst = eng.tensor_tensor(mx, mx, cvt[ci][:], ALU.add)
                return inst
            return em
        def em_convg(ti0, ti1, ci):
            def em(eng):
                inst = None
                for tb in range(ti0, ti1, 2):
                    pair = (tb, tb + 1)
                    qvs = [qkvc[:, ti * 131: ti * 131 + 131] for ti in pair]
                    mxs = [mixed[:, ti * 128:(ti + 1) * 128] for ti in pair]
                    cvs = [cvt[ci][:, 0:64].rearrange("p n -> p n") for _ in pair]
                    cvs = [cvt[1][:], cvt[2][:]]
                    for j in (0, 1):
                        inst = eng.tensor_tensor(
                            mxs[j], qvs[j][:, 3:131],
                            cw[:, pair[j] * KC + 3:pair[j] * KC + 4].to_broadcast([128, C]),
                            ALU.mult)
                    for s_ in (1, 2, 3):
                        mslice = msk[:, (s_ - 1) * W + 3 + t0 - s_:
                                     (s_ - 1) * W + 3 + t0 - s_ + C]
                        for j in (0, 1):
                            inst = eng.tensor_tensor(
                                cvs[j], qvs[j][:, 3 - s_:131 - s_],
                                cw[:, pair[j] * KC + 3 - s_:pair[j] * KC + 4 - s_].to_broadcast([128, C]),
                                ALU.mult)
                        for j in (0, 1):
                            inst = eng.tensor_tensor(cvs[j], cvs[j], mslice, ALU.mult)
                        for j in (0, 1):
                            inst = eng.tensor_tensor(mxs[j], mxs[j], cvs[j], ALU.add)
                return inst
            return em
        for grp in range(6):
            p.step("vector", em_conv(grp * 8, (grp + 1) * 8, 0),
                   reads=["qkvc", "cw", "msk"], writes=["mixed%d" % grp, "cvt0"],
                   tag="c_cv%d_%d" % (c, grp))
        for grp in range(6, 8):
            p.step("gpsimd", em_convg(grp * 8, (grp + 1) * 8, 1),
                   reads=["qkvc", "cw", "msk"], writes=["mixed%d" % grp, "cvt1", "cvt2"],
                   tag="c_cg%d_%d" % (c, grp))
        for grp in range(8):
            p.step("scalar", (lambda g_: lambda eng: eng.activation(
                mixed[:, g_ * 1024:(g_ + 1) * 1024], mixed[:, g_ * 1024:(g_ + 1) * 1024],
                AF.Silu))(grp),
                reads=["mixed%d" % grp], writes=["mixed%d" % grp],
                tag="c_sil%d_%d" % (c, grp))
        return steps

    def _chunk_machA(self, c, cp, bufs):
        steps = []

        class _P:
            @staticmethod
            def step(eng_, emit_, **kw):
                steps.append((eng_, emit_, kw))
        nc, sb, pb = self.nc, self.sb, self.pb
        p = _P
        t0 = c * C
        co = bufs["cols"][cp]
        mixed, flat, sqb = bufs["mixed"], bufs["flat"], bufs["sqb"]
        abg, sr, ident = sb["abg"], sb["sr"], sb["ident"]
        cumt, revt, cv = sb["cumt"], sb["revt"], sb["cvals"]
        onesrow, onescolb, e127 = sb["onesrow"], sb["onescolb"], sb["e127"]
        bnk2 = pb[0]
        r_b2 = "pb0"
        S = p.step

        def em_gsA(eng):
            eng.tensor_copy(co["gsA"][0:64, :], abg[0:64, t0:t0 + C])
            return eng.tensor_copy(co["gsA"][64:65, :], sr[0:1, t0:t0 + C])
        S("vector", em_gsA, reads=["abg", "sr"], writes=["gsA%d" % cp], tag="c_gsa%d" % c)
        S("tensor", lambda eng: eng.transpose(bnk2[:, 384:450], co["gsA"][0:66, :], ident[0:66, 0:66]),
          reads=["gsA%d" % cp, "ident"], writes=[r_b2], tag="c_trg%d" % c)
        S("scalar", lambda eng: eng.copy(co["gBe"][:], bnk2[:, 384:450]),
          reads=[r_b2], writes=["gBe%d" % cp], tag="c_gbe%d" % c)

        S("tensor", lambda eng: eng.matmul(bnk2[:, 384:450], cumt[:], co["gBe"][:, 0:66],
                                           start=True, stop=True),
          reads=["gBe%d" % cp, "cumt"], writes=[r_b2], tag="c_cum%d" % c)
        S("scalar", lambda eng: eng.copy(co["cumE"][:], bnk2[:, 384:450]),
          reads=[r_b2], writes=["cumE%d" % cp], tag="c_cumev%d" % c)
        S("tensor", lambda eng: eng.matmul(bnk2[:, 384:450], revt[:], co["gBe"][:, 0:66],
                                           start=True, stop=True),
          reads=["gBe%d" % cp, "revt"], writes=[r_b2], tag="c_rev%d" % c)
        S("scalar", lambda eng: eng.copy(co["revE"][:], bnk2[:, 384:450]),
          reads=[r_b2], writes=["revE%d" % cp], tag="c_revev%d" % c)

        def em_c1(eng):
            eng.mul(co["c1"][:, 0:1], co["cumE"][:, 64:65], BIG)
            return eng.mul(co["c1"][:, 1:2], co["revE"][:, 64:65], BIG)
        S("scalar", em_c1, reads=["cumE%d" % cp, "revE%d" % cp],
          writes=["c1%d" % cp], tag="c_c1%d" % c)

        def em_gp(eng):
            eng.tensor_scalar(co["gp"][:], co["cumE"][:, 0:32], co["c1"][:, 0:1], None, ALU.subtract)
            return eng.tensor_scalar(co["dp"][:], co["revE"][:, 0:32], co["c1"][:, 1:2], None, ALU.subtract)
        S("vector", em_gp, reads=["cumE%d" % cp, "revE%d" % cp, "c1%d" % cp],
          writes=["gp%d" % cp, "dp%d" % cp], tag="c_gp%d" % c)

        S("tensor", lambda eng: eng.matmul(bnk2[0:1, 384:416], e127[:], co["gp"][:],
                                           start=True, stop=True),
          reads=["gp%d" % cp, "e127"], writes=[r_b2], tag="c_cfx%d" % c)
        S("scalar", lambda eng: eng.copy(co["crow"][:], bnk2[0:1, 384:416]),
          reads=[r_b2], writes=["crow%d" % cp], tag="c_cfe%d" % c)
        S("tensor", lambda eng: eng.matmul(bnk2[:, 416:448], onesrow[:], co["crow"][:],
                                           start=True, stop=True),
          reads=["crow%d" % cp, "onesrow"], writes=[r_b2], tag="c_cfb%d" % c)
        S("scalar", lambda eng: eng.activation(co["coefC"][:], bnk2[:, 416:448], AF.Exp),
          reads=[r_b2], writes=["coefC%d" % cp], tag="c_cfc%d" % c)
        return steps

    def _chunk_machB(self, c, cp, bufs):
        steps = []

        class _P:
            @staticmethod
            def step(eng_, emit_, **kw):
                steps.append((eng_, emit_, kw))
        nc, sb, pb = self.nc, self.sb, self.pb
        p = _P
        t0 = c * C
        co = bufs["cols"][cp]
        mixed, flat, sqb = bufs["mixed"], bufs["flat"], bufs["sqb"]
        abg, sr, ident = sb["abg"], sb["sr"], sb["ident"]
        cumt, revt, cv = sb["cumt"], sb["revt"], sb["cvals"]
        onesrow, onescolb, e127 = sb["onesrow"], sb["onescolb"], sb["e127"]
        bnk2 = pb[0]
        r_b2 = "pb0"
        S = p.step

        def em_sq(hk, which):
            ti = hk if which == "q" else 16 + hk
            def em(eng):
                return eng.tensor_tensor(sqb[hk % 2][:], mixed[:, ti * 128:(ti + 1) * 128],
                                         mixed[:, ti * 128:(ti + 1) * 128], ALU.mult)
            return em, ti // 8

        def em_nmm(hk, col):
            def em(eng):
                return eng.matmul(bnk2[:, 448 + col:449 + col], sqb[hk % 2][:],
                                  onescolb[:], start=True, stop=True)
            return em
        for which in ("q", "k"):
            for hk in range(16):
                col = hk if which == "q" else 16 + hk
                emf, mgrp = em_sq(hk, which)
                S("vector", emf, reads=["mixed%d" % mgrp],
                  writes=["sqb%d" % (hk % 2)], tag="c_s%s%d_%d" % (which, c, hk))
                S("tensor", em_nmm(hk, col), reads=["sqb%d" % (hk % 2), "onescolb"],
                  writes=[r_b2], tag="c_n%s%d_%d" % (which, c, hk))
        S("scalar", lambda eng: eng.activation(co["lnr"][:], bnk2[:, 448:480], AF.Ln,
                                               bias=cv[:, 2:3]),
          reads=[r_b2, "cvals"], writes=["lnr%d" % cp], tag="c_lnr%d" % c)

        lnq_rep = co["lnr"][:, 0:16].unsqueeze(2).to_broadcast([128, 16, 2])
        lnk_rep = co["lnr"][:, 16:32].unsqueeze(2).to_broadcast([128, 16, 2])

        def v3(ap):
            return ap.rearrange("p (a b) -> p a b", b=2)

        def em_cols(eng):
            cs = co["cs"]
            eng.tensor_scalar(co["lnr"][:, 0:16], co["lnr"][:, 0:16], -0.5,
                              -0.5 * float(np.log(DK)), ALU.mult, ALU.add)
            eng.tensor_scalar(co["lnr"][:, 16:32], co["lnr"][:, 16:32], -0.5, None, ALU.mult)
            eng.tensor_tensor(cs[:, 32:64], co["gp"][:], co["gBe"][:, 32:64], ALU.add)
            eng.tensor_tensor(co["earg"][:], co["gBe"][:, 32:64], co["gp"][:], ALU.add)
            eng.tensor_tensor(v3(cs[:, 0:32]), v3(co["gp"][:]), lnk_rep, ALU.subtract)
            eng.tensor_tensor(v3(cs[:, 32:64]), v3(cs[:, 32:64]), lnk_rep, ALU.add)
            eng.tensor_tensor(v3(cs[:, 64:96]), v3(co["gp"][:]), lnq_rep, ALU.add)
            eng.tensor_tensor(v3(co["Ktl"][:]), v3(co["dp"][:]), lnk_rep, ALU.add)
            return eng.tensor_scalar(co["nrA"][:], cs[:, 0:32], -1.0, None, ALU.mult)
        S("vector", em_cols,
          reads=["gp%d" % cp, "dp%d" % cp, "gBe%d" % cp, "lnr%d" % cp],
          writes=["cs%d" % cp, "nrA%d" % cp, "earg%d" % cp, "Ktl%d" % cp, "lnr%d" % cp],
          tag="c_cols%d" % c)

        def em_exps(eng):
            eng.activation(co["bcol"][:], co["gBe"][:, 32:64], AF.Exp)
            eng.activation(co["bGam"][:], co["earg"][:], AF.Exp)
            eng.activation(co["qGam"][:], co["cs"][:, 64:96], AF.Exp)
            return eng.activation(co["Ktl"][:], co["Ktl"][:], AF.Exp)
        S("scalar", em_exps,
          reads=["gBe%d" % cp, "earg%d" % cp, "cs%d" % cp, "Ktl%d" % cp],
          writes=["bcol%d" % cp, "bGam%d" % cp, "qGam%d" % cp, "Ktl%d" % cp],
          tag="c_exps%d" % c)

        S("tensor", lambda eng: eng.transpose(bnk2[0:96, 384:512], co["cs"][:, 0:96], ident[:]),
          reads=["cs%d" % cp, "ident"], writes=[r_b2], tag="c_trr%d" % c)
        S("scalar", lambda eng: eng.copy(co["rowsA"][:], bnk2[0:96, 384:512]),
          reads=[r_b2], writes=["rowsA%d" % cp], tag="c_rae%d" % c)
        for r in range(3):
            dst = flat[:].rearrange("o (h r c) -> o h r c", r=3, c=128)[:, :, r, :]
            S("sync", (lambda r_, d_: lambda eng: eng.dma_start(
                out=d_, in_=co["rowsA"][r_ * 32:(r_ + 1) * 32, :]))(r, dst),
              reads=["rowsA%d" % cp], writes=["flat"], dma=True,
              tag="c_fl%d_%d" % (c, r))

        S("vector", lambda eng: eng.memset(co["msC"][:], 0.0),
          writes=["msC%d" % cp], tag="c_ms0%d" % c)
        return steps
        return steps

    def _chunk_tail(self, c, cp, bufs):
        if getattr(self, "notail", False):
            return
        p, sb, pb = self.p, self.sb, self.pb
        t0 = c * C
        co = bufs["cols"][cp]
        y1, yA = bufs["y1"], bufs["yA"]
        cv, normwb, identb = sb["cvals"], sb["normwb"], sb["identb"]
        S = p.step

        S("vector", lambda eng: eng.tensor_scalar(co["msC"][:], co["msC"][:],
                                                  1.0 / DV, None, ALU.mult),
          reads=["msC%d" % cp], writes=["msC%d" % cp], tag="c_msn%d" % c)

        S("scalar", lambda eng: eng.activation(co["rstd"][:], co["msC"][:], AF.Ln,
                                               bias=cv[:, 2:3]),
          reads=["msC%d" % cp, "cvals"], writes=["rstd%d" % cp], tag="c_rstda%d" % c)
        S("scalar", lambda eng: eng.activation(co["rstd"][:], co["rstd"][:], AF.Exp,
                                               scale=-0.5),
          sereads=["rstd%d" % cp], writes=["rstd%d" % cp], tag="c_rstdb%d" % c)

        for h in range(HV):
            g3 = h % 4
            BT = pb[6 + (g3 & 1)][:].bitcast(BF16)
            bto = 512 + (g3 >> 1) * 128
            rBT = "pb%d" % (6 + (g3 & 1))

            def em_yb(h_):
                def em(eng):
                    return eng.scalar_tensor_tensor(
                        y1[:, h_ * 128:(h_ + 1) * 128], y1[:, h_ * 128:(h_ + 1) * 128],
                        co["rstd"][:, h_:h_ + 1], normwb[:], ALU.mult, ALU.mult)
                return em
            S("vector", em_yb(h), reads=["y1", "rstd%d" % cp, "normwb"],
              writes=["y1"], tag="c_yb%d_%d" % (c, h))
            S("tensor", (lambda h_, BT_, o_: lambda eng: eng.transpose(
                BT_[:, o_:o_ + 128], y1[:, h_ * 128:(h_ + 1) * 128], identb[:]))(h, BT, bto),
              reads=["y1", "identb"], writes=[rBT], tag="c_ytr%d_%d" % (c, h))
            S("scalar", (lambda h_, BT_, o_: lambda eng: eng.copy(
                yA[:, h_ * 128:(h_ + 1) * 128], BT_[:, o_:o_ + 128]))(h, BT, bto),
              reads=[rBT], writes=["yA"], tag="c_yev%d_%d" % (c, h))
        for q in range(2):
            src = yA[:].rearrange("p (hh t) -> p hh t", t=128)[:, q * 16:(q + 1) * 16, :]
            dst = self.y_s.rearrange("(hh p) t -> p hh t", p=128)[:, q * 16:(q + 1) * 16, t0:t0 + C]
            S("sync", (lambda s_, d_: lambda eng: eng.dma_start(out=d_, in_=s_))(src, dst),
              reads=["yA"], dma=True, tag="c_yst%d_%d" % (c, q))

    def _head_steps(self, c, cp, h, bufs):
        nc, p, sb, pb = self.nc, self.p, self.sb, self.pb
        co = bufs["cols"][cp]
        mixed, zsil_t, Sst = bufs["mixed"], bufs["zsil"][cp], bufs["Sst"]
        flat, y1 = bufs["flat"], bufs["y1"]
        solT, Usb, Rt, VBs = bufs["solT"], bufs["Usb"], bufs["Rt"], bufs["VBs"]
        KBt, KSe, QSe, Ot = bufs["KBt"], bufs["KSe"], bufs["QSe"], bufs["Ot"]
        Esb, r3m = bufs["Esb"], bufs["r3m"]
        g3 = h % 4
        hk = h // 2
        BF = pb[1]
        rBF = "pb1"
        BC = pb[0]
        rBC = "pb0"
        BS = pb[2 + g3]
        rBS = "pb%d" % (2 + g3)
        BT = pb[6 + (g3 & 1)][:].bitcast(BF16)
        bto = (g3 >> 1) * 256
        rBT = "pb%d" % (6 + (g3 & 1))
        sol = solT[g3]
        rsol = "solT%d" % g3
        U = Usb[g3]
        rU = "Usb%d" % g3
        cm384, identb, onesrow = sb["cm384"], sb["identb"], sb["onesrow"]
        qT = mixed[:, hk * 128:(hk + 1) * 128]
        kT = mixed[:, (16 + hk) * 128:(16 + hk + 1) * 128]
        vT = mixed[:, (32 + h) * 128:(32 + h + 1) * 128]
        Sh = Sst[:, h * 128:(h + 1) * 128]
        tg = "%d_%d" % (c, h)
        steps = []

        def S(engine, emit, **kw):
            steps.append((engine, emit, kw))

        S("tensor", lambda eng: eng.matmul(BC[:, 0:384], onesrow[:],
                                           flat[0:1, h * 384:(h + 1) * 384],
                                           start=True, stop=True),
          reads=["flat", "onesrow"], writes=[rBC], tag="h_bc" + tg)
        S("vector", lambda eng: eng.tensor_tensor(r3m[g3][:], BC[:, 0:384], cm384[:], ALU.add),
          reads=[rBC, "cm384"], writes=["r3m%d" % g3], tag="h_r3m" + tg)

        def em_es(eng):
            eng.activation(Esb[g3][:, 0:128], r3m[g3][:, 0:128], AF.Exp,
                           bias=co["cs"][:, 32 + h:33 + h], scale=-1.0)
            eng.activation(Esb[g3][:, 128:256], r3m[g3][:, 128:256], AF.Exp,
                           bias=co["nrA"][:, h:h + 1])
            return eng.activation(Esb[g3][:, 256:384], r3m[g3][:, 256:384], AF.Exp,
                                  bias=co["nrA"][:, h:h + 1])
        S("scalar", em_es, reads=["r3m%d" % g3, "cs%d" % cp, "nrA%d" % cp],
          writes=["Esb%d" % g3], tag="h_es" + tg)

        def em_pp(eng):
            eng.matmul(BF[:, 0:128], kT, kT, start=True, stop=True)
            eng.matmul(BF[:, 128:256], kT, qT, start=True, stop=True)
            eng.matmul(BF[:, 256:384], kT, Sh, start=True, stop=True)
            return eng.matmul(BF[:, 384:512], qT, Sh, start=True, stop=True)
        S("tensor", em_pp, reads=["mixed%d" % (hk // 8), "mixed%d" % (2 + hk // 8), "Sst"],
          writes=[rBF], tag="h_pp" + tg)

        def em_xya(eng):
            eng.tensor_tensor(sol[:, 0:128], BF[:, 0:128], Esb[g3][:, 0:128], ALU.mult)
            eng.tensor_tensor(sol[:, 6 * 128:7 * 128], BF[:, 0:128], Esb[g3][:, 128:256], ALU.mult)
            eng.tensor_tensor(sol[:, 13 * 128:14 * 128], BF[:, 128:256],
                              Esb[g3][:, 256:384], ALU.mult)
            eng.tensor_scalar(KSe[g3][:], BF[:, 256:384],
                              co["bGam"][:, h:h + 1], None, ALU.mult)
            return eng.tensor_scalar(QSe[g3][:], BF[:, 384:512],
                                     co["qGam"][:, h:h + 1], None, ALU.mult)
        S("vector", em_xya, reads=[rBF, "Esb%d" % g3, "bGam%d" % cp, "qGam%d" % cp],
          writes=[rsol, "KSe%d" % g3, "QSe%d" % g3], tag="h_xya" + tg)

        S("tensor", lambda eng: eng.transpose(BT[:, bto:bto + 128], vT, identb[:]),
          reads=["mixed%d" % (4 + h // 8), "identb"], writes=[rBT], tag="h_vtr" + tg)
        S("tensor", lambda eng: eng.transpose(BT[:, bto + 128:bto + 256], kT, identb[:]),
          reads=["mixed%d" % (2 + hk // 8), "identb"], writes=[rBT], tag="h_ktr" + tg)

        S("vector", lambda eng: eng.tensor_scalar(VBs[g3][:], BT[:, bto:bto + 128],
                                                  co["bcol"][:, h:h + 1], None, ALU.mult),
          reads=[rBT, "bcol%d" % cp], writes=["VBs%d" % g3], tag="h_vbe" + tg)
        S("vector", lambda eng: eng.tensor_scalar(KBt[g3][:], BT[:, bto + 128:bto + 256],
                                                  co["Ktl"][:, h:h + 1], None, ALU.mult),
          reads=[rBT, "Ktl%d" % cp], writes=["KBt%d" % g3], tag="h_kbt" + tg)
        S("vector", lambda eng: eng.tensor_tensor(Rt[g3][:], VBs[g3][:], KSe[g3][:], ALU.subtract),
          reads=["VBs%d" % g3, "KSe%d" % g3], writes=["Rt%d" % g3], tag="h_r" + tg)

        S("tensor", lambda eng: eng.matmul(BS[:, 256:384], sol[:, 6 * 128:7 * 128],
                                           Rt[g3][:], start=True, stop=True),
          reads=[rsol, "Rt%d" % g3], writes=[rBS], tag="h_a0" + tg)
        S("vector", lambda eng: eng.tensor_tensor(U[:, 0:128], Rt[g3][:],
                                                  BS[:, 256:384], ALU.subtract),
          reads=["Rt%d" % g3, rBS], writes=[rU], tag="h_u1" + tg)
        for k in range(1, 7):
            xk_prev = sol[:, (k - 1) * 128:k * 128]
            yk_prev = sol[:, (6 + k - 1) * 128:(6 + k) * 128]
            ups = k % 2
            uprev = U[:, (1 - ups) * 128:(2 - ups) * 128]
            unew = U[:, ups * 128:(ups + 1) * 128]
            if k < 6:
                def em_sq(yp, xp):
                    def em(eng):
                        eng.matmul(BS[:, 0:128], yp, xp, start=True, stop=True)
                        return eng.matmul(BS[:, 128:256], xp, yp, start=True, stop=True)
                    return em
                S("tensor", em_sq(yk_prev, xk_prev), reads=[rsol], writes=[rBS],
                  tag="h_sq%d" % k + tg)

                def em_sqe(kk):
                    def em(eng):
                        eng.copy(sol[:, kk * 128:(kk + 1) * 128], BS[:, 0:128])
                        return eng.copy(sol[:, (6 + kk) * 128:(6 + kk + 1) * 128],
                                        BS[:, 128:256])
                    return em
                S("scalar", em_sqe(k), reads=[rBS], writes=[rsol], tag="h_sqe%d" % k + tg)
            else:
                S("tensor", (lambda yp, xp: lambda eng: eng.matmul(
                    BS[:, 128:256], xp, yp, start=True, stop=True))(yk_prev, xk_prev),
                  reads=[rsol], writes=[rBS], tag="h_sq%d" % k + tg)
                S("scalar", (lambda kk: lambda eng: eng.copy(
                    sol[:, (6 + kk) * 128:(6 + kk + 1) * 128], BS[:, 128:256]))(k),
                  reads=[rBS], writes=[rsol], tag="h_sqe%d" % k + tg)
            S("tensor", (lambda kk, up: lambda eng: eng.matmul(
                BS[:, 256:384], sol[:, (6 + kk) * 128:(6 + kk + 1) * 128], up,
                start=True, stop=True))(k, uprev),
              reads=[rsol, rU], writes=[rBS], tag="h_a%d" % k + tg)
            S("vector", (lambda un, up: lambda eng: eng.tensor_tensor(
                un, up, BS[:, 256:384], ALU.add))(unew, uprev),
              reads=[rU, rBS], writes=[rU], tag="h_u%d" % (k + 1) + tg)
        ufin = U[:, 0:128]

        S("tensor", lambda eng: eng.matmul(BS[:, 384:512], sol[:, 13 * 128:14 * 128],
                                           ufin, start=True, stop=True),
          reads=[rsol, rU], writes=[rBS], tag="h_oi" + tg)
        S("vector", lambda eng: eng.tensor_tensor(Ot[g3][:], QSe[g3][:], BS[:, 384:512], ALU.add),
          reads=["QSe%d" % g3, rBS], writes=["Ot%d" % g3], tag="h_o" + tg)

        S("gpsimd", (lambda h_: lambda eng: eng.tensor_tensor(
            y1[:, h_ * 128:(h_ + 1) * 128], Ot[g3][:],
            zsil_t[:, h_ * 128:(h_ + 1) * 128], ALU.mult))(h),
          reads=["Ot%d" % g3, "zsil%d" % cp], writes=["y1"], tag="h_y1" + tg)
        S("scalar", (lambda h_: lambda eng: eng.activation(
            Esb[g3][:, 0:128], y1[:, h_ * 128:(h_ + 1) * 128], AF.Square,
            accum_out=co["msC"][:, h_:h_ + 1]))(h),
          reads=["y1", "msC%d" % cp], writes=["Esb%d" % g3, "msC%d" % cp], tag="h_ms" + tg)

        S("tensor", lambda eng: eng.matmul(BS[:, 384:512], KBt[g3][:], ufin,
                                           start=True, stop=True),
          reads=["KBt%d" % g3, rU], writes=[rBS], tag="h_st" + tg)
        S("vector", (lambda h_: lambda eng: eng.scalar_tensor_tensor(
            Sst[:, h_ * 128:(h_ + 1) * 128], Sst[:, h_ * 128:(h_ + 1) * 128],
            co["coefC"][:, h_:h_ + 1], BS[:, 384:512], ALU.mult, ALU.add))(h),
          reads=["Sst", "coefC%d" % cp, rBS], writes=["Sst"], tag="h_sn" + tg)
        return steps

    def ph_d(self, hh):
        nc, p, sb, dr, pb = self.nc, self.p, self.sb, self.dr, self.pb
        T, TH = self.T, self.TH
        TQ = min(512, TH)
        nq = TH // TQ
        at = self.at
        yh = at("yh%d" % hh, [128, 32 * TH], BF16, 0)
        xnew = at("xnew%d" % hh, [128, 16 * TH], F32, 64 * 1024)
        wco = [at("wco%d_%d" % (hh, i), [128, 32 * 128], BF16, 128 * 1024 + i * 8192)
               for i in range(2)]
        xold = at("xold%d" % hh, [128, TQ], F32, 144 * 1024)
        xbf = at("xbf2_%d" % hh, [128, TQ], BF16, 147 * 1024)
        xsq = at("xsq2_%d" % hh, [128, TQ], BF16, 149 * 1024)
        dbc = at("dbc%d" % hh, [128, 2 * TQ], F32, 151 * 1024)
        drow = at("drow%d" % hh, [1, 4 * TQ], F32, 156 * 1024)
        lnsrow = at("lnsrow%d" % hh, [1, TH], F32, 168 * 1024)
        lnmrow = at("lnmrow%d" % hh, [1, TH], F32, 172 * 1024)
        t0h = hh * TH

        for q in range(4):
            src = self.y_s.rearrange("(hv p) t -> p hv t", p=128)[:, q * 8:(q + 1) * 8, t0h:t0h + TH]
            dst = yh[:].rearrange("p (hv t) -> p hv t", t=TH)[:, q * 8:(q + 1) * 8, :]
            p.step("sync", (lambda s_, d_: lambda eng: eng.dma_start(out=d_, in_=s_))(src, dst),
                   writes=["yh"], dma=True, tag="d_ly%d_%d" % (hh, q))

        for dm in range(16):
            par = dm % 2
            src = dr["Wout"].rearrange("(hv p) c -> p hv c", p=128)[:, :, dm * 128:(dm + 1) * 128]
            for q in range(2):
                p.step("gpsimd",
                       (lambda s_, d_, q_: lambda eng: eng.dma_start(
                           out=d_[:].rearrange("p (hv c) -> p hv c", c=128)[:, q_ * 16:(q_ + 1) * 16, :],
                           in_=s_[:, q_ * 16:(q_ + 1) * 16, :]))(src, wco[par], q),
                       writes=["wco%d" % par], dma=True, tag="d_lw%d_%d_%d" % (hh, dm, q))
            for tq in range(nq):
                bnk = pb[par]
                rb = "pb%d" % par

                def em_mm(par_, tq_):
                    def em(eng):
                        inst = None
                        for hv in range(32):
                            inst = eng.matmul(
                                pb[par_][:, 0:TQ], wco[par_][:, hv * 128:(hv + 1) * 128],
                                yh[:].rearrange("p (hv t) -> p hv t", t=TH)[:, hv, tq_ * TQ:(tq_ + 1) * TQ],
                                start=(hv == 0), stop=(hv == 31))
                        return inst
                    return em
                p.step("tensor", em_mm(par, tq), reads=["wco%d" % par, "yh"],
                       writes=["pb%d" % par], tag="d_mm%d_%d_%d" % (hh, dm, tq))
                src2 = dr["xT"][dm * 128:(dm + 1) * 128, t0h + tq * TQ: t0h + (tq + 1) * TQ]
                p.step("sync", (lambda s_: lambda eng: eng.dma_start(out=xold[:], in_=s_))(src2),
                       writes=["xold"], dma=True, tag="d_lx%d_%d_%d" % (hh, dm, tq))

                def em_add(par_, dm_, tq_):
                    def em(eng):
                        return eng.tensor_tensor(
                            xnew[:, dm_ * TH + tq_ * TQ: dm_ * TH + (tq_ + 1) * TQ],
                            xold[:], pb[par_][:, 0:TQ], ALU.add)
                    return em
                p.step("vector", em_add(par, dm, tq), reads=["xold", "pb%d" % par],
                       writes=["xnew"], tag="d_add%d_%d_%d" % (hh, dm, tq))

        onescolb, onesrow, cv = sb["onescolb"], sb["onesrow"], sb["cvals"]
        for tq in range(nq):
            for dm in range(16):
                def em_cast(dm_, tq_):
                    def em(eng):
                        return eng.tensor_copy(xbf[:], xnew[:, dm_ * TH + tq_ * TQ: dm_ * TH + (tq_ + 1) * TQ])
                    return em
                p.step("vector", em_cast(dm, tq), reads=["xnew"], writes=["xbf"],
                       tag="d_c%d_%d_%d" % (hh, dm, tq))
                p.step("scalar", lambda eng: eng.activation(xsq[:], xbf[:], AF.Square),
                       reads=["xbf"], writes=["xsq"], tag="d_q%d_%d_%d" % (hh, dm, tq))

                def em_smm(dm_):
                    def em(eng):
                        eng.matmul(pb[4][0:1, 0:TQ], onescolb[:], xbf[:],
                                   start=(dm_ == 0), stop=(dm_ == 15))
                        return eng.matmul(pb[5][0:1, 0:TQ], onescolb[:], xsq[:],
                                          start=(dm_ == 0), stop=(dm_ == 15))
                    return em
                p.step("tensor", em_smm(dm), reads=["xbf", "xsq", "onescolb"],
                       writes=["pb4", "pb5"], tag="d_s%d_%d_%d" % (hh, dm, tq))

            def em_rows(eng):
                eng.tensor_scalar(drow[0:1, 0:TQ], pb[4][0:1, 0:TQ], 1.0 / D, None, ALU.mult)
                eng.tensor_scalar(drow[0:1, 3 * TQ:4 * TQ], pb[5][0:1, 0:TQ], 1.0 / D, None, ALU.mult)
                eng.tensor_tensor(drow[0:1, TQ:2 * TQ], drow[0:1, 0:TQ], drow[0:1, 0:TQ], ALU.mult)
                return eng.tensor_tensor(drow[0:1, 3 * TQ:4 * TQ], drow[0:1, 3 * TQ:4 * TQ],
                                         drow[0:1, TQ:2 * TQ], ALU.subtract)
            p.step("vector", em_rows, reads=["pb4", "pb5"], writes=["drow"],
                   tag="d_r%d_%d" % (hh, tq))

            def em_rstd(eng):
                eng.activation(drow[0:1, 2 * TQ:3 * TQ], drow[0:1, 3 * TQ:4 * TQ], AF.Ln,
                               bias=cv[0:1, 0:1])
                return eng.activation(drow[0:1, TQ:2 * TQ], drow[0:1, 2 * TQ:3 * TQ],
                                      AF.Exp, scale=-0.5)
            p.step("scalar", em_rstd, reads=["drow", "cvals"], writes=["drow"],
                   tag="d_rs%d_%d" % (hh, tq))
            p.step("vector", lambda eng: eng.tensor_tensor(
                drow[0:1, 2 * TQ:3 * TQ], drow[0:1, 0:TQ], drow[0:1, TQ:2 * TQ], ALU.mult),
                reads=["drow"], writes=["drow"], tag="d_ms%d_%d" % (hh, tq))

            def em_carry(tq_):
                def em(eng):
                    eng.tensor_copy(lnsrow[0:1, tq_ * TQ:(tq_ + 1) * TQ], drow[0:1, TQ:2 * TQ])
                    return eng.tensor_copy(lnmrow[0:1, tq_ * TQ:(tq_ + 1) * TQ], drow[0:1, 2 * TQ:3 * TQ])
                return em
            p.step("vector", em_carry(tq), reads=["drow"],
                   writes=["lnsrow", "lnmrow"], tag="d_cr%d_%d" % (hh, tq))
            p.step("tensor", lambda eng: eng.matmul(pb[6][:, 0:TQ], onesrow[:],
                                                    drow[0:1, TQ:2 * TQ], start=True, stop=True),
                   reads=["drow", "onesrow"], writes=["pb6"], tag="d_bs%d_%d" % (hh, tq))
            p.step("tensor", lambda eng: eng.matmul(pb[7][:, 0:TQ], onesrow[:],
                                                    drow[0:1, 2 * TQ:3 * TQ], start=True, stop=True),
                   reads=["drow", "onesrow"], writes=["pb7"], tag="d_bm%d_%d" % (hh, tq))
            p.step("vector", lambda eng: eng.tensor_copy(dbc[:, 0:TQ], pb[6][:, 0:TQ]),
                   reads=["pb6"], writes=["dbc"], tag="d_es%d_%d" % (hh, tq))
            p.step("vector", lambda eng: eng.tensor_copy(dbc[:, TQ:2 * TQ], pb[7][:, 0:TQ]),
                   reads=["pb7"], writes=["dbc"], tag="d_em%d_%d" % (hh, tq))

            for dm in range(16):
                def em_app(dm_, tq_):
                    def em(eng):
                        xs = xnew[:, dm_ * TH + tq_ * TQ: dm_ * TH + (tq_ + 1) * TQ]
                        eng.tensor_tensor(xbf[:], xs, dbc[:, 0:TQ], ALU.mult)
                        return eng.tensor_tensor(xbf[:], xbf[:], dbc[:, TQ:2 * TQ], ALU.subtract)
                    return em
                p.step("vector", em_app(dm, tq), reads=["xnew", "dbc"],
                       writes=["xbf"], tag="d_a%d_%d_%d" % (hh, dm, tq))
                dst = self.ln2_s[dm * 128:(dm + 1) * 128, t0h + tq * TQ: t0h + (tq + 1) * TQ]
                p.step("sync", (lambda d_: lambda eng: eng.dma_start(out=d_, in_=xbf[:]))(dst),
                       reads=["xbf"], dma=True, tag="d_st%d_%d_%d" % (hh, dm, tq))

    def ph_e(self, hh):
        nc, p, sb, dr, pb = self.nc, self.p, self.sb, self.dr, self.pb
        T, TH = self.T, self.TH
        t0h = hh * TH
        at = self.at
        Hb = at("Hb%d" % hh, [128, 16 * TH], BF16, 0)
        ln2x = at("ln2x%d" % hh, [128, 16 * TH], BF16, 32 * 1024)
        xnew = self.sb["xnew%d" % hh]
        wgb = [at("wgb%d_%d" % (hh, i), [128, 16 * 128], BF16, 128 * 1024 + i * 4096)
               for i in range(2)]
        wub = [at("wub%d_%d" % (hh, i), [128, 16 * 128], BF16, 136 * 1024 + i * 4096)
               for i in range(2)]
        wdb = [at("wdb%d_%d" % (hh, i), [128, 16 * 128], BF16, 144 * 1024 + i * 4096)
               for i in range(2)]
        wrf = at("wrf%d" % hh, [128, 16 * 8], F32, 152 * 1024)
        wfin = at("wfin%d" % hh, [128, 8 * (TH // 128)], F32, 154 * 1024)
        wT = at("wT%d" % hh, [8, TH], BF16, 156 * 1024)
        wrow = at("wrow%d" % hh, [1, TH], BF16, 158 * 1024)
        wbt = at("wbt%d" % hh, [128, TH], F32, 160 * 1024)
        sG = at("sG%d" % hh, [128, TH], BF16, 164 * 1024)
        lnsrow = self.sb["lnsrow%d" % hh]
        lnmrow = self.sb["lnmrow%d" % hh]
        ntt = TH // 128

        for q in range(4):
            src = self.ln2_s.rearrange("(dt p) t -> p dt t", p=128)[:, q * 4:(q + 1) * 4, t0h:t0h + TH]
            dst = ln2x[:].rearrange("p (dt t) -> p dt t", t=TH)[:, q * 4:(q + 1) * 4, :]
            p.step("sync", (lambda s_, d_: lambda eng: eng.dma_start(out=d_, in_=s_))(src, dst),
                   writes=["ln2x"], dma=True, tag="e_ll%d_%d" % (hh, q))
        p.step("sync", lambda eng: eng.dma_start(
            out=wrf[:].rearrange("p (dt c) -> p dt c", c=8),
            in_=dr["Wr"].rearrange("(dt p) n -> p dt n", p=128)),
            writes=["wrf"], dma=True, tag="e_lwr%d" % hh)

        iota8, iota8b, neg18, cv = sb["iota8"], sb["iota8b"], sb["neg18"], sb["cvals"]
        onescol, onesrow = sb["onescol"], sb["onesrow"]
        GL = float(0.5 - np.log(4.0 / 3.0))
        NTTE = ntt * 8
        rts = self.at("rts%d" % hh, [128, 4 * NTTE], F32, 153 * 1024)
        rtv = self.at("rtv%d" % hh, [128, 4 * NTTE], F32, 153 * 1024 + 16 * NTTE)
        rtc = self.at("rtc%d" % hh, [128, 128], F32, 153 * 1024 + 32 * NTTE)

        def g3(ap):
            return ap.rearrange("p (tt e) -> p tt e", e=8)

        def cb(ap):
            return ap.unsqueeze(2).to_broadcast([128, ntt, 8])

        def eb(ap):
            return ap.unsqueeze(1).to_broadcast([128, ntt, 8])

        def em_csum(eng):
            inst = None
            for d in range(16):
                inst = eng.matmul(pb[7][0:1, 0:8], onescol[:], wrf[:, d * 8:(d + 1) * 8],
                                  start=(d == 0), stop=(d == 15))
            return inst
        p.step("tensor", em_csum, reads=["wrf", "onescol"], writes=["pb7"],
               tag="e_cs%d" % hh)
        p.step("vector", lambda eng: eng.tensor_copy(rtc[0:1, 96:104], pb[7][0:1, 0:8]),
               reads=["pb7"], writes=["rtc"], tag="e_cse%d" % hh)
        p.step("tensor", lambda eng: eng.matmul(pb[7][:, 0:8], onesrow[:], rtc[0:1, 96:104],
                                                start=True, stop=True),
               reads=["rtc", "onesrow"], writes=["pb7"], tag="e_csb%d" % hh)
        p.step("vector", lambda eng: eng.tensor_copy(rtc[:, 104:112], pb[7][:, 0:8]),
               reads=["pb7"], writes=["rtc"], tag="e_csb2%d" % hh)

        def em_rmm(tt_):
            def em(eng):
                inst = None
                for d in range(16):
                    inst = eng.matmul(pb[6][0:128, tt_ * 8:(tt_ + 1) * 8],
                                      xnew[:, d * TH + tt_ * 128: d * TH + (tt_ + 1) * 128],
                                      wrf[:, d * 8:(d + 1) * 8],
                                      start=(d == 0), stop=(d == 15))
                return inst
            return em

        def em_scol(tt_):
            def em(eng):
                eng.matmul(pb[7][:, 16 + tt_:17 + tt_], lnsrow[0:1, tt_ * 128:(tt_ + 1) * 128],
                           onesrow[0:1, 0:1], start=True, stop=True)
                return eng.matmul(pb[7][:, 24 + tt_:25 + tt_], lnmrow[0:1, tt_ * 128:(tt_ + 1) * 128],
                                  onesrow[0:1, 0:1], start=True, stop=True)
            return em
        for tt in range(ntt):
            p.step("tensor", em_rmm(tt), reads=["xnew", "wrf"], writes=["pb6"],
                   tag="e_rm%d_%d" % (hh, tt))
            p.step("tensor", em_scol(tt), reads=["lnsrow", "lnmrow", "onesrow"],
                   writes=["pb7"], tag="e_sc%d_%d" % (hh, tt))

        def em_sme(eng):
            eng.tensor_copy(rtc[:, 80:80 + ntt], pb[7][:, 16:16 + ntt])
            return eng.tensor_copy(rtc[:, 88:88 + ntt], pb[7][:, 24:24 + ntt])
        p.step("vector", em_sme, reads=["pb7"], writes=["rtc"], tag="e_sme%d" % hh)

        def em_arg(eng):
            eng.tensor_tensor(g3(rtv[:, NTTE:2 * NTTE]),
                              cb(rtc[:, 88:88 + ntt]), eb(rtc[:, 104:112]), ALU.mult)
            eng.tensor_tensor(g3(rtv[:, 0:NTTE]),
                              g3(pb[6][0:128, 0:NTTE]), cb(rtc[:, 80:80 + ntt]), ALU.mult)
            return eng.tensor_tensor(rtv[:, 0:NTTE], rtv[:, 0:NTTE],
                                     rtv[:, NTTE:2 * NTTE], ALU.subtract)
        p.step("vector", em_arg, reads=["pb6"], sereads=["rtc"], writes=["rtv"],
               tag="e_arg%d" % hh)

        def em_racts(eng):
            eng.activation(rts[:, 0:NTTE], rtv[:, 0:NTTE], AF.Sigmoid, scale=cv[:, 4:5])
            eng.activation(rts[:, NTTE:2 * NTTE], rts[:, 0:NTTE], AF.Abs, bias=cv[:, 5:6])
            eng.activation(rts[:, 2 * NTTE:3 * NTTE], rts[:, NTTE:2 * NTTE], AF.Exp, scale=-10.0)
            return eng.activation(rts[:, 3 * NTTE:4 * NTTE], rts[:, NTTE:2 * NTTE], AF.Exp,
                                  scale=float(-1.0 / 0.3))
        p.step("scalar", em_racts, reads=["rtv", "cvals"], writes=["rts"],
               tag="e_ra%d" % hh)

        inh = rts[:, 0:NTTE]
        w0 = rts[:, 2 * NTTE:3 * NTTE]
        fb = rts[:, 3 * NTTE:4 * NTTE]
        e1 = rtv[:, 0:NTTE]
        tv = rtv[:, NTTE:2 * NTTE]
        w8 = rtv[:, 2 * NTTE:3 * NTTE]
        fbw = rtv[:, 3 * NTTE:4 * NTTE]
        sA = rtc[:, 0:ntt]
        ne = rtc[:, 8:8 + ntt]
        m1 = rtc[:, 16:16 + ntt]
        i1 = rtc[:, 24:24 + ntt]
        m2 = rtc[:, 32:32 + ntt]
        i2 = rtc[:, 40:40 + ntt]
        sB = rtc[:, 48:48 + ntt]
        sC = rtc[:, 56:56 + ntt]
        sD = rtc[:, 64:64 + ntt]

        def em_v1(eng):
            eng.tensor_scalar(e1, inh, GL, None, ALU.is_ge)
            eng.tensor_scalar(tv, inh, 0.5, None, ALU.is_le)
            eng.tensor_tensor(e1, e1, tv, ALU.mult)
            eng.tensor_tensor(w8, w0, e1, ALU.mult)
            eng.tensor_reduce(sA, g3(w8), axis=AX.X, op=ALU.add)
            eng.tensor_reduce(m1, g3(fb), axis=AX.X, op=ALU.max)
            return eng.tensor_scalar(ne, sA, 1e-8, None, ALU.is_lt)
        p.step("vector", em_v1, reads=["rts"], writes=["rtv", "rtc"],
               tag="e_v1%d" % hh)

        def em_v2(eng):
            eng.tensor_tensor(g3(tv), g3(fb), cb(m1), ALU.is_equal)
            eng.scalar_tensor_tensor(g3(tv), g3(tv), -100.0, eb(iota8b[:]), ALU.mult, ALU.add)
            return eng.tensor_reduce(i1, g3(tv), axis=AX.X, op=ALU.min)
        p.step("vector", em_v2, reads=["rts"], sereads=["rtc"], writes=["rtv", "rtc"],
               tag="e_v2%d" % hh)

        def em_v3(eng):
            eng.tensor_tensor(g3(e1), eb(iota8[:]), cb(i1), ALU.is_equal)
            eng.tensor_scalar(tv, fb, 1.0, None, ALU.add)
            eng.tensor_tensor(tv, e1, tv, ALU.mult)
            eng.tensor_tensor(tv, fb, tv, ALU.subtract)
            return eng.tensor_reduce(m2, g3(tv), axis=AX.X, op=ALU.max)
        p.step("vector", em_v3, reads=["rts"], sereads=["rtc"], writes=["rtv", "rtc"],
               tag="e_v3%d" % hh)

        def em_v4(eng):
            eng.tensor_tensor(g3(fbw), g3(tv), cb(m2), ALU.is_equal)
            eng.scalar_tensor_tensor(g3(fbw), g3(fbw), -100.0, eb(iota8b[:]), ALU.mult, ALU.add)
            return eng.tensor_reduce(i2, g3(fbw), axis=AX.X, op=ALU.min)
        p.step("vector", em_v4, sereads=["rtc"], reads=["rtv"], writes=["rtv", "rtc"],
               tag="e_v4%d" % hh)

        def em_v5(eng):
            eng.tensor_tensor(g3(tv), eb(iota8[:]), cb(i2), ALU.is_equal)
            eng.tensor_tensor(e1, e1, tv, ALU.add)
            eng.tensor_tensor(fbw, fb, e1, ALU.mult)
            return eng.tensor_reduce(sB, g3(fbw), axis=AX.X, op=ALU.add)
        p.step("vector", em_v5, reads=["rts"], sereads=["rtc"], writes=["rtv", "rtc"],
               tag="e_v5%d" % hh)

        def em_v6(eng):
            eng.tensor_scalar(sC, sB, 1e-8, None, ALU.max)
            eng.tensor_scalar(tv, fbw, 1.0, None, ALU.mult)
            return eng.reciprocal(sD, sC)
        p.step("vector", em_v6, sereads=["rtc"], reads=["rtv"], writes=["rtv", "rtc"],
               tag="e_v6%d" % hh)

        def em_v7(eng):
            eng.tensor_tensor(g3(fbw), g3(fbw), cb(sD), ALU.mult)
            eng.tensor_tensor(tv, fbw, w8, ALU.subtract)
            eng.tensor_tensor(g3(tv), g3(tv), cb(ne), ALU.mult)
            eng.tensor_tensor(w8, w8, tv, ALU.add)
            return eng.tensor_reduce(sB, g3(w8), axis=AX.X, op=ALU.add)
        p.step("vector", em_v7, sereads=["rtc"], reads=["rtv"], writes=["rtv", "rtc"],
               tag="e_v7%d" % hh)

        def em_v8(eng):
            eng.tensor_scalar(sC, sB, 1e-8, None, ALU.max)
            eng.tensor_scalar(tv, w8, 1.0, None, ALU.mult)
            return eng.reciprocal(sD, sC)
        p.step("vector", em_v8, sereads=["rtc"], reads=["rtv"], writes=["rtv", "rtc"],
               tag="e_v8%d" % hh)

        def em_v9(eng):
            return eng.tensor_tensor(g3(wfin[:]), g3(w8), cb(sD), ALU.mult)
        p.step("vector", em_v9, sereads=["rtc"], reads=["rtv"], writes=["wfin"],
               tag="e_v9%d" % hh)

        for tt in range(ntt):
            p.step("tensor", (lambda tt_: lambda eng: eng.transpose(
                pb[7][0:8, 0:128], wfin[:, tt_ * 8:(tt_ + 1) * 8], sb["ident"][:]))(tt),
                   reads=["wfin", "ident"], writes=["pb7"], tag="e_wt%d_%d" % (hh, tt))
            p.step("vector", (lambda tt_: lambda eng: eng.tensor_copy(
                wT[0:8, tt_ * 128:(tt_ + 1) * 128], pb[7][0:8, 0:128]))(tt),
                   reads=["pb7"], writes=["wT"], tag="e_we%d_%d" % (hh, tt))

        NTB = TH // 512 if TH >= 512 else 1
        TBW = min(512, TH)
        for e in range(NE):
            p.step("sync", (lambda e_: lambda eng: eng.dma_start(
                out=wrow[:], in_=wT[e_:e_ + 1, :]))(e),
                reads=["wT"], writes=["wrow"], dma=True, tag="e_wr%d_%d" % (hh, e))

            def em_wb(eng):
                inst = None
                for tb in range(NTB):
                    inst = eng.matmul(pb[6][:, 0:TBW], sb["onesrow"][:],
                                      wrow[0:1, tb * TBW:(tb + 1) * TBW], start=True, stop=True)
                return inst
            for tb in range(NTB):
                bk = 6 + (tb % 2)
                p.step("tensor", (lambda tb_, bk_: lambda eng: eng.matmul(
                    pb[bk_][:, 0:TBW], sb["onesrowb"][:],
                    wrow[0:1, tb_ * TBW:(tb_ + 1) * TBW], start=True, stop=True))(tb, bk),
                    reads=["wrow", "onesrowb"], writes=["pb%d" % bk], tag="e_wb%d_%d_%d" % (hh, e, tb))
                p.step("vector", (lambda tb_, bk_: lambda eng: eng.tensor_copy(
                    wbt[:, tb_ * TBW:(tb_ + 1) * TBW], pb[bk_][:, 0:TBW]))(tb, bk),
                    reads=["pb%d" % bk], writes=["wbt"], tag="e_we2%d_%d_%d" % (hh, e, tb))

            for f in range(16):
                par = f % 2
                srcg = dr["Wg"][e].rearrange("(dt p) n -> p dt n", p=128)[:, :, f * 128:(f + 1) * 128]
                srcu = dr["Wu"][e].rearrange("(dt p) n -> p dt n", p=128)[:, :, f * 128:(f + 1) * 128]
                for q in range(2):
                    p.step("gpsimd", (lambda s_, d_, q_: lambda eng: eng.dma_start(
                        out=d_[:].rearrange("p (dt c) -> p dt c", c=128)[:, q_ * 8:(q_ + 1) * 8, :],
                        in_=s_[:, q_ * 8:(q_ + 1) * 8, :]))(srcg, wgb[par], q),
                        writes=["wgb%d" % par], dma=True, tag="e_lg%d_%d_%d_%d" % (hh, e, f, q))
                    p.step("gpsimd", (lambda s_, d_, q_: lambda eng: eng.dma_start(
                        out=d_[:].rearrange("p (dt c) -> p dt c", c=128)[:, q_ * 8:(q_ + 1) * 8, :],
                        in_=s_[:, q_ * 8:(q_ + 1) * 8, :]))(srcu, wub[par], q),
                        writes=["wub%d" % par], dma=True, tag="e_lu%d_%d_%d_%d" % (hh, e, f, q))
                gb = [par * 4, par * 4 + 1]
                ub = [2 + par * 4 - par * 3, 0]
                gb = [0 + par * 4, 1 + par * 4]
                ubk = [2 + par * 4, 3 + par * 4]

                def em_gu(par_, f_):
                    def em(eng):
                        inst = None
                        for d in range(16):
                            lg = wgb[par_][:, d * 128:(d + 1) * 128]
                            lu = wub[par_][:, d * 128:(d + 1) * 128]
                            for tb in range(NTB):
                                inst = eng.matmul(pb[par_ * 4][:, 0:TBW] if tb == 0 else pb[par_ * 4 + 1][:, 0:TBW],
                                                  lg,
                                                  ln2x[:].rearrange("p (dt t) -> p dt t", t=TH)[:, d, tb * TBW:(tb + 1) * TBW],
                                                  start=(d == 0), stop=(d == 15))
                            for tb in range(NTB):
                                inst = eng.matmul(pb[par_ * 4 + 2][:, 0:TBW] if tb == 0 else pb[par_ * 4 + 3][:, 0:TBW],
                                                  lu,
                                                  ln2x[:].rearrange("p (dt t) -> p dt t", t=TH)[:, d, tb * TBW:(tb + 1) * TBW],
                                                  start=(d == 0), stop=(d == 15))
                        return inst
                    return em
                rbanks = ["pb%d" % (par * 4 + i) for i in range(4)]
                p.step("tensor", em_gu(par, f),
                       reads=["wgb%d" % par, "wub%d" % par, "ln2x"],
                       writes=rbanks, tag="e_gu%d_%d_%d" % (hh, e, f))

                def em_sg(par_):
                    def em(eng):
                        inst = None
                        for tb in range(NTB):
                            inst = eng.activation(sG[:, tb * TBW:(tb + 1) * TBW],
                                                  pb[par_ * 4 + min(tb, 1)][:, 0:TBW], AF.Silu)
                        return inst
                    return em
                p.step("scalar", em_sg(par), reads=rbanks[:2], writes=["sG"],
                       tag="e_sg%d_%d_%d" % (hh, e, f))

                def em_h(par_, f_):
                    def em(eng):
                        inst = None
                        for tb in range(NTB):
                            hslice = Hb[:, f_ * TH + tb * TBW: f_ * TH + (tb + 1) * TBW]
                            inst = eng.tensor_tensor(hslice, pb[par_ * 4 + 2 + min(tb, 1)][:, 0:TBW],
                                                     sG[:, tb * TBW:(tb + 1) * TBW], ALU.mult)
                            inst = eng.tensor_tensor(hslice, hslice,
                                                     wbt[:, tb * TBW:(tb + 1) * TBW], ALU.mult)
                        return inst
                    return em
                p.step("vector", em_h(par, f), reads=rbanks[2:] + ["sG", "wbt"],
                       writes=["Hb"], tag="e_h%d_%d_%d" % (hh, e, f))

            for dmt in range(16):
                par = dmt % 2
                srcd = dr["Wd"][e].rearrange("(ft p) n -> p ft n", p=128)[:, :, dmt * 128:(dmt + 1) * 128]
                for q in range(2):
                    p.step("gpsimd", (lambda s_, d_, q_: lambda eng: eng.dma_start(
                        out=d_[:].rearrange("p (ft c) -> p ft c", c=128)[:, q_ * 8:(q_ + 1) * 8, :],
                        in_=s_[:, q_ * 8:(q_ + 1) * 8, :]))(srcd, wdb[par], q),
                        writes=["wdb%d" % par], dma=True, tag="e_ld%d_%d_%d_%d" % (hh, e, dmt, q))

                def em_dmm(par_, dmt_):
                    def em(eng):
                        inst = None
                        for ft in range(16):
                            lw = wdb[par_][:, ft * 128:(ft + 1) * 128]
                            for tb in range(NTB):
                                inst = eng.matmul(pb[par_ * 4 + tb][:, 0:TBW], lw,
                                                  Hb[:].rearrange("p (ft t) -> p ft t", t=TH)[:, ft, tb * TBW:(tb + 1) * TBW],
                                                  start=(ft == 0), stop=(ft == 15))
                        return inst
                    return em
                rb2 = ["pb%d" % (par * 4 + tb) for tb in range(NTB)]
                p.step("tensor", em_dmm(par, dmt), reads=["wdb%d" % par, "Hb"],
                       writes=rb2, tag="e_dm%d_%d_%d" % (hh, e, dmt))

                def em_acc(par_, dmt_):
                    def em(eng):
                        inst = None
                        for tb in range(NTB):
                            xs = xnew[:, dmt_ * TH + tb * TBW: dmt_ * TH + (tb + 1) * TBW]
                            inst = eng.tensor_tensor(xs, xs, pb[par_ * 4 + tb][:, 0:TBW], ALU.add)
                        return inst
                    return em
                p.step("vector", em_acc(par, dmt), reads=rb2 + ["xnew"],
                       writes=["xnew"], tag="e_ac%d_%d_%d" % (hh, e, dmt))

        for dm in range(16):
            dst = self.out[dm * 128:(dm + 1) * 128, t0h:t0h + TH]
            p.step("sync", (lambda d_, dm_: lambda eng: eng.dma_start(
                out=d_, in_=xnew[:, dm_ * TH:(dm_ + 1) * TH]))(dst, dm),
                reads=["xnew"], dma=True, tag="e_out%d_%d" % (hh, dm))

    def emit(self):
        nc, p = self.nc, self.p
        import contextlib
        stack = contextlib.ExitStack()
        sems, dsems = {}, {}
        for e in ENGINES:
            sems[e] = stack.enter_context(nc.semaphore("sem_" + e))
        for e in ("sync", "gpsimd", "scalar"):
            dsems[e] = stack.enter_context(nc.semaphore("dsem_" + e))
        p.finalize()
        with stack:
            with nc.Block() as block:
                @block.sync
                def _(eng):
                    p.emit_engine("sync", eng, sems, dsems)
                    nd = sum(1 for s in p.steps["sync"] if s.dma)
                    if nd:
                        eng.wait_ge(dsems["sync"], 16 * nd)

                @block.gpsimd
                def _(eng):
                    p.emit_engine("gpsimd", eng, sems, dsems)
                    nd = sum(1 for s in p.steps["gpsimd"] if s.dma)
                    if nd:
                        eng.wait_ge(dsems["gpsimd"], 16 * nd)

                @block.tensor
                def _(eng):
                    p.emit_engine("tensor", eng, sems, dsems)

                @block.vector
                def _(eng):
                    p.emit_engine("vector", eng, sems, dsems)

                @block.scalar
                def _(eng):
                    p.emit_engine("scalar", eng, sems, dsems)
                    nd = sum(1 for s in p.steps["scalar"] if s.dma)
                    if nd:
                        eng.wait_ge(dsems["scalar"], 16 * nd)


def host_inputs(T, inputs, core):
    cons = make_consts(T)
    m = {
        "xT": np.ascontiguousarray(inputs["L0_attn_x"][core, :T, :].T),
        "Wqkv": inputs["L0_attn_Wqkv"],
        "Wz": inputs["L0_attn_Wz"],
        "Wab": np.ascontiguousarray(
            np.concatenate([inputs["L0_attn_Wa"], inputs["L0_attn_Wb"]], axis=1)),
        "convw": inputs["L0_attn_conv_w"],
        "A_log": inputs["L0_attn_A_log"],
        "dt_bias": inputs["L0_attn_dt_bias"],
        "norm_w": inputs["L0_attn_norm_w"],
        "Wout": inputs["L0_attn_Wout"],
        "Wg": inputs["L0_ffn_Wg"],
        "Wu": inputs["L0_ffn_Wu"],
        "Wd": inputs["L0_ffn_Wd"],
        "Wr": inputs["L0_ffn_Wr"],
        "temperature": inputs["L0_ffn_temperature"],
        "cu_seqlens": inputs["L0_attn_cu_seqlens"].astype(np.int32),
    }
    m.update(cons)
    return m


def kernel(**inputs):
    T = 2048
    k = Ker(T)
    nc = k.build()
    in_maps = [host_inputs(T, inputs, c) for c in range(NB)]
    res = run_bass_kernel_spmd(nc, in_maps, core_ids=list(range(NB)))
    outs = [np.ascontiguousarray(r["out"].T) for r in res.results]
    return np.stack(outs, axis=0)


# version 21
# speedup vs baseline: 1.0346x; 1.0201x over previous
import numpy as np
import concourse.bass as bass
import concourse.mybir as mybir
from concourse.bass_utils import run_bass_kernel_spmd

F32 = mybir.dt.float32
BF16 = mybir.dt.bfloat16
I32 = mybir.dt.int32
AF = mybir.ActivationFunctionType
ALU = mybir.AluOpType
AX = mybir.AxisListType

D = 2048
NB = 8
HK, HV, DK, DV = 16, 32, 128, 128
KC = 4
NE, FF = 8, 2048
C = 128
EPS = 1e-6
LNEPS = 1e-5
BIG = 30000.0
BIGM = 1e7

ENGINES = ("tensor", "vector", "scalar", "gpsimd", "sync")
_DTB = {F32: 4, BF16: 2, I32: 4}


class _Step:
    __slots__ = ("engine", "emit", "deps", "dma", "idx", "tag", "need_inc",
                 "inc_idx", "dma_idx")

    def __init__(self, engine, emit, dma, tag):
        self.engine = engine
        self.emit = emit
        self.dma = dma
        self.tag = tag
        self.deps = []
        self.need_inc = False
        self.inc_idx = None
        self.dma_idx = None


class Prog:

    def __init__(self):
        self.steps = {e: [] for e in ENGINES}
        self.last_w = {}
        self.readers = {}
        self.pending_bar = {e: [] for e in ENGINES}

    def step(self, engine, emit, reads=(), writes=(), dma=False, tag="",
             sereads=()):
        st = _Step(engine, emit, dma, tag)
        deps = set()
        forced = set()
        for r in reads:
            w = self.last_w.get(r)
            if w is not None:
                deps.add(w)
        for r in sereads:
            w = self.last_w.get(r)
            if w is not None:
                deps.add(w)
                forced.add(w)
        for w_ in writes:
            pw = self.last_w.get(w_)
            if pw is not None:
                deps.add(pw)
            for rd in self.readers.get(w_, ()):
                deps.add(rd)
        for d in self.pending_bar[engine]:
            deps.add(d)
        self.pending_bar[engine] = []
        st.deps = [d for d in deps
                   if d in forced or not (d.engine == engine and (d.dma == dma))]
        st.idx = len(self.steps[engine])
        self.steps[engine].append(st)
        for r in list(reads) + list(sereads):
            self.readers.setdefault(r, []).append(st)
        for w_ in writes:
            self.last_w[w_] = st
            self.readers[w_] = []
        return st

    def barrier(self):
        sigs = []
        for e in ENGINES:
            comp = next((s for s in reversed(self.steps[e]) if not s.dma), None)
            dmas = next((s for s in reversed(self.steps[e]) if s.dma), None)
            if comp is not None:
                sigs.append(comp)
            if dmas is not None:
                sigs.append(dmas)
        for e in ENGINES:
            self.pending_bar[e] = [s for s in sigs
                                   if not (s.engine == e and not s.dma)]

    def finalize(self):
        for e in ENGINES:
            for st in self.steps[e]:
                for d in st.deps:
                    d.need_inc = True
        for e in ENGINES:
            ninc = 0
            ndma = 0
            for st in self.steps[e]:
                if st.dma:
                    ndma += 1
                    st.dma_idx = ndma
                elif st.need_inc:
                    ninc += 1
                    st.inc_idx = ninc

    def emit_engine(self, engine, eng, sems, dsems):
        waited = {}
        for st in self.steps[engine]:
            for d in st.deps:
                if d.dma:
                    sem, val = dsems[d.engine], 16 * d.dma_idx
                else:
                    sem, val = sems[d.engine], d.inc_idx
                    assert val is not None, (st.tag, d.tag)
                key = id(sem)
                if waited.get(key, -1) < val:
                    eng.wait_ge(sem, val)
                    waited[key] = val
            inst = st.emit(eng)
            if st.dma:
                inst.then_inc(dsems[engine], 16)
            elif st.need_inc:
                inst.then_inc(sems[engine], 1)


def _np_cm384():
    t = np.arange(128)[:, None]
    f = np.arange(128)[None, :]
    s0 = np.where(f >= t, BIGM, 0.0)
    s1 = np.where(f <= t, -BIGM, 0.0)
    s2 = np.where(f < t, -BIGM, 0.0)
    return np.concatenate([s0, s1, s2], axis=1).astype(np.float32)


def make_consts(T):
    j = np.arange(128)[:, None]
    t = np.arange(128)[None, :]
    return {
        "c_ident": np.eye(128, dtype=np.float32),
        "c_cum": (j <= t).astype(np.float32),
        "c_rev": (j > t).astype(np.float32),
        "c_cm384": _np_cm384(),
        "c_onesrow": np.ones((1, 128), np.float32),
        "c_onescol": np.ones((128, 1), np.float32),
        "c_e127": (np.arange(128) == 127).astype(np.float32)[:, None],
        "c_iota": np.arange(T + 8, dtype=np.float32)[None, :],
        "c_iota8": np.tile(np.arange(8, dtype=np.float32), (128, 1)).copy(),
    }


class Ker:
    def __init__(self, T=2048, taps=(), phases="abcde"):
        self.phases = phases
        self.T = T
        self.nchunk = T // C
        self.TH = T // 2
        nc = bass.Bass("TRN2", target_bir_lowering=False, debug=False,
                       detect_race_conditions=False)
        self.nc = nc
        self.p = Prog()
        self.consts_np = make_consts(T)
        self.dr = {}
        self.sb = {}
        self.taps = dict()
        T_ = T

        def din(name, shape, dt=F32):
            self.dr[name] = nc.dram_tensor(name, list(shape), dt,
                                           kind="ExternalInput").ap()

        din("xT", [D, T_])
        din("Wqkv", [D, 4 * D])
        din("Wz", [D, 2 * D])
        din("Wab", [D, 64])
        din("convw", [4 * D, KC])
        din("A_log", [HV])
        din("dt_bias", [HV])
        din("norm_w", [DV])
        din("Wout", [2 * D, D])
        din("Wg", [NE, D, FF])
        din("Wu", [NE, D, FF])
        din("Wd", [NE, FF, D])
        din("Wr", [D, NE])
        din("temperature", [1])
        din("cu_seqlens", [5], I32)
        for k, v in self.consts_np.items():
            din(k, v.shape, F32)

        self.out = nc.dram_tensor("out", [D, T_], F32, kind="ExternalOutput").ap()
        self.qkv_s = nc.dram_tensor("qkv_s", [4 * D, T_], BF16, kind="Internal").ap()
        self.z_s = nc.dram_tensor("z_s", [T_, 2 * D], BF16, kind="Internal").ap()
        self.y_s = nc.dram_tensor("y_s", [2 * D, T_], BF16, kind="Internal").ap()
        self.ln2_s = nc.dram_tensor("ln2_s", [D, T_], BF16, kind="Internal").ap()
        for name, shape, dt in taps:
            self.taps[name] = nc.dram_tensor("tap_" + name, list(shape), dt,
                                             kind="ExternalOutput").ap()

        self.pb = [nc.alloc_psum_tensor("pb%d" % i, [128, 512], F32)
                   for i in range(8)]
        self._pcur = 16512
        self._top = 229344

    def persist(self, name, shape, dt):
        nbytes = int(np.prod(shape[1:])) * _DTB[dt]
        off = (self._pcur + 31) // 32 * 32
        t = self.nc.alloc_sbuf_tensor_at(name, shape, dt, offset=off)
        self._pcur = off + nbytes
        assert self._pcur <= self._top, (name, self._pcur)
        self.sb[name] = t
        return t

    def freeze_persist(self):
        self._abase = (self._pcur + 255) // 256 * 256

    def at(self, name, shape, dt, off):
        nbytes = int(np.prod(shape[1:])) * _DTB[dt]
        a = self._abase + off
        assert a + nbytes <= self._top, (name, a + nbytes, self._top)
        t = self.nc.alloc_sbuf_tensor_at(name, shape, dt, offset=a)
        self.sb[name] = t
        return t

    def build(self):
        self.alloc_persist()
        self.ph0()
        self.p.barrier()
        self.ph_a()
        self.p.barrier()
        self.ph_b()
        self.p.barrier()
        if "c" not in self.phases:
            self.emit()
            return self.nc
        self.ph_c()
        self.p.barrier()
        if "d" not in self.phases:
            self.emit()
            return self.nc
        for hh in range(2):
            self.ph_d(hh)
            self.p.barrier()
            self.ph_e(hh)
            self.p.barrier()
        for name, src in [("qkv_s", self.qkv_s), ("z_s", self.z_s),
                          ("y_s", self.y_s), ("ln2_s", self.ln2_s)]:
            if name in self.taps:
                self.p.step("sync", (lambda d_, s_: lambda eng: eng.dma_start(out=d_, in_=s_))(
                    self.taps[name], src), dma=True, tag="tap_" + name)
        self.emit()
        return self.nc

    def alloc_persist(self):
        P = self.persist
        T = self.T
        P("ident", [128, 128], F32)
        P("identb", [128, 128], BF16)
        P("cumt", [128, 128], F32)
        P("revt", [128, 128], F32)
        P("cm384", [128, 384], F32)
        P("onesrow", [1, 128], F32)
        P("onescol", [128, 1], F32)
        P("onescolb", [128, 1], BF16)
        P("onesrowb", [1, 128], BF16)
        P("e127", [128, 1], F32)
        P("iota8", [128, 8], F32)
        P("sr", [1, T + 8], BF16)
        P("msk", [128, 3 * (T + 8)], BF16)
        P("normwb", [128, DV], F32)
        P("abg", [128, T], F32)
        P("cw", [128, 64 * KC], F32)
        P("negexpa", [32, 1], F32)
        P("dtbias", [32, 1], F32)
        P("tmisc", [1, 16], F32)
        P("cvals", [128, 8], F32)
        P("iota8b", [128, 8], F32)
        P("neg18", [128, 8], F32)
        self.freeze_persist()

    def ph0(self):
        nc, p, sb, dr = self.nc, self.p, self.sb, self.dr
        T = self.T
        W = T + 8
        t_iota = self.at("t_iota", [1, W], F32, 110 * 1024)
        t_cu = self.at("t_cu", [1, 8], F32, 119 * 1024)
        t_cui = self.at("t_cui", [1, 8], I32, int(119.5 * 1024))
        t_row = self.at("t_row", [1, W], F32, 120 * 1024)
        t_sr32 = self.at("t_sr32", [1, W], F32, 129 * 1024)
        t_ns = self.at("t_ns", [1, W], F32, 138 * 1024)
        t_nw = self.at("t_nw", [1, DV], F32, 147 * 1024)
        t_al = self.at("t_al", [64, 1], F32, int(147.5 * 1024))
        t_mrow = self.at("t_mrow", [1, 3 * W], F32, 148 * 1024)

        for nm, dst in [("c_ident", sb["ident"]), ("c_cum", sb["cumt"]),
                        ("c_rev", sb["revt"]), ("c_cm384", sb["cm384"]),
                        ("c_onesrow", sb["onesrow"]), ("c_onescol", sb["onescol"]),
                        ("c_e127", sb["e127"]), ("c_iota8", sb["iota8"]),
                        ("c_iota", t_iota)]:
            p.step("sync",
                   (lambda dst_, src_: lambda eng: eng.dma_start(out=dst_[:], in_=src_))(dst, dr[nm]),
                   writes=[dst.name], dma=True, tag="ld_" + nm)
        for nm, dst in [("c_ident", sb["identb"]), ("c_onescol", sb["onescolb"]),
                        ("c_onesrow", sb["onesrowb"])]:
            p.step("gpsimd",
                   (lambda dst_, src_: lambda eng: eng.dma_start(out=dst_[:], in_=src_))(dst, dr[nm]),
                   writes=[dst.name], dma=True, tag="ldb_" + nm)
        p.step("sync", lambda eng: eng.dma_start(
            out=sb["cw"][:].rearrange("p (a k) -> p a k", k=KC),
            in_=dr["convw"].rearrange("(a p) k -> p a k", p=128)),
            writes=["cw"], dma=True, tag="ld_cw")
        p.step("sync", lambda eng: eng.dma_start(out=t_al[0:32, :], in_=dr["A_log"][:, None]),
               writes=["t_al"], dma=True, tag="ld_alog")
        p.step("sync", lambda eng: eng.dma_start(out=t_al[32:64, :], in_=dr["dt_bias"][:, None]),
               writes=["t_al"], dma=True, tag="ld_dtb")
        p.step("sync", lambda eng: eng.dma_start(out=t_nw[:], in_=dr["norm_w"][None, :]),
               writes=["t_nw"], dma=True, tag="ld_nw")
        p.step("sync", lambda eng: eng.dma_start(out=sb["tmisc"][0:1, 0:1], in_=dr["temperature"][None, :]),
               writes=["tmisc"], dma=True, tag="ld_temp")
        p.step("sync", lambda eng: eng.dma_start(out=t_cui[0:1, 0:5], in_=dr["cu_seqlens"][None, :]),
               writes=["t_cui"], dma=True, tag="ld_cu")

        p.step("scalar", lambda eng: eng.activation(sb["negexpa"][:], t_al[0:32, :], AF.Exp),
               reads=["t_al"], writes=["negexpa"], tag="acoef_exp")
        p.step("vector", lambda eng: eng.tensor_scalar(sb["negexpa"][:], sb["negexpa"][:], -1.0, None, ALU.mult),
               reads=["negexpa"], writes=["negexpa"], tag="acoef_neg")
        p.step("scalar", lambda eng: eng.copy(sb["dtbias"][:], t_al[32:64, :]),
               reads=["t_al"], writes=["dtbias"], tag="acoef_dtb")
        p.step("vector", lambda eng: eng.tensor_copy(t_cu[0:1, 0:5], t_cui[0:1, 0:5]),
               reads=["t_cui"], writes=["t_cu"], tag="cu_f32")

        sr = sb["sr"]

        def em_starts(eng):
            eng.memset(t_sr32[:], 0.0)
            for i in range(4):
                eng.tensor_scalar(t_row[:], t_iota[:], t_cu[0:1, i:i + 1], None, ALU.is_equal)
                eng.tensor_tensor(t_sr32[:], t_sr32[:], t_row[:], ALU.max)
                eng.tensor_scalar(t_row[:], t_iota[:], float(T), t_cu[0:1, i:i + 1], ALU.add, ALU.is_equal)
                eng.tensor_tensor(t_sr32[:], t_sr32[:], t_row[:], ALU.max)
            eng.memset(t_sr32[0:1, T:W], 0.0)
            return eng.tensor_copy(sr[:], t_sr32[:])
        p.step("vector", em_starts, reads=["t_iota"], sereads=["t_cu"],
               writes=["t_sr32", "sr", "t_row"], tag="starts")

        def em_masks_rows(eng):
            eng.tensor_scalar(t_ns[:], t_sr32[:], -1.0, 1.0, ALU.mult, ALU.add)
            eng.memset(t_mrow[:], 1.0)
            eng.tensor_copy(t_mrow[0:1, 3:3 + T], t_ns[0:1, 1:T + 1])
            eng.tensor_tensor(t_mrow[0:1, W + 3:W + 3 + T], t_mrow[0:1, 3:3 + T], t_ns[0:1, 2:T + 2], ALU.mult)
            return eng.tensor_tensor(t_mrow[0:1, 2 * W + 3:2 * W + 3 + T], t_mrow[0:1, W + 3:W + 3 + T], t_ns[0:1, 3:T + 3], ALU.mult)
        p.step("vector", em_masks_rows, reads=["t_sr32"],
               writes=["t_ns", "t_mrow"], tag="maskrows")

        pb, msk = self.pb, sb["msk"]
        nburst = (3 * W + 511) // 512
        for ib in range(nburst):
            c0 = ib * 512
            c1 = min(c0 + 512, 3 * W)
            bk = ib % 4
            p.step("tensor",
                   (lambda c0_, c1_, bk_: lambda eng: eng.matmul(
                       pb[bk_][:, 0:c1_ - c0_], sb["onesrow"][:],
                       t_mrow[0:1, c0_:c1_], start=True, stop=True))(c0, c1, bk),
                   reads=["t_mrow", "onesrow"], writes=["pb%d" % bk],
                   tag="mskb_mm%d" % ib)
            p.step("vector",
                   (lambda c0_, c1_, bk_: lambda eng: eng.tensor_copy(
                       msk[:, c0_:c1_], pb[bk_][:, 0:c1_ - c0_]))(c0, c1, bk),
                   reads=["pb%d" % bk], writes=["msk"], tag="mskb_ev%d" % ib)

        p.step("tensor", lambda eng: eng.matmul(pb[4][:, 0:DV], sb["onesrow"][:],
                                                t_nw[0:1, :], start=True, stop=True),
               reads=["t_nw", "onesrow"], writes=["pb4"], tag="nw_mm")
        p.step("vector", lambda eng: eng.tensor_copy(sb["normwb"][:], pb[4][:, 0:DV]),
               reads=["pb4"], writes=["normwb"], tag="nw_ev")
        p.step("vector", lambda eng: eng.reciprocal(sb["tmisc"][0:1, 1:2], sb["tmisc"][0:1, 0:1]),
               reads=["tmisc"], writes=["tmisc"], tag="invtemp")

        def em_cvals(eng):
            cv = sb["cvals"]
            eng.memset(cv[:, 0:1], LNEPS)
            eng.memset(cv[:, 1:2], 1.0)
            eng.memset(cv[:, 2:3], EPS)
            return eng.memset(cv[:, 3:4], 0.0)
        p.step("vector", em_cvals, writes=["cvals"], tag="cvals")

        def em_misc8(eng):
            eng.tensor_scalar(sb["iota8b"][:], sb["iota8"][:], 100.0, None, ALU.add)
            eng.memset(sb["neg18"][:], -1.0)
            return eng.memset(sb["cvals"][:, 5:6], float(-1.0 / np.e))
        p.step("vector", em_misc8, reads=["iota8"], writes=["iota8b", "neg18", "cvals"],
               tag="misc8")
        p.step("tensor", lambda eng: eng.matmul(pb[5][:, 0:1], sb["onesrow"][:],
                                                sb["tmisc"][0:1, 1:2], start=True, stop=True),
               reads=["tmisc", "onesrow"], writes=["pb5"], tag="it_mm")
        p.step("vector", lambda eng: eng.tensor_copy(sb["cvals"][:, 4:5], pb[5][:, 0:1]),
               reads=["pb5"], writes=["cvals"], tag="it_ev")

    def ph_a(self):
        nc, p, sb, dr, pb = self.nc, self.p, self.sb, self.dr, self.pb
        T = self.T
        TC = min(512, T)
        ln1x = self.at("ln1x", [128, 16 * T], BF16, 0)
        xa = self.at("xa", [128, 16 * TC], F32, 64 * 1024)
        xbf = self.at("xbf", [128, 16 * TC], BF16, 96 * 1024)
        xsq = self.at("xsq", [128, 16 * TC], BF16, 112 * 1024)
        arow = self.at("arow", [1, 4 * TC], F32, 128 * 1024)
        abc = self.at("abc", [128, 2 * TC], F32, 137 * 1024)
        xT = dr["xT"]
        onescolb, onesrow = sb["onescolb"], sb["onesrow"]

        for tc in range(T // TC):
            t0 = tc * TC
            for half in range(2):
                s0 = half * 8
                src = xT.rearrange("(dt p) t -> p dt t", p=128)[:, s0:s0 + 8, t0:t0 + TC]
                dst = xa[:].rearrange("p (dt t) -> p dt t", t=TC)[:, s0:s0 + 8, :]
                p.step("sync",
                       (lambda s_, d_: lambda eng: eng.dma_start(out=d_, in_=s_))(src, dst),
                       writes=["xa"], dma=True, tag="a_ldx%d_%d" % (tc, half))
            p.step("vector", lambda eng: eng.tensor_copy(xbf[:], xa[:]),
                   reads=["xa"], writes=["xbf"], tag="a_cast%d" % tc)
            p.step("scalar", lambda eng: eng.activation(xsq[:], xbf[:], AF.Square),
                   reads=["xbf"], writes=["xsq"], tag="a_sq%d" % tc)

            def em_stats(src_t, bank):
                def em(eng):
                    inst = None
                    v = src_t[:].rearrange("p (dt t) -> p dt t", t=TC)
                    for d in range(16):
                        inst = eng.matmul(bank[0:1, 0:TC], onescolb[:],
                                          v[:, d, :], start=(d == 0), stop=(d == 15))
                    return inst
                return em
            p.step("tensor", em_stats(xbf, pb[0]), reads=["xbf", "onescolb"],
                   writes=["pb0"], tag="a_mm1_%d" % tc)
            p.step("tensor", em_stats(xsq, pb[1]), reads=["xsq", "onescolb"],
                   writes=["pb1"], tag="a_mm2_%d" % tc)

            def em_rows(eng):
                eng.tensor_scalar(arow[0:1, 0:TC], pb[0][0:1, 0:TC], 1.0 / D, None, ALU.mult)
                eng.tensor_scalar(arow[0:1, 3*TC:4*TC], pb[1][0:1, 0:TC], 1.0 / D, None, ALU.mult)
                eng.tensor_tensor(arow[0:1, TC:2*TC], arow[0:1, 0:TC], arow[0:1, 0:TC], ALU.mult)
                return eng.tensor_tensor(arow[0:1, 3*TC:4*TC], arow[0:1, 3*TC:4*TC],
                                         arow[0:1, TC:2*TC], ALU.subtract)
            p.step("vector", em_rows, reads=["pb0", "pb1"], writes=["arow"],
                   tag="a_rows%d" % tc)

            def em_rstd(eng):
                eng.activation(arow[0:1, 2*TC:3*TC], arow[0:1, 3*TC:4*TC], AF.Ln, bias=sb["cvals"][0:1, 0:1])
                return eng.activation(arow[0:1, TC:2*TC], arow[0:1, 2*TC:3*TC], AF.Exp, scale=-0.5)
            p.step("scalar", em_rstd, reads=["arow", "cvals"], writes=["arow"], tag="a_rstd%d" % tc)
            p.step("vector", lambda eng: eng.tensor_tensor(
                arow[0:1, 2*TC:3*TC], arow[0:1, 0:TC], arow[0:1, TC:2*TC], ALU.mult),
                reads=["arow"], writes=["arow"], tag="a_ms%d" % tc)
            p.step("tensor", lambda eng: eng.matmul(pb[2][:, 0:TC], onesrow[:],
                                                    arow[0:1, TC:2*TC], start=True, stop=True),
                   reads=["arow", "onesrow"], writes=["pb2"], tag="a_bs%d" % tc)
            p.step("tensor", lambda eng: eng.matmul(pb[3][:, 0:TC], onesrow[:],
                                                    arow[0:1, 2*TC:3*TC], start=True, stop=True),
                   reads=["arow", "onesrow"], writes=["pb3"], tag="a_bms%d" % tc)
            p.step("vector", lambda eng: eng.tensor_copy(abc[:, 0:TC], pb[2][:, 0:TC]),
                   reads=["pb2"], writes=["abc"], tag="a_evs%d" % tc)
            p.step("vector", lambda eng: eng.tensor_copy(abc[:, TC:2*TC], pb[3][:, 0:TC]),
                   reads=["pb3"], writes=["abc"], tag="a_evms%d" % tc)

            def em_apply(d, t0):
                def em(eng):
                    xs = xa[:, d * TC:(d + 1) * TC]
                    eng.tensor_tensor(xs, xs, abc[:, 0:TC], ALU.mult)
                    return eng.tensor_tensor(ln1x[:, d * T + t0: d * T + t0 + TC],
                                             xs, abc[:, TC:2*TC], ALU.subtract)
                return em
            for d in range(16):
                p.step("vector", em_apply(d, t0), reads=["abc", "xa"],
                       writes=["ln1x", "xa"], tag="a_app%d_%d" % (tc, d))

        if "ln1x" in self.taps:
            ap = self.taps["ln1x"]
            p.step("sync", lambda eng: eng.dma_start(out=ap, in_=ln1x[:]),
                   reads=["ln1x"], dma=True, tag="tap_ln1x")

    def ph_b(self):
        nc, p, sb, dr, pb = self.nc, self.p, self.sb, self.dr, self.pb
        T = self.T
        TB = min(512, T)
        ntb = T // TB
        ln1x = self.sb["ln1x"]
        wqb = [self.at("wqb%d" % i, [128, 16 * 128], BF16, 64 * 1024 + i * 4096)
               for i in range(2)]
        qn = [self.at("qn%d" % i, [128, T], BF16, 72 * 1024 + i * 2 * T)
              for i in range(2)]
        wzb = self.at("wzb", [128, 16 * 2048], BF16, 80 * 1024)
        zev = [self.at("zev%d" % i, [128, 2 * D], BF16, 144 * 1024 + i * 4 * D)
               for i in range(2)]

        for n in range(4 * D // 128):
            par = n % 2
            wb = wqb[par]
            src = dr["Wqkv"].rearrange("(dt p) n -> p dt n", p=128)
            for half in range(2):
                p.step("gpsimd",
                       (lambda n_, h_, wb_: lambda eng: eng.dma_start(
                           out=wb_[:].rearrange("p (dt c) -> p dt c", c=128)[:, h_ * 8:(h_ + 1) * 8, :],
                           in_=src[:, h_ * 8:(h_ + 1) * 8, n_ * 128:(n_ + 1) * 128]))(n, half, wb),
                       writes=["wqb%d" % par], dma=True, tag="b_wq%d_%d" % (n, half))

            def em_mm(wb_, par_):
                def em(eng):
                    inst = None
                    for d in range(16):
                        lhsT = wb_[:, d * 128:(d + 1) * 128]
                        for tb in range(ntb):
                            inst = eng.matmul(pb[par_ * 4 + tb][:, 0:TB], lhsT,
                                              ln1x[:].rearrange("p (dt t) -> p dt t", t=T)[:, d, tb * TB:(tb + 1) * TB],
                                              start=(d == 0), stop=(d == 15))
                    return inst
                return em
            p.step("tensor", em_mm(wb, par),
                   reads=["wqb%d" % par, "ln1x"],
                   writes=["pb%d" % (par * 4 + tb) for tb in range(ntb)],
                   tag="b_qmm%d" % n)

            def em_ev(par_, n_):
                def em(eng):
                    inst = None
                    for tb in range(ntb):
                        inst = eng.copy(qn[par_][:, tb * TB:(tb + 1) * TB],
                                        pb[par_ * 4 + tb][:, 0:TB])
                    return inst
                return em
            p.step("scalar", em_ev(par, n),
                   reads=["pb%d" % (par * 4 + tb) for tb in range(ntb)],
                   writes=["qn%d" % par], tag="b_qev%d" % n)
            p.step("sync",
                   (lambda n_, par_: lambda eng: eng.dma_start(
                       out=self.qkv_s[n_ * 128:(n_ + 1) * 128, :], in_=qn[par_][:]))(n, par),
                   reads=["qn%d" % par], dma=True, tag="b_qst%d" % n)

        NH = 2 * D // 2
        for nh in range(2):
            srcz = dr["Wz"].rearrange("(dt p) n -> p dt n", p=128)
            for q in range(4):
                p.step("gpsimd",
                       (lambda nh_, q_: lambda eng: eng.dma_start(
                           out=wzb[:].rearrange("p (dt c) -> p dt c", c=NH)[:, q_ * 4:(q_ + 1) * 4, :],
                           in_=srcz[:, q_ * 4:(q_ + 1) * 4, nh_ * NH:(nh_ + 1) * NH]))(nh, q),
                       writes=["wzb"], dma=True, tag="b_wz%d_%d" % (nh, q))
            for tt in range(T // 128):
                par = tt % 2

                def em_zmm(tt_, par_):
                    def em(eng):
                        inst = None
                        for d in range(16):
                            lhsT = ln1x[:, d * T + tt_ * 128: d * T + (tt_ + 1) * 128]
                            for nb in range(NH // 512):
                                inst = eng.matmul(pb[par_ * 4 + nb][:, 0:512], lhsT,
                                                  wzb[:, d * NH + nb * 512: d * NH + (nb + 1) * 512],
                                                  start=(d == 0), stop=(d == 15))
                        return inst
                    return em
                p.step("tensor", em_zmm(tt, par),
                       reads=["wzb", "ln1x"],
                       writes=["pb%d" % (par * 4 + nb) for nb in range(NH // 512)],
                       tag="b_zmm%d_%d" % (nh, tt))

                def em_zev(par_):
                    def em(eng):
                        inst = None
                        for nb in range(NH // 512):
                            inst = eng.copy(zev[par_][:, nb * 512:(nb + 1) * 512],
                                            pb[par_ * 4 + nb][:, 0:512])
                        return inst
                    return em
                p.step("scalar", em_zev(par),
                       reads=["pb%d" % (par * 4 + nb) for nb in range(NH // 512)],
                       writes=["zev%d" % par], tag="b_zev%d_%d" % (nh, tt))
                p.step("sync",
                       (lambda nh_, tt_, par_: lambda eng: eng.dma_start(
                           out=self.z_s[tt_ * 128:(tt_ + 1) * 128, nh_ * NH:(nh_ + 1) * NH],
                           in_=zev[par_][:, 0:NH]))(nh, tt, par),
                       reads=["zev%d" % par], dma=True, tag="b_zst%d_%d" % (nh, tt))

        p.barrier()
        wabb = self.at("wabb", [128, 16 * 64], BF16, 80 * 1024)
        sc1 = self.at("sc1", [32, T], F32, 82 * 1024)
        sc2 = self.at("sc2", [32, T], F32, 82 * 1024 + 4 * T)
        sc3 = self.at("sc3", [32, T], F32, 82 * 1024 + 8 * T)
        abt = self.at("abt", [64, T], F32, 82 * 1024 + 12 * T)
        p.step("gpsimd", lambda eng: eng.dma_start(
            out=wabb[:].rearrange("p (dt c) -> p dt c", c=64),
            in_=dr["Wab"].rearrange("(dt p) n -> p dt n", p=128)),
            writes=["wabb"], dma=True, tag="b_wab")

        def em_abmm(eng):
            inst = None
            for d in range(16):
                lhsT = wabb[:, d * 64:(d + 1) * 64]
                for tb in range(ntb):
                    inst = eng.matmul(pb[4 + tb][0:64, 0:TB], lhsT,
                                      ln1x[:].rearrange("p (dt t) -> p dt t", t=T)[:, d, tb * TB:(tb + 1) * TB],
                                      start=(d == 0), stop=(d == 15))
            return inst
        p.step("tensor", em_abmm, reads=["wabb", "ln1x"],
               writes=["pb%d" % (4 + tb) for tb in range(ntb)], tag="b_abmm")

        def em_abev(eng):
            inst = None
            for tb in range(ntb):
                inst = eng.copy(abt[:, tb * TB:(tb + 1) * TB], pb[4 + tb][0:64, 0:TB])
            return inst
        p.step("scalar", em_abev, reads=["pb%d" % (4 + tb) for tb in range(ntb)],
               writes=["abt"], tag="b_abev")

        a_, b_ = abt[0:32, :], abt[32:64, :]
        abg = sb["abg"]
        cv = sb["cvals"]

        def em_g1(eng):
            return eng.tensor_scalar(sc1[:], a_, sb["dtbias"][:], None, ALU.add)
        p.step("vector", em_g1, reads=["abt", "dtbias"], writes=["sc1"], tag="b_g1")

        def em_g2(eng):
            eng.activation(sc2[:], sc1[:], AF.Abs)
            eng.activation(sc3[:], sc2[:], AF.Exp, scale=-1.0)
            return eng.activation(sc2[:], sc3[:], AF.Ln, bias=cv[0:32, 1:2])
        p.step("scalar", em_g2, reads=["sc1", "cvals"], writes=["sc2", "sc3"], tag="b_g2")

        def em_g3(eng):
            eng.tensor_scalar(sc3[:], sc1[:], 0.0, None, ALU.max)
            eng.tensor_tensor(sc3[:], sc3[:], sc2[:], ALU.add)
            return eng.tensor_scalar(abg[0:32, :], sc3[:], sb["negexpa"][:], None, ALU.mult)
        p.step("vector", em_g3, reads=["sc1", "sc2"], sereads=["negexpa"],
               writes=["sc3", "abg"], tag="b_g3")

        def em_l1(eng):
            return eng.tensor_scalar(sc1[:], b_, -1.0, None, ALU.mult)
        p.step("vector", em_l1, reads=["abt"], writes=["sc1"], tag="b_l1")

        def em_l2(eng):
            eng.activation(sc2[:], sc1[:], AF.Abs)
            eng.activation(sc3[:], sc2[:], AF.Exp, scale=-1.0)
            return eng.activation(sc2[:], sc3[:], AF.Ln, bias=cv[0:32, 1:2])
        p.step("scalar", em_l2, reads=["sc1", "cvals"], writes=["sc2", "sc3"], tag="b_l2")

        def em_l3(eng):
            eng.tensor_scalar(sc3[:], sc1[:], 0.0, None, ALU.max)
            eng.tensor_tensor(sc3[:], sc3[:], sc2[:], ALU.add)
            return eng.tensor_scalar(abg[32:64, :], sc3[:], -1.0, None, ALU.mult)
        p.step("vector", em_l3, reads=["sc1", "sc2"], writes=["sc3", "abg"], tag="b_l3")

        if "abg" in self.taps:
            ap = self.taps["abg"]
            p.step("sync", lambda eng: eng.dma_start(out=ap, in_=abg[0:64, :]),
                   reads=["abg"], dma=True, tag="tap_abg")

    def ph_c(self):
        nc, p, sb = self.nc, self.p, self.sb
        T = self.T
        nchunk = self.nchunk
        at = self.at
        bufs = {}
        bufs["qkvc"] = at("qkvc", [128, 64 * 131], BF16, 0)
        bufs["mixed"] = at("mixed", [128, 64 * 128], BF16, 17 * 1024)
        bufs["zsil"] = [at("zsil%d" % i, [128, 2 * D], BF16, 34 * 1024 + i * 4 * D)
                        for i in range(2)]
        bufs["Sst"] = at("Sst", [128, 32 * 128], BF16, 51 * 1024)
        bufs["flat"] = at("flat", [1, 96 * 128], F32, 60 * 1024)
        bufs["y1"] = at("y1", [128, 32 * 128], BF16, 109 * 1024)
        bufs["yA"] = at("yA", [128, 32 * 128], BF16, 118 * 1024)
        bufs["solT"] = [at("solT%d" % i, [128, 14 * 128], BF16, 127 * 1024 + i * 3584)
                       for i in range(4)]
        bufs["Usb"] = [at("Usb%d" % i, [128, 2 * 128], BF16, 141 * 1024 + i * 512)
                       for i in range(4)]
        bufs["Rt"] = [at("Rt%d" % i, [128, 128], BF16, 143 * 1024 + i * 256)
                       for i in range(4)]
        bufs["VBs"] = [at("VBs%d" % i, [128, 128], BF16, 144 * 1024 + i * 256)
                       for i in range(4)]
        bufs["KBt"] = [at("KBt%d" % i, [128, 128], BF16, 145 * 1024 + i * 256)
                       for i in range(4)]
        bufs["KSe"] = [at("KSe%d" % i, [128, 128], BF16, 146 * 1024 + i * 256)
                       for i in range(4)]
        bufs["QSe"] = [at("QSe%d" % i, [128, 128], F32, 147 * 1024 + i * 512)
                       for i in range(4)]
        bufs["Ot"] = [at("Ot%d" % i, [128, 128], F32, 149 * 1024 + i * 512)
                       for i in range(4)]
        bufs["Esb"] = [at("Esb%d" % i, [128, 3 * 128], F32, 151 * 1024 + i * 1536)
                       for i in range(4)]
        bufs["r3m"] = [at("r3m%d" % i, [128, 3 * 128], F32, 157 * 1024 + i * 1536)
                       for i in range(4)]
        bufs["sqb"] = [at("sqb%d" % i, [128, 128], BF16, 163 * 1024 + i * 256)
                       for i in range(4)]
        bufs["cvt"] = [at("cvt%d" % i, [128, 128], BF16, 164 * 1024 + 256 + i * 256) for i in range(3)]
        cols = []
        for i in range(2):
            cur = [165 * 1024 + i * 4352]

            def nx(nbytes):
                a = (cur[0] + 31) // 32 * 32
                cur[0] = a + nbytes
                return a
            cols.append({
                "gBe": at("gBe%d" % i, [128, 66], F32, nx(264)),
                "cumE": at("cumE%d" % i, [128, 66], F32, nx(264)),
                "revE": at("revE%d" % i, [128, 66], F32, nx(264)),
                "gp": at("gp%d" % i, [128, 32], F32, nx(128)),
                "dp": at("dp%d" % i, [128, 32], F32, nx(128)),
                "cs": at("cs%d" % i, [128, 96], F32, nx(384)),
                "nrA": at("nrA%d" % i, [128, 32], F32, nx(128)),
                "lnr": at("lnr%d" % i, [128, 32], F32, nx(128)),
                "earg": at("earg%d" % i, [128, 32], F32, nx(128)),
                "bcol": at("bcol%d" % i, [128, 32], F32, nx(128)),
                "bGam": at("bGam%d" % i, [128, 32], F32, nx(128)),
                "qGam": at("qGam%d" % i, [128, 32], F32, nx(128)),
                "Ktl": at("Ktl%d" % i, [128, 32], F32, nx(128)),
                "coefC": at("coefC%d" % i, [128, 32], F32, nx(128)),
                "msC": at("msC%d" % i, [128, 32], F32, nx(128)),
                "rstd": at("rstd%d" % i, [128, 32], F32, nx(128)),
                "gsA": at("gsA%d" % i, [66, 128], F32, nx(512)),
                "rowsA": at("rowsA%d" % i, [96, 128], F32, nx(512)),
                "crow": at("crow%d" % i, [1, 32], F32, nx(128)),
                "c1": at("c1%d" % i, [128, 2], F32, nx(32)),
            })
        bufs["cols"] = cols

        p.step("vector", lambda eng: eng.memset(bufs["Sst"][:], 0.0),
               writes=["Sst"], tag="c_zs")
        for i in range(2):
            p.step("vector", (lambda i_: lambda eng: eng.memset(cols[i_]["gsA"][:], 0.0))(i),
                   writes=["gsA%d" % i], tag="c_zgsa%d" % i)
        p.step("vector", lambda eng: eng.memset(
            bufs["qkvc"][:].rearrange("p (ti c) -> p ti c", c=131)[:, :, 0:3], 0.0),
            writes=["qkvc"], tag="c_zq")

        nh = min(HV, getattr(self, "maxheads", HV))
        prep = (self._chunk_loads(0, 0, bufs) + self._chunk_conv(0, 0, bufs)
                + self._chunk_machA(0, 0, bufs) + self._chunk_machB(0, 0, bufs))
        for st_ in prep:
            self.p.step(st_[0], st_[1], **st_[2])
        for c in range(nchunk):
            cp = c % 2
            if c + 1 < nchunk:
                nxt_early = (self._chunk_loads(c + 1, (c + 1) % 2, bufs)
                             + self._chunk_machA(c + 1, (c + 1) % 2, bufs))
                nxt_late = (self._chunk_conv(c + 1, (c + 1) % 2, bufs)
                            + self._chunk_machB(c + 1, (c + 1) % 2, bufs))
            else:
                nxt_early, nxt_late = [], []
            pending = [self._head_steps(c, cp, h, bufs) for h in range(nh)]
            cursors = {}
            active = []
            nexth = 0
            emitted = 0
            total = sum(len(g) for g in pending)
            nxt_i = 0
            nxt_start = 2 * total // 5
            while active or nexth < nh:
                if len(active) < 4 and nexth < nh:
                    active.append(nexth)
                    cursors[nexth] = 0
                    nexth += 1
                if not active:
                    break
                for hidx in list(active):
                    g = pending[hidx]
                    if cursors[hidx] < len(g):
                        eng_, emit_, kw_ = g[cursors[hidx]]
                        self.p.step(eng_, emit_, **kw_)
                        cursors[hidx] += 1
                        emitted += 1
                        if emitted > nxt_start and nxt_i < len(nxt_early):
                            st_ = nxt_early[nxt_i]
                            self.p.step(st_[0], st_[1], **st_[2])
                            nxt_i += 1
                    if cursors[hidx] >= len(g):
                        active.remove(hidx)
            while nxt_i < len(nxt_early):
                st_ = nxt_early[nxt_i]
                self.p.step(st_[0], st_[1], **st_[2])
                nxt_i += 1
            for st_ in nxt_late:
                self.p.step(st_[0], st_[1], **st_[2])
            self._chunk_tail(c, cp, bufs)

    def _chunk_loads(self, c, cp, bufs):
        steps = []

        class _P:
            @staticmethod
            def step(eng_, emit_, **kw):
                steps.append((eng_, emit_, kw))
        p = _P
        t0 = c * C
        qkvc, zsil = bufs["qkvc"], bufs["zsil"]
        lo = max(t0 - 3, 0)
        off = 3 if c == 0 else 0
        ncols = t0 + C - lo
        for q in range(4):
            src = self.qkv_s.rearrange("(ti p) t -> p ti t", p=128)[:, q * 16:(q + 1) * 16, lo:lo + ncols]
            dst = qkvc[:].rearrange("p (ti c) -> p ti c", c=131)[:, q * 16:(q + 1) * 16, off:off + ncols]
            p.step("sync", (lambda s_, d_: lambda eng: eng.dma_start(out=d_, in_=s_))(src, dst),
                   writes=["qkvc"], dma=True, tag="c_lq%d_%d" % (c, q))
        p.step("sync", (lambda t0_, cp_: lambda eng: eng.dma_start(
            out=zsil[cp_][:], in_=self.z_s[t0_:t0_ + C, :]))(t0, cp),
            writes=["zsil%d" % cp], dma=True, tag="c_lz%d" % c)
        p.step("scalar", (lambda cp_: lambda eng: eng.activation(
            zsil[cp_][:], zsil[cp_][:], AF.Silu))(cp),
            reads=["zsil%d" % cp], writes=["zsil%d" % cp], tag="c_silz%d" % c)
        return steps

    def _chunk_conv(self, c, cp, bufs):
        steps = []

        class _P:
            @staticmethod
            def step(eng_, emit_, **kw):
                steps.append((eng_, emit_, kw))
        p = _P
        sb = self.sb
        W = self.T + 8
        t0 = c * C
        qkvc, mixed, cvt = bufs["qkvc"], bufs["mixed"], bufs["cvt"]
        msk, cw = sb["msk"], sb["cw"]

        def em_conv(ti0, ti1, ci):
            def em(eng):
                inst = None
                for ti in range(ti0, ti1):
                    qv = qkvc[:, ti * 131: ti * 131 + 131]
                    mx = mixed[:, ti * 128:(ti + 1) * 128]
                    inst = eng.tensor_scalar(mx, qv[:, 3:131],
                                             cw[:, ti * KC + 3:ti * KC + 4], None, ALU.mult)
                    for s_ in (1, 2, 3):
                        mslice = msk[:, (s_ - 1) * W + 3 + t0 - s_:
                                     (s_ - 1) * W + 3 + t0 - s_ + C]
                        inst = eng.scalar_tensor_tensor(
                            cvt[ci][:], qv[:, 3 - s_:131 - s_],
                            cw[:, ti * KC + 3 - s_:ti * KC + 4 - s_],
                            mslice, ALU.mult, ALU.mult)
                        inst = eng.tensor_tensor(mx, mx, cvt[ci][:], ALU.add)
                return inst
            return em
        def em_convg(ti0, ti1, ci):
            def em(eng):
                inst = None
                for tb in range(ti0, ti1, 2):
                    pair = (tb, tb + 1)
                    qvs = [qkvc[:, ti * 131: ti * 131 + 131] for ti in pair]
                    mxs = [mixed[:, ti * 128:(ti + 1) * 128] for ti in pair]
                    cvs = [cvt[ci][:, 0:64].rearrange("p n -> p n") for _ in pair]
                    cvs = [cvt[1][:], cvt[2][:]]
                    for j in (0, 1):
                        inst = eng.tensor_tensor(
                            mxs[j], qvs[j][:, 3:131],
                            cw[:, pair[j] * KC + 3:pair[j] * KC + 4].to_broadcast([128, C]),
                            ALU.mult)
                    for s_ in (1, 2, 3):
                        mslice = msk[:, (s_ - 1) * W + 3 + t0 - s_:
                                     (s_ - 1) * W + 3 + t0 - s_ + C]
                        for j in (0, 1):
                            inst = eng.tensor_tensor(
                                cvs[j], qvs[j][:, 3 - s_:131 - s_],
                                cw[:, pair[j] * KC + 3 - s_:pair[j] * KC + 4 - s_].to_broadcast([128, C]),
                                ALU.mult)
                        for j in (0, 1):
                            inst = eng.tensor_tensor(cvs[j], cvs[j], mslice, ALU.mult)
                        for j in (0, 1):
                            inst = eng.tensor_tensor(mxs[j], mxs[j], cvs[j], ALU.add)
                return inst
            return em
        for grp in range(6):
            p.step("vector", em_conv(grp * 8, (grp + 1) * 8, 0),
                   reads=["qkvc", "cw", "msk"], writes=["mixed%d" % grp, "cvt0"],
                   tag="c_cv%d_%d" % (c, grp))
        for grp in range(6, 8):
            p.step("gpsimd", em_convg(grp * 8, (grp + 1) * 8, 1),
                   reads=["qkvc", "cw", "msk"], writes=["mixed%d" % grp, "cvt1", "cvt2"],
                   tag="c_cg%d_%d" % (c, grp))
        for grp in range(8):
            p.step("scalar", (lambda g_: lambda eng: eng.activation(
                mixed[:, g_ * 1024:(g_ + 1) * 1024], mixed[:, g_ * 1024:(g_ + 1) * 1024],
                AF.Silu))(grp),
                reads=["mixed%d" % grp], writes=["mixed%d" % grp],
                tag="c_sil%d_%d" % (c, grp))
        return steps

    def _chunk_machA(self, c, cp, bufs):
        steps = []

        class _P:
            @staticmethod
            def step(eng_, emit_, **kw):
                steps.append((eng_, emit_, kw))
        nc, sb, pb = self.nc, self.sb, self.pb
        p = _P
        t0 = c * C
        co = bufs["cols"][cp]
        mixed, flat, sqb = bufs["mixed"], bufs["flat"], bufs["sqb"]
        abg, sr, ident = sb["abg"], sb["sr"], sb["ident"]
        cumt, revt, cv = sb["cumt"], sb["revt"], sb["cvals"]
        onesrow, onescolb, e127 = sb["onesrow"], sb["onescolb"], sb["e127"]
        bnk2 = pb[0]
        r_b2 = "pb0"
        S = p.step

        def em_gsA(eng):
            eng.tensor_copy(co["gsA"][0:64, :], abg[0:64, t0:t0 + C])
            return eng.tensor_copy(co["gsA"][64:65, :], sr[0:1, t0:t0 + C])
        S("vector", em_gsA, reads=["abg", "sr"], writes=["gsA%d" % cp], tag="c_gsa%d" % c)
        S("tensor", lambda eng: eng.transpose(bnk2[:, 384:450], co["gsA"][0:66, :], ident[0:66, 0:66]),
          reads=["gsA%d" % cp, "ident"], writes=[r_b2], tag="c_trg%d" % c)
        S("scalar", lambda eng: eng.copy(co["gBe"][:], bnk2[:, 384:450]),
          reads=[r_b2], writes=["gBe%d" % cp], tag="c_gbe%d" % c)

        S("tensor", lambda eng: eng.matmul(bnk2[:, 384:450], cumt[:], co["gBe"][:, 0:66],
                                           start=True, stop=True),
          reads=["gBe%d" % cp, "cumt"], writes=[r_b2], tag="c_cum%d" % c)
        S("scalar", lambda eng: eng.copy(co["cumE"][:], bnk2[:, 384:450]),
          reads=[r_b2], writes=["cumE%d" % cp], tag="c_cumev%d" % c)
        S("tensor", lambda eng: eng.matmul(bnk2[:, 384:450], revt[:], co["gBe"][:, 0:66],
                                           start=True, stop=True),
          reads=["gBe%d" % cp, "revt"], writes=[r_b2], tag="c_rev%d" % c)
        S("scalar", lambda eng: eng.copy(co["revE"][:], bnk2[:, 384:450]),
          reads=[r_b2], writes=["revE%d" % cp], tag="c_revev%d" % c)

        def em_c1(eng):
            eng.mul(co["c1"][:, 0:1], co["cumE"][:, 64:65], BIG)
            return eng.mul(co["c1"][:, 1:2], co["revE"][:, 64:65], BIG)
        S("scalar", em_c1, reads=["cumE%d" % cp, "revE%d" % cp],
          writes=["c1%d" % cp], tag="c_c1%d" % c)

        def em_gp(eng):
            eng.tensor_scalar(co["gp"][:], co["cumE"][:, 0:32], co["c1"][:, 0:1], None, ALU.subtract)
            return eng.tensor_scalar(co["dp"][:], co["revE"][:, 0:32], co["c1"][:, 1:2], None, ALU.subtract)
        S("vector", em_gp, reads=["cumE%d" % cp, "revE%d" % cp, "c1%d" % cp],
          writes=["gp%d" % cp, "dp%d" % cp], tag="c_gp%d" % c)

        S("tensor", lambda eng: eng.matmul(bnk2[0:1, 384:416], e127[:], co["gp"][:],
                                           start=True, stop=True),
          reads=["gp%d" % cp, "e127"], writes=[r_b2], tag="c_cfx%d" % c)
        S("scalar", lambda eng: eng.copy(co["crow"][:], bnk2[0:1, 384:416]),
          reads=[r_b2], writes=["crow%d" % cp], tag="c_cfe%d" % c)
        S("tensor", lambda eng: eng.matmul(bnk2[:, 416:448], onesrow[:], co["crow"][:],
                                           start=True, stop=True),
          reads=["crow%d" % cp, "onesrow"], writes=[r_b2], tag="c_cfb%d" % c)
        S("scalar", lambda eng: eng.activation(co["coefC"][:], bnk2[:, 416:448], AF.Exp),
          reads=[r_b2], writes=["coefC%d" % cp], tag="c_cfc%d" % c)
        return steps

    def _chunk_machB(self, c, cp, bufs):
        steps = []

        class _P:
            @staticmethod
            def step(eng_, emit_, **kw):
                steps.append((eng_, emit_, kw))
        nc, sb, pb = self.nc, self.sb, self.pb
        p = _P
        t0 = c * C
        co = bufs["cols"][cp]
        mixed, flat, sqb = bufs["mixed"], bufs["flat"], bufs["sqb"]
        abg, sr, ident = sb["abg"], sb["sr"], sb["ident"]
        cumt, revt, cv = sb["cumt"], sb["revt"], sb["cvals"]
        onesrow, onescolb, e127 = sb["onesrow"], sb["onescolb"], sb["e127"]
        bnk2 = pb[0]
        r_b2 = "pb0"
        S = p.step

        for which in ("q", "k"):
            for hk0 in range(0, 16, 4):
                tis = [(hk0 + j) if which == "q" else (16 + hk0 + j) for j in range(4)]
                grps = sorted({ti // 8 for ti in tis})

                def em_sq4(tis_):
                    def em(eng):
                        inst = None
                        for j, ti in enumerate(tis_):
                            inst = eng.tensor_tensor(
                                sqb[j][:], mixed[:, ti * 128:(ti + 1) * 128],
                                mixed[:, ti * 128:(ti + 1) * 128], ALU.mult)
                        return inst
                    return em
                S("vector", em_sq4(tis), reads=["mixed%d" % g for g in grps],
                  writes=["sqb%d" % j for j in range(4)],
                  tag="c_s%s%d_%d" % (which, c, hk0))

                def em_nmm4(cols_):
                    def em(eng):
                        inst = None
                        for j, col in enumerate(cols_):
                            inst = eng.matmul(bnk2[:, 448 + col:449 + col], sqb[j][:],
                                              onescolb[:], start=True, stop=True)
                        return inst
                    return em
                S("tensor", em_nmm4(tis),
                  reads=["sqb%d" % j for j in range(4)] + ["onescolb"],
                  writes=[r_b2], tag="c_n%s%d_%d" % (which, c, hk0))
        S("scalar", lambda eng: eng.activation(co["lnr"][:], bnk2[:, 448:480], AF.Ln,
                                               bias=cv[:, 2:3]),
          reads=[r_b2, "cvals"], writes=["lnr%d" % cp], tag="c_lnr%d" % c)

        lnq_rep = co["lnr"][:, 0:16].unsqueeze(2).to_broadcast([128, 16, 2])
        lnk_rep = co["lnr"][:, 16:32].unsqueeze(2).to_broadcast([128, 16, 2])

        def v3(ap):
            return ap.rearrange("p (a b) -> p a b", b=2)

        def em_cols(eng):
            cs = co["cs"]
            eng.tensor_scalar(co["lnr"][:, 0:16], co["lnr"][:, 0:16], -0.5,
                              -0.5 * float(np.log(DK)), ALU.mult, ALU.add)
            eng.tensor_scalar(co["lnr"][:, 16:32], co["lnr"][:, 16:32], -0.5, None, ALU.mult)
            eng.tensor_tensor(cs[:, 32:64], co["gp"][:], co["gBe"][:, 32:64], ALU.add)
            eng.tensor_tensor(co["earg"][:], co["gBe"][:, 32:64], co["gp"][:], ALU.add)
            eng.tensor_tensor(v3(cs[:, 0:32]), v3(co["gp"][:]), lnk_rep, ALU.subtract)
            eng.tensor_tensor(v3(cs[:, 32:64]), v3(cs[:, 32:64]), lnk_rep, ALU.add)
            eng.tensor_tensor(v3(cs[:, 64:96]), v3(co["gp"][:]), lnq_rep, ALU.add)
            eng.tensor_tensor(v3(co["Ktl"][:]), v3(co["dp"][:]), lnk_rep, ALU.add)
            return eng.tensor_scalar(co["nrA"][:], cs[:, 0:32], -1.0, None, ALU.mult)
        S("vector", em_cols,
          reads=["gp%d" % cp, "dp%d" % cp, "gBe%d" % cp, "lnr%d" % cp],
          writes=["cs%d" % cp, "nrA%d" % cp, "earg%d" % cp, "Ktl%d" % cp, "lnr%d" % cp],
          tag="c_cols%d" % c)

        def em_exps(eng):
            eng.activation(co["bcol"][:], co["gBe"][:, 32:64], AF.Exp)
            eng.activation(co["bGam"][:], co["earg"][:], AF.Exp)
            eng.activation(co["qGam"][:], co["cs"][:, 64:96], AF.Exp)
            return eng.activation(co["Ktl"][:], co["Ktl"][:], AF.Exp)
        S("scalar", em_exps,
          reads=["gBe%d" % cp, "earg%d" % cp, "cs%d" % cp, "Ktl%d" % cp],
          writes=["bcol%d" % cp, "bGam%d" % cp, "qGam%d" % cp, "Ktl%d" % cp],
          tag="c_exps%d" % c)

        S("tensor", lambda eng: eng.transpose(bnk2[0:96, 384:512], co["cs"][:, 0:96], ident[:]),
          reads=["cs%d" % cp, "ident"], writes=[r_b2], tag="c_trr%d" % c)
        S("scalar", lambda eng: eng.copy(co["rowsA"][:], bnk2[0:96, 384:512]),
          reads=[r_b2], writes=["rowsA%d" % cp], tag="c_rae%d" % c)
        for r in range(3):
            dst = flat[:].rearrange("o (h r c) -> o h r c", r=3, c=128)[:, :, r, :]
            S("sync", (lambda r_, d_: lambda eng: eng.dma_start(
                out=d_, in_=co["rowsA"][r_ * 32:(r_ + 1) * 32, :]))(r, dst),
              reads=["rowsA%d" % cp], writes=["flat"], dma=True,
              tag="c_fl%d_%d" % (c, r))

        S("vector", lambda eng: eng.memset(co["msC"][:], 0.0),
          writes=["msC%d" % cp], tag="c_ms0%d" % c)
        return steps
        return steps

    def _chunk_tail(self, c, cp, bufs):
        if getattr(self, "notail", False):
            return
        p, sb, pb = self.p, self.sb, self.pb
        t0 = c * C
        co = bufs["cols"][cp]
        y1, yA = bufs["y1"], bufs["yA"]
        cv, normwb, identb = sb["cvals"], sb["normwb"], sb["identb"]
        S = p.step

        S("vector", lambda eng: eng.tensor_scalar(co["msC"][:], co["msC"][:],
                                                  1.0 / DV, None, ALU.mult),
          reads=["msC%d" % cp], writes=["msC%d" % cp], tag="c_msn%d" % c)

        S("scalar", lambda eng: eng.activation(co["rstd"][:], co["msC"][:], AF.Ln,
                                               bias=cv[:, 2:3]),
          reads=["msC%d" % cp, "cvals"], writes=["rstd%d" % cp], tag="c_rstda%d" % c)
        S("scalar", lambda eng: eng.activation(co["rstd"][:], co["rstd"][:], AF.Exp,
                                               scale=-0.5),
          sereads=["rstd%d" % cp], writes=["rstd%d" % cp], tag="c_rstdb%d" % c)

        def tslot(h_):
            g3_ = h_ % 4
            return (pb[6 + (g3_ & 1)][:].bitcast(BF16), 512 + (g3_ >> 1) * 128,
                    "pb%d" % (6 + (g3_ & 1)))
        for h0 in range(0, HV, 2):
            pair = (h0, h0 + 1)

            def em_yb2(pr):
                def em(eng):
                    inst = None
                    for h_ in pr:
                        inst = eng.scalar_tensor_tensor(
                            y1[:, h_ * 128:(h_ + 1) * 128], y1[:, h_ * 128:(h_ + 1) * 128],
                            co["rstd"][:, h_:h_ + 1], normwb[:], ALU.mult, ALU.mult)
                    return inst
                return em
            S("vector", em_yb2(pair), reads=["y1", "rstd%d" % cp, "normwb"],
              writes=["y1"], tag="c_yb%d_%d" % (c, h0))
            for h in pair:
                BT, bto, rBT = tslot(h)
                S("tensor", (lambda h_, BT_, o_: lambda eng: eng.transpose(
                    BT_[:, o_:o_ + 128], y1[:, h_ * 128:(h_ + 1) * 128], identb[:]))(h, BT, bto),
                  reads=["y1", "identb"], writes=[rBT], tag="c_ytr%d_%d" % (c, h))

            def em_yev2(pr):
                def em(eng):
                    inst = None
                    for h_ in pr:
                        BT_, o_, _ = tslot(h_)
                        inst = eng.copy(yA[:, h_ * 128:(h_ + 1) * 128], BT_[:, o_:o_ + 128])
                    return inst
                return em
            S("scalar", em_yev2(pair), reads=[tslot(h0)[2], tslot(h0 + 1)[2]],
              writes=["yA"], tag="c_yev%d_%d" % (c, h0))
        for q in range(2):
            src = yA[:].rearrange("p (hh t) -> p hh t", t=128)[:, q * 16:(q + 1) * 16, :]
            dst = self.y_s.rearrange("(hh p) t -> p hh t", p=128)[:, q * 16:(q + 1) * 16, t0:t0 + C]
            S("sync", (lambda s_, d_: lambda eng: eng.dma_start(out=d_, in_=s_))(src, dst),
              reads=["yA"], dma=True, tag="c_yst%d_%d" % (c, q))

    def _head_steps(self, c, cp, h, bufs):
        nc, p, sb, pb = self.nc, self.p, self.sb, self.pb
        co = bufs["cols"][cp]
        mixed, zsil_t, Sst = bufs["mixed"], bufs["zsil"][cp], bufs["Sst"]
        flat, y1 = bufs["flat"], bufs["y1"]
        solT, Usb, Rt, VBs = bufs["solT"], bufs["Usb"], bufs["Rt"], bufs["VBs"]
        KBt, KSe, QSe, Ot = bufs["KBt"], bufs["KSe"], bufs["QSe"], bufs["Ot"]
        Esb, r3m = bufs["Esb"], bufs["r3m"]
        g3 = h % 4
        hk = h // 2
        BF = pb[1]
        rBF = "pb1"
        BC = pb[0]
        rBC = "pb0"
        BS = pb[2 + g3]
        rBS = "pb%d" % (2 + g3)
        BT = pb[6 + (g3 & 1)][:].bitcast(BF16)
        bto = (g3 >> 1) * 256
        rBT = "pb%d" % (6 + (g3 & 1))
        sol = solT[g3]
        rsol = "solT%d" % g3
        U = Usb[g3]
        rU = "Usb%d" % g3
        cm384, identb, onesrow = sb["cm384"], sb["identb"], sb["onesrow"]
        qT = mixed[:, hk * 128:(hk + 1) * 128]
        kT = mixed[:, (16 + hk) * 128:(16 + hk + 1) * 128]
        vT = mixed[:, (32 + h) * 128:(32 + h + 1) * 128]
        Sh = Sst[:, h * 128:(h + 1) * 128]
        tg = "%d_%d" % (c, h)
        steps = []

        def S(engine, emit, **kw):
            steps.append((engine, emit, kw))

        S("tensor", lambda eng: eng.matmul(BC[:, 0:384], onesrow[:],
                                           flat[0:1, h * 384:(h + 1) * 384],
                                           start=True, stop=True),
          reads=["flat", "onesrow"], writes=[rBC], tag="h_bc" + tg)
        S("vector", lambda eng: eng.tensor_tensor(r3m[g3][:], BC[:, 0:384], cm384[:], ALU.add),
          reads=[rBC, "cm384"], writes=["r3m%d" % g3], tag="h_r3m" + tg)

        def em_es(eng):
            eng.activation(Esb[g3][:, 0:128], r3m[g3][:, 0:128], AF.Exp,
                           bias=co["cs"][:, 32 + h:33 + h], scale=-1.0)
            eng.activation(Esb[g3][:, 128:256], r3m[g3][:, 128:256], AF.Exp,
                           bias=co["nrA"][:, h:h + 1])
            return eng.activation(Esb[g3][:, 256:384], r3m[g3][:, 256:384], AF.Exp,
                                  bias=co["nrA"][:, h:h + 1])
        S("scalar", em_es, reads=["r3m%d" % g3, "cs%d" % cp, "nrA%d" % cp],
          writes=["Esb%d" % g3], tag="h_es" + tg)

        def em_pp(eng):
            eng.matmul(BF[:, 0:128], kT, kT, start=True, stop=True)
            eng.matmul(BF[:, 128:256], kT, qT, start=True, stop=True)
            eng.matmul(BF[:, 256:384], kT, Sh, start=True, stop=True)
            return eng.matmul(BF[:, 384:512], qT, Sh, start=True, stop=True)
        S("tensor", em_pp, reads=["mixed%d" % (hk // 8), "mixed%d" % (2 + hk // 8), "Sst"],
          writes=[rBF], tag="h_pp" + tg)

        def em_xya(eng):
            eng.tensor_tensor(sol[:, 0:128], BF[:, 0:128], Esb[g3][:, 0:128], ALU.mult)
            eng.tensor_tensor(sol[:, 6 * 128:7 * 128], BF[:, 0:128], Esb[g3][:, 128:256], ALU.mult)
            eng.tensor_tensor(sol[:, 13 * 128:14 * 128], BF[:, 128:256],
                              Esb[g3][:, 256:384], ALU.mult)
            eng.tensor_scalar(KSe[g3][:], BF[:, 256:384],
                              co["bGam"][:, h:h + 1], None, ALU.mult)
            return eng.tensor_scalar(QSe[g3][:], BF[:, 384:512],
                                     co["qGam"][:, h:h + 1], None, ALU.mult)
        S("vector", em_xya, reads=[rBF, "Esb%d" % g3, "bGam%d" % cp, "qGam%d" % cp],
          writes=[rsol, "KSe%d" % g3, "QSe%d" % g3], tag="h_xya" + tg)

        S("tensor", lambda eng: eng.transpose(BT[:, bto:bto + 128], vT, identb[:]),
          reads=["mixed%d" % (4 + h // 8), "identb"], writes=[rBT], tag="h_vtr" + tg)
        S("tensor", lambda eng: eng.transpose(BT[:, bto + 128:bto + 256], kT, identb[:]),
          reads=["mixed%d" % (2 + hk // 8), "identb"], writes=[rBT], tag="h_ktr" + tg)

        def em_vkr(eng):
            eng.tensor_scalar(VBs[g3][:], BT[:, bto:bto + 128],
                              co["bcol"][:, h:h + 1], None, ALU.mult)
            eng.tensor_scalar(KBt[g3][:], BT[:, bto + 128:bto + 256],
                              co["Ktl"][:, h:h + 1], None, ALU.mult)
            return eng.tensor_tensor(Rt[g3][:], VBs[g3][:], KSe[g3][:], ALU.subtract)
        S("vector", em_vkr, reads=[rBT, "bcol%d" % cp, "Ktl%d" % cp, "KSe%d" % g3],
          writes=["VBs%d" % g3, "KBt%d" % g3, "Rt%d" % g3], tag="h_vkr" + tg)

        S("tensor", lambda eng: eng.matmul(BS[:, 256:384], sol[:, 6 * 128:7 * 128],
                                           Rt[g3][:], start=True, stop=True),
          reads=[rsol, "Rt%d" % g3], writes=[rBS], tag="h_a0" + tg)
        S("vector", lambda eng: eng.tensor_tensor(U[:, 0:128], Rt[g3][:],
                                                  BS[:, 256:384], ALU.subtract),
          reads=["Rt%d" % g3, rBS], writes=[rU], tag="h_u1" + tg)
        for k in range(1, 7):
            xk_prev = sol[:, (k - 1) * 128:k * 128]
            yk_prev = sol[:, (6 + k - 1) * 128:(6 + k) * 128]
            ups = k % 2
            uprev = U[:, (1 - ups) * 128:(2 - ups) * 128]
            unew = U[:, ups * 128:(ups + 1) * 128]
            if k < 6:
                def em_sq(yp, xp):
                    def em(eng):
                        eng.matmul(BS[:, 0:128], yp, xp, start=True, stop=True)
                        return eng.matmul(BS[:, 128:256], xp, yp, start=True, stop=True)
                    return em
                S("tensor", em_sq(yk_prev, xk_prev), reads=[rsol], writes=[rBS],
                  tag="h_sq%d" % k + tg)

                def em_sqe(kk):
                    def em(eng):
                        eng.copy(sol[:, kk * 128:(kk + 1) * 128], BS[:, 0:128])
                        return eng.copy(sol[:, (6 + kk) * 128:(6 + kk + 1) * 128],
                                        BS[:, 128:256])
                    return em
                S("scalar", em_sqe(k), reads=[rBS], writes=[rsol], tag="h_sqe%d" % k + tg)
            else:
                S("tensor", (lambda yp, xp: lambda eng: eng.matmul(
                    BS[:, 128:256], xp, yp, start=True, stop=True))(yk_prev, xk_prev),
                  reads=[rsol], writes=[rBS], tag="h_sq%d" % k + tg)
                S("scalar", (lambda kk: lambda eng: eng.copy(
                    sol[:, (6 + kk) * 128:(6 + kk + 1) * 128], BS[:, 128:256]))(k),
                  reads=[rBS], writes=[rsol], tag="h_sqe%d" % k + tg)
            S("tensor", (lambda kk, up: lambda eng: eng.matmul(
                BS[:, 256:384], sol[:, (6 + kk) * 128:(6 + kk + 1) * 128], up,
                start=True, stop=True))(k, uprev),
              reads=[rsol, rU], writes=[rBS], tag="h_a%d" % k + tg)
            S("vector", (lambda un, up: lambda eng: eng.tensor_tensor(
                un, up, BS[:, 256:384], ALU.add))(unew, uprev),
              reads=[rU, rBS], writes=[rU], tag="h_u%d" % (k + 1) + tg)
        ufin = U[:, 0:128]

        S("tensor", lambda eng: eng.matmul(BS[:, 384:512], sol[:, 13 * 128:14 * 128],
                                           ufin, start=True, stop=True),
          reads=[rsol, rU], writes=[rBS], tag="h_oi" + tg)
        S("vector", lambda eng: eng.tensor_tensor(Ot[g3][:], QSe[g3][:], BS[:, 384:512], ALU.add),
          reads=["QSe%d" % g3, rBS], writes=["Ot%d" % g3], tag="h_o" + tg)

        S("gpsimd", (lambda h_: lambda eng: eng.tensor_tensor(
            y1[:, h_ * 128:(h_ + 1) * 128], Ot[g3][:],
            zsil_t[:, h_ * 128:(h_ + 1) * 128], ALU.mult))(h),
          reads=["Ot%d" % g3, "zsil%d" % cp], writes=["y1"], tag="h_y1" + tg)
        S("scalar", (lambda h_: lambda eng: eng.activation(
            Esb[g3][:, 0:128], y1[:, h_ * 128:(h_ + 1) * 128], AF.Square,
            accum_out=co["msC"][:, h_:h_ + 1]))(h),
          reads=["y1", "msC%d" % cp], writes=["Esb%d" % g3, "msC%d" % cp], tag="h_ms" + tg)

        S("tensor", lambda eng: eng.matmul(BS[:, 384:512], KBt[g3][:], ufin,
                                           start=True, stop=True),
          reads=["KBt%d" % g3, rU], writes=[rBS], tag="h_st" + tg)
        S("vector", (lambda h_: lambda eng: eng.scalar_tensor_tensor(
            Sst[:, h_ * 128:(h_ + 1) * 128], Sst[:, h_ * 128:(h_ + 1) * 128],
            co["coefC"][:, h_:h_ + 1], BS[:, 384:512], ALU.mult, ALU.add))(h),
          reads=["Sst", "coefC%d" % cp, rBS], writes=["Sst"], tag="h_sn" + tg)
        return steps

    def ph_d(self, hh):
        nc, p, sb, dr, pb = self.nc, self.p, self.sb, self.dr, self.pb
        T, TH = self.T, self.TH
        TQ = min(512, TH)
        nq = TH // TQ
        at = self.at
        yh = at("yh%d" % hh, [128, 32 * TH], BF16, 0)
        xnew = at("xnew%d" % hh, [128, 16 * TH], F32, 64 * 1024)
        wco = [at("wco%d_%d" % (hh, i), [128, 32 * 128], BF16, 128 * 1024 + i * 8192)
               for i in range(2)]
        xold = at("xold%d" % hh, [128, TQ], F32, 144 * 1024)
        xbf = at("xbf2_%d" % hh, [128, TQ], BF16, 147 * 1024)
        xsq = at("xsq2_%d" % hh, [128, TQ], BF16, 149 * 1024)
        dbc = at("dbc%d" % hh, [128, 2 * TQ], F32, 151 * 1024)
        drow = at("drow%d" % hh, [1, 4 * TQ], F32, 156 * 1024)
        lnsrow = at("lnsrow%d" % hh, [1, TH], F32, 168 * 1024)
        lnmrow = at("lnmrow%d" % hh, [1, TH], F32, 172 * 1024)
        t0h = hh * TH

        for q in range(4):
            src = self.y_s.rearrange("(hv p) t -> p hv t", p=128)[:, q * 8:(q + 1) * 8, t0h:t0h + TH]
            dst = yh[:].rearrange("p (hv t) -> p hv t", t=TH)[:, q * 8:(q + 1) * 8, :]
            p.step("sync", (lambda s_, d_: lambda eng: eng.dma_start(out=d_, in_=s_))(src, dst),
                   writes=["yh"], dma=True, tag="d_ly%d_%d" % (hh, q))

        for dm in range(16):
            par = dm % 2
            src = dr["Wout"].rearrange("(hv p) c -> p hv c", p=128)[:, :, dm * 128:(dm + 1) * 128]
            for q in range(2):
                p.step("gpsimd",
                       (lambda s_, d_, q_: lambda eng: eng.dma_start(
                           out=d_[:].rearrange("p (hv c) -> p hv c", c=128)[:, q_ * 16:(q_ + 1) * 16, :],
                           in_=s_[:, q_ * 16:(q_ + 1) * 16, :]))(src, wco[par], q),
                       writes=["wco%d" % par], dma=True, tag="d_lw%d_%d_%d" % (hh, dm, q))
            for tq in range(nq):
                bnk = pb[par]
                rb = "pb%d" % par

                def em_mm(par_, tq_):
                    def em(eng):
                        inst = None
                        for hv in range(32):
                            inst = eng.matmul(
                                pb[par_][:, 0:TQ], wco[par_][:, hv * 128:(hv + 1) * 128],
                                yh[:].rearrange("p (hv t) -> p hv t", t=TH)[:, hv, tq_ * TQ:(tq_ + 1) * TQ],
                                start=(hv == 0), stop=(hv == 31))
                        return inst
                    return em
                p.step("tensor", em_mm(par, tq), reads=["wco%d" % par, "yh"],
                       writes=["pb%d" % par], tag="d_mm%d_%d_%d" % (hh, dm, tq))
                src2 = dr["xT"][dm * 128:(dm + 1) * 128, t0h + tq * TQ: t0h + (tq + 1) * TQ]
                p.step("sync", (lambda s_: lambda eng: eng.dma_start(out=xold[:], in_=s_))(src2),
                       writes=["xold"], dma=True, tag="d_lx%d_%d_%d" % (hh, dm, tq))

                def em_add(par_, dm_, tq_):
                    def em(eng):
                        return eng.tensor_tensor(
                            xnew[:, dm_ * TH + tq_ * TQ: dm_ * TH + (tq_ + 1) * TQ],
                            xold[:], pb[par_][:, 0:TQ], ALU.add)
                    return em
                p.step("vector", em_add(par, dm, tq), reads=["xold", "pb%d" % par],
                       writes=["xnew"], tag="d_add%d_%d_%d" % (hh, dm, tq))

        onescolb, onesrow, cv = sb["onescolb"], sb["onesrow"], sb["cvals"]
        for tq in range(nq):
            for dm in range(16):
                def em_cast(dm_, tq_):
                    def em(eng):
                        return eng.tensor_copy(xbf[:], xnew[:, dm_ * TH + tq_ * TQ: dm_ * TH + (tq_ + 1) * TQ])
                    return em
                p.step("vector", em_cast(dm, tq), reads=["xnew"], writes=["xbf"],
                       tag="d_c%d_%d_%d" % (hh, dm, tq))
                p.step("scalar", lambda eng: eng.activation(xsq[:], xbf[:], AF.Square),
                       reads=["xbf"], writes=["xsq"], tag="d_q%d_%d_%d" % (hh, dm, tq))

                def em_smm(dm_):
                    def em(eng):
                        eng.matmul(pb[4][0:1, 0:TQ], onescolb[:], xbf[:],
                                   start=(dm_ == 0), stop=(dm_ == 15))
                        return eng.matmul(pb[5][0:1, 0:TQ], onescolb[:], xsq[:],
                                          start=(dm_ == 0), stop=(dm_ == 15))
                    return em
                p.step("tensor", em_smm(dm), reads=["xbf", "xsq", "onescolb"],
                       writes=["pb4", "pb5"], tag="d_s%d_%d_%d" % (hh, dm, tq))

            def em_rows(eng):
                eng.tensor_scalar(drow[0:1, 0:TQ], pb[4][0:1, 0:TQ], 1.0 / D, None, ALU.mult)
                eng.tensor_scalar(drow[0:1, 3 * TQ:4 * TQ], pb[5][0:1, 0:TQ], 1.0 / D, None, ALU.mult)
                eng.tensor_tensor(drow[0:1, TQ:2 * TQ], drow[0:1, 0:TQ], drow[0:1, 0:TQ], ALU.mult)
                return eng.tensor_tensor(drow[0:1, 3 * TQ:4 * TQ], drow[0:1, 3 * TQ:4 * TQ],
                                         drow[0:1, TQ:2 * TQ], ALU.subtract)
            p.step("vector", em_rows, reads=["pb4", "pb5"], writes=["drow"],
                   tag="d_r%d_%d" % (hh, tq))

            def em_rstd(eng):
                eng.activation(drow[0:1, 2 * TQ:3 * TQ], drow[0:1, 3 * TQ:4 * TQ], AF.Ln,
                               bias=cv[0:1, 0:1])
                return eng.activation(drow[0:1, TQ:2 * TQ], drow[0:1, 2 * TQ:3 * TQ],
                                      AF.Exp, scale=-0.5)
            p.step("scalar", em_rstd, reads=["drow", "cvals"], writes=["drow"],
                   tag="d_rs%d_%d" % (hh, tq))
            p.step("vector", lambda eng: eng.tensor_tensor(
                drow[0:1, 2 * TQ:3 * TQ], drow[0:1, 0:TQ], drow[0:1, TQ:2 * TQ], ALU.mult),
                reads=["drow"], writes=["drow"], tag="d_ms%d_%d" % (hh, tq))

            def em_carry(tq_):
                def em(eng):
                    eng.tensor_copy(lnsrow[0:1, tq_ * TQ:(tq_ + 1) * TQ], drow[0:1, TQ:2 * TQ])
                    return eng.tensor_copy(lnmrow[0:1, tq_ * TQ:(tq_ + 1) * TQ], drow[0:1, 2 * TQ:3 * TQ])
                return em
            p.step("vector", em_carry(tq), reads=["drow"],
                   writes=["lnsrow", "lnmrow"], tag="d_cr%d_%d" % (hh, tq))
            p.step("tensor", lambda eng: eng.matmul(pb[6][:, 0:TQ], onesrow[:],
                                                    drow[0:1, TQ:2 * TQ], start=True, stop=True),
                   reads=["drow", "onesrow"], writes=["pb6"], tag="d_bs%d_%d" % (hh, tq))
            p.step("tensor", lambda eng: eng.matmul(pb[7][:, 0:TQ], onesrow[:],
                                                    drow[0:1, 2 * TQ:3 * TQ], start=True, stop=True),
                   reads=["drow", "onesrow"], writes=["pb7"], tag="d_bm%d_%d" % (hh, tq))
            p.step("vector", lambda eng: eng.tensor_copy(dbc[:, 0:TQ], pb[6][:, 0:TQ]),
                   reads=["pb6"], writes=["dbc"], tag="d_es%d_%d" % (hh, tq))
            p.step("vector", lambda eng: eng.tensor_copy(dbc[:, TQ:2 * TQ], pb[7][:, 0:TQ]),
                   reads=["pb7"], writes=["dbc"], tag="d_em%d_%d" % (hh, tq))

            for dm in range(16):
                def em_app(dm_, tq_):
                    def em(eng):
                        xs = xnew[:, dm_ * TH + tq_ * TQ: dm_ * TH + (tq_ + 1) * TQ]
                        eng.tensor_tensor(xbf[:], xs, dbc[:, 0:TQ], ALU.mult)
                        return eng.tensor_tensor(xbf[:], xbf[:], dbc[:, TQ:2 * TQ], ALU.subtract)
                    return em
                p.step("vector", em_app(dm, tq), reads=["xnew", "dbc"],
                       writes=["xbf"], tag="d_a%d_%d_%d" % (hh, dm, tq))
                dst = self.ln2_s[dm * 128:(dm + 1) * 128, t0h + tq * TQ: t0h + (tq + 1) * TQ]
                p.step("sync", (lambda d_: lambda eng: eng.dma_start(out=d_, in_=xbf[:]))(dst),
                       reads=["xbf"], dma=True, tag="d_st%d_%d_%d" % (hh, dm, tq))

    def ph_e(self, hh):
        nc, p, sb, dr, pb = self.nc, self.p, self.sb, self.dr, self.pb
        T, TH = self.T, self.TH
        t0h = hh * TH
        at = self.at
        Hb = at("Hb%d" % hh, [128, 16 * TH], BF16, 0)
        ln2x = at("ln2x%d" % hh, [128, 16 * TH], BF16, 32 * 1024)
        xnew = self.sb["xnew%d" % hh]
        wgb = [at("wgb%d_%d" % (hh, i), [128, 16 * 128], BF16, 128 * 1024 + i * 4096)
               for i in range(2)]
        wub = [at("wub%d_%d" % (hh, i), [128, 16 * 128], BF16, 136 * 1024 + i * 4096)
               for i in range(2)]
        wdb = [at("wdb%d_%d" % (hh, i), [128, 16 * 128], BF16, 144 * 1024 + i * 4096)
               for i in range(2)]
        wrf = at("wrf%d" % hh, [128, 16 * 8], F32, 152 * 1024)
        wfin = at("wfin%d" % hh, [128, 8 * (TH // 128)], F32, 154 * 1024)
        wT = at("wT%d" % hh, [8, TH], BF16, 156 * 1024)
        wrow = at("wrow%d" % hh, [1, TH], BF16, 158 * 1024)
        wbt = at("wbt%d" % hh, [128, TH], F32, 160 * 1024)
        sG = at("sG%d" % hh, [128, TH], BF16, 164 * 1024)
        lnsrow = self.sb["lnsrow%d" % hh]
        lnmrow = self.sb["lnmrow%d" % hh]
        ntt = TH // 128

        for q in range(4):
            src = self.ln2_s.rearrange("(dt p) t -> p dt t", p=128)[:, q * 4:(q + 1) * 4, t0h:t0h + TH]
            dst = ln2x[:].rearrange("p (dt t) -> p dt t", t=TH)[:, q * 4:(q + 1) * 4, :]
            p.step("sync", (lambda s_, d_: lambda eng: eng.dma_start(out=d_, in_=s_))(src, dst),
                   writes=["ln2x"], dma=True, tag="e_ll%d_%d" % (hh, q))
        p.step("sync", lambda eng: eng.dma_start(
            out=wrf[:].rearrange("p (dt c) -> p dt c", c=8),
            in_=dr["Wr"].rearrange("(dt p) n -> p dt n", p=128)),
            writes=["wrf"], dma=True, tag="e_lwr%d" % hh)

        iota8, iota8b, neg18, cv = sb["iota8"], sb["iota8b"], sb["neg18"], sb["cvals"]
        onescol, onesrow = sb["onescol"], sb["onesrow"]
        GL = float(0.5 - np.log(4.0 / 3.0))
        NTTE = ntt * 8
        rts = self.at("rts%d" % hh, [128, 4 * NTTE], F32, 153 * 1024)
        rtv = self.at("rtv%d" % hh, [128, 4 * NTTE], F32, 153 * 1024 + 16 * NTTE)
        rtc = self.at("rtc%d" % hh, [128, 128], F32, 153 * 1024 + 32 * NTTE)

        def g3(ap):
            return ap.rearrange("p (tt e) -> p tt e", e=8)

        def cb(ap):
            return ap.unsqueeze(2).to_broadcast([128, ntt, 8])

        def eb(ap):
            return ap.unsqueeze(1).to_broadcast([128, ntt, 8])

        def em_csum(eng):
            inst = None
            for d in range(16):
                inst = eng.matmul(pb[7][0:1, 0:8], onescol[:], wrf[:, d * 8:(d + 1) * 8],
                                  start=(d == 0), stop=(d == 15))
            return inst
        p.step("tensor", em_csum, reads=["wrf", "onescol"], writes=["pb7"],
               tag="e_cs%d" % hh)
        p.step("vector", lambda eng: eng.tensor_copy(rtc[0:1, 96:104], pb[7][0:1, 0:8]),
               reads=["pb7"], writes=["rtc"], tag="e_cse%d" % hh)
        p.step("tensor", lambda eng: eng.matmul(pb[7][:, 0:8], onesrow[:], rtc[0:1, 96:104],
                                                start=True, stop=True),
               reads=["rtc", "onesrow"], writes=["pb7"], tag="e_csb%d" % hh)
        p.step("vector", lambda eng: eng.tensor_copy(rtc[:, 104:112], pb[7][:, 0:8]),
               reads=["pb7"], writes=["rtc"], tag="e_csb2%d" % hh)

        def em_rmm(tt_):
            def em(eng):
                inst = None
                for d in range(16):
                    inst = eng.matmul(pb[6][0:128, tt_ * 8:(tt_ + 1) * 8],
                                      xnew[:, d * TH + tt_ * 128: d * TH + (tt_ + 1) * 128],
                                      wrf[:, d * 8:(d + 1) * 8],
                                      start=(d == 0), stop=(d == 15))
                return inst
            return em

        def em_scol(tt_):
            def em(eng):
                eng.matmul(pb[7][:, 16 + tt_:17 + tt_], lnsrow[0:1, tt_ * 128:(tt_ + 1) * 128],
                           onesrow[0:1, 0:1], start=True, stop=True)
                return eng.matmul(pb[7][:, 24 + tt_:25 + tt_], lnmrow[0:1, tt_ * 128:(tt_ + 1) * 128],
                                  onesrow[0:1, 0:1], start=True, stop=True)
            return em
        for tt in range(ntt):
            p.step("tensor", em_rmm(tt), reads=["xnew", "wrf"], writes=["pb6"],
                   tag="e_rm%d_%d" % (hh, tt))
            p.step("tensor", em_scol(tt), reads=["lnsrow", "lnmrow", "onesrow"],
                   writes=["pb7"], tag="e_sc%d_%d" % (hh, tt))

        def em_sme(eng):
            eng.tensor_copy(rtc[:, 80:80 + ntt], pb[7][:, 16:16 + ntt])
            return eng.tensor_copy(rtc[:, 88:88 + ntt], pb[7][:, 24:24 + ntt])
        p.step("vector", em_sme, reads=["pb7"], writes=["rtc"], tag="e_sme%d" % hh)

        def em_arg(eng):
            eng.tensor_tensor(g3(rtv[:, NTTE:2 * NTTE]),
                              cb(rtc[:, 88:88 + ntt]), eb(rtc[:, 104:112]), ALU.mult)
            eng.tensor_tensor(g3(rtv[:, 0:NTTE]),
                              g3(pb[6][0:128, 0:NTTE]), cb(rtc[:, 80:80 + ntt]), ALU.mult)
            return eng.tensor_tensor(rtv[:, 0:NTTE], rtv[:, 0:NTTE],
                                     rtv[:, NTTE:2 * NTTE], ALU.subtract)
        p.step("vector", em_arg, reads=["pb6"], sereads=["rtc"], writes=["rtv"],
               tag="e_arg%d" % hh)

        def em_racts(eng):
            eng.activation(rts[:, 0:NTTE], rtv[:, 0:NTTE], AF.Sigmoid, scale=cv[:, 4:5])
            eng.activation(rts[:, NTTE:2 * NTTE], rts[:, 0:NTTE], AF.Abs, bias=cv[:, 5:6])
            eng.activation(rts[:, 2 * NTTE:3 * NTTE], rts[:, NTTE:2 * NTTE], AF.Exp, scale=-10.0)
            return eng.activation(rts[:, 3 * NTTE:4 * NTTE], rts[:, NTTE:2 * NTTE], AF.Exp,
                                  scale=float(-1.0 / 0.3))
        p.step("scalar", em_racts, reads=["rtv", "cvals"], writes=["rts"],
               tag="e_ra%d" % hh)

        inh = rts[:, 0:NTTE]
        w0 = rts[:, 2 * NTTE:3 * NTTE]
        fb = rts[:, 3 * NTTE:4 * NTTE]
        e1 = rtv[:, 0:NTTE]
        tv = rtv[:, NTTE:2 * NTTE]
        w8 = rtv[:, 2 * NTTE:3 * NTTE]
        fbw = rtv[:, 3 * NTTE:4 * NTTE]
        sA = rtc[:, 0:ntt]
        ne = rtc[:, 8:8 + ntt]
        m1 = rtc[:, 16:16 + ntt]
        i1 = rtc[:, 24:24 + ntt]
        m2 = rtc[:, 32:32 + ntt]
        i2 = rtc[:, 40:40 + ntt]
        sB = rtc[:, 48:48 + ntt]
        sC = rtc[:, 56:56 + ntt]
        sD = rtc[:, 64:64 + ntt]

        def em_v1(eng):
            eng.tensor_scalar(e1, inh, GL, None, ALU.is_ge)
            eng.tensor_scalar(tv, inh, 0.5, None, ALU.is_le)
            eng.tensor_tensor(e1, e1, tv, ALU.mult)
            eng.tensor_tensor(w8, w0, e1, ALU.mult)
            eng.tensor_reduce(sA, g3(w8), axis=AX.X, op=ALU.add)
            eng.tensor_reduce(m1, g3(fb), axis=AX.X, op=ALU.max)
            return eng.tensor_scalar(ne, sA, 1e-8, None, ALU.is_lt)
        p.step("vector", em_v1, reads=["rts"], writes=["rtv", "rtc"],
               tag="e_v1%d" % hh)

        def em_v2(eng):
            eng.tensor_tensor(g3(tv), g3(fb), cb(m1), ALU.is_equal)
            eng.scalar_tensor_tensor(g3(tv), g3(tv), -100.0, eb(iota8b[:]), ALU.mult, ALU.add)
            return eng.tensor_reduce(i1, g3(tv), axis=AX.X, op=ALU.min)
        p.step("vector", em_v2, reads=["rts"], sereads=["rtc"], writes=["rtv", "rtc"],
               tag="e_v2%d" % hh)

        def em_v3(eng):
            eng.tensor_tensor(g3(e1), eb(iota8[:]), cb(i1), ALU.is_equal)
            eng.tensor_scalar(tv, fb, 1.0, None, ALU.add)
            eng.tensor_tensor(tv, e1, tv, ALU.mult)
            eng.tensor_tensor(tv, fb, tv, ALU.subtract)
            return eng.tensor_reduce(m2, g3(tv), axis=AX.X, op=ALU.max)
        p.step("vector", em_v3, reads=["rts"], sereads=["rtc"], writes=["rtv", "rtc"],
               tag="e_v3%d" % hh)

        def em_v4(eng):
            eng.tensor_tensor(g3(fbw), g3(tv), cb(m2), ALU.is_equal)
            eng.scalar_tensor_tensor(g3(fbw), g3(fbw), -100.0, eb(iota8b[:]), ALU.mult, ALU.add)
            return eng.tensor_reduce(i2, g3(fbw), axis=AX.X, op=ALU.min)
        p.step("vector", em_v4, sereads=["rtc"], reads=["rtv"], writes=["rtv", "rtc"],
               tag="e_v4%d" % hh)

        def em_v5(eng):
            eng.tensor_tensor(g3(tv), eb(iota8[:]), cb(i2), ALU.is_equal)
            eng.tensor_tensor(e1, e1, tv, ALU.add)
            eng.tensor_tensor(fbw, fb, e1, ALU.mult)
            return eng.tensor_reduce(sB, g3(fbw), axis=AX.X, op=ALU.add)
        p.step("vector", em_v5, reads=["rts"], sereads=["rtc"], writes=["rtv", "rtc"],
               tag="e_v5%d" % hh)

        def em_v6(eng):
            eng.tensor_scalar(sC, sB, 1e-8, None, ALU.max)
            eng.tensor_scalar(tv, fbw, 1.0, None, ALU.mult)
            return eng.reciprocal(sD, sC)
        p.step("vector", em_v6, sereads=["rtc"], reads=["rtv"], writes=["rtv", "rtc"],
               tag="e_v6%d" % hh)

        def em_v7(eng):
            eng.tensor_tensor(g3(fbw), g3(fbw), cb(sD), ALU.mult)
            eng.tensor_tensor(tv, fbw, w8, ALU.subtract)
            eng.tensor_tensor(g3(tv), g3(tv), cb(ne), ALU.mult)
            eng.tensor_tensor(w8, w8, tv, ALU.add)
            return eng.tensor_reduce(sB, g3(w8), axis=AX.X, op=ALU.add)
        p.step("vector", em_v7, sereads=["rtc"], reads=["rtv"], writes=["rtv", "rtc"],
               tag="e_v7%d" % hh)

        def em_v8(eng):
            eng.tensor_scalar(sC, sB, 1e-8, None, ALU.max)
            eng.tensor_scalar(tv, w8, 1.0, None, ALU.mult)
            return eng.reciprocal(sD, sC)
        p.step("vector", em_v8, sereads=["rtc"], reads=["rtv"], writes=["rtv", "rtc"],
               tag="e_v8%d" % hh)

        def em_v9(eng):
            return eng.tensor_tensor(g3(wfin[:]), g3(w8), cb(sD), ALU.mult)
        p.step("vector", em_v9, sereads=["rtc"], reads=["rtv"], writes=["wfin"],
               tag="e_v9%d" % hh)

        for tt in range(ntt):
            p.step("tensor", (lambda tt_: lambda eng: eng.transpose(
                pb[7][0:8, 0:128], wfin[:, tt_ * 8:(tt_ + 1) * 8], sb["ident"][:]))(tt),
                   reads=["wfin", "ident"], writes=["pb7"], tag="e_wt%d_%d" % (hh, tt))
            p.step("vector", (lambda tt_: lambda eng: eng.tensor_copy(
                wT[0:8, tt_ * 128:(tt_ + 1) * 128], pb[7][0:8, 0:128]))(tt),
                   reads=["pb7"], writes=["wT"], tag="e_we%d_%d" % (hh, tt))

        NTB = TH // 512 if TH >= 512 else 1
        TBW = min(512, TH)
        for e in range(NE):
            p.step("sync", (lambda e_: lambda eng: eng.dma_start(
                out=wrow[:], in_=wT[e_:e_ + 1, :]))(e),
                reads=["wT"], writes=["wrow"], dma=True, tag="e_wr%d_%d" % (hh, e))

            def em_wb(eng):
                inst = None
                for tb in range(NTB):
                    inst = eng.matmul(pb[6][:, 0:TBW], sb["onesrow"][:],
                                      wrow[0:1, tb * TBW:(tb + 1) * TBW], start=True, stop=True)
                return inst
            for tb in range(NTB):
                bk = 6 + (tb % 2)
                p.step("tensor", (lambda tb_, bk_: lambda eng: eng.matmul(
                    pb[bk_][:, 0:TBW], sb["onesrowb"][:],
                    wrow[0:1, tb_ * TBW:(tb_ + 1) * TBW], start=True, stop=True))(tb, bk),
                    reads=["wrow", "onesrowb"], writes=["pb%d" % bk], tag="e_wb%d_%d_%d" % (hh, e, tb))
                p.step("vector", (lambda tb_, bk_: lambda eng: eng.tensor_copy(
                    wbt[:, tb_ * TBW:(tb_ + 1) * TBW], pb[bk_][:, 0:TBW]))(tb, bk),
                    reads=["pb%d" % bk], writes=["wbt"], tag="e_we2%d_%d_%d" % (hh, e, tb))

            for f in range(16):
                par = f % 2
                srcg = dr["Wg"][e].rearrange("(dt p) n -> p dt n", p=128)[:, :, f * 128:(f + 1) * 128]
                srcu = dr["Wu"][e].rearrange("(dt p) n -> p dt n", p=128)[:, :, f * 128:(f + 1) * 128]
                for q in range(2):
                    p.step("gpsimd", (lambda s_, d_, q_: lambda eng: eng.dma_start(
                        out=d_[:].rearrange("p (dt c) -> p dt c", c=128)[:, q_ * 8:(q_ + 1) * 8, :],
                        in_=s_[:, q_ * 8:(q_ + 1) * 8, :]))(srcg, wgb[par], q),
                        writes=["wgb%d" % par], dma=True, tag="e_lg%d_%d_%d_%d" % (hh, e, f, q))
                    p.step("gpsimd", (lambda s_, d_, q_: lambda eng: eng.dma_start(
                        out=d_[:].rearrange("p (dt c) -> p dt c", c=128)[:, q_ * 8:(q_ + 1) * 8, :],
                        in_=s_[:, q_ * 8:(q_ + 1) * 8, :]))(srcu, wub[par], q),
                        writes=["wub%d" % par], dma=True, tag="e_lu%d_%d_%d_%d" % (hh, e, f, q))
                gb = [par * 4, par * 4 + 1]
                ub = [2 + par * 4 - par * 3, 0]
                gb = [0 + par * 4, 1 + par * 4]
                ubk = [2 + par * 4, 3 + par * 4]

                def em_gu(par_, f_):
                    def em(eng):
                        inst = None
                        for d in range(16):
                            lg = wgb[par_][:, d * 128:(d + 1) * 128]
                            lu = wub[par_][:, d * 128:(d + 1) * 128]
                            for tb in range(NTB):
                                inst = eng.matmul(pb[par_ * 4][:, 0:TBW] if tb == 0 else pb[par_ * 4 + 1][:, 0:TBW],
                                                  lg,
                                                  ln2x[:].rearrange("p (dt t) -> p dt t", t=TH)[:, d, tb * TBW:(tb + 1) * TBW],
                                                  start=(d == 0), stop=(d == 15))
                            for tb in range(NTB):
                                inst = eng.matmul(pb[par_ * 4 + 2][:, 0:TBW] if tb == 0 else pb[par_ * 4 + 3][:, 0:TBW],
                                                  lu,
                                                  ln2x[:].rearrange("p (dt t) -> p dt t", t=TH)[:, d, tb * TBW:(tb + 1) * TBW],
                                                  start=(d == 0), stop=(d == 15))
                        return inst
                    return em
                rbanks = ["pb%d" % (par * 4 + i) for i in range(4)]
                p.step("tensor", em_gu(par, f),
                       reads=["wgb%d" % par, "wub%d" % par, "ln2x"],
                       writes=rbanks, tag="e_gu%d_%d_%d" % (hh, e, f))

                def em_sg(par_):
                    def em(eng):
                        inst = None
                        for tb in range(NTB):
                            inst = eng.activation(sG[:, tb * TBW:(tb + 1) * TBW],
                                                  pb[par_ * 4 + min(tb, 1)][:, 0:TBW], AF.Silu)
                        return inst
                    return em
                p.step("scalar", em_sg(par), reads=rbanks[:2], writes=["sG"],
                       tag="e_sg%d_%d_%d" % (hh, e, f))

                def em_h(par_, f_):
                    def em(eng):
                        inst = None
                        for tb in range(NTB):
                            hslice = Hb[:, f_ * TH + tb * TBW: f_ * TH + (tb + 1) * TBW]
                            inst = eng.tensor_tensor(hslice, pb[par_ * 4 + 2 + min(tb, 1)][:, 0:TBW],
                                                     sG[:, tb * TBW:(tb + 1) * TBW], ALU.mult)
                            inst = eng.tensor_tensor(hslice, hslice,
                                                     wbt[:, tb * TBW:(tb + 1) * TBW], ALU.mult)
                        return inst
                    return em
                p.step("vector", em_h(par, f), reads=rbanks[2:] + ["sG", "wbt"],
                       writes=["Hb"], tag="e_h%d_%d_%d" % (hh, e, f))

            for dmt in range(16):
                par = dmt % 2
                srcd = dr["Wd"][e].rearrange("(ft p) n -> p ft n", p=128)[:, :, dmt * 128:(dmt + 1) * 128]
                for q in range(2):
                    p.step("gpsimd", (lambda s_, d_, q_: lambda eng: eng.dma_start(
                        out=d_[:].rearrange("p (ft c) -> p ft c", c=128)[:, q_ * 8:(q_ + 1) * 8, :],
                        in_=s_[:, q_ * 8:(q_ + 1) * 8, :]))(srcd, wdb[par], q),
                        writes=["wdb%d" % par], dma=True, tag="e_ld%d_%d_%d_%d" % (hh, e, dmt, q))

                def em_dmm(par_, dmt_):
                    def em(eng):
                        inst = None
                        for ft in range(16):
                            lw = wdb[par_][:, ft * 128:(ft + 1) * 128]
                            for tb in range(NTB):
                                inst = eng.matmul(pb[par_ * 4 + tb][:, 0:TBW], lw,
                                                  Hb[:].rearrange("p (ft t) -> p ft t", t=TH)[:, ft, tb * TBW:(tb + 1) * TBW],
                                                  start=(ft == 0), stop=(ft == 15))
                        return inst
                    return em
                rb2 = ["pb%d" % (par * 4 + tb) for tb in range(NTB)]
                p.step("tensor", em_dmm(par, dmt), reads=["wdb%d" % par, "Hb"],
                       writes=rb2, tag="e_dm%d_%d_%d" % (hh, e, dmt))

                def em_acc(par_, dmt_):
                    def em(eng):
                        inst = None
                        for tb in range(NTB):
                            xs = xnew[:, dmt_ * TH + tb * TBW: dmt_ * TH + (tb + 1) * TBW]
                            inst = eng.tensor_tensor(xs, xs, pb[par_ * 4 + tb][:, 0:TBW], ALU.add)
                        return inst
                    return em
                p.step("vector", em_acc(par, dmt), reads=rb2 + ["xnew"],
                       writes=["xnew"], tag="e_ac%d_%d_%d" % (hh, e, dmt))

        for dm in range(16):
            dst = self.out[dm * 128:(dm + 1) * 128, t0h:t0h + TH]
            p.step("sync", (lambda d_, dm_: lambda eng: eng.dma_start(
                out=d_, in_=xnew[:, dm_ * TH:(dm_ + 1) * TH]))(dst, dm),
                reads=["xnew"], dma=True, tag="e_out%d_%d" % (hh, dm))

    def emit(self):
        nc, p = self.nc, self.p
        import contextlib
        stack = contextlib.ExitStack()
        sems, dsems = {}, {}
        for e in ENGINES:
            sems[e] = stack.enter_context(nc.semaphore("sem_" + e))
        for e in ("sync", "gpsimd", "scalar"):
            dsems[e] = stack.enter_context(nc.semaphore("dsem_" + e))
        p.finalize()
        with stack:
            with nc.Block() as block:
                @block.sync
                def _(eng):
                    p.emit_engine("sync", eng, sems, dsems)
                    nd = sum(1 for s in p.steps["sync"] if s.dma)
                    if nd:
                        eng.wait_ge(dsems["sync"], 16 * nd)

                @block.gpsimd
                def _(eng):
                    p.emit_engine("gpsimd", eng, sems, dsems)
                    nd = sum(1 for s in p.steps["gpsimd"] if s.dma)
                    if nd:
                        eng.wait_ge(dsems["gpsimd"], 16 * nd)

                @block.tensor
                def _(eng):
                    p.emit_engine("tensor", eng, sems, dsems)

                @block.vector
                def _(eng):
                    p.emit_engine("vector", eng, sems, dsems)

                @block.scalar
                def _(eng):
                    p.emit_engine("scalar", eng, sems, dsems)
                    nd = sum(1 for s in p.steps["scalar"] if s.dma)
                    if nd:
                        eng.wait_ge(dsems["scalar"], 16 * nd)


def host_inputs(T, inputs, core):
    cons = make_consts(T)
    m = {
        "xT": np.ascontiguousarray(inputs["L0_attn_x"][core, :T, :].T),
        "Wqkv": inputs["L0_attn_Wqkv"],
        "Wz": inputs["L0_attn_Wz"],
        "Wab": np.ascontiguousarray(
            np.concatenate([inputs["L0_attn_Wa"], inputs["L0_attn_Wb"]], axis=1)),
        "convw": inputs["L0_attn_conv_w"],
        "A_log": inputs["L0_attn_A_log"],
        "dt_bias": inputs["L0_attn_dt_bias"],
        "norm_w": inputs["L0_attn_norm_w"],
        "Wout": inputs["L0_attn_Wout"],
        "Wg": inputs["L0_ffn_Wg"],
        "Wu": inputs["L0_ffn_Wu"],
        "Wd": inputs["L0_ffn_Wd"],
        "Wr": inputs["L0_ffn_Wr"],
        "temperature": inputs["L0_ffn_temperature"],
        "cu_seqlens": inputs["L0_attn_cu_seqlens"].astype(np.int32),
    }
    m.update(cons)
    return m


def kernel(**inputs):
    T = 2048
    k = Ker(T)
    nc = k.build()
    in_maps = [host_inputs(T, inputs, c) for c in range(NB)]
    res = run_bass_kernel_spmd(nc, in_maps, core_ids=list(range(NB)))
    outs = [np.ascontiguousarray(r["out"].T) for r in res.results]
    return np.stack(outs, axis=0)


# version 23 (final)
# speedup vs baseline: 1.0612x; 1.0258x over previous
import numpy as np
import concourse.bass as bass
import concourse.mybir as mybir
from concourse.bass_utils import run_bass_kernel_spmd

F32 = mybir.dt.float32
BF16 = mybir.dt.bfloat16
I32 = mybir.dt.int32
AF = mybir.ActivationFunctionType
ALU = mybir.AluOpType
AX = mybir.AxisListType

D = 2048
NB = 8
HK, HV, DK, DV = 16, 32, 128, 128
KC = 4
NE, FF = 8, 2048
C = 128
EPS = 1e-6
LNEPS = 1e-5
BIG = 30000.0
BIGM = 1e7

ENGINES = ("tensor", "vector", "scalar", "gpsimd", "sync")
_DTB = {F32: 4, BF16: 2, I32: 4}


class _Step:
    __slots__ = ("engine", "emit", "deps", "dma", "idx", "tag", "need_inc",
                 "inc_idx", "dma_idx")

    def __init__(self, engine, emit, dma, tag):
        self.engine = engine
        self.emit = emit
        self.dma = dma
        self.tag = tag
        self.deps = []
        self.need_inc = False
        self.inc_idx = None
        self.dma_idx = None


class Prog:

    def __init__(self):
        self.steps = {e: [] for e in ENGINES}
        self.last_w = {}
        self.readers = {}
        self.pending_bar = {e: [] for e in ENGINES}

    def step(self, engine, emit, reads=(), writes=(), dma=False, tag="",
             sereads=()):
        st = _Step(engine, emit, dma, tag)
        deps = set()
        forced = set()
        for r in reads:
            w = self.last_w.get(r)
            if w is not None:
                deps.add(w)
        for r in sereads:
            w = self.last_w.get(r)
            if w is not None:
                deps.add(w)
                forced.add(w)
        for w_ in writes:
            pw = self.last_w.get(w_)
            if pw is not None:
                deps.add(pw)
            for rd in self.readers.get(w_, ()):
                deps.add(rd)
        for d in self.pending_bar[engine]:
            deps.add(d)
        self.pending_bar[engine] = []
        st.deps = [d for d in deps
                   if d in forced or not (d.engine == engine and (d.dma == dma))]
        st.idx = len(self.steps[engine])
        self.steps[engine].append(st)
        for r in list(reads) + list(sereads):
            self.readers.setdefault(r, []).append(st)
        for w_ in writes:
            self.last_w[w_] = st
            self.readers[w_] = []
        return st

    def barrier(self):
        sigs = []
        for e in ENGINES:
            comp = next((s for s in reversed(self.steps[e]) if not s.dma), None)
            dmas = next((s for s in reversed(self.steps[e]) if s.dma), None)
            if comp is not None:
                sigs.append(comp)
            if dmas is not None:
                sigs.append(dmas)
        for e in ENGINES:
            self.pending_bar[e] = [s for s in sigs
                                   if not (s.engine == e and not s.dma)]

    def finalize(self):
        for e in ENGINES:
            for st in self.steps[e]:
                for d in st.deps:
                    d.need_inc = True
        for e in ENGINES:
            ninc = 0
            ndma = 0
            for st in self.steps[e]:
                if st.dma:
                    ndma += 1
                    st.dma_idx = ndma
                elif st.need_inc:
                    ninc += 1
                    st.inc_idx = ninc

    def emit_engine(self, engine, eng, sems, dsems):
        waited = {}
        for st in self.steps[engine]:
            for d in st.deps:
                if d.dma:
                    sem, val = dsems[d.engine], 16 * d.dma_idx
                else:
                    sem, val = sems[d.engine], d.inc_idx
                    assert val is not None, (st.tag, d.tag)
                key = id(sem)
                if waited.get(key, -1) < val:
                    eng.wait_ge(sem, val)
                    waited[key] = val
            inst = st.emit(eng)
            if st.dma:
                inst.then_inc(dsems[engine], 16)
            elif st.need_inc:
                inst.then_inc(sems[engine], 1)


def _np_cm384():
    t = np.arange(128)[:, None]
    f = np.arange(128)[None, :]
    s0 = np.where(f >= t, BIGM, 0.0)
    s1 = np.where(f <= t, -BIGM, 0.0)
    s2 = np.where(f < t, -BIGM, 0.0)
    return np.concatenate([s0, s1, s2], axis=1).astype(np.float32)


def make_consts(T):
    j = np.arange(128)[:, None]
    t = np.arange(128)[None, :]
    return {
        "c_ident": np.eye(128, dtype=np.float32),
        "c_cum": (j <= t).astype(np.float32),
        "c_rev": (j > t).astype(np.float32),
        "c_cm384": _np_cm384(),
        "c_onesrow": np.ones((1, 128), np.float32),
        "c_onescol": np.ones((128, 1), np.float32),
        "c_e127": (np.arange(128) == 127).astype(np.float32)[:, None],
        "c_iota": np.arange(T + 8, dtype=np.float32)[None, :],
        "c_iota8": np.tile(np.arange(8, dtype=np.float32), (128, 1)).copy(),
    }


class Ker:
    def __init__(self, T=2048, taps=(), phases="abcde"):
        self.phases = phases
        self.T = T
        self.nchunk = T // C
        self.TH = T // 2
        nc = bass.Bass("TRN2", target_bir_lowering=False, debug=False,
                       detect_race_conditions=False)
        self.nc = nc
        self.p = Prog()
        self.consts_np = make_consts(T)
        self.dr = {}
        self.sb = {}
        self.taps = dict()
        T_ = T

        def din(name, shape, dt=F32):
            self.dr[name] = nc.dram_tensor(name, list(shape), dt,
                                           kind="ExternalInput").ap()

        din("xT", [D, T_])
        din("Wqkv", [D, 4 * D])
        din("Wz", [D, 2 * D])
        din("Wab", [D, 64])
        din("convw", [4 * D, KC])
        din("A_log", [HV])
        din("dt_bias", [HV])
        din("norm_w", [DV])
        din("Wout", [2 * D, D])
        din("Wg", [NE, D, FF])
        din("Wu", [NE, D, FF])
        din("Wd", [NE, FF, D])
        din("Wr", [D, NE])
        din("temperature", [1])
        din("cu_seqlens", [5], I32)
        for k, v in self.consts_np.items():
            din(k, v.shape, F32)

        self.out = nc.dram_tensor("out", [D, T_], F32, kind="ExternalOutput").ap()
        self.qkv_s = nc.dram_tensor("qkv_s", [4 * D, T_], BF16, kind="Internal").ap()
        self.z_s = nc.dram_tensor("z_s", [T_, 2 * D], BF16, kind="Internal").ap()
        self.y_s = nc.dram_tensor("y_s", [2 * D, T_], BF16, kind="Internal").ap()
        self.ln2_s = nc.dram_tensor("ln2_s", [D, T_], BF16, kind="Internal").ap()
        for name, shape, dt in taps:
            self.taps[name] = nc.dram_tensor("tap_" + name, list(shape), dt,
                                             kind="ExternalOutput").ap()

        self.pb = [nc.alloc_psum_tensor("pb%d" % i, [128, 512], F32)
                   for i in range(8)]
        self._pcur = 16512
        self._top = 229344

    def persist(self, name, shape, dt):
        nbytes = int(np.prod(shape[1:])) * _DTB[dt]
        off = (self._pcur + 31) // 32 * 32
        t = self.nc.alloc_sbuf_tensor_at(name, shape, dt, offset=off)
        self._pcur = off + nbytes
        assert self._pcur <= self._top, (name, self._pcur)
        self.sb[name] = t
        return t

    def freeze_persist(self):
        self._abase = (self._pcur + 255) // 256 * 256

    def at(self, name, shape, dt, off):
        nbytes = int(np.prod(shape[1:])) * _DTB[dt]
        a = self._abase + off
        assert a + nbytes <= self._top, (name, a + nbytes, self._top)
        t = self.nc.alloc_sbuf_tensor_at(name, shape, dt, offset=a)
        self.sb[name] = t
        return t

    def build(self):
        self.alloc_persist()
        self.ph0()
        self.p.barrier()
        self.ph_a()
        self.p.barrier()
        self.ph_b()
        self.p.barrier()
        if "c" not in self.phases:
            self.emit()
            return self.nc
        self.ph_c()
        self.p.barrier()
        if "d" not in self.phases:
            self.emit()
            return self.nc
        for hh in range(2):
            self.ph_d(hh)
            self.p.barrier()
            self.ph_e(hh)
            self.p.barrier()
        for name, src in [("qkv_s", self.qkv_s), ("z_s", self.z_s),
                          ("y_s", self.y_s), ("ln2_s", self.ln2_s)]:
            if name in self.taps:
                self.p.step("sync", (lambda d_, s_: lambda eng: eng.dma_start(out=d_, in_=s_))(
                    self.taps[name], src), dma=True, tag="tap_" + name)
        self.emit()
        return self.nc

    def alloc_persist(self):
        P = self.persist
        T = self.T
        P("ident", [128, 128], F32)
        P("identb", [128, 128], BF16)
        P("cumt", [128, 128], F32)
        P("revt", [128, 128], F32)
        P("cm384", [128, 384], F32)
        P("onesrow", [1, 128], F32)
        P("onescol", [128, 1], F32)
        P("onescolb", [128, 1], BF16)
        P("onesrowb", [1, 128], BF16)
        P("e127", [128, 1], F32)
        P("iota8", [128, 8], F32)
        P("sr", [1, T + 8], BF16)
        P("msk", [128, 3 * (T + 8)], BF16)
        P("normwb", [128, DV], F32)
        P("abg", [128, T], F32)
        P("cw", [128, 64 * KC], F32)
        P("negexpa", [32, 1], F32)
        P("dtbias", [32, 1], F32)
        P("tmisc", [1, 16], F32)
        P("cvals", [128, 8], F32)
        P("iota8b", [128, 8], F32)
        P("neg18", [128, 8], F32)
        self.freeze_persist()

    def ph0(self):
        nc, p, sb, dr = self.nc, self.p, self.sb, self.dr
        T = self.T
        W = T + 8
        t_iota = self.at("t_iota", [1, W], F32, 110 * 1024)
        t_cu = self.at("t_cu", [1, 8], F32, 119 * 1024)
        t_cui = self.at("t_cui", [1, 8], I32, int(119.5 * 1024))
        t_row = self.at("t_row", [1, W], F32, 120 * 1024)
        t_sr32 = self.at("t_sr32", [1, W], F32, 129 * 1024)
        t_ns = self.at("t_ns", [1, W], F32, 138 * 1024)
        t_nw = self.at("t_nw", [1, DV], F32, 147 * 1024)
        t_al = self.at("t_al", [64, 1], F32, int(147.5 * 1024))
        t_mrow = self.at("t_mrow", [1, 3 * W], F32, 148 * 1024)

        for nm, dst in [("c_ident", sb["ident"]), ("c_cum", sb["cumt"]),
                        ("c_rev", sb["revt"]), ("c_cm384", sb["cm384"]),
                        ("c_onesrow", sb["onesrow"]), ("c_onescol", sb["onescol"]),
                        ("c_e127", sb["e127"]), ("c_iota8", sb["iota8"]),
                        ("c_iota", t_iota)]:
            p.step("sync",
                   (lambda dst_, src_: lambda eng: eng.dma_start(out=dst_[:], in_=src_))(dst, dr[nm]),
                   writes=[dst.name], dma=True, tag="ld_" + nm)
        for nm, dst in [("c_ident", sb["identb"]), ("c_onescol", sb["onescolb"]),
                        ("c_onesrow", sb["onesrowb"])]:
            p.step("gpsimd",
                   (lambda dst_, src_: lambda eng: eng.dma_start(out=dst_[:], in_=src_))(dst, dr[nm]),
                   writes=[dst.name], dma=True, tag="ldb_" + nm)
        p.step("sync", lambda eng: eng.dma_start(
            out=sb["cw"][:].rearrange("p (a k) -> p a k", k=KC),
            in_=dr["convw"].rearrange("(a p) k -> p a k", p=128)),
            writes=["cw"], dma=True, tag="ld_cw")
        p.step("sync", lambda eng: eng.dma_start(out=t_al[0:32, :], in_=dr["A_log"][:, None]),
               writes=["t_al"], dma=True, tag="ld_alog")
        p.step("sync", lambda eng: eng.dma_start(out=t_al[32:64, :], in_=dr["dt_bias"][:, None]),
               writes=["t_al"], dma=True, tag="ld_dtb")
        p.step("sync", lambda eng: eng.dma_start(out=t_nw[:], in_=dr["norm_w"][None, :]),
               writes=["t_nw"], dma=True, tag="ld_nw")
        p.step("sync", lambda eng: eng.dma_start(out=sb["tmisc"][0:1, 0:1], in_=dr["temperature"][None, :]),
               writes=["tmisc"], dma=True, tag="ld_temp")
        p.step("sync", lambda eng: eng.dma_start(out=t_cui[0:1, 0:5], in_=dr["cu_seqlens"][None, :]),
               writes=["t_cui"], dma=True, tag="ld_cu")

        p.step("scalar", lambda eng: eng.activation(sb["negexpa"][:], t_al[0:32, :], AF.Exp),
               reads=["t_al"], writes=["negexpa"], tag="acoef_exp")
        p.step("vector", lambda eng: eng.tensor_scalar(sb["negexpa"][:], sb["negexpa"][:], -1.0, None, ALU.mult),
               reads=["negexpa"], writes=["negexpa"], tag="acoef_neg")
        p.step("scalar", lambda eng: eng.copy(sb["dtbias"][:], t_al[32:64, :]),
               reads=["t_al"], writes=["dtbias"], tag="acoef_dtb")
        p.step("vector", lambda eng: eng.tensor_copy(t_cu[0:1, 0:5], t_cui[0:1, 0:5]),
               reads=["t_cui"], writes=["t_cu"], tag="cu_f32")

        sr = sb["sr"]

        def em_starts(eng):
            eng.memset(t_sr32[:], 0.0)
            for i in range(4):
                eng.tensor_scalar(t_row[:], t_iota[:], t_cu[0:1, i:i + 1], None, ALU.is_equal)
                eng.tensor_tensor(t_sr32[:], t_sr32[:], t_row[:], ALU.max)
                eng.tensor_scalar(t_row[:], t_iota[:], float(T), t_cu[0:1, i:i + 1], ALU.add, ALU.is_equal)
                eng.tensor_tensor(t_sr32[:], t_sr32[:], t_row[:], ALU.max)
            eng.memset(t_sr32[0:1, T:W], 0.0)
            return eng.tensor_copy(sr[:], t_sr32[:])
        p.step("vector", em_starts, reads=["t_iota"], sereads=["t_cu"],
               writes=["t_sr32", "sr", "t_row"], tag="starts")

        def em_masks_rows(eng):
            eng.tensor_scalar(t_ns[:], t_sr32[:], -1.0, 1.0, ALU.mult, ALU.add)
            eng.memset(t_mrow[:], 1.0)
            eng.tensor_copy(t_mrow[0:1, 3:3 + T], t_ns[0:1, 1:T + 1])
            eng.tensor_tensor(t_mrow[0:1, W + 3:W + 3 + T], t_mrow[0:1, 3:3 + T], t_ns[0:1, 2:T + 2], ALU.mult)
            return eng.tensor_tensor(t_mrow[0:1, 2 * W + 3:2 * W + 3 + T], t_mrow[0:1, W + 3:W + 3 + T], t_ns[0:1, 3:T + 3], ALU.mult)
        p.step("vector", em_masks_rows, reads=["t_sr32"],
               writes=["t_ns", "t_mrow"], tag="maskrows")

        pb, msk = self.pb, sb["msk"]
        nburst = (3 * W + 511) // 512
        for ib in range(nburst):
            c0 = ib * 512
            c1 = min(c0 + 512, 3 * W)
            bk = ib % 4
            p.step("tensor",
                   (lambda c0_, c1_, bk_: lambda eng: eng.matmul(
                       pb[bk_][:, 0:c1_ - c0_], sb["onesrow"][:],
                       t_mrow[0:1, c0_:c1_], start=True, stop=True))(c0, c1, bk),
                   reads=["t_mrow", "onesrow"], writes=["pb%d" % bk],
                   tag="mskb_mm%d" % ib)
            p.step("vector",
                   (lambda c0_, c1_, bk_: lambda eng: eng.tensor_copy(
                       msk[:, c0_:c1_], pb[bk_][:, 0:c1_ - c0_]))(c0, c1, bk),
                   reads=["pb%d" % bk], writes=["msk"], tag="mskb_ev%d" % ib)

        p.step("tensor", lambda eng: eng.matmul(pb[4][:, 0:DV], sb["onesrow"][:],
                                                t_nw[0:1, :], start=True, stop=True),
               reads=["t_nw", "onesrow"], writes=["pb4"], tag="nw_mm")
        p.step("vector", lambda eng: eng.tensor_copy(sb["normwb"][:], pb[4][:, 0:DV]),
               reads=["pb4"], writes=["normwb"], tag="nw_ev")
        p.step("vector", lambda eng: eng.reciprocal(sb["tmisc"][0:1, 1:2], sb["tmisc"][0:1, 0:1]),
               reads=["tmisc"], writes=["tmisc"], tag="invtemp")

        def em_cvals(eng):
            cv = sb["cvals"]
            eng.memset(cv[:, 0:1], LNEPS)
            eng.memset(cv[:, 1:2], 1.0)
            eng.memset(cv[:, 2:3], EPS)
            return eng.memset(cv[:, 3:4], 0.0)
        p.step("vector", em_cvals, writes=["cvals"], tag="cvals")

        def em_misc8(eng):
            eng.tensor_scalar(sb["iota8b"][:], sb["iota8"][:], 100.0, None, ALU.add)
            eng.memset(sb["neg18"][:], -1.0)
            return eng.memset(sb["cvals"][:, 5:6], float(-1.0 / np.e))
        p.step("vector", em_misc8, reads=["iota8"], writes=["iota8b", "neg18", "cvals"],
               tag="misc8")
        p.step("tensor", lambda eng: eng.matmul(pb[5][:, 0:1], sb["onesrow"][:],
                                                sb["tmisc"][0:1, 1:2], start=True, stop=True),
               reads=["tmisc", "onesrow"], writes=["pb5"], tag="it_mm")
        p.step("vector", lambda eng: eng.tensor_copy(sb["cvals"][:, 4:5], pb[5][:, 0:1]),
               reads=["pb5"], writes=["cvals"], tag="it_ev")

    def ph_a(self):
        nc, p, sb, dr, pb = self.nc, self.p, self.sb, self.dr, self.pb
        T = self.T
        TC = min(512, T)
        ln1x = self.at("ln1x", [128, 16 * T], BF16, 0)
        xa = self.at("xa", [128, 16 * TC], F32, 64 * 1024)
        xbf = self.at("xbf", [128, 16 * TC], BF16, 96 * 1024)
        xsq = self.at("xsq", [128, 16 * TC], BF16, 112 * 1024)
        arow = self.at("arow", [1, 4 * TC], F32, 128 * 1024)
        abc = self.at("abc", [128, 2 * TC], F32, 137 * 1024)
        xT = dr["xT"]
        onescolb, onesrow = sb["onescolb"], sb["onesrow"]

        for tc in range(T // TC):
            t0 = tc * TC
            for half in range(2):
                s0 = half * 8
                src = xT.rearrange("(dt p) t -> p dt t", p=128)[:, s0:s0 + 8, t0:t0 + TC]
                dst = xa[:].rearrange("p (dt t) -> p dt t", t=TC)[:, s0:s0 + 8, :]
                p.step("sync",
                       (lambda s_, d_: lambda eng: eng.dma_start(out=d_, in_=s_))(src, dst),
                       writes=["xa"], dma=True, tag="a_ldx%d_%d" % (tc, half))
            p.step("vector", lambda eng: eng.tensor_copy(xbf[:], xa[:]),
                   reads=["xa"], writes=["xbf"], tag="a_cast%d" % tc)
            p.step("scalar", lambda eng: eng.activation(xsq[:], xbf[:], AF.Square),
                   reads=["xbf"], writes=["xsq"], tag="a_sq%d" % tc)

            def em_stats(src_t, bank):
                def em(eng):
                    inst = None
                    v = src_t[:].rearrange("p (dt t) -> p dt t", t=TC)
                    for d in range(16):
                        inst = eng.matmul(bank[0:1, 0:TC], onescolb[:],
                                          v[:, d, :], start=(d == 0), stop=(d == 15))
                    return inst
                return em
            p.step("tensor", em_stats(xbf, pb[0]), reads=["xbf", "onescolb"],
                   writes=["pb0"], tag="a_mm1_%d" % tc)
            p.step("tensor", em_stats(xsq, pb[1]), reads=["xsq", "onescolb"],
                   writes=["pb1"], tag="a_mm2_%d" % tc)

            def em_rows(eng):
                eng.tensor_scalar(arow[0:1, 0:TC], pb[0][0:1, 0:TC], 1.0 / D, None, ALU.mult)
                eng.tensor_scalar(arow[0:1, 3*TC:4*TC], pb[1][0:1, 0:TC], 1.0 / D, None, ALU.mult)
                eng.tensor_tensor(arow[0:1, TC:2*TC], arow[0:1, 0:TC], arow[0:1, 0:TC], ALU.mult)
                return eng.tensor_tensor(arow[0:1, 3*TC:4*TC], arow[0:1, 3*TC:4*TC],
                                         arow[0:1, TC:2*TC], ALU.subtract)
            p.step("vector", em_rows, reads=["pb0", "pb1"], writes=["arow"],
                   tag="a_rows%d" % tc)

            def em_rstd(eng):
                eng.activation(arow[0:1, 2*TC:3*TC], arow[0:1, 3*TC:4*TC], AF.Ln, bias=sb["cvals"][0:1, 0:1])
                return eng.activation(arow[0:1, TC:2*TC], arow[0:1, 2*TC:3*TC], AF.Exp, scale=-0.5)
            p.step("scalar", em_rstd, reads=["arow", "cvals"], writes=["arow"], tag="a_rstd%d" % tc)
            p.step("vector", lambda eng: eng.tensor_tensor(
                arow[0:1, 2*TC:3*TC], arow[0:1, 0:TC], arow[0:1, TC:2*TC], ALU.mult),
                reads=["arow"], writes=["arow"], tag="a_ms%d" % tc)
            p.step("tensor", lambda eng: eng.matmul(pb[2][:, 0:TC], onesrow[:],
                                                    arow[0:1, TC:2*TC], start=True, stop=True),
                   reads=["arow", "onesrow"], writes=["pb2"], tag="a_bs%d" % tc)
            p.step("tensor", lambda eng: eng.matmul(pb[3][:, 0:TC], onesrow[:],
                                                    arow[0:1, 2*TC:3*TC], start=True, stop=True),
                   reads=["arow", "onesrow"], writes=["pb3"], tag="a_bms%d" % tc)
            p.step("vector", lambda eng: eng.tensor_copy(abc[:, 0:TC], pb[2][:, 0:TC]),
                   reads=["pb2"], writes=["abc"], tag="a_evs%d" % tc)
            p.step("vector", lambda eng: eng.tensor_copy(abc[:, TC:2*TC], pb[3][:, 0:TC]),
                   reads=["pb3"], writes=["abc"], tag="a_evms%d" % tc)

            def em_apply(d, t0):
                def em(eng):
                    xs = xa[:, d * TC:(d + 1) * TC]
                    eng.tensor_tensor(xs, xs, abc[:, 0:TC], ALU.mult)
                    return eng.tensor_tensor(ln1x[:, d * T + t0: d * T + t0 + TC],
                                             xs, abc[:, TC:2*TC], ALU.subtract)
                return em
            for d in range(16):
                p.step("vector", em_apply(d, t0), reads=["abc", "xa"],
                       writes=["ln1x", "xa"], tag="a_app%d_%d" % (tc, d))

        if "ln1x" in self.taps:
            ap = self.taps["ln1x"]
            p.step("sync", lambda eng: eng.dma_start(out=ap, in_=ln1x[:]),
                   reads=["ln1x"], dma=True, tag="tap_ln1x")

    def ph_b(self):
        nc, p, sb, dr, pb = self.nc, self.p, self.sb, self.dr, self.pb
        T = self.T
        TB = min(512, T)
        ntb = T // TB
        ln1x = self.sb["ln1x"]
        wqb = [self.at("wqb%d" % i, [128, 16 * 128], BF16, 64 * 1024 + i * 4096)
               for i in range(2)]
        qn = [self.at("qn%d" % i, [128, T], BF16, 72 * 1024 + i * 2 * T)
              for i in range(2)]
        wzb = self.at("wzb", [128, 16 * 2048], BF16, 80 * 1024)
        zev = [self.at("zev%d" % i, [128, 2 * D], BF16, 144 * 1024 + i * 4 * D)
               for i in range(2)]

        for n in range(4 * D // 128):
            par = n % 2
            wb = wqb[par]
            src = dr["Wqkv"].rearrange("(dt p) n -> p dt n", p=128)
            for half in range(2):
                p.step("gpsimd",
                       (lambda n_, h_, wb_: lambda eng: eng.dma_start(
                           out=wb_[:].rearrange("p (dt c) -> p dt c", c=128)[:, h_ * 8:(h_ + 1) * 8, :],
                           in_=src[:, h_ * 8:(h_ + 1) * 8, n_ * 128:(n_ + 1) * 128]))(n, half, wb),
                       writes=["wqb%d" % par], dma=True, tag="b_wq%d_%d" % (n, half))

            def em_mm(wb_, par_):
                def em(eng):
                    inst = None
                    for d in range(16):
                        lhsT = wb_[:, d * 128:(d + 1) * 128]
                        for tb in range(ntb):
                            inst = eng.matmul(pb[par_ * 4 + tb][:, 0:TB], lhsT,
                                              ln1x[:].rearrange("p (dt t) -> p dt t", t=T)[:, d, tb * TB:(tb + 1) * TB],
                                              start=(d == 0), stop=(d == 15))
                    return inst
                return em
            p.step("tensor", em_mm(wb, par),
                   reads=["wqb%d" % par, "ln1x"],
                   writes=["pb%d" % (par * 4 + tb) for tb in range(ntb)],
                   tag="b_qmm%d" % n)

            def em_ev(par_, n_):
                def em(eng):
                    inst = None
                    for tb in range(ntb):
                        inst = eng.copy(qn[par_][:, tb * TB:(tb + 1) * TB],
                                        pb[par_ * 4 + tb][:, 0:TB])
                    return inst
                return em
            p.step("scalar", em_ev(par, n),
                   reads=["pb%d" % (par * 4 + tb) for tb in range(ntb)],
                   writes=["qn%d" % par], tag="b_qev%d" % n)
            p.step("sync",
                   (lambda n_, par_: lambda eng: eng.dma_start(
                       out=self.qkv_s[n_ * 128:(n_ + 1) * 128, :], in_=qn[par_][:]))(n, par),
                   reads=["qn%d" % par], dma=True, tag="b_qst%d" % n)

        NH = 2 * D // 2
        for nh in range(2):
            srcz = dr["Wz"].rearrange("(dt p) n -> p dt n", p=128)
            for q in range(4):
                p.step("gpsimd",
                       (lambda nh_, q_: lambda eng: eng.dma_start(
                           out=wzb[:].rearrange("p (dt c) -> p dt c", c=NH)[:, q_ * 4:(q_ + 1) * 4, :],
                           in_=srcz[:, q_ * 4:(q_ + 1) * 4, nh_ * NH:(nh_ + 1) * NH]))(nh, q),
                       writes=["wzb"], dma=True, tag="b_wz%d_%d" % (nh, q))
            for tt in range(T // 128):
                par = tt % 2

                def em_zmm(tt_, par_):
                    def em(eng):
                        inst = None
                        for d in range(16):
                            lhsT = ln1x[:, d * T + tt_ * 128: d * T + (tt_ + 1) * 128]
                            for nb in range(NH // 512):
                                inst = eng.matmul(pb[par_ * 4 + nb][:, 0:512], lhsT,
                                                  wzb[:, d * NH + nb * 512: d * NH + (nb + 1) * 512],
                                                  start=(d == 0), stop=(d == 15))
                        return inst
                    return em
                p.step("tensor", em_zmm(tt, par),
                       reads=["wzb", "ln1x"],
                       writes=["pb%d" % (par * 4 + nb) for nb in range(NH // 512)],
                       tag="b_zmm%d_%d" % (nh, tt))

                def em_zev(par_):
                    def em(eng):
                        inst = None
                        for nb in range(NH // 512):
                            inst = eng.copy(zev[par_][:, nb * 512:(nb + 1) * 512],
                                            pb[par_ * 4 + nb][:, 0:512])
                        return inst
                    return em
                p.step("scalar", em_zev(par),
                       reads=["pb%d" % (par * 4 + nb) for nb in range(NH // 512)],
                       writes=["zev%d" % par], tag="b_zev%d_%d" % (nh, tt))
                p.step("sync",
                       (lambda nh_, tt_, par_: lambda eng: eng.dma_start(
                           out=self.z_s[tt_ * 128:(tt_ + 1) * 128, nh_ * NH:(nh_ + 1) * NH],
                           in_=zev[par_][:, 0:NH]))(nh, tt, par),
                       reads=["zev%d" % par], dma=True, tag="b_zst%d_%d" % (nh, tt))

        p.barrier()
        wabb = self.at("wabb", [128, 16 * 64], BF16, 80 * 1024)
        sc1 = self.at("sc1", [32, T], F32, 82 * 1024)
        sc2 = self.at("sc2", [32, T], F32, 82 * 1024 + 4 * T)
        sc3 = self.at("sc3", [32, T], F32, 82 * 1024 + 8 * T)
        abt = self.at("abt", [64, T], F32, 82 * 1024 + 12 * T)
        p.step("gpsimd", lambda eng: eng.dma_start(
            out=wabb[:].rearrange("p (dt c) -> p dt c", c=64),
            in_=dr["Wab"].rearrange("(dt p) n -> p dt n", p=128)),
            writes=["wabb"], dma=True, tag="b_wab")

        def em_abmm(eng):
            inst = None
            for d in range(16):
                lhsT = wabb[:, d * 64:(d + 1) * 64]
                for tb in range(ntb):
                    inst = eng.matmul(pb[4 + tb][0:64, 0:TB], lhsT,
                                      ln1x[:].rearrange("p (dt t) -> p dt t", t=T)[:, d, tb * TB:(tb + 1) * TB],
                                      start=(d == 0), stop=(d == 15))
            return inst
        p.step("tensor", em_abmm, reads=["wabb", "ln1x"],
               writes=["pb%d" % (4 + tb) for tb in range(ntb)], tag="b_abmm")

        def em_abev(eng):
            inst = None
            for tb in range(ntb):
                inst = eng.copy(abt[:, tb * TB:(tb + 1) * TB], pb[4 + tb][0:64, 0:TB])
            return inst
        p.step("scalar", em_abev, reads=["pb%d" % (4 + tb) for tb in range(ntb)],
               writes=["abt"], tag="b_abev")

        a_, b_ = abt[0:32, :], abt[32:64, :]
        abg = sb["abg"]
        cv = sb["cvals"]

        def em_g1(eng):
            return eng.tensor_scalar(sc1[:], a_, sb["dtbias"][:], None, ALU.add)
        p.step("vector", em_g1, reads=["abt", "dtbias"], writes=["sc1"], tag="b_g1")

        def em_g2(eng):
            eng.activation(sc2[:], sc1[:], AF.Abs)
            eng.activation(sc3[:], sc2[:], AF.Exp, scale=-1.0)
            return eng.activation(sc2[:], sc3[:], AF.Ln, bias=cv[0:32, 1:2])
        p.step("scalar", em_g2, reads=["sc1", "cvals"], writes=["sc2", "sc3"], tag="b_g2")

        def em_g3(eng):
            eng.tensor_scalar(sc3[:], sc1[:], 0.0, None, ALU.max)
            eng.tensor_tensor(sc3[:], sc3[:], sc2[:], ALU.add)
            return eng.tensor_scalar(abg[0:32, :], sc3[:], sb["negexpa"][:], None, ALU.mult)
        p.step("vector", em_g3, reads=["sc1", "sc2"], sereads=["negexpa"],
               writes=["sc3", "abg"], tag="b_g3")

        def em_l1(eng):
            return eng.tensor_scalar(sc1[:], b_, -1.0, None, ALU.mult)
        p.step("vector", em_l1, reads=["abt"], writes=["sc1"], tag="b_l1")

        def em_l2(eng):
            eng.activation(sc2[:], sc1[:], AF.Abs)
            eng.activation(sc3[:], sc2[:], AF.Exp, scale=-1.0)
            return eng.activation(sc2[:], sc3[:], AF.Ln, bias=cv[0:32, 1:2])
        p.step("scalar", em_l2, reads=["sc1", "cvals"], writes=["sc2", "sc3"], tag="b_l2")

        def em_l3(eng):
            eng.tensor_scalar(sc3[:], sc1[:], 0.0, None, ALU.max)
            eng.tensor_tensor(sc3[:], sc3[:], sc2[:], ALU.add)
            return eng.tensor_scalar(abg[32:64, :], sc3[:], -1.0, None, ALU.mult)
        p.step("vector", em_l3, reads=["sc1", "sc2"], writes=["sc3", "abg"], tag="b_l3")

        if "abg" in self.taps:
            ap = self.taps["abg"]
            p.step("sync", lambda eng: eng.dma_start(out=ap, in_=abg[0:64, :]),
                   reads=["abg"], dma=True, tag="tap_abg")

    def ph_c(self):
        nc, p, sb = self.nc, self.p, self.sb
        T = self.T
        nchunk = self.nchunk
        at = self.at
        bufs = {}
        bufs["qkvc"] = at("qkvc", [128, 64 * 131], BF16, 0)
        bufs["mixed"] = at("mixed", [128, 64 * 128], BF16, 17 * 1024)
        bufs["zsil"] = [at("zsil%d" % i, [128, 2 * D], BF16, 34 * 1024 + i * 4 * D)
                        for i in range(2)]
        bufs["Sst"] = at("Sst", [128, 32 * 128], BF16, 51 * 1024)
        bufs["flat"] = at("flat", [1, 96 * 128], F32, 60 * 1024)
        bufs["y1"] = at("y1", [128, 32 * 128], BF16, 109 * 1024)
        bufs["yA"] = at("yA", [128, 32 * 128], BF16, 118 * 1024)
        bufs["solT"] = [at("solT%d" % i, [128, 14 * 128], BF16, 127 * 1024 + i * 3584)
                       for i in range(4)]
        bufs["Usb"] = [at("Usb%d" % i, [128, 2 * 128], BF16, 141 * 1024 + i * 512)
                       for i in range(4)]
        bufs["Rt"] = [at("Rt%d" % i, [128, 128], BF16, 143 * 1024 + i * 256)
                       for i in range(4)]
        bufs["VBs"] = [at("VBs%d" % i, [128, 128], BF16, 144 * 1024 + i * 256)
                       for i in range(4)]
        bufs["KBt"] = [at("KBt%d" % i, [128, 128], BF16, 145 * 1024 + i * 256)
                       for i in range(4)]
        bufs["KSe"] = [at("KSe%d" % i, [128, 128], BF16, 146 * 1024 + i * 256)
                       for i in range(4)]
        bufs["QSe"] = [at("QSe%d" % i, [128, 128], F32, 147 * 1024 + i * 512)
                       for i in range(4)]
        bufs["Ot"] = [at("Ot%d" % i, [128, 128], F32, 149 * 1024 + i * 512)
                       for i in range(4)]
        bufs["Esb"] = [at("Esb%d" % i, [128, 3 * 128], F32, 151 * 1024 + i * 1536)
                       for i in range(4)]
        bufs["r3m"] = [at("r3m%d" % i, [128, 3 * 128], F32, 157 * 1024 + i * 1536)
                       for i in range(4)]
        bufs["sqb"] = [at("sqb%d" % i, [128, 128], BF16, 163 * 1024 + i * 256)
                       for i in range(4)]
        bufs["cvt"] = [at("cvt%d" % i, [128, 128], BF16, 164 * 1024 + 256 + i * 256) for i in range(3)]
        cols = []
        for i in range(2):
            cur = [165 * 1024 + i * 4352]

            def nx(nbytes):
                a = (cur[0] + 31) // 32 * 32
                cur[0] = a + nbytes
                return a
            cols.append({
                "gBe": at("gBe%d" % i, [128, 66], F32, nx(264)),
                "cumE": at("cumE%d" % i, [128, 66], F32, nx(264)),
                "revE": at("revE%d" % i, [128, 66], F32, nx(264)),
                "gp": at("gp%d" % i, [128, 32], F32, nx(128)),
                "dp": at("dp%d" % i, [128, 32], F32, nx(128)),
                "cs": at("cs%d" % i, [128, 96], F32, nx(384)),
                "nrA": at("nrA%d" % i, [128, 32], F32, nx(128)),
                "lnr": at("lnr%d" % i, [128, 32], F32, nx(128)),
                "earg": at("earg%d" % i, [128, 32], F32, nx(128)),
                "bcol": at("bcol%d" % i, [128, 32], F32, nx(128)),
                "bGam": at("bGam%d" % i, [128, 32], F32, nx(128)),
                "qGam": at("qGam%d" % i, [128, 32], F32, nx(128)),
                "Ktl": at("Ktl%d" % i, [128, 32], F32, nx(128)),
                "coefC": at("coefC%d" % i, [128, 32], F32, nx(128)),
                "msC": at("msC%d" % i, [128, 32], F32, nx(128)),
                "rstd": at("rstd%d" % i, [128, 32], F32, nx(128)),
                "gsA": at("gsA%d" % i, [66, 128], F32, nx(512)),
                "rowsA": at("rowsA%d" % i, [96, 128], F32, nx(512)),
                "crow": at("crow%d" % i, [1, 32], F32, nx(128)),
                "c1": at("c1%d" % i, [128, 2], F32, nx(32)),
            })
        bufs["cols"] = cols

        p.step("vector", lambda eng: eng.memset(bufs["Sst"][:], 0.0),
               writes=["Sst"], tag="c_zs")
        for i in range(2):
            p.step("vector", (lambda i_: lambda eng: eng.memset(cols[i_]["gsA"][:], 0.0))(i),
                   writes=["gsA%d" % i], tag="c_zgsa%d" % i)
        p.step("vector", lambda eng: eng.memset(
            bufs["qkvc"][:].rearrange("p (ti c) -> p ti c", c=131)[:, :, 0:3], 0.0),
            writes=["qkvc"], tag="c_zq")

        nh = min(HV, getattr(self, "maxheads", HV))
        prep = (self._chunk_loads(0, 0, bufs) + self._chunk_conv(0, 0, bufs)
                + self._chunk_machA(0, 0, bufs) + self._chunk_machB(0, 0, bufs))
        for st_ in prep:
            self.p.step(st_[0], st_[1], **st_[2])
        for c in range(nchunk):
            cp = c % 2
            if c + 1 < nchunk:
                nxt_early = (self._chunk_loads(c + 1, (c + 1) % 2, bufs)
                             + self._chunk_machA(c + 1, (c + 1) % 2, bufs))
                nxt_late = (self._chunk_conv(c + 1, (c + 1) % 2, bufs)
                            + self._chunk_machB(c + 1, (c + 1) % 2, bufs))
            else:
                nxt_early, nxt_late = [], []
            pending = [self._head_steps(c, cp, h, bufs) for h in range(nh)]
            cursors = {}
            active = []
            nexth = 0
            emitted = 0
            total = sum(len(g) for g in pending)
            nxt_i = 0
            nxt_start = 2 * total // 5
            while active or nexth < nh:
                if len(active) < 4 and nexth < nh:
                    active.append(nexth)
                    cursors[nexth] = 0
                    nexth += 1
                if not active:
                    break
                for hidx in list(active):
                    g = pending[hidx]
                    if cursors[hidx] < len(g):
                        eng_, emit_, kw_ = g[cursors[hidx]]
                        self.p.step(eng_, emit_, **kw_)
                        cursors[hidx] += 1
                        emitted += 1
                        if emitted > nxt_start and nxt_i < len(nxt_early):
                            st_ = nxt_early[nxt_i]
                            self.p.step(st_[0], st_[1], **st_[2])
                            nxt_i += 1
                    if cursors[hidx] >= len(g):
                        active.remove(hidx)
            while nxt_i < len(nxt_early):
                st_ = nxt_early[nxt_i]
                self.p.step(st_[0], st_[1], **st_[2])
                nxt_i += 1
            for st_ in nxt_late:
                self.p.step(st_[0], st_[1], **st_[2])
            self._chunk_tail(c, cp, bufs)

    def _chunk_loads(self, c, cp, bufs):
        steps = []

        class _P:
            @staticmethod
            def step(eng_, emit_, **kw):
                steps.append((eng_, emit_, kw))
        p = _P
        t0 = c * C
        qkvc, zsil = bufs["qkvc"], bufs["zsil"]
        lo = max(t0 - 3, 0)
        off = 3 if c == 0 else 0
        ncols = t0 + C - lo
        for q in range(4):
            src = self.qkv_s.rearrange("(ti p) t -> p ti t", p=128)[:, q * 16:(q + 1) * 16, lo:lo + ncols]
            dst = qkvc[:].rearrange("p (ti c) -> p ti c", c=131)[:, q * 16:(q + 1) * 16, off:off + ncols]
            p.step("sync", (lambda s_, d_: lambda eng: eng.dma_start(out=d_, in_=s_))(src, dst),
                   writes=["qkvc"], dma=True, tag="c_lq%d_%d" % (c, q))
        p.step("sync", (lambda t0_, cp_: lambda eng: eng.dma_start(
            out=zsil[cp_][:], in_=self.z_s[t0_:t0_ + C, :]))(t0, cp),
            writes=["zsil%d" % cp], dma=True, tag="c_lz%d" % c)
        p.step("scalar", (lambda cp_: lambda eng: eng.activation(
            zsil[cp_][:], zsil[cp_][:], AF.Silu))(cp),
            reads=["zsil%d" % cp], writes=["zsil%d" % cp], tag="c_silz%d" % c)
        return steps

    def _chunk_conv(self, c, cp, bufs):
        steps = []

        class _P:
            @staticmethod
            def step(eng_, emit_, **kw):
                steps.append((eng_, emit_, kw))
        p = _P
        sb = self.sb
        W = self.T + 8
        t0 = c * C
        qkvc, mixed, cvt = bufs["qkvc"], bufs["mixed"], bufs["cvt"]
        msk, cw = sb["msk"], sb["cw"]

        def em_conv(ti0, ti1, ci):
            def em(eng):
                inst = None
                for ti in range(ti0, ti1):
                    qv = qkvc[:, ti * 131: ti * 131 + 131]
                    mx = mixed[:, ti * 128:(ti + 1) * 128]
                    inst = eng.tensor_scalar(mx, qv[:, 3:131],
                                             cw[:, ti * KC + 3:ti * KC + 4], None, ALU.mult)
                    for s_ in (1, 2, 3):
                        mslice = msk[:, (s_ - 1) * W + 3 + t0 - s_:
                                     (s_ - 1) * W + 3 + t0 - s_ + C]
                        inst = eng.scalar_tensor_tensor(
                            cvt[ci][:], qv[:, 3 - s_:131 - s_],
                            cw[:, ti * KC + 3 - s_:ti * KC + 4 - s_],
                            mslice, ALU.mult, ALU.mult)
                        inst = eng.tensor_tensor(mx, mx, cvt[ci][:], ALU.add)
                return inst
            return em
        def em_convg(ti0, ti1, ci):
            def em(eng):
                inst = None
                for tb in range(ti0, ti1, 2):
                    pair = (tb, tb + 1)
                    qvs = [qkvc[:, ti * 131: ti * 131 + 131] for ti in pair]
                    mxs = [mixed[:, ti * 128:(ti + 1) * 128] for ti in pair]
                    cvs = [cvt[ci][:, 0:64].rearrange("p n -> p n") for _ in pair]
                    cvs = [cvt[1][:], cvt[2][:]]
                    for j in (0, 1):
                        inst = eng.tensor_tensor(
                            mxs[j], qvs[j][:, 3:131],
                            cw[:, pair[j] * KC + 3:pair[j] * KC + 4].to_broadcast([128, C]),
                            ALU.mult)
                    for s_ in (1, 2, 3):
                        mslice = msk[:, (s_ - 1) * W + 3 + t0 - s_:
                                     (s_ - 1) * W + 3 + t0 - s_ + C]
                        for j in (0, 1):
                            inst = eng.tensor_tensor(
                                cvs[j], qvs[j][:, 3 - s_:131 - s_],
                                cw[:, pair[j] * KC + 3 - s_:pair[j] * KC + 4 - s_].to_broadcast([128, C]),
                                ALU.mult)
                        for j in (0, 1):
                            inst = eng.tensor_tensor(cvs[j], cvs[j], mslice, ALU.mult)
                        for j in (0, 1):
                            inst = eng.tensor_tensor(mxs[j], mxs[j], cvs[j], ALU.add)
                return inst
            return em
        for grp in range(6):
            p.step("vector", em_conv(grp * 8, (grp + 1) * 8, 0),
                   reads=["qkvc", "cw", "msk"], writes=["mixed%d" % grp, "cvt0"],
                   tag="c_cv%d_%d" % (c, grp))
        for grp in range(6, 8):
            p.step("gpsimd", em_convg(grp * 8, (grp + 1) * 8, 1),
                   reads=["qkvc", "cw", "msk"], writes=["mixed%d" % grp, "cvt1", "cvt2"],
                   tag="c_cg%d_%d" % (c, grp))
        for grp in range(8):
            p.step("scalar", (lambda g_: lambda eng: eng.activation(
                mixed[:, g_ * 1024:(g_ + 1) * 1024], mixed[:, g_ * 1024:(g_ + 1) * 1024],
                AF.Silu))(grp),
                reads=["mixed%d" % grp], writes=["mixed%d" % grp],
                tag="c_sil%d_%d" % (c, grp))
        return steps

    def _chunk_machA(self, c, cp, bufs):
        steps = []

        class _P:
            @staticmethod
            def step(eng_, emit_, **kw):
                steps.append((eng_, emit_, kw))
        nc, sb, pb = self.nc, self.sb, self.pb
        p = _P
        t0 = c * C
        co = bufs["cols"][cp]
        mixed, flat, sqb = bufs["mixed"], bufs["flat"], bufs["sqb"]
        abg, sr, ident = sb["abg"], sb["sr"], sb["ident"]
        cumt, revt, cv = sb["cumt"], sb["revt"], sb["cvals"]
        onesrow, onescolb, e127 = sb["onesrow"], sb["onescolb"], sb["e127"]
        bnk2 = pb[0]
        r_b2 = "pb0"
        S = p.step

        def em_gsA(eng):
            eng.tensor_copy(co["gsA"][0:64, :], abg[0:64, t0:t0 + C])
            return eng.tensor_copy(co["gsA"][64:65, :], sr[0:1, t0:t0 + C])
        S("vector", em_gsA, reads=["abg", "sr"], writes=["gsA%d" % cp], tag="c_gsa%d" % c)
        S("tensor", lambda eng: eng.transpose(bnk2[:, 384:450], co["gsA"][0:66, :], ident[0:66, 0:66]),
          reads=["gsA%d" % cp, "ident"], writes=[r_b2], tag="c_trg%d" % c)
        S("scalar", lambda eng: eng.copy(co["gBe"][:], bnk2[:, 384:450]),
          reads=[r_b2], writes=["gBe%d" % cp], tag="c_gbe%d" % c)

        S("tensor", lambda eng: eng.matmul(bnk2[:, 384:450], cumt[:], co["gBe"][:, 0:66],
                                           start=True, stop=True),
          reads=["gBe%d" % cp, "cumt"], writes=[r_b2], tag="c_cum%d" % c)
        S("scalar", lambda eng: eng.copy(co["cumE"][:], bnk2[:, 384:450]),
          reads=[r_b2], writes=["cumE%d" % cp], tag="c_cumev%d" % c)
        S("tensor", lambda eng: eng.matmul(bnk2[:, 384:450], revt[:], co["gBe"][:, 0:66],
                                           start=True, stop=True),
          reads=["gBe%d" % cp, "revt"], writes=[r_b2], tag="c_rev%d" % c)
        S("scalar", lambda eng: eng.copy(co["revE"][:], bnk2[:, 384:450]),
          reads=[r_b2], writes=["revE%d" % cp], tag="c_revev%d" % c)

        def em_c1(eng):
            eng.mul(co["c1"][:, 0:1], co["cumE"][:, 64:65], BIG)
            return eng.mul(co["c1"][:, 1:2], co["revE"][:, 64:65], BIG)
        S("scalar", em_c1, reads=["cumE%d" % cp, "revE%d" % cp],
          writes=["c1%d" % cp], tag="c_c1%d" % c)

        def em_gp(eng):
            eng.tensor_scalar(co["gp"][:], co["cumE"][:, 0:32], co["c1"][:, 0:1], None, ALU.subtract)
            return eng.tensor_scalar(co["dp"][:], co["revE"][:, 0:32], co["c1"][:, 1:2], None, ALU.subtract)
        S("vector", em_gp, reads=["cumE%d" % cp, "revE%d" % cp, "c1%d" % cp],
          writes=["gp%d" % cp, "dp%d" % cp], tag="c_gp%d" % c)

        S("tensor", lambda eng: eng.matmul(bnk2[0:1, 384:416], e127[:], co["gp"][:],
                                           start=True, stop=True),
          reads=["gp%d" % cp, "e127"], writes=[r_b2], tag="c_cfx%d" % c)
        S("scalar", lambda eng: eng.copy(co["crow"][:], bnk2[0:1, 384:416]),
          reads=[r_b2], writes=["crow%d" % cp], tag="c_cfe%d" % c)
        S("tensor", lambda eng: eng.matmul(bnk2[:, 416:448], onesrow[:], co["crow"][:],
                                           start=True, stop=True),
          reads=["crow%d" % cp, "onesrow"], writes=[r_b2], tag="c_cfb%d" % c)
        S("scalar", lambda eng: eng.activation(co["coefC"][:], bnk2[:, 416:448], AF.Exp),
          reads=[r_b2], writes=["coefC%d" % cp], tag="c_cfc%d" % c)
        return steps

    def _chunk_machB(self, c, cp, bufs):
        steps = []

        class _P:
            @staticmethod
            def step(eng_, emit_, **kw):
                steps.append((eng_, emit_, kw))
        nc, sb, pb = self.nc, self.sb, self.pb
        p = _P
        t0 = c * C
        co = bufs["cols"][cp]
        mixed, flat, sqb = bufs["mixed"], bufs["flat"], bufs["sqb"]
        abg, sr, ident = sb["abg"], sb["sr"], sb["ident"]
        cumt, revt, cv = sb["cumt"], sb["revt"], sb["cvals"]
        onesrow, onescolb, e127 = sb["onesrow"], sb["onescolb"], sb["e127"]
        bnk2 = pb[0]
        r_b2 = "pb0"
        S = p.step

        for which in ("q", "k"):
            for hk0 in range(0, 16, 4):
                tis = [(hk0 + j) if which == "q" else (16 + hk0 + j) for j in range(4)]
                grps = sorted({ti // 8 for ti in tis})

                def em_sq4(tis_):
                    def em(eng):
                        inst = None
                        for j, ti in enumerate(tis_):
                            inst = eng.tensor_tensor(
                                sqb[j][:], mixed[:, ti * 128:(ti + 1) * 128],
                                mixed[:, ti * 128:(ti + 1) * 128], ALU.mult)
                        return inst
                    return em
                S("vector", em_sq4(tis), reads=["mixed%d" % g for g in grps],
                  writes=["sqb%d" % j for j in range(4)],
                  tag="c_s%s%d_%d" % (which, c, hk0))

                def em_nmm4(cols_):
                    def em(eng):
                        inst = None
                        for j, col in enumerate(cols_):
                            inst = eng.matmul(bnk2[:, 448 + col:449 + col], sqb[j][:],
                                              onescolb[:], start=True, stop=True)
                        return inst
                    return em
                S("tensor", em_nmm4(tis),
                  reads=["sqb%d" % j for j in range(4)] + ["onescolb"],
                  writes=[r_b2], tag="c_n%s%d_%d" % (which, c, hk0))
        S("scalar", lambda eng: eng.activation(co["lnr"][:], bnk2[:, 448:480], AF.Ln,
                                               bias=cv[:, 2:3]),
          reads=[r_b2, "cvals"], writes=["lnr%d" % cp], tag="c_lnr%d" % c)

        lnq_rep = co["lnr"][:, 0:16].unsqueeze(2).to_broadcast([128, 16, 2])
        lnk_rep = co["lnr"][:, 16:32].unsqueeze(2).to_broadcast([128, 16, 2])

        def v3(ap):
            return ap.rearrange("p (a b) -> p a b", b=2)

        def em_cols(eng):
            cs = co["cs"]
            eng.tensor_scalar(co["lnr"][:, 0:16], co["lnr"][:, 0:16], -0.5,
                              -0.5 * float(np.log(DK)), ALU.mult, ALU.add)
            eng.tensor_scalar(co["lnr"][:, 16:32], co["lnr"][:, 16:32], -0.5, None, ALU.mult)
            eng.tensor_tensor(cs[:, 32:64], co["gp"][:], co["gBe"][:, 32:64], ALU.add)
            eng.tensor_tensor(co["earg"][:], co["gBe"][:, 32:64], co["gp"][:], ALU.add)
            eng.tensor_tensor(v3(cs[:, 0:32]), v3(co["gp"][:]), lnk_rep, ALU.subtract)
            eng.tensor_tensor(v3(cs[:, 32:64]), v3(cs[:, 32:64]), lnk_rep, ALU.add)
            eng.tensor_tensor(v3(cs[:, 64:96]), v3(co["gp"][:]), lnq_rep, ALU.add)
            eng.tensor_tensor(v3(co["Ktl"][:]), v3(co["dp"][:]), lnk_rep, ALU.add)
            return eng.tensor_scalar(co["nrA"][:], cs[:, 0:32], -1.0, None, ALU.mult)
        S("vector", em_cols,
          reads=["gp%d" % cp, "dp%d" % cp, "gBe%d" % cp, "lnr%d" % cp],
          writes=["cs%d" % cp, "nrA%d" % cp, "earg%d" % cp, "Ktl%d" % cp, "lnr%d" % cp],
          tag="c_cols%d" % c)

        def em_exps(eng):
            eng.activation(co["bcol"][:], co["gBe"][:, 32:64], AF.Exp)
            eng.activation(co["bGam"][:], co["earg"][:], AF.Exp)
            eng.activation(co["qGam"][:], co["cs"][:, 64:96], AF.Exp)
            return eng.activation(co["Ktl"][:], co["Ktl"][:], AF.Exp)
        S("scalar", em_exps,
          reads=["gBe%d" % cp, "earg%d" % cp, "cs%d" % cp, "Ktl%d" % cp],
          writes=["bcol%d" % cp, "bGam%d" % cp, "qGam%d" % cp, "Ktl%d" % cp],
          tag="c_exps%d" % c)

        S("tensor", lambda eng: eng.transpose(bnk2[0:96, 384:512], co["cs"][:, 0:96], ident[:]),
          reads=["cs%d" % cp, "ident"], writes=[r_b2], tag="c_trr%d" % c)
        S("scalar", lambda eng: eng.copy(co["rowsA"][:], bnk2[0:96, 384:512]),
          reads=[r_b2], writes=["rowsA%d" % cp], tag="c_rae%d" % c)
        for r in range(3):
            dst = flat[:].rearrange("o (h r c) -> o h r c", r=3, c=128)[:, :, r, :]
            S("sync", (lambda r_, d_: lambda eng: eng.dma_start(
                out=d_, in_=co["rowsA"][r_ * 32:(r_ + 1) * 32, :]))(r, dst),
              reads=["rowsA%d" % cp], writes=["flat"], dma=True,
              tag="c_fl%d_%d" % (c, r))

        S("vector", lambda eng: eng.memset(co["msC"][:], 0.0),
          writes=["msC%d" % cp], tag="c_ms0%d" % c)
        return steps
        return steps

    def _chunk_tail(self, c, cp, bufs):
        if getattr(self, "notail", False):
            return
        p, sb, pb = self.p, self.sb, self.pb
        t0 = c * C
        co = bufs["cols"][cp]
        y1, yA = bufs["y1"], bufs["yA"]
        cv, normwb, identb = sb["cvals"], sb["normwb"], sb["identb"]
        S = p.step

        S("vector", lambda eng: eng.tensor_scalar(co["msC"][:], co["msC"][:],
                                                  1.0 / DV, None, ALU.mult),
          reads=["msC%d" % cp], writes=["msC%d" % cp], tag="c_msn%d" % c)

        S("scalar", lambda eng: eng.activation(co["rstd"][:], co["msC"][:], AF.Ln,
                                               bias=cv[:, 2:3]),
          reads=["msC%d" % cp, "cvals"], writes=["rstd%d" % cp], tag="c_rstda%d" % c)
        S("scalar", lambda eng: eng.activation(co["rstd"][:], co["rstd"][:], AF.Exp,
                                               scale=-0.5),
          sereads=["rstd%d" % cp], writes=["rstd%d" % cp], tag="c_rstdb%d" % c)

        def tslot(h_):
            g3_ = h_ % 4
            return (pb[6 + (g3_ & 1)][:].bitcast(BF16), 512 + (g3_ >> 1) * 128,
                    "pb%d" % (6 + (g3_ & 1)))
        for h0 in range(0, HV, 2):
            pair = (h0, h0 + 1)

            def em_yb2(pr):
                def em(eng):
                    inst = None
                    for h_ in pr:
                        inst = eng.scalar_tensor_tensor(
                            y1[:, h_ * 128:(h_ + 1) * 128], y1[:, h_ * 128:(h_ + 1) * 128],
                            co["rstd"][:, h_:h_ + 1], normwb[:], ALU.mult, ALU.mult)
                    return inst
                return em
            S("vector", em_yb2(pair), reads=["y1", "rstd%d" % cp, "normwb"],
              writes=["y1"], tag="c_yb%d_%d" % (c, h0))
            for h in pair:
                BT, bto, rBT = tslot(h)
                S("tensor", (lambda h_, BT_, o_: lambda eng: eng.transpose(
                    BT_[:, o_:o_ + 128], y1[:, h_ * 128:(h_ + 1) * 128], identb[:]))(h, BT, bto),
                  reads=["y1", "identb"], writes=[rBT], tag="c_ytr%d_%d" % (c, h))

            def em_yev2(pr):
                def em(eng):
                    inst = None
                    for h_ in pr:
                        BT_, o_, _ = tslot(h_)
                        inst = eng.copy(yA[:, h_ * 128:(h_ + 1) * 128], BT_[:, o_:o_ + 128])
                    return inst
                return em
            S("scalar", em_yev2(pair), reads=[tslot(h0)[2], tslot(h0 + 1)[2]],
              writes=["yA"], tag="c_yev%d_%d" % (c, h0))
        for q in range(2):
            src = yA[:].rearrange("p (hh t) -> p hh t", t=128)[:, q * 16:(q + 1) * 16, :]
            dst = self.y_s.rearrange("(hh p) t -> p hh t", p=128)[:, q * 16:(q + 1) * 16, t0:t0 + C]
            S("sync", (lambda s_, d_: lambda eng: eng.dma_start(out=d_, in_=s_))(src, dst),
              reads=["yA"], dma=True, tag="c_yst%d_%d" % (c, q))

    def _head_steps(self, c, cp, h, bufs):
        nc, p, sb, pb = self.nc, self.p, self.sb, self.pb
        co = bufs["cols"][cp]
        mixed, zsil_t, Sst = bufs["mixed"], bufs["zsil"][cp], bufs["Sst"]
        flat, y1 = bufs["flat"], bufs["y1"]
        solT, Usb, Rt, VBs = bufs["solT"], bufs["Usb"], bufs["Rt"], bufs["VBs"]
        KBt, KSe, QSe, Ot = bufs["KBt"], bufs["KSe"], bufs["QSe"], bufs["Ot"]
        Esb, r3m = bufs["Esb"], bufs["r3m"]
        g3 = h % 4
        hk = h // 2
        BF = pb[1]
        rBF = "pb1"
        BC = pb[0]
        rBC = "pb0"
        BS = pb[2 + g3]
        rBS = "pb%d" % (2 + g3)
        BT = pb[6 + (g3 & 1)][:].bitcast(BF16)
        bto = (g3 >> 1) * 256
        rBT = "pb%d" % (6 + (g3 & 1))
        sol = solT[g3]
        rsol = "solT%d" % g3
        U = Usb[g3]
        rU = "Usb%d" % g3
        cm384, identb, onesrow = sb["cm384"], sb["identb"], sb["onesrow"]
        qT = mixed[:, hk * 128:(hk + 1) * 128]
        kT = mixed[:, (16 + hk) * 128:(16 + hk + 1) * 128]
        vT = mixed[:, (32 + h) * 128:(32 + h + 1) * 128]
        Sh = Sst[:, h * 128:(h + 1) * 128]
        tg = "%d_%d" % (c, h)
        steps = []

        def S(engine, emit, **kw):
            steps.append((engine, emit, kw))

        S("tensor", lambda eng: eng.matmul(BC[:, 0:384], onesrow[:],
                                           flat[0:1, h * 384:(h + 1) * 384],
                                           start=True, stop=True),
          reads=["flat", "onesrow"], writes=[rBC], tag="h_bc" + tg)
        S("vector", lambda eng: eng.tensor_tensor(r3m[g3][:], BC[:, 0:384], cm384[:], ALU.add),
          reads=[rBC, "cm384"], writes=["r3m%d" % g3], tag="h_r3m" + tg)

        def em_es(eng):
            eng.activation(Esb[g3][:, 0:128], r3m[g3][:, 0:128], AF.Exp,
                           bias=co["cs"][:, 32 + h:33 + h], scale=-1.0)
            eng.activation(Esb[g3][:, 128:256], r3m[g3][:, 128:256], AF.Exp,
                           bias=co["nrA"][:, h:h + 1])
            return eng.activation(Esb[g3][:, 256:384], r3m[g3][:, 256:384], AF.Exp,
                                  bias=co["nrA"][:, h:h + 1])
        S("scalar", em_es, reads=["r3m%d" % g3, "cs%d" % cp, "nrA%d" % cp],
          writes=["Esb%d" % g3], tag="h_es" + tg)

        def em_pp(eng):
            eng.matmul(BF[:, 0:128], kT, kT, start=True, stop=True)
            eng.matmul(BF[:, 128:256], kT, qT, start=True, stop=True)
            eng.matmul(BF[:, 256:384], kT, Sh, start=True, stop=True)
            return eng.matmul(BF[:, 384:512], qT, Sh, start=True, stop=True)
        S("tensor", em_pp, reads=["mixed%d" % (hk // 8), "mixed%d" % (2 + hk // 8), "Sst"],
          writes=[rBF], tag="h_pp" + tg)

        def em_xya(eng):
            eng.tensor_tensor(sol[:, 0:128], BF[:, 0:128], Esb[g3][:, 0:128], ALU.mult)
            eng.tensor_tensor(sol[:, 6 * 128:7 * 128], BF[:, 0:128], Esb[g3][:, 128:256], ALU.mult)
            eng.tensor_tensor(sol[:, 13 * 128:14 * 128], BF[:, 128:256],
                              Esb[g3][:, 256:384], ALU.mult)
            eng.tensor_scalar(KSe[g3][:], BF[:, 256:384],
                              co["bGam"][:, h:h + 1], None, ALU.mult)
            return eng.tensor_scalar(QSe[g3][:], BF[:, 384:512],
                                     co["qGam"][:, h:h + 1], None, ALU.mult)
        S("vector", em_xya, reads=[rBF, "Esb%d" % g3, "bGam%d" % cp, "qGam%d" % cp],
          writes=[rsol, "KSe%d" % g3, "QSe%d" % g3], tag="h_xya" + tg)

        S("tensor", lambda eng: eng.transpose(BT[:, bto:bto + 128], vT, identb[:]),
          reads=["mixed%d" % (4 + h // 8), "identb"], writes=[rBT], tag="h_vtr" + tg)
        S("tensor", lambda eng: eng.transpose(BT[:, bto + 128:bto + 256], kT, identb[:]),
          reads=["mixed%d" % (2 + hk // 8), "identb"], writes=[rBT], tag="h_ktr" + tg)

        def em_vkr(eng):
            eng.tensor_scalar(VBs[g3][:], BT[:, bto:bto + 128],
                              co["bcol"][:, h:h + 1], None, ALU.mult)
            eng.tensor_scalar(KBt[g3][:], BT[:, bto + 128:bto + 256],
                              co["Ktl"][:, h:h + 1], None, ALU.mult)
            return eng.tensor_tensor(Rt[g3][:], VBs[g3][:], KSe[g3][:], ALU.subtract)
        S("vector", em_vkr, reads=[rBT, "bcol%d" % cp, "Ktl%d" % cp, "KSe%d" % g3],
          writes=["VBs%d" % g3, "KBt%d" % g3, "Rt%d" % g3], tag="h_vkr" + tg)

        S("tensor", lambda eng: eng.matmul(BS[:, 256:384], sol[:, 6 * 128:7 * 128],
                                           Rt[g3][:], start=True, stop=True),
          reads=[rsol, "Rt%d" % g3], writes=[rBS], tag="h_a0" + tg)
        S("vector", lambda eng: eng.tensor_tensor(U[:, 0:128], Rt[g3][:],
                                                  BS[:, 256:384], ALU.subtract),
          reads=["Rt%d" % g3, rBS], writes=[rU], tag="h_u1" + tg)
        for k in range(1, 7):
            xk_prev = sol[:, (k - 1) * 128:k * 128]
            yk_prev = sol[:, (6 + k - 1) * 128:(6 + k) * 128]
            ups = k % 2
            uprev = U[:, (1 - ups) * 128:(2 - ups) * 128]
            unew = U[:, ups * 128:(ups + 1) * 128]
            if k < 6:
                def em_sq(yp, xp):
                    def em(eng):
                        eng.matmul(BS[:, 0:128], yp, xp, start=True, stop=True)
                        return eng.matmul(BS[:, 128:256], xp, yp, start=True, stop=True)
                    return em
                S("tensor", em_sq(yk_prev, xk_prev), reads=[rsol], writes=[rBS],
                  tag="h_sq%d" % k + tg)

                def em_sqe(kk):
                    def em(eng):
                        eng.copy(sol[:, kk * 128:(kk + 1) * 128], BS[:, 0:128])
                        return eng.copy(sol[:, (6 + kk) * 128:(6 + kk + 1) * 128],
                                        BS[:, 128:256])
                    return em
                S("scalar", em_sqe(k), reads=[rBS], writes=[rsol], tag="h_sqe%d" % k + tg)
            else:
                S("tensor", (lambda yp, xp: lambda eng: eng.matmul(
                    BS[:, 128:256], xp, yp, start=True, stop=True))(yk_prev, xk_prev),
                  reads=[rsol], writes=[rBS], tag="h_sq%d" % k + tg)
                S("scalar", (lambda kk: lambda eng: eng.copy(
                    sol[:, (6 + kk) * 128:(6 + kk + 1) * 128], BS[:, 128:256]))(k),
                  reads=[rBS], writes=[rsol], tag="h_sqe%d" % k + tg)
            S("tensor", (lambda kk, up: lambda eng: eng.matmul(
                BS[:, 256:384], sol[:, (6 + kk) * 128:(6 + kk + 1) * 128], up,
                start=True, stop=True))(k, uprev),
              reads=[rsol, rU], writes=[rBS], tag="h_a%d" % k + tg)
            S("vector", (lambda un, up: lambda eng: eng.tensor_tensor(
                un, up, BS[:, 256:384], ALU.add))(unew, uprev),
              reads=[rU, rBS], writes=[rU], tag="h_u%d" % (k + 1) + tg)
        ufin = U[:, 0:128]

        def em_oist(eng):
            eng.matmul(BS[:, 384:512], sol[:, 13 * 128:14 * 128], ufin,
                       start=True, stop=True)
            return eng.matmul(BS[:, 0:128], KBt[g3][:], ufin, start=True, stop=True)
        S("tensor", em_oist, reads=[rsol, "KBt%d" % g3, rU], writes=[rBS],
          tag="h_oi" + tg)

        def em_osn(h_):
            def em(eng):
                eng.tensor_tensor(Ot[g3][:], QSe[g3][:], BS[:, 384:512], ALU.add)
                return eng.scalar_tensor_tensor(
                    Sst[:, h_ * 128:(h_ + 1) * 128], Sst[:, h_ * 128:(h_ + 1) * 128],
                    co["coefC"][:, h_:h_ + 1], BS[:, 0:128], ALU.mult, ALU.add)
            return em
        S("vector", em_osn(h), reads=["QSe%d" % g3, "coefC%d" % cp, "Sst", rBS],
          writes=["Ot%d" % g3, "Sst"], tag="h_osn" + tg)

        S("gpsimd", (lambda h_: lambda eng: eng.tensor_tensor(
            y1[:, h_ * 128:(h_ + 1) * 128], Ot[g3][:],
            zsil_t[:, h_ * 128:(h_ + 1) * 128], ALU.mult))(h),
          reads=["Ot%d" % g3, "zsil%d" % cp], writes=["y1"], tag="h_y1" + tg)
        S("scalar", (lambda h_: lambda eng: eng.activation(
            Esb[g3][:, 0:128], y1[:, h_ * 128:(h_ + 1) * 128], AF.Square,
            accum_out=co["msC"][:, h_:h_ + 1]))(h),
          reads=["y1", "msC%d" % cp], writes=["Esb%d" % g3, "msC%d" % cp], tag="h_ms" + tg)
        return steps

    def ph_d(self, hh):
        nc, p, sb, dr, pb = self.nc, self.p, self.sb, self.dr, self.pb
        T, TH = self.T, self.TH
        TQ = min(512, TH)
        nq = TH // TQ
        at = self.at
        yh = at("yh%d" % hh, [128, 32 * TH], BF16, 0)
        xnew = at("xnew%d" % hh, [128, 16 * TH], F32, 64 * 1024)
        wco = [at("wco%d_%d" % (hh, i), [128, 32 * 128], BF16, 128 * 1024 + i * 8192)
               for i in range(2)]
        xold = at("xold%d" % hh, [128, TQ], F32, 144 * 1024)
        xbf = at("xbf2_%d" % hh, [128, TQ], BF16, 147 * 1024)
        xsq = at("xsq2_%d" % hh, [128, TQ], BF16, 149 * 1024)
        dbc = at("dbc%d" % hh, [128, 2 * TQ], F32, 151 * 1024)
        drow = at("drow%d" % hh, [1, 4 * TQ], F32, 156 * 1024)
        lnsrow = at("lnsrow%d" % hh, [1, TH], F32, 168 * 1024)
        lnmrow = at("lnmrow%d" % hh, [1, TH], F32, 172 * 1024)
        t0h = hh * TH

        for q in range(4):
            src = self.y_s.rearrange("(hv p) t -> p hv t", p=128)[:, q * 8:(q + 1) * 8, t0h:t0h + TH]
            dst = yh[:].rearrange("p (hv t) -> p hv t", t=TH)[:, q * 8:(q + 1) * 8, :]
            p.step("sync", (lambda s_, d_: lambda eng: eng.dma_start(out=d_, in_=s_))(src, dst),
                   writes=["yh"], dma=True, tag="d_ly%d_%d" % (hh, q))

        for dm in range(16):
            par = dm % 2
            src = dr["Wout"].rearrange("(hv p) c -> p hv c", p=128)[:, :, dm * 128:(dm + 1) * 128]
            for q in range(2):
                p.step("gpsimd",
                       (lambda s_, d_, q_: lambda eng: eng.dma_start(
                           out=d_[:].rearrange("p (hv c) -> p hv c", c=128)[:, q_ * 16:(q_ + 1) * 16, :],
                           in_=s_[:, q_ * 16:(q_ + 1) * 16, :]))(src, wco[par], q),
                       writes=["wco%d" % par], dma=True, tag="d_lw%d_%d_%d" % (hh, dm, q))
            for tq in range(nq):
                bnk = pb[par]
                rb = "pb%d" % par

                def em_mm(par_, tq_):
                    def em(eng):
                        inst = None
                        for hv in range(32):
                            inst = eng.matmul(
                                pb[par_][:, 0:TQ], wco[par_][:, hv * 128:(hv + 1) * 128],
                                yh[:].rearrange("p (hv t) -> p hv t", t=TH)[:, hv, tq_ * TQ:(tq_ + 1) * TQ],
                                start=(hv == 0), stop=(hv == 31))
                        return inst
                    return em
                p.step("tensor", em_mm(par, tq), reads=["wco%d" % par, "yh"],
                       writes=["pb%d" % par], tag="d_mm%d_%d_%d" % (hh, dm, tq))
                src2 = dr["xT"][dm * 128:(dm + 1) * 128, t0h + tq * TQ: t0h + (tq + 1) * TQ]
                p.step("sync", (lambda s_: lambda eng: eng.dma_start(out=xold[:], in_=s_))(src2),
                       writes=["xold"], dma=True, tag="d_lx%d_%d_%d" % (hh, dm, tq))

                def em_add(par_, dm_, tq_):
                    def em(eng):
                        return eng.tensor_tensor(
                            xnew[:, dm_ * TH + tq_ * TQ: dm_ * TH + (tq_ + 1) * TQ],
                            xold[:], pb[par_][:, 0:TQ], ALU.add)
                    return em
                p.step("vector", em_add(par, dm, tq), reads=["xold", "pb%d" % par],
                       writes=["xnew"], tag="d_add%d_%d_%d" % (hh, dm, tq))

        onescolb, onesrow, cv = sb["onescolb"], sb["onesrow"], sb["cvals"]
        for tq in range(nq):
            for dm in range(16):
                def em_cast(dm_, tq_):
                    def em(eng):
                        return eng.tensor_copy(xbf[:], xnew[:, dm_ * TH + tq_ * TQ: dm_ * TH + (tq_ + 1) * TQ])
                    return em
                p.step("vector", em_cast(dm, tq), reads=["xnew"], writes=["xbf"],
                       tag="d_c%d_%d_%d" % (hh, dm, tq))
                p.step("scalar", lambda eng: eng.activation(xsq[:], xbf[:], AF.Square),
                       reads=["xbf"], writes=["xsq"], tag="d_q%d_%d_%d" % (hh, dm, tq))

                def em_smm(dm_):
                    def em(eng):
                        eng.matmul(pb[4][0:1, 0:TQ], onescolb[:], xbf[:],
                                   start=(dm_ == 0), stop=(dm_ == 15))
                        return eng.matmul(pb[5][0:1, 0:TQ], onescolb[:], xsq[:],
                                          start=(dm_ == 0), stop=(dm_ == 15))
                    return em
                p.step("tensor", em_smm(dm), reads=["xbf", "xsq", "onescolb"],
                       writes=["pb4", "pb5"], tag="d_s%d_%d_%d" % (hh, dm, tq))

            def em_rows(eng):
                eng.tensor_scalar(drow[0:1, 0:TQ], pb[4][0:1, 0:TQ], 1.0 / D, None, ALU.mult)
                eng.tensor_scalar(drow[0:1, 3 * TQ:4 * TQ], pb[5][0:1, 0:TQ], 1.0 / D, None, ALU.mult)
                eng.tensor_tensor(drow[0:1, TQ:2 * TQ], drow[0:1, 0:TQ], drow[0:1, 0:TQ], ALU.mult)
                return eng.tensor_tensor(drow[0:1, 3 * TQ:4 * TQ], drow[0:1, 3 * TQ:4 * TQ],
                                         drow[0:1, TQ:2 * TQ], ALU.subtract)
            p.step("vector", em_rows, reads=["pb4", "pb5"], writes=["drow"],
                   tag="d_r%d_%d" % (hh, tq))

            def em_rstd(eng):
                eng.activation(drow[0:1, 2 * TQ:3 * TQ], drow[0:1, 3 * TQ:4 * TQ], AF.Ln,
                               bias=cv[0:1, 0:1])
                return eng.activation(drow[0:1, TQ:2 * TQ], drow[0:1, 2 * TQ:3 * TQ],
                                      AF.Exp, scale=-0.5)
            p.step("scalar", em_rstd, reads=["drow", "cvals"], writes=["drow"],
                   tag="d_rs%d_%d" % (hh, tq))
            p.step("vector", lambda eng: eng.tensor_tensor(
                drow[0:1, 2 * TQ:3 * TQ], drow[0:1, 0:TQ], drow[0:1, TQ:2 * TQ], ALU.mult),
                reads=["drow"], writes=["drow"], tag="d_ms%d_%d" % (hh, tq))

            def em_carry(tq_):
                def em(eng):
                    eng.tensor_copy(lnsrow[0:1, tq_ * TQ:(tq_ + 1) * TQ], drow[0:1, TQ:2 * TQ])
                    return eng.tensor_copy(lnmrow[0:1, tq_ * TQ:(tq_ + 1) * TQ], drow[0:1, 2 * TQ:3 * TQ])
                return em
            p.step("vector", em_carry(tq), reads=["drow"],
                   writes=["lnsrow", "lnmrow"], tag="d_cr%d_%d" % (hh, tq))
            p.step("tensor", lambda eng: eng.matmul(pb[6][:, 0:TQ], onesrow[:],
                                                    drow[0:1, TQ:2 * TQ], start=True, stop=True),
                   reads=["drow", "onesrow"], writes=["pb6"], tag="d_bs%d_%d" % (hh, tq))
            p.step("tensor", lambda eng: eng.matmul(pb[7][:, 0:TQ], onesrow[:],
                                                    drow[0:1, 2 * TQ:3 * TQ], start=True, stop=True),
                   reads=["drow", "onesrow"], writes=["pb7"], tag="d_bm%d_%d" % (hh, tq))
            p.step("vector", lambda eng: eng.tensor_copy(dbc[:, 0:TQ], pb[6][:, 0:TQ]),
                   reads=["pb6"], writes=["dbc"], tag="d_es%d_%d" % (hh, tq))
            p.step("vector", lambda eng: eng.tensor_copy(dbc[:, TQ:2 * TQ], pb[7][:, 0:TQ]),
                   reads=["pb7"], writes=["dbc"], tag="d_em%d_%d" % (hh, tq))

            for dm in range(16):
                def em_app(dm_, tq_):
                    def em(eng):
                        xs = xnew[:, dm_ * TH + tq_ * TQ: dm_ * TH + (tq_ + 1) * TQ]
                        eng.tensor_tensor(xbf[:], xs, dbc[:, 0:TQ], ALU.mult)
                        return eng.tensor_tensor(xbf[:], xbf[:], dbc[:, TQ:2 * TQ], ALU.subtract)
                    return em
                p.step("vector", em_app(dm, tq), reads=["xnew", "dbc"],
                       writes=["xbf"], tag="d_a%d_%d_%d" % (hh, dm, tq))
                dst = self.ln2_s[dm * 128:(dm + 1) * 128, t0h + tq * TQ: t0h + (tq + 1) * TQ]
                p.step("sync", (lambda d_: lambda eng: eng.dma_start(out=d_, in_=xbf[:]))(dst),
                       reads=["xbf"], dma=True, tag="d_st%d_%d_%d" % (hh, dm, tq))

    def ph_e(self, hh):
        nc, p, sb, dr, pb = self.nc, self.p, self.sb, self.dr, self.pb
        T, TH = self.T, self.TH
        t0h = hh * TH
        at = self.at
        Hb = at("Hb%d" % hh, [128, 16 * TH], BF16, 0)
        ln2x = at("ln2x%d" % hh, [128, 16 * TH], BF16, 32 * 1024)
        xnew = self.sb["xnew%d" % hh]
        wgb = [at("wgb%d_%d" % (hh, i), [128, 16 * 128], BF16, 128 * 1024 + i * 4096)
               for i in range(2)]
        wub = [at("wub%d_%d" % (hh, i), [128, 16 * 128], BF16, 136 * 1024 + i * 4096)
               for i in range(2)]
        wdb = [at("wdb%d_%d" % (hh, i), [128, 16 * 128], BF16, 144 * 1024 + i * 4096)
               for i in range(2)]
        wrf = at("wrf%d" % hh, [128, 16 * 8], F32, 152 * 1024)
        wfin = at("wfin%d" % hh, [128, 8 * (TH // 128)], F32, 154 * 1024)
        wT = at("wT%d" % hh, [8, TH], BF16, 156 * 1024)
        wrow = at("wrow%d" % hh, [1, TH], BF16, 158 * 1024)
        wbt = at("wbt%d" % hh, [128, TH], F32, 160 * 1024)
        sG = at("sG%d" % hh, [128, TH], BF16, 164 * 1024)
        lnsrow = self.sb["lnsrow%d" % hh]
        lnmrow = self.sb["lnmrow%d" % hh]
        ntt = TH // 128

        for q in range(4):
            src = self.ln2_s.rearrange("(dt p) t -> p dt t", p=128)[:, q * 4:(q + 1) * 4, t0h:t0h + TH]
            dst = ln2x[:].rearrange("p (dt t) -> p dt t", t=TH)[:, q * 4:(q + 1) * 4, :]
            p.step("sync", (lambda s_, d_: lambda eng: eng.dma_start(out=d_, in_=s_))(src, dst),
                   writes=["ln2x"], dma=True, tag="e_ll%d_%d" % (hh, q))
        p.step("sync", lambda eng: eng.dma_start(
            out=wrf[:].rearrange("p (dt c) -> p dt c", c=8),
            in_=dr["Wr"].rearrange("(dt p) n -> p dt n", p=128)),
            writes=["wrf"], dma=True, tag="e_lwr%d" % hh)

        iota8, iota8b, neg18, cv = sb["iota8"], sb["iota8b"], sb["neg18"], sb["cvals"]
        onescol, onesrow = sb["onescol"], sb["onesrow"]
        GL = float(0.5 - np.log(4.0 / 3.0))
        NTTE = ntt * 8
        rts = self.at("rts%d" % hh, [128, 4 * NTTE], F32, 153 * 1024)
        rtv = self.at("rtv%d" % hh, [128, 4 * NTTE], F32, 153 * 1024 + 16 * NTTE)
        rtc = self.at("rtc%d" % hh, [128, 128], F32, 153 * 1024 + 32 * NTTE)

        def g3(ap):
            return ap.rearrange("p (tt e) -> p tt e", e=8)

        def cb(ap):
            return ap.unsqueeze(2).to_broadcast([128, ntt, 8])

        def eb(ap):
            return ap.unsqueeze(1).to_broadcast([128, ntt, 8])

        def em_csum(eng):
            inst = None
            for d in range(16):
                inst = eng.matmul(pb[7][0:1, 0:8], onescol[:], wrf[:, d * 8:(d + 1) * 8],
                                  start=(d == 0), stop=(d == 15))
            return inst
        p.step("tensor", em_csum, reads=["wrf", "onescol"], writes=["pb7"],
               tag="e_cs%d" % hh)
        p.step("vector", lambda eng: eng.tensor_copy(rtc[0:1, 96:104], pb[7][0:1, 0:8]),
               reads=["pb7"], writes=["rtc"], tag="e_cse%d" % hh)
        p.step("tensor", lambda eng: eng.matmul(pb[7][:, 0:8], onesrow[:], rtc[0:1, 96:104],
                                                start=True, stop=True),
               reads=["rtc", "onesrow"], writes=["pb7"], tag="e_csb%d" % hh)
        p.step("vector", lambda eng: eng.tensor_copy(rtc[:, 104:112], pb[7][:, 0:8]),
               reads=["pb7"], writes=["rtc"], tag="e_csb2%d" % hh)

        def em_rmm(tt_):
            def em(eng):
                inst = None
                for d in range(16):
                    inst = eng.matmul(pb[6][0:128, tt_ * 8:(tt_ + 1) * 8],
                                      xnew[:, d * TH + tt_ * 128: d * TH + (tt_ + 1) * 128],
                                      wrf[:, d * 8:(d + 1) * 8],
                                      start=(d == 0), stop=(d == 15))
                return inst
            return em

        def em_scol(tt_):
            def em(eng):
                eng.matmul(pb[7][:, 16 + tt_:17 + tt_], lnsrow[0:1, tt_ * 128:(tt_ + 1) * 128],
                           onesrow[0:1, 0:1], start=True, stop=True)
                return eng.matmul(pb[7][:, 24 + tt_:25 + tt_], lnmrow[0:1, tt_ * 128:(tt_ + 1) * 128],
                                  onesrow[0:1, 0:1], start=True, stop=True)
            return em
        for tt in range(ntt):
            p.step("tensor", em_rmm(tt), reads=["xnew", "wrf"], writes=["pb6"],
                   tag="e_rm%d_%d" % (hh, tt))
            p.step("tensor", em_scol(tt), reads=["lnsrow", "lnmrow", "onesrow"],
                   writes=["pb7"], tag="e_sc%d_%d" % (hh, tt))

        def em_sme(eng):
            eng.tensor_copy(rtc[:, 80:80 + ntt], pb[7][:, 16:16 + ntt])
            return eng.tensor_copy(rtc[:, 88:88 + ntt], pb[7][:, 24:24 + ntt])
        p.step("vector", em_sme, reads=["pb7"], writes=["rtc"], tag="e_sme%d" % hh)

        def em_arg(eng):
            eng.tensor_tensor(g3(rtv[:, NTTE:2 * NTTE]),
                              cb(rtc[:, 88:88 + ntt]), eb(rtc[:, 104:112]), ALU.mult)
            eng.tensor_tensor(g3(rtv[:, 0:NTTE]),
                              g3(pb[6][0:128, 0:NTTE]), cb(rtc[:, 80:80 + ntt]), ALU.mult)
            return eng.tensor_tensor(rtv[:, 0:NTTE], rtv[:, 0:NTTE],
                                     rtv[:, NTTE:2 * NTTE], ALU.subtract)
        p.step("vector", em_arg, reads=["pb6"], sereads=["rtc"], writes=["rtv"],
               tag="e_arg%d" % hh)

        def em_racts(eng):
            eng.activation(rts[:, 0:NTTE], rtv[:, 0:NTTE], AF.Sigmoid, scale=cv[:, 4:5])
            eng.activation(rts[:, NTTE:2 * NTTE], rts[:, 0:NTTE], AF.Abs, bias=cv[:, 5:6])
            eng.activation(rts[:, 2 * NTTE:3 * NTTE], rts[:, NTTE:2 * NTTE], AF.Exp, scale=-10.0)
            return eng.activation(rts[:, 3 * NTTE:4 * NTTE], rts[:, NTTE:2 * NTTE], AF.Exp,
                                  scale=float(-1.0 / 0.3))
        p.step("scalar", em_racts, reads=["rtv", "cvals"], writes=["rts"],
               tag="e_ra%d" % hh)

        inh = rts[:, 0:NTTE]
        w0 = rts[:, 2 * NTTE:3 * NTTE]
        fb = rts[:, 3 * NTTE:4 * NTTE]
        e1 = rtv[:, 0:NTTE]
        tv = rtv[:, NTTE:2 * NTTE]
        w8 = rtv[:, 2 * NTTE:3 * NTTE]
        fbw = rtv[:, 3 * NTTE:4 * NTTE]
        sA = rtc[:, 0:ntt]
        ne = rtc[:, 8:8 + ntt]
        m1 = rtc[:, 16:16 + ntt]
        i1 = rtc[:, 24:24 + ntt]
        m2 = rtc[:, 32:32 + ntt]
        i2 = rtc[:, 40:40 + ntt]
        sB = rtc[:, 48:48 + ntt]
        sC = rtc[:, 56:56 + ntt]
        sD = rtc[:, 64:64 + ntt]

        def em_v1(eng):
            eng.tensor_scalar(e1, inh, GL, None, ALU.is_ge)
            eng.tensor_scalar(tv, inh, 0.5, None, ALU.is_le)
            eng.tensor_tensor(e1, e1, tv, ALU.mult)
            eng.tensor_tensor(w8, w0, e1, ALU.mult)
            eng.tensor_reduce(sA, g3(w8), axis=AX.X, op=ALU.add)
            eng.tensor_reduce(m1, g3(fb), axis=AX.X, op=ALU.max)
            return eng.tensor_scalar(ne, sA, 1e-8, None, ALU.is_lt)
        p.step("vector", em_v1, reads=["rts"], writes=["rtv", "rtc"],
               tag="e_v1%d" % hh)

        def em_v2(eng):
            eng.tensor_tensor(g3(tv), g3(fb), cb(m1), ALU.is_equal)
            eng.scalar_tensor_tensor(g3(tv), g3(tv), -100.0, eb(iota8b[:]), ALU.mult, ALU.add)
            return eng.tensor_reduce(i1, g3(tv), axis=AX.X, op=ALU.min)
        p.step("vector", em_v2, reads=["rts"], sereads=["rtc"], writes=["rtv", "rtc"],
               tag="e_v2%d" % hh)

        def em_v3(eng):
            eng.tensor_tensor(g3(e1), eb(iota8[:]), cb(i1), ALU.is_equal)
            eng.tensor_scalar(tv, fb, 1.0, None, ALU.add)
            eng.tensor_tensor(tv, e1, tv, ALU.mult)
            eng.tensor_tensor(tv, fb, tv, ALU.subtract)
            return eng.tensor_reduce(m2, g3(tv), axis=AX.X, op=ALU.max)
        p.step("vector", em_v3, reads=["rts"], sereads=["rtc"], writes=["rtv", "rtc"],
               tag="e_v3%d" % hh)

        def em_v4(eng):
            eng.tensor_tensor(g3(fbw), g3(tv), cb(m2), ALU.is_equal)
            eng.scalar_tensor_tensor(g3(fbw), g3(fbw), -100.0, eb(iota8b[:]), ALU.mult, ALU.add)
            return eng.tensor_reduce(i2, g3(fbw), axis=AX.X, op=ALU.min)
        p.step("vector", em_v4, sereads=["rtc"], reads=["rtv"], writes=["rtv", "rtc"],
               tag="e_v4%d" % hh)

        def em_v5(eng):
            eng.tensor_tensor(g3(tv), eb(iota8[:]), cb(i2), ALU.is_equal)
            eng.tensor_tensor(e1, e1, tv, ALU.add)
            eng.tensor_tensor(fbw, fb, e1, ALU.mult)
            return eng.tensor_reduce(sB, g3(fbw), axis=AX.X, op=ALU.add)
        p.step("vector", em_v5, reads=["rts"], sereads=["rtc"], writes=["rtv", "rtc"],
               tag="e_v5%d" % hh)

        def em_v6(eng):
            eng.tensor_scalar(sC, sB, 1e-8, None, ALU.max)
            eng.tensor_scalar(tv, fbw, 1.0, None, ALU.mult)
            return eng.reciprocal(sD, sC)
        p.step("vector", em_v6, sereads=["rtc"], reads=["rtv"], writes=["rtv", "rtc"],
               tag="e_v6%d" % hh)

        def em_v7(eng):
            eng.tensor_tensor(g3(fbw), g3(fbw), cb(sD), ALU.mult)
            eng.tensor_tensor(tv, fbw, w8, ALU.subtract)
            eng.tensor_tensor(g3(tv), g3(tv), cb(ne), ALU.mult)
            eng.tensor_tensor(w8, w8, tv, ALU.add)
            return eng.tensor_reduce(sB, g3(w8), axis=AX.X, op=ALU.add)
        p.step("vector", em_v7, sereads=["rtc"], reads=["rtv"], writes=["rtv", "rtc"],
               tag="e_v7%d" % hh)

        def em_v8(eng):
            eng.tensor_scalar(sC, sB, 1e-8, None, ALU.max)
            eng.tensor_scalar(tv, w8, 1.0, None, ALU.mult)
            return eng.reciprocal(sD, sC)
        p.step("vector", em_v8, sereads=["rtc"], reads=["rtv"], writes=["rtv", "rtc"],
               tag="e_v8%d" % hh)

        def em_v9(eng):
            return eng.tensor_tensor(g3(wfin[:]), g3(w8), cb(sD), ALU.mult)
        p.step("vector", em_v9, sereads=["rtc"], reads=["rtv"], writes=["wfin"],
               tag="e_v9%d" % hh)

        for tt in range(ntt):
            p.step("tensor", (lambda tt_: lambda eng: eng.transpose(
                pb[7][0:8, 0:128], wfin[:, tt_ * 8:(tt_ + 1) * 8], sb["ident"][:]))(tt),
                   reads=["wfin", "ident"], writes=["pb7"], tag="e_wt%d_%d" % (hh, tt))
            p.step("vector", (lambda tt_: lambda eng: eng.tensor_copy(
                wT[0:8, tt_ * 128:(tt_ + 1) * 128], pb[7][0:8, 0:128]))(tt),
                   reads=["pb7"], writes=["wT"], tag="e_we%d_%d" % (hh, tt))

        NTB = TH // 512 if TH >= 512 else 1
        TBW = min(512, TH)
        for e in range(NE):
            p.step("sync", (lambda e_: lambda eng: eng.dma_start(
                out=wrow[:], in_=wT[e_:e_ + 1, :]))(e),
                reads=["wT"], writes=["wrow"], dma=True, tag="e_wr%d_%d" % (hh, e))

            def em_wb(eng):
                inst = None
                for tb in range(NTB):
                    inst = eng.matmul(pb[6][:, 0:TBW], sb["onesrow"][:],
                                      wrow[0:1, tb * TBW:(tb + 1) * TBW], start=True, stop=True)
                return inst
            for tb in range(NTB):
                bk = 6 + (tb % 2)
                p.step("tensor", (lambda tb_, bk_: lambda eng: eng.matmul(
                    pb[bk_][:, 0:TBW], sb["onesrowb"][:],
                    wrow[0:1, tb_ * TBW:(tb_ + 1) * TBW], start=True, stop=True))(tb, bk),
                    reads=["wrow", "onesrowb"], writes=["pb%d" % bk], tag="e_wb%d_%d_%d" % (hh, e, tb))
                p.step("vector", (lambda tb_, bk_: lambda eng: eng.tensor_copy(
                    wbt[:, tb_ * TBW:(tb_ + 1) * TBW], pb[bk_][:, 0:TBW]))(tb, bk),
                    reads=["pb%d" % bk], writes=["wbt"], tag="e_we2%d_%d_%d" % (hh, e, tb))

            for f in range(16):
                par = f % 2
                srcg = dr["Wg"][e].rearrange("(dt p) n -> p dt n", p=128)[:, :, f * 128:(f + 1) * 128]
                srcu = dr["Wu"][e].rearrange("(dt p) n -> p dt n", p=128)[:, :, f * 128:(f + 1) * 128]
                for q in range(2):
                    p.step("gpsimd", (lambda s_, d_, q_: lambda eng: eng.dma_start(
                        out=d_[:].rearrange("p (dt c) -> p dt c", c=128)[:, q_ * 8:(q_ + 1) * 8, :],
                        in_=s_[:, q_ * 8:(q_ + 1) * 8, :]))(srcg, wgb[par], q),
                        writes=["wgb%d" % par], dma=True, tag="e_lg%d_%d_%d_%d" % (hh, e, f, q))
                    p.step("gpsimd", (lambda s_, d_, q_: lambda eng: eng.dma_start(
                        out=d_[:].rearrange("p (dt c) -> p dt c", c=128)[:, q_ * 8:(q_ + 1) * 8, :],
                        in_=s_[:, q_ * 8:(q_ + 1) * 8, :]))(srcu, wub[par], q),
                        writes=["wub%d" % par], dma=True, tag="e_lu%d_%d_%d_%d" % (hh, e, f, q))
                gb = [par * 4, par * 4 + 1]
                ub = [2 + par * 4 - par * 3, 0]
                gb = [0 + par * 4, 1 + par * 4]
                ubk = [2 + par * 4, 3 + par * 4]

                def em_gu(par_, f_):
                    def em(eng):
                        inst = None
                        for d in range(16):
                            lg = wgb[par_][:, d * 128:(d + 1) * 128]
                            lu = wub[par_][:, d * 128:(d + 1) * 128]
                            for tb in range(NTB):
                                inst = eng.matmul(pb[par_ * 4][:, 0:TBW] if tb == 0 else pb[par_ * 4 + 1][:, 0:TBW],
                                                  lg,
                                                  ln2x[:].rearrange("p (dt t) -> p dt t", t=TH)[:, d, tb * TBW:(tb + 1) * TBW],
                                                  start=(d == 0), stop=(d == 15))
                            for tb in range(NTB):
                                inst = eng.matmul(pb[par_ * 4 + 2][:, 0:TBW] if tb == 0 else pb[par_ * 4 + 3][:, 0:TBW],
                                                  lu,
                                                  ln2x[:].rearrange("p (dt t) -> p dt t", t=TH)[:, d, tb * TBW:(tb + 1) * TBW],
                                                  start=(d == 0), stop=(d == 15))
                        return inst
                    return em
                rbanks = ["pb%d" % (par * 4 + i) for i in range(4)]
                p.step("tensor", em_gu(par, f),
                       reads=["wgb%d" % par, "wub%d" % par, "ln2x"],
                       writes=rbanks, tag="e_gu%d_%d_%d" % (hh, e, f))

                def em_sg(par_):
                    def em(eng):
                        inst = None
                        for tb in range(NTB):
                            inst = eng.activation(sG[:, tb * TBW:(tb + 1) * TBW],
                                                  pb[par_ * 4 + min(tb, 1)][:, 0:TBW], AF.Silu)
                        return inst
                    return em
                p.step("scalar", em_sg(par), reads=rbanks[:2], writes=["sG"],
                       tag="e_sg%d_%d_%d" % (hh, e, f))

                def em_h(par_, f_):
                    def em(eng):
                        inst = None
                        for tb in range(NTB):
                            hslice = Hb[:, f_ * TH + tb * TBW: f_ * TH + (tb + 1) * TBW]
                            inst = eng.tensor_tensor(hslice, pb[par_ * 4 + 2 + min(tb, 1)][:, 0:TBW],
                                                     sG[:, tb * TBW:(tb + 1) * TBW], ALU.mult)
                            inst = eng.tensor_tensor(hslice, hslice,
                                                     wbt[:, tb * TBW:(tb + 1) * TBW], ALU.mult)
                        return inst
                    return em
                p.step("vector", em_h(par, f), reads=rbanks[2:] + ["sG", "wbt"],
                       writes=["Hb"], tag="e_h%d_%d_%d" % (hh, e, f))

            for dmt in range(16):
                par = dmt % 2
                srcd = dr["Wd"][e].rearrange("(ft p) n -> p ft n", p=128)[:, :, dmt * 128:(dmt + 1) * 128]
                for q in range(2):
                    p.step("gpsimd", (lambda s_, d_, q_: lambda eng: eng.dma_start(
                        out=d_[:].rearrange("p (ft c) -> p ft c", c=128)[:, q_ * 8:(q_ + 1) * 8, :],
                        in_=s_[:, q_ * 8:(q_ + 1) * 8, :]))(srcd, wdb[par], q),
                        writes=["wdb%d" % par], dma=True, tag="e_ld%d_%d_%d_%d" % (hh, e, dmt, q))

                def em_dmm(par_, dmt_):
                    def em(eng):
                        inst = None
                        for ft in range(16):
                            lw = wdb[par_][:, ft * 128:(ft + 1) * 128]
                            for tb in range(NTB):
                                inst = eng.matmul(pb[par_ * 4 + tb][:, 0:TBW], lw,
                                                  Hb[:].rearrange("p (ft t) -> p ft t", t=TH)[:, ft, tb * TBW:(tb + 1) * TBW],
                                                  start=(ft == 0), stop=(ft == 15))
                        return inst
                    return em
                rb2 = ["pb%d" % (par * 4 + tb) for tb in range(NTB)]
                p.step("tensor", em_dmm(par, dmt), reads=["wdb%d" % par, "Hb"],
                       writes=rb2, tag="e_dm%d_%d_%d" % (hh, e, dmt))

                def em_acc(par_, dmt_):
                    def em(eng):
                        inst = None
                        for tb in range(NTB):
                            xs = xnew[:, dmt_ * TH + tb * TBW: dmt_ * TH + (tb + 1) * TBW]
                            inst = eng.tensor_tensor(xs, xs, pb[par_ * 4 + tb][:, 0:TBW], ALU.add)
                        return inst
                    return em
                p.step("vector", em_acc(par, dmt), reads=rb2 + ["xnew"],
                       writes=["xnew"], tag="e_ac%d_%d_%d" % (hh, e, dmt))

        for dm in range(16):
            dst = self.out[dm * 128:(dm + 1) * 128, t0h:t0h + TH]
            p.step("sync", (lambda d_, dm_: lambda eng: eng.dma_start(
                out=d_, in_=xnew[:, dm_ * TH:(dm_ + 1) * TH]))(dst, dm),
                reads=["xnew"], dma=True, tag="e_out%d_%d" % (hh, dm))

    def emit(self):
        nc, p = self.nc, self.p
        import contextlib
        stack = contextlib.ExitStack()
        sems, dsems = {}, {}
        for e in ENGINES:
            sems[e] = stack.enter_context(nc.semaphore("sem_" + e))
        for e in ("sync", "gpsimd", "scalar"):
            dsems[e] = stack.enter_context(nc.semaphore("dsem_" + e))
        p.finalize()
        with stack:
            with nc.Block() as block:
                @block.sync
                def _(eng):
                    p.emit_engine("sync", eng, sems, dsems)
                    nd = sum(1 for s in p.steps["sync"] if s.dma)
                    if nd:
                        eng.wait_ge(dsems["sync"], 16 * nd)

                @block.gpsimd
                def _(eng):
                    p.emit_engine("gpsimd", eng, sems, dsems)
                    nd = sum(1 for s in p.steps["gpsimd"] if s.dma)
                    if nd:
                        eng.wait_ge(dsems["gpsimd"], 16 * nd)

                @block.tensor
                def _(eng):
                    p.emit_engine("tensor", eng, sems, dsems)

                @block.vector
                def _(eng):
                    p.emit_engine("vector", eng, sems, dsems)

                @block.scalar
                def _(eng):
                    p.emit_engine("scalar", eng, sems, dsems)
                    nd = sum(1 for s in p.steps["scalar"] if s.dma)
                    if nd:
                        eng.wait_ge(dsems["scalar"], 16 * nd)


def host_inputs(T, inputs, core):
    cons = make_consts(T)
    m = {
        "xT": np.ascontiguousarray(inputs["L0_attn_x"][core, :T, :].T),
        "Wqkv": inputs["L0_attn_Wqkv"],
        "Wz": inputs["L0_attn_Wz"],
        "Wab": np.ascontiguousarray(
            np.concatenate([inputs["L0_attn_Wa"], inputs["L0_attn_Wb"]], axis=1)),
        "convw": inputs["L0_attn_conv_w"],
        "A_log": inputs["L0_attn_A_log"],
        "dt_bias": inputs["L0_attn_dt_bias"],
        "norm_w": inputs["L0_attn_norm_w"],
        "Wout": inputs["L0_attn_Wout"],
        "Wg": inputs["L0_ffn_Wg"],
        "Wu": inputs["L0_ffn_Wu"],
        "Wd": inputs["L0_ffn_Wd"],
        "Wr": inputs["L0_ffn_Wr"],
        "temperature": inputs["L0_ffn_temperature"],
        "cu_seqlens": inputs["L0_attn_cu_seqlens"].astype(np.int32),
    }
    m.update(cons)
    return m


def kernel(**inputs):
    T = 2048
    k = Ker(T)
    nc = k.build()
    in_maps = [host_inputs(T, inputs, c) for c in range(NB)]
    res = run_bass_kernel_spmd(nc, in_maps, core_ids=list(range(NB)))
    outs = [np.ascontiguousarray(r["out"].T) for r in res.results]
    return np.stack(outs, axis=0)
